# version 01
# baseline (speedup 1.0000x reference)
import numpy as np
import ml_dtypes
import concourse.bass as bass
import concourse.mybir as mybir
from concourse.bass_utils import run_bass_kernel_spmd
from contextlib import ExitStack

F32 = mybir.dt.float32
BF16 = mybir.dt.bfloat16
AF = mybir.ActivationFunctionType
ALU = mybir.AluOpType
AX = mybir.AxisListType

CONST_VALS = (1e-5, 1e-6, 1.0, 0.5, 2.0, -1.0, 0.0, 1e-30)

T = 8192
D = 1024
DI_A = 2048
NCHUNK = 16
CHUNK = 512
NE = 32
H = 16
DK = DV = 64
CB = 128
NCB = T // CB



class Ctr:
    def __init__(self, sem, name=""):
        self.sem = sem
        self.n = 0
        self.name = name

    def inc(self, k=1):
        self.n += k
        return self.n


class Emit:
    ENGINES = ("sync", "gpsimd", "tensor", "vector", "scalar")

    def __init__(self):
        self.q = {e: [] for e in self.ENGINES}

    def _add(self, eng, fn):
        self.q[eng].append(fn)

    def sy(self, fn): self._add("sync", fn)
    def gp(self, fn): self._add("gpsimd", fn)
    def te(self, fn): self._add("tensor", fn)
    def ve(self, fn): self._add("vector", fn)
    def se(self, fn): self._add("scalar", fn)


class Ctx:
    def __init__(self, nc, top):
        self.nc = nc
        self.top = top
        self.em = Emit()
        E = top.enter_context
        self.ld = Ctr(E(nc.semaphore("s_ld")), "ld")
        self.st = Ctr(E(nc.semaphore("s_st")), "st")
        self.cs = Ctr(E(nc.semaphore("s_cs")), "cs")
        self.te = Ctr(E(nc.semaphore("s_te")), "te")
        self.ve = Ctr(E(nc.semaphore("s_ve")), "ve")
        self.se = Ctr(E(nc.semaphore("s_se")), "se")
        self.ge = Ctr(E(nc.semaphore("s_ge")), "ge")
        self.bar = Ctr(E(nc.semaphore("s_bar")), "bar")
        self.ps = [E(nc.psum_tensor(f"ps{i}", [128, 512], F32)) for i in range(6)]
        self.pt = [E(nc.psum_tensor(f"pt{i}", [128, 1024], BF16)) for i in range(2)]

    def barrier(self):
        em, c = self.em, self
        ld_n, st_n, cs_n = c.ld.n, c.st.n, c.cs.n
        em.sy(lambda e: e.wait_ge(c.ld.sem, ld_n))
        em.sy(lambda e: e.wait_ge(c.st.sem, st_n))
        em.gp(lambda e: e.wait_ge(c.cs.sem, cs_n))
        target = self.bar.n + 5
        for eng in Emit.ENGINES:
            self.em._add(eng, lambda e: e.sem_inc(c.bar.sem, 1))
            self.em._add(eng, lambda e: e.wait_ge(c.bar.sem, target))
        self.bar.n = target


def _r(ap_, pat, **kw):
    return ap_.rearrange(pat, **kw)



def emit_prologue(c, dram, sb):
    em, nc = c.em, c.nc

    def cast_dma(dst, src):
        em.gp(lambda e, a=dst, b=src: e.dma_start(out=a, in_=b).then_inc(c.cs.sem, 16))
        c.cs.inc(16)

    for e_i in range(NE):
        cast_dma(dram["guw_bf"][e_i], dram["guwT"][e_i])
        cast_dma(dram["dw_bf"][e_i], dram["dwT"][e_i])
    for name in ("ff_w1", "ff_w3", "ff_w2"):
        w = dram[name]
        nrow = w.shape[0]
        for i in range(0, nrow, 1024):
            cast_dma(dram[name + "_bf"][i:i + 1024], w[i:i + 1024])

    def ld(dst, src):
        em.sy(lambda e, a=dst, b=src: e.dma_start(out=a, in_=b).then_inc(c.ld.sem, 16))
        c.ld.inc(16)

    def ldcast(dst, src):
        cast_dma(dst, src)

    ld(sb["ident"][:], dram["ident_bf"])
    ld(sb["ident_f32"][:], dram["ident_f32"])
    ld(sb["ones_col_bf"][:], dram["ones_col_bf"])
    ld(sb["ones_row_bf"][:], dram["ones_row_bf"])
    ld(sb["ones_col_f32"][:], dram["ones_col_f32"])
    ld(sb["ones_row_f32"][:], dram["ones_row_f32"])
    ld(sb["mask_a"][:], dram["mask_a"])
    ld(sb["consts"][:], dram["const_row"])
    for i in range(6):
        em.ve(lambda e, i=i: e.memset(c.ps[i][:, :], 0.0))
    em.ve(lambda e: e.engine_nop().then_inc(c.ve.sem, 1))
    c.ve.inc()
    c.barrier()



def emit_attn_a_p1(c, dram, sb, st):
    em, nc = c.em, c.nc
    E = st.enter_context
    xt = E(nc.sbuf_tensor("a1_xt", [128, 8 * 512], F32))
    xn = E(nc.sbuf_tensor("a1_xn", [128, 8 * 512], BF16))
    st_m = E(nc.sbuf_tensor("a1_st_m", [1, 512], F32))
    st_ms = E(nc.sbuf_tensor("a1_st_ms", [1, 512], F32))
    st_r = E(nc.sbuf_tensor("a1_st_r", [1, 512], F32))
    st_mr = E(nc.sbuf_tensor("a1_st_mr", [1, 512], F32))
    rdB = E(nc.sbuf_tensor("a1_rdB", [128, 512], F32))
    mrB = E(nc.sbuf_tensor("a1_mrB", [128, 512], F32))
    tmp1 = E(nc.sbuf_tensor("a1_tmp1", [128, 512], F32))
    tmp2 = E(nc.sbuf_tensor("a1_tmp2", [128, 512], F32))
    qT = E(nc.sbuf_tensor("a1_qT", [128, 16 * 512], BF16))
    kT = E(nc.sbuf_tensor("a1_kT", [128, 16 * 512], BF16))
    vt = E(nc.sbuf_tensor("a1_vt", [128, 4 * 2048], BF16))
    wq = E(nc.sbuf_tensor("a1_wq", [128, 8 * 2048], BF16))
    wk = E(nc.sbuf_tensor("a1_wk", [128, 8 * 2048], BF16))
    wv = E(nc.sbuf_tensor("a1_wv", [128, 8 * 2048], BF16))
    bq = E(nc.sbuf_tensor("a1_bq", [128, 16], F32))
    bk = E(nc.sbuf_tensor("a1_bk", [128, 16], F32))
    nbq = E(nc.sbuf_tensor("a1_nbq", [128, 16], F32))
    nbk = E(nc.sbuf_tensor("a1_nbk", [128, 16], F32))
    bv_row = E(nc.sbuf_tensor("a1_bv_row", [1, 2048], BF16))

    for w_sb, name in ((wq, "aa_wq"), (wk, "aa_wk"), (wv, "aa_wv")):
        em.gp(lambda e, a=w_sb, b=dram[name]: e.dma_start(
            out=_r(a[:], "p (k n) -> p k n", n=2048),
            in_=_r(b, "(k p) n -> p k n", p=128)).then_inc(c.cs.sem, 16))
        c.cs.inc(16)
    em.sy(lambda e: e.dma_start(out=bq[:], in_=dram["aa_bq_col"]).then_inc(c.ld.sem, 16)); c.ld.inc(16)
    em.sy(lambda e: e.dma_start(out=bk[:], in_=dram["aa_bk_col"]).then_inc(c.ld.sem, 16)); c.ld.inc(16)
    em.sy(lambda e: e.dma_start(out=nbq[:], in_=dram["aa_nbq_col"]).then_inc(c.ld.sem, 16)); c.ld.inc(16)
    em.sy(lambda e: e.dma_start(out=nbk[:], in_=dram["aa_nbk_col"]).then_inc(c.ld.sem, 16)); c.ld.inc(16)
    em.gp(lambda e: e.dma_start(out=bv_row[:], in_=dram["aa_bv_row"]).then_inc(c.cs.sem, 16)); c.cs.inc(16)
    ld_w_done = c.ld.n
    cs_w_done = c.cs.n

    x0r = _r(dram["xT_in"], "(k p) t -> p k t", p=128)
    qr = _r(dram["aa_q"], "(m p) t -> p m t", p=128)
    kr = _r(dram["aa_k"], "(m p) t -> p m t", p=128)

    ps = c.ps

    def chunk(ci):
        t0 = ci * CHUNK
        em.ve(lambda e, v=c.st.n: e.wait_ge(c.st.sem, v))
        em.sy(lambda e: e.dma_start(out=_r(xt[:], "p (k t) -> p k t", t=512),
                                    in_=x0r[:, :, t0:t0 + 512]).then_inc(c.ld.sem, 16))
        ldx = c.ld.inc(16)
        em.te(lambda e: e.wait_ge(c.ld.sem, max(ldx, ld_w_done)))
        em.te(lambda e: e.wait_ge(c.cs.sem, cs_w_done))
        for i in range(8):
            em.te(lambda e, i=i: e.matmul(ps[0][0:1, :], sb["ones_col_f32"][:],
                                          xt[:, i * 512:(i + 1) * 512],
                                          start=(i == 0), stop=(i == 7)))
        em.se(lambda e, v=ldx: e.wait_ge(c.ld.sem, v))
        for i in range(8):
            buf = tmp1 if i % 2 == 0 else tmp2
            if i >= 2:
                em.se(lambda e, v=c.te.n + (i - 1): e.wait_ge(c.te.sem, v))
            em.se(lambda e, i=i, buf=buf: e.activation(
                buf[:], xt[:, i * 512:(i + 1) * 512], AF.Square).then_inc(c.se.sem, 1))
            se_sq = c.se.inc()
            em.te(lambda e, v=se_sq: e.wait_ge(c.se.sem, v))
            mm = lambda e, i=i, buf=buf: e.matmul(ps[0][32:33, :], sb["ones_col_f32"][:],
                                                  buf[:], start=(i == 0), stop=(i == 7))
            if i == 7:
                em.te(lambda e, mm=mm: mm(e).then_inc(c.te.sem, 1))
            else:
                em.te(lambda e, mm=mm: mm(e).then_inc(c.te.sem, 1))
        te_stat = c.te.n + 8
        c.te.n = te_stat
        em.ve(lambda e: e.wait_ge(c.te.sem, te_stat))
        em.ve(lambda e: e.tensor_scalar(st_m[:], ps[0][0:1, :], 1.0 / D, None, ALU.mult))
        em.ve(lambda e: e.tensor_scalar(st_ms[:], ps[0][32:33, :], 1.0 / D, None, ALU.mult))
        em.ve(lambda e: e.tensor_tensor(st_r[:], st_m[:], st_m[:], ALU.mult))
        em.ve(lambda e: e.tensor_tensor(st_r[:], st_ms[:], st_r[:], ALU.subtract))
        em.ve(lambda e: e.engine_nop().then_inc(c.ve.sem, 1))
        ve_var = c.ve.inc()
        em.se(lambda e: e.wait_ge(c.ve.sem, ve_var))
        em.se(lambda e: e.activation(st_r[:], st_r[:], AF.Ln, bias=1e-5))
        em.se(lambda e: e.activation(st_r[:], st_r[:], AF.Exp, scale=-0.5))
        em.se(lambda e: e.copy(st_mr[:], st_m[:]).then_inc(c.se.sem, 1))
        se_rstd = c.se.inc()
        em.ve(lambda e: e.wait_ge(c.se.sem, se_rstd))
        em.ve(lambda e: e.tensor_tensor(st_mr[:], st_mr[:], st_r[:], ALU.mult))
        em.ve(lambda e: e.engine_nop().then_inc(c.ve.sem, 1))
        ve_stats = c.ve.inc()
        em.te(lambda e: e.wait_ge(c.ve.sem, ve_stats))
        em.te(lambda e: e.matmul(ps[1][:, :], sb["ones_row_f32"][:], st_r[:],
                                 start=True, stop=True))
        em.te(lambda e: e.matmul(ps[2][:, :], sb["ones_row_f32"][:], st_mr[:],
                                 start=True, stop=True).then_inc(c.te.sem, 1))
        te_bc = c.te.inc()
        em.ve(lambda e: e.wait_ge(c.te.sem, te_bc))
        em.ve(lambda e: e.tensor_copy(rdB[:], ps[1][:, :]))
        em.ve(lambda e: e.tensor_copy(mrB[:], ps[2][:, :]))
        for i in range(8):
            em.ve(lambda e, i=i: e.tensor_tensor(
                tmp1[:], xt[:, i * 512:(i + 1) * 512], rdB[:], ALU.mult))
            em.ve(lambda e, i=i: e.tensor_tensor(
                xn[:, i * 512:(i + 1) * 512], tmp1[:], mrB[:], ALU.subtract))
        em.ve(lambda e: e.engine_nop().then_inc(c.ve.sem, 1))
        ve_xn = c.ve.inc()
        em.te(lambda e: e.wait_ge(c.ve.sem, ve_xn))

        for (w_sb, b_col, nb_col, out_sb) in ((wq, bq, nbq, qT), (wk, bk, nbk, kT)):
            def proj_qk(w_sb=w_sb, b_col=b_col, nb_col=nb_col, out_sb=out_sb):
                for m in range(16):
                    bank = 3 + (m % 2)
                    ve_need = c.ve.n
                    if m >= 2:
                        em.te(lambda e, v=ve_need: e.wait_ge(c.ve.sem, v))
                    for k in range(8):
                        mmf = lambda e, m=m, k=k, bank=bank, w_sb=w_sb: e.matmul(
                            ps[bank][:, :], w_sb[:, k * 2048 + m * 128: k * 2048 + m * 128 + 128],
                            xn[:, k * 512:(k + 1) * 512], start=(k == 0), stop=(k == 7))
                        if k == 7:
                            em.te(lambda e, mmf=mmf: mmf(e).then_inc(c.te.sem, 1))
                        else:
                            em.te(mmf)
                    te_mm = c.te.inc()
                    em.se(lambda e, v=te_mm: e.wait_ge(c.te.sem, v))
                    em.se(lambda e, v=c.ve.n: e.wait_ge(c.ve.sem, v))
                    em.se(lambda e, m=m, bank=bank, b_col=b_col: e.activation(
                        tmp1[:], ps[bank][:, :], AF.Relu, bias=b_col[:, m:m + 1]))
                    em.se(lambda e, m=m, bank=bank, nb_col=nb_col: e.activation(
                        tmp2[:], ps[bank][:, :], AF.Relu, bias=nb_col[:, m:m + 1], scale=-1.0))
                    em.se(lambda e: e.activation(tmp2[:], tmp2[:], AF.Exp, scale=-1.0).then_inc(c.se.sem, 1))
                    se_phi = c.se.inc()
                    em.ve(lambda e, v=se_phi: e.wait_ge(c.se.sem, v))
                    em.ve(lambda e, m=m, out_sb=out_sb: e.tensor_tensor(
                        out_sb[:, m * 512:(m + 1) * 512], tmp1[:], tmp2[:], ALU.add))
                    em.ve(lambda e: e.engine_nop().then_inc(c.ve.sem, 1))
                    c.ve.inc()
            proj_qk()
        cs_bv = cs_w_done
        em.te(lambda e: e.wait_ge(c.cs.sem, cs_bv))
        em.te(lambda e, v=c.ve.n: e.wait_ge(c.ve.sem, v))
        for tt in range(4):
            for n in range(4):
                g = tt * 4 + n
                bank = 3 + (g % 2)
                if g >= 2:
                    em.te(lambda e, v=c.ve.n: e.wait_ge(c.ve.sem, v))
                for k in range(8):
                    em.te(lambda e, tt=tt, n=n, k=k, bank=bank: e.matmul(
                        ps[bank][:, :], xn[:, k * 512 + tt * 128: k * 512 + tt * 128 + 128],
                        wv[:, k * 2048 + n * 512: k * 2048 + n * 512 + 512],
                        start=(k == 0), stop=False))
                em.te(lambda e, n=n, bank=bank: e.matmul(
                    ps[bank][:, :], sb["ones_row_bf"][0:1, 0:128],
                    bv_row[0:1, n * 512:(n + 1) * 512],
                    start=False, stop=True).then_inc(c.te.sem, 1))
                te_v = c.te.inc()
                em.ve(lambda e, v=te_v: e.wait_ge(c.te.sem, v))
                em.ve(lambda e, tt=tt, n=n, bank=bank: e.tensor_copy(
                    vt[:, tt * 2048 + n * 512: tt * 2048 + n * 512 + 512], ps[bank][:, :]))
                em.ve(lambda e: e.engine_nop().then_inc(c.ve.sem, 1))
                c.ve.inc()
        ve_all = c.ve.n
        em.sy(lambda e, v=ve_all: e.wait_ge(c.ve.sem, v))
        em.sy(lambda e, t0=t0: e.dma_start(
            out=qr[:, :, t0:t0 + 512],
            in_=_r(qT[:], "p (m t) -> p m t", t=512)).then_inc(c.st.sem, 16))
        c.st.inc(16)
        em.sy(lambda e, t0=t0: e.dma_start(
            out=kr[:, :, t0:t0 + 512],
            in_=_r(kT[:], "p (m t) -> p m t", t=512)).then_inc(c.st.sem, 16))
        c.st.inc(16)
        em.sy(lambda e, t0=t0: e.dma_start(
            out=_r(dram["aa_v"][t0:t0 + 512, :], "(c p) n -> p c n", p=128),
            in_=_r(vt[:], "p (c n) -> p c n", n=2048)).then_inc(c.st.sem, 16))
        c.st.inc(16)
        em.sy(lambda e, v=c.te.n: e.wait_ge(c.te.sem, v))

    for ci in range(NCHUNK):
        chunk(ci)
    c.barrier()



def emit_attn_a_p2(c, dram, sb, st):
    em, nc = c.em, c.nc
    E = st.enter_context
    S = E(nc.sbuf_tensor("a2_S", [128, 16 * 2048], BF16))
    z = E(nc.sbuf_tensor("a2_z", [128, 16], F32))
    zbf = E(nc.sbuf_tensor("a2_zbf", [128, 16], BF16))
    qT = E(nc.sbuf_tensor("a2_qT", [128, 16 * 512], BF16))
    kT = E(nc.sbuf_tensor("a2_kT", [128, 16 * 512], BF16))
    vt = E(nc.sbuf_tensor("a2_vt", [128, 4 * 2048], BF16))
    ktok = E(nc.sbuf_tensor("a2_ktok", [128, 4 * 2048], BF16))
    msc = E(nc.sbuf_tensor("a2_msc", [128, 4 * 512], BF16))
    yT = E(nc.sbuf_tensor("a2_yT", [128, 16 * 512], BF16))
    den = E(nc.sbuf_tensor("a2_den", [1, 512], F32))
    rdB = E(nc.sbuf_tensor("a2_rdB", [128, 512], F32))

    em.gp(lambda e: e.memset(S[:], 0.0))
    em.gp(lambda e: e.memset(z[:], 0.0))
    em.gp(lambda e: e.memset(zbf[:], 0.0))
    em.gp(lambda e: e.engine_nop().then_inc(c.ge.sem, 1))
    ge_init = c.ge.inc()
    cs_w = c.cs.n
    em.te(lambda e: e.wait_ge(c.ge.sem, ge_init))
    em.te(lambda e: e.wait_ge(c.cs.sem, cs_w))
    em.ve(lambda e: e.wait_ge(c.ge.sem, ge_init))

    qr = _r(dram["aa_q"], "(m p) t -> p m t", p=128)
    kr = _r(dram["aa_k"], "(m p) t -> p m t", p=128)
    yr = _r(dram["aa_y"], "(m p) t -> p m t", p=128)
    ps, pt = c.ps, c.pt

    def chunk(ci):
        t0 = ci * CHUNK
        em.ve(lambda e, v=c.st.n: e.wait_ge(c.st.sem, v))
        em.sy(lambda e: e.dma_start(out=_r(qT[:], "p (m t) -> p m t", t=512),
                                    in_=qr[:, :, t0:t0 + 512]).then_inc(c.ld.sem, 16))
        c.ld.inc(16)
        em.sy(lambda e: e.dma_start(out=_r(kT[:], "p (m t) -> p m t", t=512),
                                    in_=kr[:, :, t0:t0 + 512]).then_inc(c.ld.sem, 16))
        c.ld.inc(16)
        em.sy(lambda e: e.dma_start(
            out=_r(vt[:], "p (c n) -> p c n", n=2048),
            in_=_r(dram["aa_v"][t0:t0 + 512, :], "(c p) n -> p c n", p=128)).then_inc(c.ld.sem, 16))
        c.ld.inc(16)
        ldx = c.ld.n
        em.te(lambda e, v=ldx: e.wait_ge(c.ld.sem, v))

        for m in range(4):
            bank = m % 2
            if ci > 0 or m >= 2:
                em.te(lambda e, v=c.ve.n: e.wait_ge(c.ve.sem, v))
            for i in range(16):
                mmf = lambda e, m=m, i=i, bank=bank: e.matmul(
                    ps[bank][:, :], kT[:, i * 512 + m * 128: i * 512 + m * 128 + 128],
                    qT[:, i * 512:(i + 1) * 512], start=(i == 0), stop=(i == 15))
                if i == 15:
                    em.te(lambda e, mmf=mmf: mmf(e).then_inc(c.te.sem, 1))
                else:
                    em.te(mmf)
            te_s = c.te.inc()
            em.ve(lambda e, v=te_s: e.wait_ge(c.te.sem, v))
            em.ve(lambda e, m=m, bank=bank: e.tensor_tensor(
                msc[:, m * 512:(m + 1) * 512], ps[bank][:, :],
                sb["mask_a"][:, m * 512:(m + 1) * 512], ALU.mult))
            em.ve(lambda e: e.engine_nop().then_inc(c.ve.sem, 1))
            c.ve.inc()
        ve_msc = c.ve.n

        for tt in range(4):
            ptb = pt[tt % 2]
            if tt >= 2:
                em.te(lambda e, v=c.ve.n: e.wait_ge(c.ve.sem, v))
            for half in range(2):
                if half == 1:
                    em.te(lambda e, v=c.ve.n: e.wait_ge(c.ve.sem, v))
                for i8 in range(8):
                    i = half * 8 + i8
                    trf = lambda e, tt=tt, i=i, i8=i8, ptb=ptb: e.transpose(
                        ptb[:, i8 * 128:(i8 + 1) * 128],
                        kT[:, i * 512 + tt * 128: i * 512 + tt * 128 + 128], sb["ident"][:])
                    if i8 == 7:
                        em.te(lambda e, trf=trf: trf(e).then_inc(c.te.sem, 1))
                    else:
                        em.te(trf)
                te_tr = c.te.inc()
                em.ve(lambda e, v=te_tr: e.wait_ge(c.te.sem, v))
                em.ve(lambda e, tt=tt, half=half, ptb=ptb: e.tensor_copy(
                    ktok[:, tt * 2048 + half * 1024: tt * 2048 + half * 1024 + 1024], ptb[:, :]))
                em.ve(lambda e: e.engine_nop().then_inc(c.ve.sem, 1))
                c.ve.inc()

        em.te(lambda e, v=ve_msc: e.wait_ge(c.ve.sem, v))
        for m in range(4):
            em.te(lambda e, m=m: e.matmul(
                ps[5][0:1, :], sb["ones_col_bf"][:], msc[:, m * 512:(m + 1) * 512],
                start=(m == 0), stop=False))
        for i in range(16):
            mmf = lambda e, i=i: e.matmul(
                ps[5][0:1, :], zbf[:, i:i + 1], qT[:, i * 512:(i + 1) * 512],
                start=False, stop=(i == 15))
            if i == 15:
                em.te(lambda e, mmf=mmf: mmf(e).then_inc(c.te.sem, 1))
            else:
                em.te(mmf)
        te_den = c.te.inc()
        em.ve(lambda e, v=te_den: e.wait_ge(c.te.sem, v))
        em.ve(lambda e: e.tensor_scalar(den[:], ps[5][0:1, :], 1e-6, None, ALU.add))
        em.ve(lambda e: e.tensor_scalar(sb["junk"][:], sb["junk"][:], 1.0, None, ALU.mult))
        em.ve(lambda e: e.reciprocal(den[:], den[:]))
        em.ve(lambda e: e.engine_nop().then_inc(c.ve.sem, 1))
        ve_den = c.ve.inc()
        em.te(lambda e, v=ve_den: e.wait_ge(c.ve.sem, v))
        em.te(lambda e: e.matmul(ps[4][:, :], sb["ones_row_f32"][:], den[:],
                                 start=True, stop=True).then_inc(c.te.sem, 1))
        te_bc = c.te.inc()
        em.ve(lambda e, v=te_bc: e.wait_ge(c.te.sem, v))
        em.ve(lambda e: e.tensor_copy(rdB[:], ps[4][:, :]))
        em.ve(lambda e: e.engine_nop().then_inc(c.ve.sem, 1))
        ve_rdb = c.ve.inc()

        em.te(lambda e, v=ve_rdb: e.wait_ge(c.ve.sem, v))
        for j in range(16):
            bank = 2 + (j % 2)
            if j >= 2:
                em.te(lambda e, v=c.ve.n: e.wait_ge(c.ve.sem, v))
            first = True
            if ci > 0:
                for i in range(16):
                    em.te(lambda e, i=i, j=j, bank=bank, first=first and i == 0: e.matmul(
                        ps[bank][:, :], S[:, i * 2048 + j * 128: i * 2048 + j * 128 + 128],
                        qT[:, i * 512:(i + 1) * 512], start=(i == 0), stop=False))
                first = False
            for m in range(4):
                last = (m == 3)
                mmf = lambda e, m=m, j=j, bank=bank, first=(first and m == 0), last=last: e.matmul(
                    ps[bank][:, :], vt[:, m * 2048 + j * 128: m * 2048 + j * 128 + 128],
                    msc[:, m * 512:(m + 1) * 512], start=first, stop=last)
                if last:
                    em.te(lambda e, mmf=mmf: mmf(e).then_inc(c.te.sem, 1))
                else:
                    em.te(mmf)
            te_num = c.te.inc()
            em.ve(lambda e, v=te_num: e.wait_ge(c.te.sem, v))
            em.ve(lambda e, j=j, bank=bank: e.tensor_tensor(
                yT[:, j * 512:(j + 1) * 512], ps[bank][:, :], rdB[:], ALU.mult))
            em.ve(lambda e: e.engine_nop().then_inc(c.ve.sem, 1))
            c.ve.inc()

        for i in range(16):
            for n in range(4):
                g = i * 4 + n
                bank = (g % 2)
                if g >= 2:
                    em.te(lambda e, v=c.ve.n: e.wait_ge(c.ve.sem, v))
                for tt in range(4):
                    last = (tt == 3)
                    mmf = lambda e, i=i, n=n, tt=tt, bank=bank, last=last: e.matmul(
                        ps[bank][:, :], ktok[:, tt * 2048 + i * 128: tt * 2048 + i * 128 + 128],
                        vt[:, tt * 2048 + n * 512: tt * 2048 + n * 512 + 512],
                        start=(tt == 0), stop=last)
                    if last:
                        em.te(lambda e, mmf=mmf: mmf(e).then_inc(c.te.sem, 1))
                    else:
                        em.te(mmf)
                te_sd = c.te.inc()
                em.ve(lambda e, v=te_sd: e.wait_ge(c.te.sem, v))
                em.ve(lambda e, i=i, n=n, bank=bank: e.tensor_tensor(
                    S[:, i * 2048 + n * 512: i * 2048 + n * 512 + 512],
                    ps[bank][:, :],
                    S[:, i * 2048 + n * 512: i * 2048 + n * 512 + 512], ALU.add))
                em.ve(lambda e: e.engine_nop().then_inc(c.ve.sem, 1))
                c.ve.inc()
        em.te(lambda e, v=c.ve.n: e.wait_ge(c.ve.sem, v))
        for i in range(16):
            for tt in range(4):
                last = (tt == 3)
                mmf = lambda e, i=i, tt=tt, last=last: e.matmul(
                    ps[5][:, i:i + 1], ktok[:, tt * 2048 + i * 128: tt * 2048 + i * 128 + 128],
                    sb["ones_col_bf"][:], start=(tt == 0), stop=last)
                if last and i == 15:
                    em.te(lambda e, mmf=mmf: mmf(e).then_inc(c.te.sem, 1))
                else:
                    em.te(mmf)
        te_z = c.te.inc()
        em.ve(lambda e, v=te_z: e.wait_ge(c.te.sem, v))
        em.ve(lambda e: e.tensor_tensor(zbf[:], ps[5][:, 0:16], z[:], ALU.add))
        em.ve(lambda e: e.tensor_tensor(z[:], ps[5][:, 0:16], z[:], ALU.add))
        em.ve(lambda e: e.engine_nop().then_inc(c.ve.sem, 1))
        c.ve.inc()

        ve_yT = c.ve.n
        em.sy(lambda e, v=ve_yT: e.wait_ge(c.ve.sem, v))
        em.sy(lambda e, t0=t0: e.dma_start(
            out=yr[:, :, t0:t0 + 512],
            in_=_r(yT[:], "p (m t) -> p m t", t=512)).then_inc(c.st.sem, 16))
        c.st.inc(16)
        em.sy(lambda e, v=c.te.n: e.wait_ge(c.te.sem, v))
        em.sy(lambda e, v=c.ve.n: e.wait_ge(c.ve.sem, v))

    for ci in range(NCHUNK):
        chunk(ci)
    if "dbg_z" in dram:
        em.sy(lambda e, v=c.ve.n: e.wait_ge(c.ve.sem, v))
        em.sy(lambda e: e.dma_start(out=dram["dbg_z"], in_=z[:]).then_inc(c.st.sem, 16))
        c.st.inc(16)
        em.sy(lambda e: e.dma_start(out=dram["dbg_den"], in_=den[:]).then_inc(c.st.sem, 16))
        c.st.inc(16)
        em.sy(lambda e: e.dma_start(out=dram["dbg_rdB"], in_=rdB[:]).then_inc(c.st.sem, 16))
        c.st.inc(16)
    c.barrier()



def emit_attn_a_p3(c, dram, sb, st):
    em, nc = c.em, c.nc
    E = st.enter_context
    yT = E(nc.sbuf_tensor("a3_yT", [128, 16 * 512], BF16))
    xt = E(nc.sbuf_tensor("a3_xt", [128, 8 * 512], F32))
    wo = E(nc.sbuf_tensor("a3_wo", [128, 16 * 1024], BF16))
    bo_row = E(nc.sbuf_tensor("a3_bo_row", [1, 1024], BF16))

    em.gp(lambda e: e.dma_start(
        out=_r(wo[:], "p (k n) -> p k n", n=1024),
        in_=_r(dram["aa_wo"], "(k p) n -> p k n", p=128)).then_inc(c.cs.sem, 16))
    c.cs.inc(16)
    em.gp(lambda e: e.dma_start(out=bo_row[:], in_=dram["aa_bo_row"]).then_inc(c.cs.sem, 16))
    c.cs.inc(16)
    cs_w = c.cs.n
    em.te(lambda e: e.wait_ge(c.cs.sem, cs_w))

    yr = _r(dram["aa_y"], "(m p) t -> p m t", p=128)
    x0r = _r(dram["xT_in"], "(k p) t -> p k t", p=128)
    x1r = _r(dram["x1T"], "(k p) t -> p k t", p=128)
    ps = c.ps

    def chunk(ci):
        t0 = ci * CHUNK
        em.ve(lambda e, v=c.st.n: e.wait_ge(c.st.sem, v))
        em.sy(lambda e, t0=t0: e.dma_start(out=_r(yT[:], "p (m t) -> p m t", t=512),
                                           in_=yr[:, :, t0:t0 + 512]).then_inc(c.ld.sem, 16))
        c.ld.inc(16)
        em.sy(lambda e, t0=t0: e.dma_start(out=_r(xt[:], "p (k t) -> p k t", t=512),
                                           in_=x0r[:, :, t0:t0 + 512]).then_inc(c.ld.sem, 16))
        ldx = c.ld.inc(16)
        em.te(lambda e, v=ldx: e.wait_ge(c.ld.sem, v))
        for m in range(8):
            bank = m % 2
            if ci > 0 or m >= 2:
                em.te(lambda e, v=c.ve.n: e.wait_ge(c.ve.sem, v))
            for i in range(16):
                em.te(lambda e, m=m, i=i, bank=bank: e.matmul(
                    ps[bank][:, :], wo[:, i * 1024 + m * 128: i * 1024 + m * 128 + 128],
                    yT[:, i * 512:(i + 1) * 512], start=(i == 0), stop=False))
            em.te(lambda e, m=m, bank=bank: e.matmul(
                ps[bank][:, :], bo_row[0:1, m * 128:(m + 1) * 128],
                sb["ones_row_bf"][:], start=False, stop=True).then_inc(c.te.sem, 1))
            te_o = c.te.inc()
            em.ve(lambda e, v=te_o: e.wait_ge(c.te.sem, v))
            em.ve(lambda e, m=m, bank=bank: e.tensor_tensor(
                xt[:, m * 512:(m + 1) * 512], ps[bank][:, :],
                xt[:, m * 512:(m + 1) * 512], ALU.add))
            em.ve(lambda e: e.engine_nop().then_inc(c.ve.sem, 1))
            c.ve.inc()
        ve_x1 = c.ve.n
        em.sy(lambda e, v=ve_x1: e.wait_ge(c.ve.sem, v))
        em.sy(lambda e, t0=t0: e.dma_start(
            out=x1r[:, :, t0:t0 + 512],
            in_=_r(xt[:], "p (k t) -> p k t", t=512)).then_inc(c.st.sem, 16))
        c.st.inc(16)
        em.sy(lambda e, v=c.te.n: e.wait_ge(c.te.sem, v))

    for ci in range(NCHUNK):
        chunk(ci)
    c.barrier()




def emit_ln_to(c, sb, xt, xn, aux, ldx, ld_extra=0):
    em, ps = c.em, c.ps
    em.te(lambda e, v=max(ldx, ld_extra): e.wait_ge(c.ld.sem, v))
    for i in range(8):
        em.te(lambda e, i=i: e.matmul(ps[0][0:1, :], sb["ones_col_f32"][:],
                                      xt[:, i * 512:(i + 1) * 512],
                                      start=(i == 0), stop=(i == 7)))
    em.se(lambda e, v=ldx: e.wait_ge(c.ld.sem, v))
    te0 = c.te.n
    for i in range(8):
        buf = aux["tmp1"] if i % 2 == 0 else aux["tmp2"]
        if i >= 2:
            em.se(lambda e, v=te0 + (i - 1): e.wait_ge(c.te.sem, v))
        em.se(lambda e, i=i, buf=buf: e.activation(
            buf[:], xt[:, i * 512:(i + 1) * 512], AF.Square).then_inc(c.se.sem, 1))
        se_sq = c.se.inc()
        em.te(lambda e, v=se_sq: e.wait_ge(c.se.sem, v))
        em.te(lambda e, i=i, buf=buf: e.matmul(
            ps[0][32:33, :], sb["ones_col_f32"][:], buf[:],
            start=(i == 0), stop=(i == 7)).then_inc(c.te.sem, 1))
        c.te.inc()
    te_stat = c.te.n
    em.ve(lambda e, v=te_stat: e.wait_ge(c.te.sem, v))
    em.ve(lambda e: e.tensor_scalar(aux["st_m"][:], ps[0][0:1, :], 1.0 / D, None, ALU.mult))
    em.ve(lambda e: e.tensor_scalar(aux["st_ms"][:], ps[0][32:33, :], 1.0 / D, None, ALU.mult))
    em.ve(lambda e: e.tensor_tensor(aux["st_r"][:], aux["st_m"][:], aux["st_m"][:], ALU.mult))
    em.ve(lambda e: e.tensor_tensor(aux["st_r"][:], aux["st_ms"][:], aux["st_r"][:], ALU.subtract))
    em.ve(lambda e: e.engine_nop().then_inc(c.ve.sem, 1))
    ve_var = c.ve.inc()
    em.se(lambda e, v=ve_var: e.wait_ge(c.ve.sem, v))
    em.se(lambda e: e.activation(aux["st_r"][:], aux["st_r"][:], AF.Ln, bias=1e-5))
    em.se(lambda e: e.activation(aux["st_r"][:], aux["st_r"][:], AF.Exp, scale=-0.5))
    em.se(lambda e: e.copy(aux["st_mr"][:], aux["st_m"][:]).then_inc(c.se.sem, 1))
    se_rstd = c.se.inc()
    em.ve(lambda e, v=se_rstd: e.wait_ge(c.se.sem, v))
    em.ve(lambda e: e.tensor_tensor(aux["st_mr"][:], aux["st_mr"][:], aux["st_r"][:], ALU.mult))
    em.ve(lambda e: e.engine_nop().then_inc(c.ve.sem, 1))
    ve_stats = c.ve.inc()
    em.te(lambda e, v=ve_stats: e.wait_ge(c.ve.sem, v))
    em.te(lambda e: e.matmul(ps[1][:, :], sb["ones_row_f32"][:], aux["st_r"][:],
                             start=True, stop=True))
    em.te(lambda e: e.matmul(ps[2][:, :], sb["ones_row_f32"][:], aux["st_mr"][:],
                             start=True, stop=True).then_inc(c.te.sem, 1))
    te_bc = c.te.inc()
    em.ve(lambda e, v=te_bc: e.wait_ge(c.te.sem, v))
    em.ve(lambda e: e.tensor_copy(aux["rdB"][:], ps[1][:, :]))
    em.ve(lambda e: e.tensor_copy(aux["mrB"][:], ps[2][:, :]))
    for i in range(8):
        em.ve(lambda e, i=i: e.tensor_tensor(
            aux["tmp1"][:], xt[:, i * 512:(i + 1) * 512], aux["rdB"][:], ALU.mult))
        em.ve(lambda e, i=i: e.tensor_tensor(
            xn[:, i * 512:(i + 1) * 512], aux["tmp1"][:], aux["mrB"][:], ALU.subtract))
    em.ve(lambda e: e.engine_nop().then_inc(c.ve.sem, 1))
    return c.ve.inc()


def _mk_ln_aux(c, st, pref):
    E = st.enter_context
    nc = c.nc
    return {
        "st_m": E(nc.sbuf_tensor(pref + "_m", [1, 512], F32)),
        "st_ms": E(nc.sbuf_tensor(pref + "_ms", [1, 512], F32)),
        "st_r": E(nc.sbuf_tensor(pref + "_r", [1, 512], F32)),
        "st_mr": E(nc.sbuf_tensor(pref + "_mr", [1, 512], F32)),
        "rdB": E(nc.sbuf_tensor(pref + "_rdB", [128, 512], F32)),
        "mrB": E(nc.sbuf_tensor(pref + "_mrB", [128, 512], F32)),
        "tmp1": E(nc.sbuf_tensor(pref + "_tmp1", [128, 512], F32)),
        "tmp2": E(nc.sbuf_tensor(pref + "_tmp2", [128, 512], F32)),
    }


def emit_moe_gate(c, dram, sb, st, x_in, tag=""):
    em, nc = c.em, c.nc
    E = st.enter_context
    xt = E(nc.sbuf_tensor(tag + "mg_xt", [128, 8 * 512], F32))
    xn = E(nc.sbuf_tensor(tag + "mg_xn", [128, 8 * 512], BF16))
    aux = _mk_ln_aux(c, st, tag + "mg")
    gw = E(nc.sbuf_tensor(tag + "mg_gw", [128, 8 * 32], BF16))
    pexp = [E(nc.sbuf_tensor(f"{tag}mg_p{tt}", [128, 32], F32)) for tt in range(4)]
    work = [E(nc.sbuf_tensor(f"{tag}mg_w{tt}", [128, 32], F32)) for tt in range(4)]
    tmp = [E(nc.sbuf_tensor(f"{tag}mg_t{tt}", [128, 32], F32)) for tt in range(4)]
    mcol = [E(nc.sbuf_tensor(f"{tag}mg_mc{tt}", [128, 1], F32)) for tt in range(4)]
    nmx = [E(nc.sbuf_tensor(f"{tag}mg_nm{tt}", [128, 1], F32)) for tt in range(4)]
    comb = E(nc.sbuf_tensor(tag + "mg_comb", [128, 4 * 32], F32))

    em.gp(lambda e: e.dma_start(
        out=_r(gw[:], "p (k n) -> p k n", n=32),
        in_=_r(dram["gate_wT"], "(k p) n -> p k n", p=128)).then_inc(c.cs.sem, 16))
    cs_w = c.cs.inc(16)
    em.te(lambda e: e.wait_ge(c.cs.sem, cs_w))

    xr = _r(dram[x_in], "(k p) t -> p k t", p=128)
    xnr = _r(dram["moe_xn"], "(k p) t -> p k t", p=128)
    ps = c.ps

    def chunk(ci):
        t0 = ci * CHUNK
        em.ve(lambda e, v=c.st.n: e.wait_ge(c.st.sem, v))
        em.sy(lambda e: e.dma_start(out=_r(xt[:], "p (k t) -> p k t", t=512),
                                    in_=xr[:, :, t0:t0 + 512]).then_inc(c.ld.sem, 16))
        ldx = c.ld.inc(16)
        ve_xn = emit_ln_to(c, sb, xt, xn, aux, ldx)
        em.sy(lambda e, v=ve_xn: e.wait_ge(c.ve.sem, v))
        em.sy(lambda e, t0=t0: e.dma_start(
            out=xnr[:, :, t0:t0 + 512],
            in_=_r(xn[:], "p (k t) -> p k t", t=512)).then_inc(c.st.sem, 16))
        c.st.inc(16)
        em.te(lambda e, v=ve_xn: e.wait_ge(c.ve.sem, v))
        for tt in range(4):
            bank = 2 + tt
            for k in range(8):
                mm = lambda e, tt=tt, k=k, bank=bank: e.matmul(
                    ps[bank][:, 0:32], xn[:, k * 512 + tt * 128: k * 512 + tt * 128 + 128],
                    gw[:, k * 32:(k + 1) * 32], start=(k == 0), stop=(k == 7))
                if k == 7:
                    em.te(lambda e, mm=mm: mm(e).then_inc(c.te.sem, 1))
                else:
                    em.te(mm)
            c.te.inc()
        te_lg = c.te.n
        for tt in range(4):
            em.ve(lambda e, v=te_lg - 3 + tt: e.wait_ge(c.te.sem, v))
            em.ve(lambda e, tt=tt: e.reduce_max(mcol[tt][:], ps[2 + tt][:, 0:32], AX.X))
        for tt in range(4):
            em.ve(lambda e, tt=tt: e.tensor_scalar(nmx[tt][:], mcol[tt][:], -1.0, None, ALU.mult))
        em.ve(lambda e: e.engine_nop().then_inc(c.ve.sem, 1))
        ve_mx = c.ve.inc()
        em.se(lambda e, v=ve_mx: e.wait_ge(c.ve.sem, v))
        for tt in range(4):
            em.se(lambda e, tt=tt: e.activation(pexp[tt][:], ps[2 + tt][:, 0:32],
                                                AF.Exp, bias=nmx[tt][:]))
        em.se(lambda e: e.copy(work[0][:], pexp[0][:]))
        em.se(lambda e: e.copy(work[1][:], pexp[1][:]))
        em.se(lambda e: e.copy(work[2][:], pexp[2][:]))
        em.se(lambda e: e.copy(work[3][:], pexp[3][:]).then_inc(c.se.sem, 1))
        se_p = c.se.inc()
        em.ve(lambda e, v=se_p: e.wait_ge(c.se.sem, v))
        for it in range(8):
            for tt in range(4):
                em.ve(lambda e, tt=tt: e.reduce_max(mcol[tt][:], work[tt][:], AX.X))
            if it < 7:
                for tt in range(4):
                    em.ve(lambda e, tt=tt: e.tensor_scalar(
                        tmp[tt][:], work[tt][:], mcol[tt][:], -1e30, ALU.is_ge, ALU.mult))
                for tt in range(4):
                    em.ve(lambda e, tt=tt: e.tensor_tensor(
                        work[tt][:], work[tt][:], tmp[tt][:], ALU.add))
        for tt in range(4):
            em.ve(lambda e, tt=tt: e.tensor_scalar(
                tmp[tt][:], pexp[tt][:], mcol[tt][:], None, ALU.is_ge))
        for tt in range(4):
            em.ve(lambda e, tt=tt: e.tensor_tensor(
                comb[:, tt * 32:(tt + 1) * 32], pexp[tt][:], tmp[tt][:], ALU.mult))
        for tt in range(4):
            em.ve(lambda e, tt=tt: e.reduce_sum(mcol[tt][:], comb[:, tt * 32:(tt + 1) * 32], AX.X))
        for tt in range(4):
            em.ve(lambda e, tt=tt: e.reciprocal(nmx[tt][:], mcol[tt][:]))
        for tt in range(4):
            em.ve(lambda e, tt=tt: e.tensor_scalar(
                comb[:, tt * 32:(tt + 1) * 32], comb[:, tt * 32:(tt + 1) * 32],
                nmx[tt][:], None, ALU.mult))
        em.ve(lambda e: e.engine_nop().then_inc(c.ve.sem, 1))
        ve_cb = c.ve.inc()
        em.sy(lambda e, v=ve_cb: e.wait_ge(c.ve.sem, v))
        em.sy(lambda e, t0=t0: e.dma_start(
            out=_r(dram["moe_comb"][t0:t0 + 512, :], "(tt p) e -> p tt e", p=128),
            in_=_r(comb[:], "p (tt e) -> p tt e", e=32)).then_inc(c.st.sem, 16))
        c.st.inc(16)
        em.sy(lambda e, v=c.te.n: e.wait_ge(c.te.sem, v))
        em.sy(lambda e, v=c.ve.n: e.wait_ge(c.ve.sem, v))

    for ci in range(NCHUNK):
        chunk(ci)
    c.barrier()


def emit_moe_exp(c, dram, sb, st, x_in, x_out, tag=""):
    em, nc = c.em, c.nc
    E = st.enter_context
    BLK = 2048
    acc = E(nc.sbuf_tensor(tag + "me_acc", [128, 16 * 1024], F32))
    xnb = E(nc.sbuf_tensor(tag + "me_xn", [128, 8 * 2048], BF16))
    wg = [E(nc.sbuf_tensor(f"{tag}me_wg{b}", [128, 8 * 1024], BF16)) for b in range(2)]
    wd = [E(nc.sbuf_tensor(f"{tag}me_wd{b}", [128, 4 * 1024], BF16)) for b in range(2)]
    combb = E(nc.sbuf_tensor(tag + "me_comb", [128, 16 * 32], F32))
    hT = E(nc.sbuf_tensor(tag + "me_hT", [128, 4 * 512], BF16))
    sg = E(nc.sbuf_tensor(tag + "me_sg", [128, 512], BF16))
    x1c = E(nc.sbuf_tensor(tag + "me_x1c", [128, 8 * 512], F32))
    x2c = E(nc.sbuf_tensor(tag + "me_x2c", [128, 8 * 512], F32))

    xr = _r(dram[x_in], "(k p) t -> p k t", p=128)
    xo = _r(dram[x_out], "(k p) t -> p k t", p=128)
    xnr = _r(dram["moe_xn"], "(k p) t -> p k t", p=128)
    ps = c.ps

    def block(bi):
        tb = bi * BLK
        em.sy(lambda e, v=c.ve.n: e.wait_ge(c.ve.sem, v))
        em.sy(lambda e, tb=tb: e.dma_start(
            out=_r(xnb[:], "p (k t) -> p k t", t=BLK),
            in_=xnr[:, :, tb:tb + BLK]).then_inc(c.ld.sem, 16))
        c.ld.inc(16)
        em.sy(lambda e, tb=tb: e.dma_start(
            out=_r(combb[:], "p (tt e) -> p tt e", e=32),
            in_=_r(dram["moe_comb"][tb:tb + BLK, :], "(tt p) e -> p tt e", p=128)).then_inc(c.ld.sem, 16))
        ld_blk = c.ld.inc(16)
        em.gp(lambda e, v=c.te.n: e.wait_ge(c.te.sem, v))
        em.gp(lambda e: e.memset(acc[:], 0.0).then_inc(c.ge.sem, 1))
        ge_acc = c.ge.inc()
        em.te(lambda e, v=ld_blk: e.wait_ge(c.ld.sem, v))
        em.ve(lambda e, v=ld_blk: e.wait_ge(c.ld.sem, v))
        em.ve(lambda e, v=ge_acc: e.wait_ge(c.ge.sem, v))

        te_expert_done = {}
        for ei in range(NE):
            b = ei % 2
            if ei >= 2:
                em.sy(lambda e, v=te_expert_done[ei - 2]: e.wait_ge(c.te.sem, v))
            em.sy(lambda e, ei=ei, b=b: e.dma_start(
                out=_r(wg[b][:], "p (k n) -> p k n", n=1024),
                in_=_r(dram["guw_bf"][ei], "(k p) n -> p k n", p=128)).then_inc(c.ld.sem, 16))
            c.ld.inc(16)
            em.sy(lambda e, ei=ei, b=b: e.dma_start(
                out=_r(wd[b][:], "p (k n) -> p k n", n=1024),
                in_=_r(dram["dw_bf"][ei], "(k p) n -> p k n", p=128)).then_inc(c.ld.sem, 16))
            ld_w = c.ld.inc(16)
            em.te(lambda e, v=ld_w: e.wait_ge(c.ld.sem, v))
            for g4 in range(4):
                tg = g4 * 512
                for pr in range(4):
                    bg = pr % 2
                    bu = 2 + pr % 2
                    if pr >= 2 or g4 > 0 or ei > 0 or bi > 0:
                        em.te(lambda e, v=c.ve.n: e.wait_ge(c.ve.sem, v))
                    for k in range(8):
                        em.te(lambda e, pr=pr, k=k, bg=bg, b=b, tg=tg: e.matmul(
                            ps[bg][:, :], wg[b][:, k * 1024 + pr * 128: k * 1024 + pr * 128 + 128],
                            xnb[:, k * 2048 + tg: k * 2048 + tg + 512],
                            start=(k == 0), stop=(k == 7)))
                    for k in range(8):
                        mm = lambda e, pr=pr, k=k, bu=bu, b=b, tg=tg: e.matmul(
                            ps[bu][:, :], wg[b][:, k * 1024 + 512 + pr * 128: k * 1024 + 512 + pr * 128 + 128],
                            xnb[:, k * 2048 + tg: k * 2048 + tg + 512],
                            start=(k == 0), stop=(k == 7))
                        if k == 7:
                            em.te(lambda e, mm=mm: mm(e).then_inc(c.te.sem, 1))
                        else:
                            em.te(mm)
                    te_gu = c.te.inc()
                    em.se(lambda e, v=te_gu: e.wait_ge(c.te.sem, v))
                    em.se(lambda e, v=c.ve.n: e.wait_ge(c.ve.sem, v))
                    em.se(lambda e, bg=bg: e.activation(sg[:], ps[bg][:, :], AF.Silu).then_inc(c.se.sem, 1))
                    se_sg = c.se.inc()
                    em.ve(lambda e, v=se_sg: e.wait_ge(c.se.sem, v))
                    em.ve(lambda e, pr=pr, bu=bu: e.tensor_tensor(
                        hT[:, pr * 512:(pr + 1) * 512], ps[bu][:, :], sg[:], ALU.mult))
                    em.ve(lambda e: e.engine_nop().then_inc(c.ve.sem, 1))
                    c.ve.inc()
                ve_h = c.ve.n
                em.te(lambda e, v=ve_h: e.wait_ge(c.ve.sem, v))
                for tt4 in range(4):
                    tok = g4 * 4 + tt4
                    for nh in range(2):
                        bank = 4 + (tt4 * 2 + nh) % 2
                        if tt4 * 2 + nh >= 2:
                            em.te(lambda e, v=c.ve.n: e.wait_ge(c.ve.sem, v))
                        for k in range(4):
                            mm = lambda e, tt4=tt4, nh=nh, k=k, bank=bank, b=b: e.matmul(
                                ps[bank][:, :], hT[:, k * 512 + tt4 * 128: k * 512 + tt4 * 128 + 128],
                                wd[b][:, k * 1024 + nh * 512: k * 1024 + nh * 512 + 512],
                                start=(k == 0), stop=(k == 3))
                            if k == 3:
                                em.te(lambda e, mm=mm: mm(e).then_inc(c.te.sem, 1))
                            else:
                                em.te(mm)
                        te_dn = c.te.inc()
                        em.ve(lambda e, v=te_dn: e.wait_ge(c.te.sem, v))
                        em.ve(lambda e, tok=tok, nh=nh, bank=bank, ei=ei: e.scalar_tensor_tensor(
                            acc[:, tok * 1024 + nh * 512: tok * 1024 + nh * 512 + 512],
                            ps[bank][:, :], combb[:, tok * 32 + ei: tok * 32 + ei + 1],
                            acc[:, tok * 1024 + nh * 512: tok * 1024 + nh * 512 + 512],
                            ALU.mult, ALU.add))
                        em.ve(lambda e: e.engine_nop().then_inc(c.ve.sem, 1))
                        c.ve.inc()
            te_expert_done[ei] = c.te.n
        for g4 in range(4):
            t0 = tb + g4 * 512
            em.sy(lambda e, t0=t0, g4=g4: e.dma_start(
                out=_r(x1c[:], "p (k t) -> p k t", t=512),
                in_=xr[:, :, t0:t0 + 512]).then_inc(c.ld.sem, 16))
            ld_x1 = c.ld.inc(16)
            em.te(lambda e, v=ld_x1: e.wait_ge(c.ld.sem, v))
            em.ve(lambda e, v=ld_x1: e.wait_ge(c.ld.sem, v))
            for j in range(8):
                bank = j % 2
                if g4 > 0 or j >= 2:
                    em.te(lambda e, v=c.ve.n: e.wait_ge(c.ve.sem, v))
                for tt4 in range(4):
                    tok = g4 * 4 + tt4
                    mm = lambda e, j=j, tok=tok, tt4=tt4, bank=bank: e.transpose(
                        ps[bank][:, tt4 * 128:(tt4 + 1) * 128],
                        acc[:, tok * 1024 + j * 128: tok * 1024 + j * 128 + 128],
                        sb["ident_f32"][:])
                    if tt4 == 3:
                        em.te(lambda e, mm=mm: mm(e).then_inc(c.te.sem, 1))
                    else:
                        em.te(mm)
                te_tr = c.te.inc()
                em.ve(lambda e, v=te_tr: e.wait_ge(c.te.sem, v))
                em.ve(lambda e, j=j, bank=bank: e.tensor_tensor(
                    x2c[:, j * 512:(j + 1) * 512], ps[bank][:, :],
                    x1c[:, j * 512:(j + 1) * 512], ALU.add))
                em.ve(lambda e: e.engine_nop().then_inc(c.ve.sem, 1))
                c.ve.inc()
            ve_x2 = c.ve.n
            em.sy(lambda e, v=ve_x2: e.wait_ge(c.ve.sem, v))
            em.sy(lambda e, t0=t0: e.dma_start(
                out=xo[:, :, t0:t0 + 512],
                in_=_r(x2c[:], "p (k t) -> p k t", t=512)).then_inc(c.st.sem, 16))
            c.st.inc(16)
            em.ve(lambda e, v=c.st.n: e.wait_ge(c.st.sem, v))
        em.sy(lambda e, v=c.te.n: e.wait_ge(c.te.sem, v))
        em.sy(lambda e, v=c.ve.n: e.wait_ge(c.ve.sem, v))

    for bi in range(T // BLK):
        block(bi)
    c.barrier()




def emit_attn_b_p1(c, dram, sb, st, x_in):
    em, nc = c.em, c.nc
    E = st.enter_context
    xt = E(nc.sbuf_tensor("b1_xt", [128, 8 * 512], F32))
    xln = E(nc.sbuf_tensor("b1_xln", [128, 8 * 512], F32))
    xln_bf = E(nc.sbuf_tensor("b1_xlnbf", [128, 8 * 512], BF16))
    aux = _mk_ln_aux(c, st, "b1")
    hist = E(nc.sbuf_tensor("b1_hist", [128, 8 * 3], F32))
    scr = E(nc.sbuf_tensor("b1_scr", [128, 8 * 515], F32))
    h0 = xt
    cpre = xt
    ext = scr
    qtok = scr
    lasb = scr
    h_bf = E(nc.sbuf_tensor("b1_hbf", [128, 8 * 512], BF16))
    qsq = E(nc.sbuf_tensor("b1_qsq", [128, 1024], F32))
    ssq = E(nc.sbuf_tensor("b1_ssq", [128, 16], F32))
    rn = E(nc.sbuf_tensor("b1_rn", [128, 32], F32))
    qv = E(nc.sbuf_tensor("b1_qv", [128, 4 * 1024], BF16))
    qhat = qv
    vsb = qv
    cg = E(nc.sbuf_tensor("b1_cg", [128, 8 * 512], BF16))
    qch = cg
    gsb = cg
    absb = E(nc.sbuf_tensor("b1_a", [128, 512], F32))
    btsb = E(nc.sbuf_tensor("b1_bt", [128, 4 * 16], F32))
    s1 = E(nc.sbuf_tensor("b1_s1", [64, 512], BF16))
    wq = E(nc.sbuf_tensor("b1_wq", [128, 8 * 1024], BF16))
    wk = E(nc.sbuf_tensor("b1_wk", [128, 8 * 1024], BF16))
    wv = E(nc.sbuf_tensor("b1_wv", [128, 8 * 1024], BF16))
    wa = E(nc.sbuf_tensor("b1_wa", [128, 8 * 1024], BF16))
    wb = E(nc.sbuf_tensor("b1_wb", [128, 8 * 16], BF16))
    wu1 = E(nc.sbuf_tensor("b1_wu1", [128, 8 * 64], BF16))
    wu2 = E(nc.sbuf_tensor("b1_wu2", [64, 1024], BF16))
    nw_col = E(nc.sbuf_tensor("b1_nw", [128, 8], F32))
    cw_col = E(nc.sbuf_tensor("b1_cw", [128, 32], F32))
    cb_col = E(nc.sbuf_tensor("b1_cb", [128, 8], F32))
    ab_col = E(nc.sbuf_tensor("b1_ab", [128, 8], F32))
    bb_row = E(nc.sbuf_tensor("b1_bb", [1, 16], BF16))
    u1b_col = E(nc.sbuf_tensor("b1_u1b", [64, 1], F32))
    u2b_col = E(nc.sbuf_tensor("b1_u2b", [128, 8], F32))

    for w_sb, name, n in ((wq, "ab_q_w", 1024), (wk, "ab_k_w", 1024),
                          (wv, "ab_v_w", 1024), (wa, "ab_alpha_w", 1024),
                          (wb, "ab_beta_w", 16), (wu1, "ab_u1_w", 64)):
        em.gp(lambda e, a=w_sb, b=dram[name], n=n: e.dma_start(
            out=_r(a[:], "p (k n) -> p k n", n=n),
            in_=_r(b, "(k p) n -> p k n", p=128)).then_inc(c.cs.sem, 16))
        c.cs.inc(16)
    em.gp(lambda e: e.dma_start(out=wu2[:], in_=dram["ab_u2_w"]).then_inc(c.cs.sem, 16))
    c.cs.inc(16)
    em.gp(lambda e: e.dma_start(out=bb_row[:], in_=_r(dram["ab_beta_b"], "a b -> a b")).then_inc(c.cs.sem, 16))
    c.cs.inc(16)
    for d_sb, name in ((nw_col, "ab_norm_in_w"), (cw_col, "ab_conv_w"),
                       (cb_col, "ab_conv_b"), (ab_col, "ab_alpha_b"),
                       (u1b_col, "ab_u1_b"), (u2b_col, "ab_u2_b")):
        em.sy(lambda e, a=d_sb, b=dram[name]: e.dma_start(out=a[:], in_=b).then_inc(c.ld.sem, 16))
        c.ld.inc(16)
    em.gp(lambda e: e.memset(hist[:], 0.0).then_inc(c.ge.sem, 1))
    ge_h = c.ge.inc()
    cs_w = c.cs.n
    ld_w = c.ld.n
    em.te(lambda e: e.wait_ge(c.cs.sem, cs_w))
    em.se(lambda e: e.wait_ge(c.ld.sem, ld_w))
    em.ve(lambda e: e.wait_ge(c.ge.sem, ge_h))
    em.ve(lambda e: e.wait_ge(c.ld.sem, ld_w))

    xr = _r(dram[x_in], "(k p) t -> p k t", p=128)
    xlnr = _r(dram["ab_xln"], "(k p) t -> p k t", p=128)
    qhr = _r(dram["ab_q"], "(k p) t -> p k t", p=128)
    khr = _r(dram["ab_k"], "(k p) t -> p k t", p=128)
    lar = _r(dram["ab_la"], "(k p) t -> p k t", p=128)
    gr = _r(dram["ab_gate"], "(k p) t -> p k t", p=128)
    ps, pt = c.ps, c.pt

    def chunk(ci):
        t0 = ci * CHUNK
        em.ve(lambda e, v=c.st.n: e.wait_ge(c.st.sem, v))
        em.sy(lambda e: e.dma_start(out=_r(xt[:], "p (k t) -> p k t", t=512),
                                    in_=xr[:, :, t0:t0 + 512]).then_inc(c.ld.sem, 16))
        ldx = c.ld.inc(16)
        ve_xln = emit_ln_to(c, sb, xt, xln, aux, ldx)
        for i in range(8):
            em.ve(lambda e, i=i: e.tensor_copy(
                xln_bf[:, i * 512:(i + 1) * 512], xln[:, i * 512:(i + 1) * 512]))
        em.ve(lambda e: e.engine_nop().then_inc(c.ve.sem, 1))
        ve_xbf = c.ve.inc()
        em.sy(lambda e, v=ve_xbf: e.wait_ge(c.ve.sem, v))
        em.sy(lambda e, t0=t0: e.dma_start(
            out=xlnr[:, :, t0:t0 + 512],
            in_=_r(xln[:], "p (k t) -> p k t", t=512)).then_inc(c.st.sem, 16))
        c.st.inc(16)
        em.te(lambda e, v=ve_xln: e.wait_ge(c.ve.sem, v))
        te0 = c.te.n
        for i in range(8):
            buf = aux["tmp1"] if i % 2 == 0 else aux["tmp2"]
            em.se(lambda e, v=ve_xln: e.wait_ge(c.ve.sem, v)) if i == 0 else None
            if i >= 2:
                em.se(lambda e, v=te0 + (i - 1): e.wait_ge(c.te.sem, v))
            em.se(lambda e, i=i, buf=buf: e.activation(
                buf[:], xln[:, i * 512:(i + 1) * 512], AF.Square).then_inc(c.se.sem, 1))
            se_sq = c.se.inc()
            em.te(lambda e, v=se_sq: e.wait_ge(c.se.sem, v))
            em.te(lambda e, i=i, buf=buf: e.matmul(
                ps[0][0:1, :], sb["ones_col_f32"][:], buf[:],
                start=(i == 0), stop=(i == 7)).then_inc(c.te.sem, 1))
            c.te.inc()
        te_rms = c.te.n
        em.ve(lambda e, v=te_rms: e.wait_ge(c.te.sem, v))
        em.ve(lambda e: e.tensor_scalar(aux["st_ms"][:], ps[0][0:1, :], 1.0 / D, None, ALU.mult))
        em.ve(lambda e: e.engine_nop().then_inc(c.ve.sem, 1))
        ve_ms = c.ve.inc()
        em.se(lambda e, v=ve_ms: e.wait_ge(c.ve.sem, v))
        em.se(lambda e: e.activation(aux["st_r"][:], aux["st_ms"][:], AF.Ln, bias=1e-6))
        em.se(lambda e: e.activation(aux["st_r"][:], aux["st_r"][:], AF.Exp, scale=-0.5).then_inc(c.se.sem, 1))
        se_r = c.se.inc()
        em.te(lambda e, v=se_r: e.wait_ge(c.se.sem, v))
        em.te(lambda e: e.matmul(ps[1][:, :], sb["ones_row_f32"][:], aux["st_r"][:],
                                 start=True, stop=True).then_inc(c.te.sem, 1))
        te_bc = c.te.inc()
        em.ve(lambda e, v=te_bc: e.wait_ge(c.te.sem, v))
        em.ve(lambda e: e.tensor_copy(aux["rdB"][:], ps[1][:, :]))
        em.ve(lambda e: e.tensor_scalar(sb["junk"][:], sb["junk"][:], 1.0, None, ALU.mult))
        for i in range(8):
            em.ve(lambda e, i=i: e.tensor_tensor(
                aux["tmp1"][:], xln[:, i * 512:(i + 1) * 512], aux["rdB"][:], ALU.mult))
            em.ve(lambda e, i=i: e.tensor_scalar(
                h0[:, i * 512:(i + 1) * 512], aux["tmp1"][:], nw_col[:, i:i + 1], None, ALU.mult))
        for i in range(8):
            em.ve(lambda e, i=i: e.tensor_copy(ext[:, i * 515: i * 515 + 3],
                                               hist[:, i * 3:(i + 1) * 3]))
            em.ve(lambda e, i=i: e.tensor_copy(ext[:, i * 515 + 3: i * 515 + 515],
                                               h0[:, i * 512:(i + 1) * 512]))
        for i in range(8):
            em.ve(lambda e, i=i: e.tensor_scalar(
                cpre[:, i * 512:(i + 1) * 512], ext[:, i * 515: i * 515 + 512],
                cw_col[:, i * 4 + 0: i * 4 + 1], None, ALU.mult))
        for j in range(1, 4):
            for i in range(8):
                em.ve(lambda e, i=i, j=j: e.scalar_tensor_tensor(
                    cpre[:, i * 512:(i + 1) * 512], ext[:, i * 515 + j: i * 515 + j + 512],
                    cw_col[:, i * 4 + j: i * 4 + j + 1],
                    cpre[:, i * 512:(i + 1) * 512], ALU.mult, ALU.add))
        for i in range(8):
            em.ve(lambda e, i=i: e.tensor_copy(hist[:, i * 3:(i + 1) * 3],
                                               ext[:, i * 515 + 512: i * 515 + 515]))
        em.ve(lambda e: e.engine_nop().then_inc(c.ve.sem, 1))
        ve_cpre = c.ve.inc()
        em.se(lambda e, v=ve_cpre: e.wait_ge(c.ve.sem, v))
        for i in range(8):
            em.se(lambda e, i=i: e.activation(
                h_bf[:, i * 512:(i + 1) * 512], cpre[:, i * 512:(i + 1) * 512],
                AF.Silu, bias=cb_col[:, i:i + 1]).then_inc(c.se.sem, 1))
            c.se.inc()
        se_h = c.se.n
        em.te(lambda e, v=se_h: e.wait_ge(c.se.sem, v))

        for (w_sb, out_r) in ((wq, qhr), (wk, khr)):
            def qk(w_sb=w_sb, out_r=out_r):
                for tt in range(4):
                    for nh in range(2):
                        bank = (tt * 2 + nh) % 2
                        em.te(lambda e, v=c.ve.n: e.wait_ge(c.ve.sem, v))
                        for k in range(8):
                            mm = lambda e, tt=tt, nh=nh, k=k, bank=bank, w_sb=w_sb: e.matmul(
                                ps[bank][:, :], h_bf[:, k * 512 + tt * 128: k * 512 + tt * 128 + 128],
                                w_sb[:, k * 1024 + nh * 512: k * 1024 + nh * 512 + 512],
                                start=(k == 0), stop=(k == 7))
                            if k == 7:
                                em.te(lambda e, mm=mm: mm(e).then_inc(c.te.sem, 1))
                            else:
                                em.te(mm)
                        te_q = c.te.inc()
                        em.ve(lambda e, v=te_q: e.wait_ge(c.te.sem, v))
                        em.ve(lambda e, tt=tt, nh=nh, bank=bank: e.tensor_copy(
                            qtok[:, tt * 1024 + nh * 512: tt * 1024 + nh * 512 + 512],
                            ps[bank][:, :]))
                        em.ve(lambda e: e.engine_nop().then_inc(c.ve.sem, 1))
                        c.ve.inc()
                for tt in range(4):
                    em.se(lambda e, v=c.ve.n: e.wait_ge(c.ve.sem, v))
                    em.se(lambda e, tt=tt: e.activation(
                        qsq[:, 0:512], qtok[:, tt * 1024: tt * 1024 + 512], AF.Square))
                    em.se(lambda e, tt=tt: e.activation(
                        qsq[:, 512:1024], qtok[:, tt * 1024 + 512: tt * 1024 + 1024],
                        AF.Square).then_inc(c.se.sem, 1))
                    se_sq2 = c.se.inc()
                    em.ve(lambda e, v=se_sq2: e.wait_ge(c.se.sem, v))
                    for hh in range(16):
                        em.ve(lambda e, hh=hh: e.reduce_sum(
                            ssq[:, hh:hh + 1], qsq[:, hh * 64:(hh + 1) * 64], AX.X))
                    em.ve(lambda e: e.engine_nop().then_inc(c.ve.sem, 1))
                    ve_ss = c.ve.inc()
                    em.se(lambda e, v=ve_ss: e.wait_ge(c.ve.sem, v))
                    em.se(lambda e: e.activation(rn[:, 0:16], ssq[:, 0:16], AF.Ln, bias=1e-30))
                    em.se(lambda e: e.activation(rn[:, 0:16], rn[:, 0:16], AF.Exp, scale=0.5).then_inc(c.se.sem, 1))
                    se_rn = c.se.inc()
                    em.ve(lambda e, v=se_rn: e.wait_ge(c.se.sem, v))
                    em.ve(lambda e: e.tensor_scalar(rn[:, 16:32], rn[:, 0:16], 1e-6, None, ALU.add))
                    em.ve(lambda e: e.tensor_scalar(sb["junk"][:], sb["junk"][:], 1.0, None, ALU.mult))
                    em.ve(lambda e: e.reciprocal(rn[:, 0:16], rn[:, 16:32]))
                    em.ve(lambda e: e.tensor_scalar(sb["junk"][:], sb["junk"][:], 1.0, None, ALU.mult))
                    for hh in range(16):
                        em.ve(lambda e, tt=tt, hh=hh: e.tensor_scalar(
                            qhat[:, tt * 1024 + hh * 64: tt * 1024 + (hh + 1) * 64],
                            qtok[:, tt * 1024 + hh * 64: tt * 1024 + (hh + 1) * 64],
                            rn[:, hh:hh + 1], None, ALU.mult))
                    em.ve(lambda e: e.engine_nop().then_inc(c.ve.sem, 1))
                    c.ve.inc()
                ve_qh = c.ve.n
                em.te(lambda e, v=ve_qh: e.wait_ge(c.ve.sem, v))
                em.ve(lambda e, v=c.st.n: e.wait_ge(c.st.sem, v))
                for tt in range(4):
                    ptb = pt[tt % 2]
                    if tt >= 2:
                        em.te(lambda e, v=c.ve.n: e.wait_ge(c.ve.sem, v))
                    for j in range(8):
                        mm = lambda e, tt=tt, j=j, ptb=ptb: e.transpose(
                            ptb[:, j * 128:(j + 1) * 128],
                            qhat[:, tt * 1024 + j * 128: tt * 1024 + j * 128 + 128],
                            sb["ident"][:])
                        if j == 7:
                            em.te(lambda e, mm=mm: mm(e).then_inc(c.te.sem, 1))
                        else:
                            em.te(mm)
                    te_tr = c.te.inc()
                    em.ve(lambda e, v=te_tr: e.wait_ge(c.te.sem, v))
                    for j in range(8):
                        em.ve(lambda e, tt=tt, j=j, ptb=ptb: e.tensor_copy(
                            qch[:, j * 512 + tt * 128: j * 512 + tt * 128 + 128],
                            ptb[:, j * 128:(j + 1) * 128]))
                    em.ve(lambda e: e.engine_nop().then_inc(c.ve.sem, 1))
                    c.ve.inc()
                ve_qch = c.ve.n
                em.sy(lambda e, v=ve_qch: e.wait_ge(c.ve.sem, v))
                em.sy(lambda e, t0=t0, out_r=out_r: e.dma_start(
                    out=out_r[:, :, t0:t0 + 512],
                    in_=_r(qch[:], "p (k t) -> p k t", t=512)).then_inc(c.st.sem, 16))
                c.st.inc(16)
            qk()
        for tt in range(4):
            for nh in range(2):
                g = tt * 2 + nh
                bank = g % 2
                em.te(lambda e, v=c.ve.n: e.wait_ge(c.ve.sem, v))
                for k in range(8):
                    mm = lambda e, tt=tt, nh=nh, k=k, bank=bank: e.matmul(
                        ps[bank][:, :], h_bf[:, k * 512 + tt * 128: k * 512 + tt * 128 + 128],
                        wv[:, k * 1024 + nh * 512: k * 1024 + nh * 512 + 512],
                        start=(k == 0), stop=(k == 7))
                    if k == 7:
                        em.te(lambda e, mm=mm: mm(e).then_inc(c.te.sem, 1))
                    else:
                        em.te(mm)
                te_v = c.te.inc()
                em.ve(lambda e, v=te_v: e.wait_ge(c.te.sem, v))
                em.ve(lambda e, tt=tt, nh=nh, bank=bank: e.tensor_copy(
                    vsb[:, tt * 1024 + nh * 512: tt * 1024 + nh * 512 + 512], ps[bank][:, :]))
                em.ve(lambda e: e.engine_nop().then_inc(c.ve.sem, 1))
                c.ve.inc()
        em.sy(lambda e, v=c.ve.n: e.wait_ge(c.ve.sem, v))
        em.sy(lambda e, t0=t0: e.dma_start(
            out=_r(dram["ab_v"][t0:t0 + 512, :], "(q p) n -> p q n", p=128),
            in_=_r(vsb[:], "p (q n) -> p q n", n=1024)).then_inc(c.st.sem, 16))
        c.st.inc(16)
        for m in range(8):
            bank = m % 2
            em.te(lambda e, v=c.ve.n: e.wait_ge(c.ve.sem, v))
            em.te(lambda e, v=c.se.n: e.wait_ge(c.se.sem, v))
            for k in range(8):
                mm = lambda e, m=m, k=k, bank=bank: e.matmul(
                    ps[bank][:, :], wa[:, k * 1024 + m * 128: k * 1024 + m * 128 + 128],
                    h_bf[:, k * 512:(k + 1) * 512], start=(k == 0), stop=(k == 7))
                if k == 7:
                    em.te(lambda e, mm=mm: mm(e).then_inc(c.te.sem, 1))
                else:
                    em.te(mm)
            te_a = c.te.inc()
            em.se(lambda e, v=te_a: e.wait_ge(c.te.sem, v))
            em.se(lambda e, m=m, bank=bank: e.activation(
                absb[:], ps[bank][:, :], AF.Sigmoid, bias=ab_col[:, m:m + 1]))
            em.se(lambda e, m=m: e.activation(
                lasb[:, m * 512:(m + 1) * 512], absb[:], AF.Ln).then_inc(c.se.sem, 1))
            c.se.inc()
        se_la = c.se.n
        em.sy(lambda e, v=se_la: e.wait_ge(c.se.sem, v))
        em.sy(lambda e, t0=t0: e.dma_start(
            out=lar[:, :, t0:t0 + 512],
            in_=_r(lasb[:, 0:4096], "p (k t) -> p k t", t=512)).then_inc(c.st.sem, 16))
        c.st.inc(16)
        for tt in range(4):
            em.te(lambda e, v=c.se.n: e.wait_ge(c.se.sem, v))
            for k in range(8):
                em.te(lambda e, tt=tt, k=k: e.matmul(
                    ps[4][:, 0:16], h_bf[:, k * 512 + tt * 128: k * 512 + tt * 128 + 128],
                    wb[:, k * 16:(k + 1) * 16], start=(k == 0), stop=False))
            em.te(lambda e: e.matmul(
                ps[4][:, 0:16], sb["ones_row_bf"][0:1, 0:128], bb_row[:],
                start=False, stop=True).then_inc(c.te.sem, 1))
            te_b = c.te.inc()
            em.se(lambda e, v=te_b: e.wait_ge(c.te.sem, v))
            em.se(lambda e, tt=tt: e.activation(
                btsb[:, tt * 16:(tt + 1) * 16], ps[4][:, 0:16], AF.Sigmoid).then_inc(c.se.sem, 1))
            c.se.inc()
        se_bt = c.se.n
        em.sy(lambda e, v=se_bt: e.wait_ge(c.se.sem, v))
        em.sy(lambda e, t0=t0: e.dma_start(
            out=_r(dram["ab_beta"][t0:t0 + 512, :], "(q p) n -> p q n", p=128),
            in_=_r(btsb[:], "p (q n) -> p q n", n=16)).then_inc(c.st.sem, 16))
        c.st.inc(16)
        em.se(lambda e, v=c.st.n: e.wait_ge(c.st.sem, v))
        em.te(lambda e, v=ve_xbf: e.wait_ge(c.ve.sem, v))
        em.te(lambda e, v=c.se.n: e.wait_ge(c.se.sem, v))
        for k in range(8):
            mm = lambda e, k=k: e.matmul(
                ps[5][0:64, :], wu1[:, k * 64:(k + 1) * 64],
                xln_bf[:, k * 512:(k + 1) * 512], start=(k == 0), stop=(k == 7))
            if k == 7:
                em.te(lambda e, mm=mm: mm(e).then_inc(c.te.sem, 1))
            else:
                em.te(mm)
        te_u1 = c.te.inc()
        em.se(lambda e, v=te_u1: e.wait_ge(c.te.sem, v))
        em.se(lambda e: e.activation(s1[:], ps[5][0:64, :], AF.Silu,
                                     bias=u1b_col[:]).then_inc(c.se.sem, 1))
        se_s1 = c.se.inc()
        em.te(lambda e, v=se_s1: e.wait_ge(c.se.sem, v))
        for m in range(8):
            bank = 2 + m % 2
            if m >= 2:
                em.te(lambda e, v=c.se.n: e.wait_ge(c.se.sem, v))
            em.te(lambda e, m=m, bank=bank: e.matmul(
                ps[bank][:, :], wu2[:, m * 128:(m + 1) * 128], s1[:],
                start=True, stop=True).then_inc(c.te.sem, 1))
            te_u2 = c.te.inc()
            em.se(lambda e, v=te_u2: e.wait_ge(c.te.sem, v))
            em.se(lambda e, m=m, bank=bank: e.activation(
                gsb[:, m * 512:(m + 1) * 512], ps[bank][:, :], AF.Sigmoid,
                bias=u2b_col[:, m:m + 1]).then_inc(c.se.sem, 1))
            c.se.inc()
        se_g = c.se.n
        em.sy(lambda e, v=se_g: e.wait_ge(c.se.sem, v))
        em.sy(lambda e, t0=t0: e.dma_start(
            out=gr[:, :, t0:t0 + 512],
            in_=_r(gsb[:], "p (k t) -> p k t", t=512)).then_inc(c.st.sem, 16))
        c.st.inc(16)
        em.sy(lambda e, v=c.te.n: e.wait_ge(c.te.sem, v))
        em.sy(lambda e, v=c.ve.n: e.wait_ge(c.ve.sem, v))
        em.sy(lambda e, v=c.se.n: e.wait_ge(c.se.sem, v))

    for ci in range(NCHUNK):
        chunk(ci)
    c.barrier()



def emit_attn_b_p2(c, dram, sb, st):
    em, nc = c.em, c.nc
    E = st.enter_context
    qc = E(nc.sbuf_tensor("b2_q", [64, 16 * 128], BF16))
    kc = E(nc.sbuf_tensor("b2_k", [64, 16 * 128], BF16))
    lac = E(nc.sbuf_tensor("b2_la", [64, 16 * 128], F32))
    lam = E(nc.sbuf_tensor("b2_lam", [64, 16 * 128], F32))
    zer = E(nc.sbuf_tensor("b2_zer", [64, 128], F32))
    anc = E(nc.sbuf_tensor("b2_anc", [64, 16 * 2], F32))
    anc2 = E(nc.sbuf_tensor("b2_anc2", [64, 16 * 2], F32))
    e1 = E(nc.sbuf_tensor("b2_e1", [64, 16 * 128], F32))
    eg = E(nc.sbuf_tensor("b2_eg", [64, 16 * 128], F32))
    egx = E(nc.sbuf_tensor("b2_egx", [64, 16 * 64], F32))
    fk = E(nc.sbuf_tensor("b2_fk", [64, 16 * 128], BF16))
    fq = E(nc.sbuf_tensor("b2_fq", [64, 16 * 128], BF16))
    gk = E(nc.sbuf_tensor("b2_gk", [64, 16 * 128], BF16))
    gkx = E(nc.sbuf_tensor("b2_gkx", [64, 16 * 64], BF16))
    ktl = E(nc.sbuf_tensor("b2_ktl", [64, 16 * 128], BF16))
    qtl = E(nc.sbuf_tensor("b2_qtl", [64, 16 * 128], BF16))
    eLC = E(nc.sbuf_tensor("b2_eLC", [64, 16], F32))
    scU = E(nc.sbuf_tensor("b2_scU", [64, 16], F32))
    Usb = E(nc.sbuf_tensor("b2_U", [64, 16 * 128], BF16))
    Utk = E(nc.sbuf_tensor("b2_Utk", [128, 16 * 64], BF16))
    vsb = E(nc.sbuf_tensor("b2_v", [128, 16 * 64], BF16))
    vb = E(nc.sbuf_tensor("b2_vb", [128, 16 * 64], F32))
    btsb = E(nc.sbuf_tensor("b2_bt", [128, 16], F32))
    nbt = E(nc.sbuf_tensor("b2_nbt", [128, 16], F32))
    Nsb = E(nc.sbuf_tensor("b2_N", [128, 16 * 128], BF16))
    NTsb = E(nc.sbuf_tensor("b2_NT", [128, 16 * 128], BF16))
    N2 = E(nc.sbuf_tensor("b2_N2", [128, 16 * 128], BF16))
    NT2 = E(nc.sbuf_tensor("b2_NT2", [128, 16 * 128], BF16))
    AqT = E(nc.sbuf_tensor("b2_AqT", [128, 16 * 128], BF16))
    Xsb = E(nc.sbuf_tensor("b2_X", [128, 16 * 64], BF16))
    Ssb = E(nc.sbuf_tensor("b2_S", [64, 16 * 64], BF16))
    ysb = E(nc.sbuf_tensor("b2_y", [128, 16 * 64], F32))
    ysq = E(nc.sbuf_tensor("b2_ysq", [128, 16 * 64], F32))
    yss = E(nc.sbuf_tensor("b2_yss", [128, 16], F32))
    yhat = E(nc.sbuf_tensor("b2_yhat", [128, 16 * 64], BF16))
    hnB = E(nc.sbuf_tensor("b2_hnB", [128, 1024], F32))
    ml = E(nc.sbuf_tensor("b2_ml", [128, 128], BF16))
    mu = E(nc.sbuf_tensor("b2_mu", [128, 128], BF16))
    hn_row = E(nc.sbuf_tensor("b2_hn", [1, 1024], F32))

    em.sy(lambda e: e.dma_start(out=ml[:], in_=dram["mask_sl"]).then_inc(c.ld.sem, 16))
    c.ld.inc(16)
    em.sy(lambda e: e.dma_start(out=mu[:], in_=dram["mask_iu"]).then_inc(c.ld.sem, 16))
    c.ld.inc(16)
    em.sy(lambda e: e.dma_start(out=hn_row[:], in_=dram["ab_head_norm_row"]).then_inc(c.ld.sem, 16))
    ld_cst = c.ld.inc(16)
    em.gp(lambda e: e.memset(Ssb[:], 0.0))
    em.gp(lambda e: e.memset(zer[:], 0.0).then_inc(c.ge.sem, 1))
    ge_i = c.ge.inc()
    em.te(lambda e, v=ld_cst: e.wait_ge(c.ld.sem, v))
    em.te(lambda e: e.matmul(c.ps[0][:, 0:512], sb["ones_row_f32"][:], hn_row[0:1, 0:512],
                             start=True, stop=True))
    em.te(lambda e: e.matmul(c.ps[1][:, 0:512], sb["ones_row_f32"][:], hn_row[0:1, 512:1024],
                             start=True, stop=True).then_inc(c.te.sem, 1))
    te_hn = c.te.inc()
    em.ve(lambda e, v=te_hn: e.wait_ge(c.te.sem, v))
    em.ve(lambda e: e.tensor_copy(hnB[:, 0:512], c.ps[0][:, 0:512]))
    em.ve(lambda e: e.tensor_copy(hnB[:, 512:1024], c.ps[1][:, 0:512]))
    em.ve(lambda e, v=ge_i: e.wait_ge(c.ge.sem, v))
    em.ve(lambda e: e.engine_nop().then_inc(c.ve.sem, 1))
    c.ve.inc()

    qhr = _r(dram["ab_q"], "(h p) t -> p h t", p=64)
    khr = _r(dram["ab_k"], "(h p) t -> p h t", p=64)
    lar = _r(dram["ab_la"], "(h p) t -> p h t", p=64)
    ps, pt = c.ps, c.pt

    def chunk(ci):
        t0 = ci * CB
        em.sy(lambda e, v=c.te.n: e.wait_ge(c.te.sem, v))
        em.sy(lambda e, v=c.ve.n: e.wait_ge(c.ve.sem, v))
        em.sy(lambda e, t0=t0: e.dma_start(
            out=_r(qc[:], "p (h t) -> p h t", t=128),
            in_=qhr[:, :, t0:t0 + CB]).then_inc(c.ld.sem, 16))
        c.ld.inc(16)
        em.sy(lambda e, t0=t0: e.dma_start(
            out=_r(kc[:], "p (h t) -> p h t", t=128),
            in_=khr[:, :, t0:t0 + CB]).then_inc(c.ld.sem, 16))
        c.ld.inc(16)
        em.sy(lambda e, t0=t0: e.dma_start(
            out=_r(lac[:], "p (h t) -> p h t", t=128),
            in_=lar[:, :, t0:t0 + CB]).then_inc(c.ld.sem, 16))
        c.ld.inc(16)
        em.sy(lambda e, t0=t0: e.dma_start(
            out=vsb[:], in_=dram["ab_v"][t0:t0 + CB, :]).then_inc(c.ld.sem, 16))
        c.ld.inc(16)
        em.sy(lambda e, t0=t0: e.dma_start(
            out=btsb[:],
            in_=dram["ab_beta"][t0:t0 + CB, :]).then_inc(c.ld.sem, 16))
        ldq = c.ld.inc(16)
        em.ve(lambda e, v=ldq: e.wait_ge(c.ld.sem, v))
        for h in range(16):
            em.ve(lambda e, h=h: e.tensor_tensor_scan(
                lam[:, h * 128:(h + 1) * 128], lac[:, h * 128:(h + 1) * 128],
                zer[:], 0.0, ALU.add, ALU.add))
        for h in range(16):
            for i in range(2):
                em.ve(lambda e, h=h, i=i: e.tensor_tensor(
                    anc[:, h * 2 + i: h * 2 + i + 1],
                    lam[:, h * 128 + i * 64: h * 128 + i * 64 + 1],
                    lam[:, h * 128 + i * 64 + 63: h * 128 + i * 64 + 64], ALU.add))
        em.ve(lambda e: e.tensor_copy(
            _r(eLC[:], "p (h o) -> p h o", o=1),
            _r(lam[:], "p (h t) -> p h t", t=128)[:, :, 127:128]))
        em.ve(lambda e: e.tensor_scalar(anc2[:], anc[:], 0.5, None, ALU.mult))
        em.ve(lambda e: e.engine_nop())
        em.ve(lambda e: e.engine_nop())
        for h in range(16):
            for i in range(2):
                em.ve(lambda e, h=h, i=i: e.tensor_scalar(
                    e1[:, h * 128 + i * 64: h * 128 + (i + 1) * 64],
                    lam[:, h * 128 + i * 64: h * 128 + (i + 1) * 64],
                    anc2[:, h * 2 + i: h * 2 + i + 1], None, ALU.subtract))
                em.ve(lambda e, h=h, i=i: e.tensor_scalar(
                    eg[:, h * 128 + i * 64: h * 128 + (i + 1) * 64],
                    lam[:, h * 128 + i * 64: h * 128 + (i + 1) * 64],
                    anc2[:, h * 2 + i: h * 2 + i + 1], -1.0, ALU.subtract, ALU.mult))
            em.ve(lambda e, h=h: e.tensor_scalar(
                egx[:, h * 64:(h + 1) * 64], lam[:, h * 128: h * 128 + 64],
                anc2[:, h * 2 + 1: h * 2 + 2], -1.0, ALU.subtract, ALU.mult))
        for h in range(16):
            em.ve(lambda e, h=h: e.tensor_tensor(
                scU[:, h:h + 1], eLC[:, h:h + 1], anc2[:, h * 2 + 1: h * 2 + 2], ALU.subtract))
        em.ve(lambda e: e.engine_nop().then_inc(c.ve.sem, 1))
        ve_exp_in = c.ve.inc()
        em.se(lambda e, v=ve_exp_in: e.wait_ge(c.ve.sem, v))
        em.se(lambda e: e.activation(e1[:], e1[:], AF.Exp))
        em.se(lambda e: e.activation(eg[:], eg[:], AF.Exp))
        em.se(lambda e: e.activation(egx[:], egx[:], AF.Exp))
        em.se(lambda e: e.activation(lam[:], lam[:], AF.Exp))
        em.se(lambda e: e.activation(eLC[:], eLC[:], AF.Exp))
        em.se(lambda e: e.activation(scU[:], scU[:], AF.Exp).then_inc(c.se.sem, 1))
        se_exp = c.se.inc()
        em.ve(lambda e, v=se_exp: e.wait_ge(c.se.sem, v))
        em.ve(lambda e: e.tensor_tensor(fk[:], kc[:], e1[:], ALU.mult))
        em.ve(lambda e: e.tensor_tensor(fq[:], qc[:], e1[:], ALU.mult))
        em.ve(lambda e: e.tensor_tensor(gk[:], kc[:], eg[:], ALU.mult))
        for h in range(16):
            em.ve(lambda e, h=h: e.tensor_tensor(
                gkx[:, h * 64:(h + 1) * 64], kc[:, h * 128: h * 128 + 64],
                egx[:, h * 64:(h + 1) * 64], ALU.mult))
        em.ve(lambda e: e.tensor_tensor(ktl[:], kc[:], lam[:], ALU.mult))
        em.ve(lambda e: e.tensor_tensor(qtl[:], qc[:], lam[:], ALU.mult))
        for h in range(16):
            em.ve(lambda e, h=h: e.tensor_scalar(
                Usb[:, h * 128: h * 128 + 64], gkx[:, h * 64:(h + 1) * 64],
                scU[:, h:h + 1], None, ALU.mult))
            em.ve(lambda e, h=h: e.tensor_scalar(
                Usb[:, h * 128 + 64:(h + 1) * 128], gk[:, h * 128 + 64: h * 128 + 128],
                scU[:, h:h + 1], None, ALU.mult))
        em.ve(lambda e: e.tensor_scalar(nbt[:], btsb[:], -1.0, None, ALU.mult))
        for h in range(16):
            em.ve(lambda e, h=h: e.tensor_scalar(
                vb[:, h * 64:(h + 1) * 64], vsb[:, h * 64:(h + 1) * 64],
                btsb[:, h:h + 1], None, ALU.mult))
        em.ve(lambda e: e.engine_nop().then_inc(c.ve.sem, 1))
        ve_fact = c.ve.inc()
        em.te(lambda e, v=ve_fact: e.wait_ge(c.ve.sem, v))
        if ci == 0 and "d2_lam" in dram:
            for nm, buf in (("d2_lam", lam), ("d2_e1", e1), ("d2_fk", fk), ("d2_gk", gk)):
                em.sy(lambda e, v=ve_fact: e.wait_ge(c.ve.sem, v))
                em.sy(lambda e, nm=nm, buf=buf: e.dma_start(out=dram[nm], in_=buf[:]).then_inc(c.st.sem, 16))
                c.st.inc(16)

        for h in range(16):
            bkN = h % 2
            bkA = 2 + h % 2
            if h >= 2:
                em.te(lambda e, v=c.ve.n: e.wait_ge(c.ve.sem, v))
            hs = slice(h * 128, h * 128 + 64)
            em.te(lambda e, h=h, bkN=bkN: e.matmul(
                ps[bkN][0:64, 0:64], fk[:, h * 128: h * 128 + 64],
                gk[:, h * 128: h * 128 + 64], start=True, stop=True))
            em.te(lambda e, h=h, bkN=bkN: e.matmul(
                ps[bkN][64:128, 0:64], fk[:, h * 128 + 64: h * 128 + 128],
                gkx[:, h * 64:(h + 1) * 64], start=True, stop=True,
                tile_position=(0, 64)))
            em.te(lambda e, h=h, bkN=bkN: e.matmul(
                ps[bkN][64:128, 64:128], fk[:, h * 128 + 64: h * 128 + 128],
                gk[:, h * 128 + 64: h * 128 + 128], start=True, stop=True,
                tile_position=(0, 64)).then_inc(c.te.sem, 1))
            te_N = c.te.inc()
            em.te(lambda e, h=h, bkA=bkA: e.matmul(
                ps[bkA][0:64, 0:64], gk[:, h * 128: h * 128 + 64],
                fq[:, h * 128: h * 128 + 64], start=True, stop=True))
            em.te(lambda e, h=h, bkA=bkA: e.matmul(
                ps[bkA][0:64, 64:128], gkx[:, h * 64:(h + 1) * 64],
                fq[:, h * 128 + 64: h * 128 + 128], start=True, stop=True))
            em.te(lambda e, h=h, bkA=bkA: e.matmul(
                ps[bkA][64:128, 64:128], gk[:, h * 128 + 64: h * 128 + 128],
                fq[:, h * 128 + 64: h * 128 + 128], start=True, stop=True,
                tile_position=(0, 64)).then_inc(c.te.sem, 1))
            te_A = c.te.inc()
            em.ve(lambda e, v=te_A: e.wait_ge(c.te.sem, v))
            em.ve(lambda e, h=h, bkN=bkN: e.tensor_tensor(
                Nsb[:, h * 128:(h + 1) * 128], ps[bkN][:, 0:128], ml[:], ALU.mult))
            em.ve(lambda e, h=h, bkA=bkA: e.tensor_tensor(
                AqT[:, h * 128:(h + 1) * 128], ps[bkA][:, 0:128], mu[:], ALU.mult))
            em.ve(lambda e, h=h: e.tensor_scalar(
                Nsb[:, h * 128:(h + 1) * 128], Nsb[:, h * 128:(h + 1) * 128],
                nbt[:, h:h + 1], None, ALU.mult))
            em.ve(lambda e: e.engine_nop().then_inc(c.ve.sem, 1))
            c.ve.inc()
        ve_N = c.ve.n
        em.te(lambda e, v=ve_N: e.wait_ge(c.ve.sem, v))
        for half in range(2):
            ptb = pt[half]
            for hh in range(8):
                h = half * 8 + hh
                mm = lambda e, h=h, hh=hh, ptb=ptb: e.transpose(
                    ptb[:, hh * 128:(hh + 1) * 128], Nsb[:, h * 128:(h + 1) * 128],
                    sb["ident"][:])
                if hh == 7:
                    em.te(lambda e, mm=mm: mm(e).then_inc(c.te.sem, 1))
                else:
                    em.te(mm)
            te_tr = c.te.inc()
            em.ve(lambda e, v=te_tr: e.wait_ge(c.te.sem, v))
            em.ve(lambda e, half=half, ptb=ptb: e.tensor_copy(
                NTsb[:, half * 1024:(half + 1) * 1024], ptb[:, :]))
            em.ve(lambda e: e.engine_nop().then_inc(c.ve.sem, 1))
            c.ve.inc()
        if ci == 0 and "d2_N" in dram:
            for nm, buf in (("d2_N", Nsb), ("d2_AqT", AqT), ("d2_NT", NTsb)):
                em.sy(lambda e, v=c.ve.n: e.wait_ge(c.ve.sem, v))
                em.sy(lambda e, nm=nm, buf=buf: e.dma_start(out=dram[nm], in_=buf[:]).then_inc(c.st.sem, 16))
                c.st.inc(16)
        em.te(lambda e, v=c.ve.n: e.wait_ge(c.ve.sem, v))
        for h in range(16):
            mm = lambda e, h=h: e.transpose(
                pt[0][:, h * 64:(h + 1) * 64], Usb[:, h * 128:(h + 1) * 128],
                sb["ident"][0:64, 0:64])
            if h == 15:
                em.te(lambda e, mm=mm: mm(e).then_inc(c.te.sem, 1))
            else:
                em.te(mm)
        te_utr = c.te.inc()
        em.ve(lambda e, v=te_utr: e.wait_ge(c.te.sem, v))
        em.ve(lambda e: e.tensor_copy(Utk[:], pt[0][:, 0:1024]))
        em.ve(lambda e: e.engine_nop().then_inc(c.ve.sem, 1))
        c.ve.inc()

        em.te(lambda e, v=c.ve.n: e.wait_ge(c.ve.sem, v))
        for h in range(16):
            bank = 4 + h % 2
            if h >= 2:
                em.te(lambda e, v=c.ve.n: e.wait_ge(c.ve.sem, v))
            mm = lambda e, h=h, bank=bank: e.matmul(
                ps[bank][:, 0:64], ktl[:, h * 128:(h + 1) * 128],
                Ssb[:, h * 64:(h + 1) * 64], start=True, stop=True)
            em.te(lambda e, mm=mm: mm(e).then_inc(c.te.sem, 1))
            te_t1 = c.te.inc()
            em.ve(lambda e, v=te_t1: e.wait_ge(c.te.sem, v))
            em.ve(lambda e, h=h, bank=bank: e.scalar_tensor_tensor(
                Xsb[:, h * 64:(h + 1) * 64], ps[bank][:, 0:64],
                nbt[:, h:h + 1], vb[:, h * 64:(h + 1) * 64], ALU.mult, ALU.add))
            em.ve(lambda e: e.engine_nop().then_inc(c.ve.sem, 1))
            c.ve.inc()
        if ci == 0 and "d2_X0" in dram:
            em.sy(lambda e, v=c.ve.n: e.wait_ge(c.ve.sem, v))
            em.sy(lambda e: e.dma_start(out=dram["d2_X0"], in_=Xsb[:]).then_inc(c.st.sem, 16))
            c.st.inc(16)
            em.sy(lambda e: e.dma_start(out=dram["d2_U"], in_=Utk[:]).then_inc(c.st.sem, 16))
            c.st.inc(16)
        curN, curNT = Nsb, NTsb
        nxtN, nxtNT = N2, NT2
        for j in range(7):
            for half in range(2):
                bank = half
                em.te(lambda e, v=c.ve.n: e.wait_ge(c.ve.sem, v))
                for hh in range(8):
                    h = half * 8 + hh
                    em.te(lambda e, h=h, hh=hh, bank=bank: e.matmul(
                        ps[bank][:, hh * 64:(hh + 1) * 64], sb["ident"][:],
                        Xsb[:, h * 64:(h + 1) * 64], start=True, stop=False))
                    mm = lambda e, h=h, hh=hh, bank=bank, curNT=curNT: e.matmul(
                        ps[bank][:, hh * 64:(hh + 1) * 64], curNT[:, h * 128:(h + 1) * 128],
                        Xsb[:, h * 64:(h + 1) * 64], start=False, stop=True)
                    if hh == 7:
                        em.te(lambda e, mm=mm: mm(e).then_inc(c.te.sem, 1))
                    else:
                        em.te(mm)
                te_ap = c.te.inc()
                em.ve(lambda e, v=te_ap: e.wait_ge(c.te.sem, v))
                em.ve(lambda e, half=half, bank=bank: e.tensor_copy(
                    Xsb[:, half * 512:(half + 1) * 512], ps[bank][:, :]))
                em.ve(lambda e: e.engine_nop().then_inc(c.ve.sem, 1))
                c.ve.inc()
            if j < 6:
                for h in range(16):
                    bank = 2 + h % 2
                    bankT = 4 + h % 2
                    if h >= 2:
                        em.te(lambda e, v=c.ve.n: e.wait_ge(c.ve.sem, v))
                    em.te(lambda e, h=h, bank=bank, curN=curN, curNT=curNT: e.matmul(
                        ps[bank][:, 0:128], curNT[:, h * 128:(h + 1) * 128],
                        curN[:, h * 128:(h + 1) * 128], start=True, stop=True).then_inc(c.te.sem, 1))
                    te_s1_ = c.te.inc()
                    em.te(lambda e, h=h, bankT=bankT, curN=curN, curNT=curNT: e.matmul(
                        ps[bankT][:, 0:128], curN[:, h * 128:(h + 1) * 128],
                        curNT[:, h * 128:(h + 1) * 128], start=True, stop=True).then_inc(c.te.sem, 1))
                    te_s2_ = c.te.inc()
                    em.ve(lambda e, v=te_s1_: e.wait_ge(c.te.sem, v))
                    em.ve(lambda e, h=h, bank=bank, nxtN=nxtN: e.tensor_copy(
                        nxtN[:, h * 128:(h + 1) * 128], ps[bank][:, 0:128]))
                    em.ve(lambda e, v=te_s2_: e.wait_ge(c.te.sem, v))
                    em.ve(lambda e, h=h, bankT=bankT, nxtNT=nxtNT: e.tensor_copy(
                        nxtNT[:, h * 128:(h + 1) * 128], ps[bankT][:, 0:128]))
                    em.ve(lambda e: e.engine_nop().then_inc(c.ve.sem, 1))
                    c.ve.inc()
                curN, curNT, nxtN, nxtNT = nxtN, nxtNT, curN, curNT
        if ci == 0 and "d2_X" in dram:
            em.sy(lambda e, v=c.ve.n: e.wait_ge(c.ve.sem, v))
            em.sy(lambda e: e.dma_start(out=dram["d2_X"], in_=Xsb[:]).then_inc(c.st.sem, 16))
            c.st.inc(16)
        for half in range(2):
            bank = half
            em.te(lambda e, v=c.ve.n: e.wait_ge(c.ve.sem, v))
            for hh in range(8):
                h = half * 8 + hh
                em.te(lambda e, h=h, hh=hh, bank=bank: e.matmul(
                    ps[bank][:, hh * 64:(hh + 1) * 64], qtl[:, h * 128:(h + 1) * 128],
                    Ssb[:, h * 64:(h + 1) * 64], start=True, stop=False))
                mm = lambda e, h=h, hh=hh, bank=bank: e.matmul(
                    ps[bank][:, hh * 64:(hh + 1) * 64], AqT[:, h * 128:(h + 1) * 128],
                    Xsb[:, h * 64:(h + 1) * 64], start=False, stop=True)
                if hh == 7:
                    em.te(lambda e, mm=mm: mm(e).then_inc(c.te.sem, 1))
                else:
                    em.te(mm)
            te_out = c.te.inc()
            em.ve(lambda e, v=te_out: e.wait_ge(c.te.sem, v))
            em.ve(lambda e, half=half, bank=bank: e.tensor_copy(
                ysb[:, half * 512:(half + 1) * 512], ps[bank][:, :]))
            em.ve(lambda e: e.engine_nop().then_inc(c.ve.sem, 1))
            c.ve.inc()
        ve_y = c.ve.n
        for h in range(16):
            bank = 2 + h % 2
            if h >= 2:
                em.te(lambda e, v=c.ve.n: e.wait_ge(c.ve.sem, v))
            mm = lambda e, h=h, bank=bank: e.matmul(
                ps[bank][0:64, 0:64], Utk[:, h * 64:(h + 1) * 64],
                Xsb[:, h * 64:(h + 1) * 64], start=True, stop=True)
            em.te(lambda e, mm=mm: mm(e).then_inc(c.te.sem, 1))
            te_su = c.te.inc()
            em.ve(lambda e, v=te_su: e.wait_ge(c.te.sem, v))
            em.ve(lambda e, h=h, bank=bank: e.scalar_tensor_tensor(
                Ssb[:, h * 64:(h + 1) * 64], Ssb[:, h * 64:(h + 1) * 64],
                eLC[:, h:h + 1], ps[bank][0:64, 0:64], ALU.mult, ALU.add))
            em.ve(lambda e: e.engine_nop().then_inc(c.ve.sem, 1))
            c.ve.inc()
        if ci == 0 and "d2_y" in dram:
            em.sy(lambda e, v=c.ve.n: e.wait_ge(c.ve.sem, v))
            em.sy(lambda e: e.dma_start(out=dram["d2_y"], in_=ysb[:]).then_inc(c.st.sem, 16))
            c.st.inc(16)
            em.sy(lambda e: e.dma_start(out=dram["d2_S"], in_=Ssb[:]).then_inc(c.st.sem, 16))
            c.st.inc(16)
        em.se(lambda e, v=ve_y: e.wait_ge(c.ve.sem, v))
        em.se(lambda e: e.activation(ysq[:], ysb[:], AF.Square).then_inc(c.se.sem, 1))
        se_ysq = c.se.inc()
        em.ve(lambda e, v=se_ysq: e.wait_ge(c.se.sem, v))
        for h in range(16):
            em.ve(lambda e, h=h: e.reduce_sum(yss[:, h:h + 1], ysq[:, h * 64:(h + 1) * 64], AX.X))
        em.ve(lambda e: e.engine_nop())
        em.ve(lambda e: e.engine_nop())
        em.ve(lambda e: e.tensor_scalar(yss[:], yss[:], 1.0 / 64.0, None, ALU.mult))
        em.ve(lambda e: e.engine_nop().then_inc(c.ve.sem, 1))
        ve_yss = c.ve.inc()
        em.se(lambda e, v=ve_yss: e.wait_ge(c.ve.sem, v))
        em.se(lambda e: e.activation(yss[:], yss[:], AF.Ln, bias=1e-6))
        em.se(lambda e: e.activation(yss[:], yss[:], AF.Exp, scale=-0.5).then_inc(c.se.sem, 1))
        se_yr = c.se.inc()
        em.ve(lambda e, v=se_yr: e.wait_ge(c.se.sem, v))
        for h in range(16):
            em.ve(lambda e, h=h: e.tensor_scalar(
                ysb[:, h * 64:(h + 1) * 64], ysb[:, h * 64:(h + 1) * 64],
                yss[:, h:h + 1], None, ALU.mult))
        em.ve(lambda e: e.tensor_tensor(yhat[:], ysb[:], hnB[:], ALU.mult))
        em.ve(lambda e: e.engine_nop().then_inc(c.ve.sem, 1))
        ve_yhat = c.ve.inc()
        em.sy(lambda e, v=ve_yhat: e.wait_ge(c.ve.sem, v))
        em.sy(lambda e, t0=t0: e.dma_start(
            out=dram["ab_y"][t0:t0 + CB, :],
            in_=yhat[:]).then_inc(c.st.sem, 16))
        c.st.inc(16)

    for ci in range(NCB):
        chunk(ci)
    c.barrier()



def emit_attn_b_p3(c, dram, sb, st, x_in, x_out):
    em, nc = c.em, c.nc
    E = st.enter_context
    yh = E(nc.sbuf_tensor("b3_yh", [128, 4 * 1024], BF16))
    yhT = E(nc.sbuf_tensor("b3_yhT", [128, 8 * 512], BF16))
    gsb = E(nc.sbuf_tensor("b3_g", [128, 8 * 512], BF16))
    xlns = E(nc.sbuf_tensor("b3_xln", [128, 8 * 512], F32))
    x2s = E(nc.sbuf_tensor("b3_x2", [128, 8 * 512], F32))
    yc = E(nc.sbuf_tensor("b3_yc", [128, 8 * 512], F32))
    y2 = E(nc.sbuf_tensor("b3_y2", [128, 8 * 512], BF16))
    gall = E(nc.sbuf_tensor("b3_gall", [128, 32 * 512], BF16))
    sgt = E(nc.sbuf_tensor("b3_sgt", [128, 512], BF16))
    wo = E(nc.sbuf_tensor("b3_wo", [128, 8 * 1024], BF16))
    w13 = [E(nc.sbuf_tensor(f"b3_w13_{b}", [128, 2 * 8 * 512], BF16)) for b in range(2)]
    w2b = [E(nc.sbuf_tensor(f"b3_w2_{b}", [128, 4 * 1024], BF16)) for b in range(2)]
    stm = E(nc.sbuf_tensor("b3_stm", [1, 512], F32))
    str_ = E(nc.sbuf_tensor("b3_str", [1, 512], F32))
    rB = E(nc.sbuf_tensor("b3_rB", [128, 512], F32))
    tmp = E(nc.sbuf_tensor("b3_tmp", [128, 512], F32))
    tmp2 = E(nc.sbuf_tensor("b3_tmp2", [128, 512], F32))
    fn_col = E(nc.sbuf_tensor("b3_fn", [128, 8], F32))

    em.gp(lambda e: e.dma_start(
        out=_r(wo[:], "p (k n) -> p k n", n=1024),
        in_=_r(dram["ab_out_w"], "(k p) n -> p k n", p=128)).then_inc(c.cs.sem, 16))
    c.cs.inc(16)
    em.sy(lambda e: e.dma_start(out=fn_col[:], in_=dram["ab_ff_norm_w"]).then_inc(c.ld.sem, 16))
    c.ld.inc(16)
    cs_w = c.cs.n
    ld_w = c.ld.n
    em.te(lambda e: e.wait_ge(c.cs.sem, cs_w))
    em.ve(lambda e: e.wait_ge(c.ld.sem, ld_w))

    xlnr = _r(dram["ab_xln"], "(k p) t -> p k t", p=128)
    gr = _r(dram["ab_gate"], "(k p) t -> p k t", p=128)
    xr = _r(dram[x_in], "(k p) t -> p k t", p=128)
    xo = _r(dram[x_out], "(k p) t -> p k t", p=128)
    ps, pt = c.ps, c.pt

    def chunk(ci):
        t0 = ci * CHUNK
        em.ve(lambda e, v=c.st.n: e.wait_ge(c.st.sem, v))
        em.sy(lambda e, v=c.te.n: e.wait_ge(c.te.sem, v))
        em.sy(lambda e, t0=t0: e.dma_start(
            out=_r(yh[:], "p (q n) -> p q n", n=1024),
            in_=_r(dram["ab_y"][t0:t0 + 512, :], "(q p) n -> p q n", p=128)).then_inc(c.ld.sem, 16))
        c.ld.inc(16)
        em.sy(lambda e, t0=t0: e.dma_start(
            out=_r(gsb[:], "p (k t) -> p k t", t=512),
            in_=gr[:, :, t0:t0 + 512]).then_inc(c.ld.sem, 16))
        c.ld.inc(16)
        em.sy(lambda e, t0=t0: e.dma_start(
            out=_r(xlns[:], "p (k t) -> p k t", t=512),
            in_=xlnr[:, :, t0:t0 + 512]).then_inc(c.ld.sem, 16))
        c.ld.inc(16)
        em.sy(lambda e, t0=t0: e.dma_start(
            out=_r(x2s[:], "p (k t) -> p k t", t=512),
            in_=xr[:, :, t0:t0 + 512]).then_inc(c.ld.sem, 16))
        ldx = c.ld.inc(16)
        em.te(lambda e, v=ldx: e.wait_ge(c.ld.sem, v))
        for tt in range(4):
            ptb = pt[tt % 2]
            if tt >= 2:
                em.te(lambda e, v=c.ve.n: e.wait_ge(c.ve.sem, v))
            for j in range(8):
                mm = lambda e, tt=tt, j=j, ptb=ptb: e.transpose(
                    ptb[:, j * 128:(j + 1) * 128],
                    yh[:, tt * 1024 + j * 128: tt * 1024 + j * 128 + 128], sb["ident"][:])
                if j == 7:
                    em.te(lambda e, mm=mm: mm(e).then_inc(c.te.sem, 1))
                else:
                    em.te(mm)
            te_tr = c.te.inc()
            em.ve(lambda e, v=te_tr: e.wait_ge(c.te.sem, v))
            for j in range(8):
                em.ve(lambda e, tt=tt, j=j, ptb=ptb: e.tensor_copy(
                    yhT[:, j * 512 + tt * 128: j * 512 + tt * 128 + 128],
                    ptb[:, j * 128:(j + 1) * 128]))
            em.ve(lambda e: e.engine_nop().then_inc(c.ve.sem, 1))
            c.ve.inc()
        ve_yhT = c.ve.n
        em.te(lambda e, v=ve_yhT: e.wait_ge(c.ve.sem, v))
        for m in range(8):
            bank = m % 2
            if m >= 2:
                em.te(lambda e, v=c.ve.n: e.wait_ge(c.ve.sem, v))
            for k in range(8):
                mm = lambda e, m=m, k=k, bank=bank: e.matmul(
                    ps[bank][:, :], wo[:, k * 1024 + m * 128: k * 1024 + m * 128 + 128],
                    yhT[:, k * 512:(k + 1) * 512], start=(k == 0), stop=(k == 7))
                if k == 7:
                    em.te(lambda e, mm=mm: mm(e).then_inc(c.te.sem, 1))
                else:
                    em.te(mm)
            te_o = c.te.inc()
            em.ve(lambda e, v=te_o: e.wait_ge(c.te.sem, v))
            em.ve(lambda e, m=m, bank=bank: e.tensor_tensor(
                tmp[:], ps[bank][:, :], gsb[:, m * 512:(m + 1) * 512], ALU.mult))
            em.ve(lambda e: e.tensor_scalar(sb["junk"][:], sb["junk"][:], 1.0, None, ALU.mult))
            em.ve(lambda e, m=m: e.tensor_tensor(
                yc[:, m * 512:(m + 1) * 512], tmp[:], xlns[:, m * 512:(m + 1) * 512], ALU.add))
            em.ve(lambda e: e.engine_nop().then_inc(c.ve.sem, 1))
            c.ve.inc()
        ve_yc = c.ve.n
        te0 = c.te.n
        em.te(lambda e, v=ve_yc: e.wait_ge(c.ve.sem, v))
        for i in range(8):
            buf = tmp if i % 2 == 0 else tmp2
            if i >= 2:
                em.se(lambda e, v=te0 + (i - 1): e.wait_ge(c.te.sem, v))
            else:
                em.se(lambda e, v=ve_yc: e.wait_ge(c.ve.sem, v))
            em.se(lambda e, i=i, buf=buf: e.activation(
                buf[:], yc[:, i * 512:(i + 1) * 512], AF.Square).then_inc(c.se.sem, 1))
            se_sq = c.se.inc()
            em.te(lambda e, v=se_sq: e.wait_ge(c.se.sem, v))
            em.te(lambda e, i=i, buf=buf: e.matmul(
                ps[4][0:1, :], sb["ones_col_f32"][:], buf[:],
                start=(i == 0), stop=(i == 7)).then_inc(c.te.sem, 1))
            c.te.inc()
        te_ss = c.te.n
        em.ve(lambda e, v=te_ss: e.wait_ge(c.te.sem, v))
        em.ve(lambda e: e.tensor_scalar(stm[:], ps[4][0:1, :], 1.0 / D, None, ALU.mult))
        em.ve(lambda e: e.engine_nop().then_inc(c.ve.sem, 1))
        ve_ss = c.ve.inc()
        em.se(lambda e, v=ve_ss: e.wait_ge(c.ve.sem, v))
        em.se(lambda e: e.activation(str_[:], stm[:], AF.Ln, bias=1e-6))
        em.se(lambda e: e.activation(str_[:], str_[:], AF.Exp, scale=-0.5).then_inc(c.se.sem, 1))
        se_rs = c.se.inc()
        em.te(lambda e, v=se_rs: e.wait_ge(c.se.sem, v))
        em.te(lambda e: e.matmul(ps[5][:, :], sb["ones_row_f32"][:], str_[:],
                                 start=True, stop=True).then_inc(c.te.sem, 1))
        te_rb = c.te.inc()
        em.ve(lambda e, v=te_rb: e.wait_ge(c.te.sem, v))
        em.ve(lambda e: e.tensor_copy(rB[:], ps[5][:, :]))
        em.ve(lambda e: e.tensor_scalar(sb["junk"][:], sb["junk"][:], 1.0, None, ALU.mult))
        for i in range(8):
            em.ve(lambda e, i=i: e.tensor_tensor(
                tmp[:], yc[:, i * 512:(i + 1) * 512], rB[:], ALU.mult))
            em.ve(lambda e, i=i: e.tensor_scalar(
                y2[:, i * 512:(i + 1) * 512], tmp[:], fn_col[:, i:i + 1], None, ALU.mult))
        em.ve(lambda e: e.engine_nop().then_inc(c.ve.sem, 1))
        ve_y2 = c.ve.inc()
        em.te(lambda e, v=ve_y2: e.wait_ge(c.ve.sem, v))
        te_n13 = {}
        for n in range(8):
            b = n % 2
            if n >= 2:
                em.sy(lambda e, v=te_n13[n - 2]: e.wait_ge(c.te.sem, v))
            em.sy(lambda e, n=n, b=b: e.dma_start(
                out=_r(w13[b][:, 0:8 * 512], "p (k t) -> p k t", t=512),
                in_=_r(dram["ff_w1_bf"][:, n * 512:(n + 1) * 512], "(k p) t -> p k t", p=128)).then_inc(c.ld.sem, 16))
            c.ld.inc(16)
            em.sy(lambda e, n=n, b=b: e.dma_start(
                out=_r(w13[b][:, 8 * 512:16 * 512], "p (k t) -> p k t", t=512),
                in_=_r(dram["ff_w3_bf"][:, n * 512:(n + 1) * 512], "(k p) t -> p k t", p=128)).then_inc(c.ld.sem, 16))
            ld_w13 = c.ld.inc(16)
            em.te(lambda e, v=ld_w13: e.wait_ge(c.ld.sem, v))
            for mm4 in range(4):
                bank = mm4 % 2
                em.te(lambda e, v=c.ve.n: e.wait_ge(c.ve.sem, v))
                for k in range(8):
                    em.te(lambda e, n=n, b=b, mm4=mm4, k=k, bank=bank: e.matmul(
                        ps[bank][:, :], w13[b][:, k * 512 + mm4 * 128: k * 512 + mm4 * 128 + 128],
                        y2[:, k * 512:(k + 1) * 512], start=(k == 0), stop=(k == 7)))
                for k in range(8):
                    mm = lambda e, n=n, b=b, mm4=mm4, k=k, bank=bank: e.matmul(
                        ps[2 + bank][:, :], w13[b][:, 8 * 512 + k * 512 + mm4 * 128: 8 * 512 + k * 512 + mm4 * 128 + 128],
                        y2[:, k * 512:(k + 1) * 512], start=(k == 0), stop=(k == 7))
                    if k == 7:
                        em.te(lambda e, mm=mm: mm(e).then_inc(c.te.sem, 1))
                    else:
                        em.te(mm)
                te_g = c.te.inc()
                em.se(lambda e, v=te_g: e.wait_ge(c.te.sem, v))
                em.se(lambda e, v=c.ve.n: e.wait_ge(c.ve.sem, v))
                em.se(lambda e, bank=bank: e.activation(
                    sgt[:], ps[bank][:, :], AF.Silu).then_inc(c.se.sem, 1))
                se_sg = c.se.inc()
                em.ve(lambda e, v=se_sg: e.wait_ge(c.se.sem, v))
                em.ve(lambda e, n=n, mm4=mm4, bank=bank: e.tensor_tensor(
                    gall[:, n * 4 * 512 + mm4 * 512: n * 4 * 512 + (mm4 + 1) * 512],
                    ps[2 + bank][:, :], sgt[:], ALU.mult))
                em.ve(lambda e: e.engine_nop().then_inc(c.ve.sem, 1))
                c.ve.inc()
            te_n13[n] = c.te.n
        ve_gall = c.ve.n
        if ci == 0 and "d3_yc" in dram:
            em.sy(lambda e, v=ve_gall: e.wait_ge(c.ve.sem, v))
            em.sy(lambda e: e.dma_start(out=dram["d3_yc"], in_=yc[:]).then_inc(c.st.sem, 16))
            c.st.inc(16)
            em.sy(lambda e: e.dma_start(out=dram["d3_y2"], in_=y2[:]).then_inc(c.st.sem, 16))
            c.st.inc(16)
            em.sy(lambda e: e.dma_start(out=dram["d3_gall"], in_=gall[:]).then_inc(c.st.sem, 16))
            c.st.inc(16)
        em.te(lambda e, v=ve_gall: e.wait_ge(c.ve.sem, v))
        for half in range(2):
            ms = [half * 4 + i for i in range(4)]
            em.te(lambda e, v=c.ve.n: e.wait_ge(c.ve.sem, v))
            te_n_done = {}
            for n in range(8):
                b2_ = n % 2
                if n >= 1:
                    em.sy(lambda e, v=te_n_done[n - 1]: e.wait_ge(c.te.sem, v))
                em.sy(lambda e, n=n, b2_=b2_: e.dma_start(
                    out=_r(w2b[b2_][:, 0:4 * 1024], "p (k t) -> p k t", t=1024),
                    in_=_r(dram["ff_w2_bf"][n * 512:(n + 1) * 512, :],
                           "(k p) t -> p k t", p=128)).then_inc(c.ld.sem, 16))
                ld_w2 = c.ld.inc(16)
                em.te(lambda e, v=ld_w2: e.wait_ge(c.ld.sem, v))
                for mi, m in enumerate(ms):
                    bank = 2 + mi
                    for k4 in range(4):
                        mm = lambda e, n=n, m=m, k4=k4, b2_=b2_, bank=bank: e.matmul(
                            ps[bank][:, :],
                            w2b[b2_][:, k4 * 1024 + m * 128: k4 * 1024 + m * 128 + 128],
                            gall[:, (n * 4 + k4) * 512: (n * 4 + k4 + 1) * 512],
                            start=(n == 0 and k4 == 0), stop=(n == 7 and k4 == 3))
                        if mi == 3 and k4 == 3:
                            em.te(lambda e, mm=mm: mm(e).then_inc(c.te.sem, 1))
                        else:
                            em.te(mm)
                te_n_done[n] = c.te.inc()
            for mi, m in enumerate(ms):
                bank = 2 + mi
                em.ve(lambda e, v=te_n_done[7]: e.wait_ge(c.te.sem, v))
                em.ve(lambda e, m=m, bank=bank: e.tensor_tensor(
                    tmp[:], ps[bank][:, :], yc[:, m * 512:(m + 1) * 512], ALU.add))
                em.ve(lambda e: e.tensor_scalar(sb["junk"][:], sb["junk"][:], 1.0, None, ALU.mult))
                em.ve(lambda e, m=m: e.tensor_tensor(
                    x2s[:, m * 512:(m + 1) * 512], tmp[:],
                    x2s[:, m * 512:(m + 1) * 512], ALU.add))
                em.ve(lambda e: e.engine_nop().then_inc(c.ve.sem, 1))
                c.ve.inc()
        ve_x3 = c.ve.n
        em.sy(lambda e, v=ve_x3: e.wait_ge(c.ve.sem, v))
        em.sy(lambda e, t0=t0: e.dma_start(
            out=xo[:, :, t0:t0 + 512],
            in_=_r(x2s[:], "p (k t) -> p k t", t=512)).then_inc(c.st.sem, 16))
        c.st.inc(16)
        em.sy(lambda e, v=c.te.n: e.wait_ge(c.te.sem, v))
        em.sy(lambda e, v=c.ve.n: e.wait_ge(c.ve.sem, v))

    for ci in range(NCHUNK):
        chunk(ci)
    c.barrier()



def build_program(n_stages=99, dbg=(), only=None):
    nc = bass.Bass("TRN2", target_bir_lowering=False)
    dram = {}

    def din(name, shape, dtype=F32):
        dram[name] = nc.dram_tensor(name, shape, dtype, kind="ExternalInput").ap()

    def dint(name, shape, dtype=F32):
        kind = "ExternalOutput" if name in dbg else "Internal"
        dram[name] = nc.dram_tensor(name, shape, dtype, kind=kind).ap()

    din("xT_in", [D, T])
    din("aa_wq", [D, DI_A]); din("aa_wk", [D, DI_A]); din("aa_wv", [D, DI_A])
    din("aa_wo", [DI_A, D])
    din("aa_bq_col", [128, 16]); din("aa_bk_col", [128, 16])
    din("aa_nbq_col", [128, 16]); din("aa_nbk_col", [128, 16])
    din("aa_bv_row", [1, DI_A], BF16); din("aa_bo_row", [1, D], BF16)
    din("gate_wT", [D, NE])
    din("guwT", [NE, D, D]); din("dwT", [NE, 512, D])
    din("ab_norm_in_w", [128, 8]); din("ab_conv_w", [128, 32]); din("ab_conv_b", [128, 8])
    din("ab_q_w", [D, D]); din("ab_k_w", [D, D]); din("ab_v_w", [D, D])
    din("ab_alpha_w", [D, D]); din("ab_alpha_b", [128, 8])
    din("ab_beta_w", [D, H]); din("ab_beta_b", [1, H])
    din("ab_head_norm_row", [1, D])
    din("ab_u1_w", [D, 64]); din("ab_u1_b", [64, 1])
    din("ab_u2_w", [64, D]); din("ab_u2_b", [128, 8])
    din("ab_out_w", [D, D]); din("ab_ff_norm_w", [128, 8])
    din("ff_w1", [D, 4096]); din("ff_w3", [D, 4096]); din("ff_w2", [4096, D])
    din("ident_bf", [128, 128], BF16); din("ident_f32", [128, 128])
    din("ones_col_bf", [128, 1], BF16); din("ones_row_bf", [1, 512], BF16)
    din("ones_col_f32", [128, 1]); din("ones_row_f32", [1, 128])
    din("mask_a", [128, 4 * 512], BF16)
    din("mask_sl", [128, 128], BF16); din("mask_iu", [128, 128], BF16)
    din("const_row", [128, 8])
    dram["outT"] = nc.dram_tensor("outT", [D, T], F32, kind="ExternalOutput").ap()
    dint("x1T", [D, T]); dint("x2T", [D, T]); dint("x3T", [D, T])
    dint("aa_q", [DI_A, T], BF16); dint("aa_k", [DI_A, T], BF16)
    dint("aa_v", [T, DI_A], BF16); dint("aa_y", [DI_A, T], BF16)
    if "dbg_b2" in dbg:
        dram["d3_yc"] = nc.dram_tensor("d3_yc", [128, 4096], F32, kind="ExternalOutput").ap()
        dram["d3_y2"] = nc.dram_tensor("d3_y2", [128, 4096], BF16, kind="ExternalOutput").ap()
        dram["d3_gall"] = nc.dram_tensor("d3_gall", [128, 32 * 512], BF16, kind="ExternalOutput").ap()
        for nm, shp, dt in (("d2_lam", [64, 2048], F32), ("d2_e1", [64, 2048], F32),
                            ("d2_fk", [64, 2048], BF16), ("d2_gk", [64, 2048], BF16),
                            ("d2_N", [128, 2048], BF16), ("d2_AqT", [128, 2048], BF16),
                            ("d2_NT", [128, 2048], BF16), ("d2_X0", [128, 1024], BF16),
                            ("d2_X", [128, 1024], BF16), ("d2_y", [128, 1024], F32),
                            ("d2_U", [128, 1024], BF16), ("d2_S", [64, 1024], BF16)):
            dram[nm] = nc.dram_tensor(nm, shp, dt, kind="ExternalOutput").ap()
    if "dbg_z" in dbg:
        dram["dbg_z"] = nc.dram_tensor("dbg_z", [128, 16], F32, kind="ExternalOutput").ap()
        dram["dbg_den"] = nc.dram_tensor("dbg_den", [1, 512], F32, kind="ExternalOutput").ap()
        dram["dbg_rdB"] = nc.dram_tensor("dbg_rdB", [128, 512], F32, kind="ExternalOutput").ap()
    dint("guw_bf", [NE, D, D], BF16); dint("dw_bf", [NE, 512, D], BF16)
    dint("moe_xn", [D, T], BF16); dint("moe_comb", [T, NE])
    dint("ab_xln", [D, T]); dint("ab_q", [D, T], BF16); dint("ab_k", [D, T], BF16)
    dint("ab_v", [T, D], BF16); dint("ab_la", [D, T]); dint("ab_beta", [T, H])
    dint("ab_gate", [D, T], BF16); dint("ab_y", [T, D], BF16)
    dint("ff_w1_bf", [D, 4096], BF16); dint("ff_w3_bf", [D, 4096], BF16)
    dint("ff_w2_bf", [4096, D], BF16)

    with ExitStack() as top:
        c = Ctx(nc, top)
        E = top.enter_context
        sb = {
            "ident": E(nc.sbuf_tensor("g_ident", [128, 128], BF16)),
            "ident_f32": E(nc.sbuf_tensor("g_ident_f32", [128, 128], F32)),
            "ones_col_bf": E(nc.sbuf_tensor("g_ones_col_bf", [128, 1], BF16)),
            "ones_row_bf": E(nc.sbuf_tensor("g_ones_row_bf", [1, 512], BF16)),
            "ones_col_f32": E(nc.sbuf_tensor("g_ones_col_f32", [128, 1], F32)),
            "ones_row_f32": E(nc.sbuf_tensor("g_ones_row_f32", [1, 128], F32)),
            "mask_a": E(nc.sbuf_tensor("g_mask_a", [128, 4 * 512], BF16)),
            "consts": E(nc.sbuf_tensor("g_consts", [128, 8], F32)),
            "junk": E(nc.sbuf_tensor("g_junk", [128, 16], F32)),
        }
        for j, v in enumerate(CONST_VALS):
            nc.const_aps.register(F32, v, sb["consts"][:, j:j + 1])
        emit_prologue(c, dram, sb)
        if only == "attnb":
            stages = [
                lambda c, d, s2, st: emit_attn_b_p1(c, d, s2, st, "xT_in"),
                lambda c, d, s2, st: emit_attn_b_p2(c, d, s2, st),
                lambda c, d, s2, st: emit_attn_b_p3(c, d, s2, st, "xT_in", "x3T"),
            ]
        else:
            stages = [
            emit_attn_a_p1, emit_attn_a_p2, emit_attn_a_p3,
            lambda c, d, s2, st: emit_moe_gate(c, d, s2, st, "x1T", "m1"),
            lambda c, d, s2, st: emit_moe_exp(c, d, s2, st, "x1T", "x2T", "m1"),
            lambda c, d, s2, st: emit_attn_b_p1(c, d, s2, st, "x2T"),
            lambda c, d, s2, st: emit_attn_b_p2(c, d, s2, st),
            lambda c, d, s2, st: emit_attn_b_p3(c, d, s2, st, "x2T", "x3T"),
            lambda c, d, s2, st: emit_moe_gate(c, d, s2, st, "x3T", "m2"),
            lambda c, d, s2, st: emit_moe_exp(c, d, s2, st, "x3T", "outT", "m2"),
            ]
        for i, fn in enumerate(stages):
            if i >= n_stages:
                break
            with ExitStack() as st:
                fn(c, dram, sb, st)
        block = E(nc.Block())
        for eng in Emit.ENGINES:
            fns = c.em.q[eng]
            if fns:
                getattr(block, eng)(lambda e, fns=fns: [f(e) for f in fns])
    return nc



def _bf(x):
    return np.ascontiguousarray(x).astype(ml_dtypes.bfloat16)


K_CONST = CONST_VALS


def _prep_inputs(inputs):
    g = {}
    f32 = lambda x: np.ascontiguousarray(np.asarray(x), dtype=np.float32)
    tr = lambda x: np.ascontiguousarray(np.asarray(x).T)
    g["aa_wq"] = f32(inputs["L0_attn_a_Wq"]); g["aa_wk"] = f32(inputs["L0_attn_a_Wk"])
    g["aa_wv"] = f32(inputs["L0_attn_a_Wv"]); g["aa_wo"] = f32(inputs["L0_attn_a_Wo"])
    bq = f32(inputs["L0_attn_a_bq"]); bk = f32(inputs["L0_attn_a_bk"])
    col = lambda b: np.ascontiguousarray(b.reshape(-1, 128).T)
    g["aa_bq_col"] = col(bq); g["aa_bk_col"] = col(bk)
    g["aa_nbq_col"] = col(-bq); g["aa_nbk_col"] = col(-bk)
    g["aa_bv_row"] = _bf(f32(inputs["L0_attn_a_bv"]).reshape(1, -1))
    g["aa_bo_row"] = _bf(f32(inputs["L0_attn_a_bo"]).reshape(1, -1))
    g["gate_wT"] = tr(f32(inputs["L0_ffn_gate_w"]))
    g["guwT"] = np.ascontiguousarray(np.asarray(inputs["L0_ffn_gate_up_w"], dtype=np.float32).transpose(0, 2, 1))
    g["dwT"] = np.ascontiguousarray(np.asarray(inputs["L0_ffn_down_w"], dtype=np.float32).transpose(0, 2, 1))
    g["ab_norm_in_w"] = col(f32(inputs["L0_attn_b_norm_in_w"]))
    cw = f32(inputs["L0_attn_b_conv_w"]).reshape(1024, 4).reshape(8, 128, 4)
    g["ab_conv_w"] = np.ascontiguousarray(cw.transpose(1, 0, 2).reshape(128, 32))
    g["ab_conv_b"] = col(f32(inputs["L0_attn_b_conv_b"]))
    g["ab_q_w"] = f32(inputs["L0_attn_b_q_w"]); g["ab_k_w"] = f32(inputs["L0_attn_b_k_w"])
    g["ab_v_w"] = f32(inputs["L0_attn_b_v_w"]); g["ab_alpha_w"] = f32(inputs["L0_attn_b_alpha_w"])
    g["ab_alpha_b"] = col(f32(inputs["L0_attn_b_alpha_b"]))
    g["ab_beta_w"] = f32(inputs["L0_attn_b_beta_w"])
    g["ab_beta_b"] = f32(inputs["L0_attn_b_beta_b"]).reshape(1, -1)
    g["ab_head_norm_row"] = f32(inputs["L0_attn_b_head_norm_w"]).reshape(1, -1)
    g["ab_u1_w"] = f32(inputs["L0_attn_b_u1_w"])
    g["ab_u1_b"] = f32(inputs["L0_attn_b_u1_b"]).reshape(-1, 1)
    g["ab_u2_w"] = f32(inputs["L0_attn_b_u2_w"])
    g["ab_u2_b"] = col(f32(inputs["L0_attn_b_u2_b"]))
    g["ab_out_w"] = f32(inputs["L0_attn_b_out_w"])
    g["ab_ff_norm_w"] = col(f32(inputs["L0_attn_b_ff_norm_w"]))
    g["ff_w1"] = f32(inputs["L0_attn_b_ff_w1"]); g["ff_w3"] = f32(inputs["L0_attn_b_ff_w3"])
    g["ff_w2"] = f32(inputs["L0_attn_b_ff_w2"])
    g["ident_bf"] = _bf(np.eye(128, dtype=np.float32))
    g["ident_f32"] = np.eye(128, dtype=np.float32)
    g["ones_col_bf"] = _bf(np.ones((128, 1), np.float32))
    g["ones_row_bf"] = _bf(np.ones((1, 512), np.float32))
    g["ones_col_f32"] = np.ones((128, 1), np.float32)
    g["const_row"] = np.tile(np.array(K_CONST, np.float32), (128, 1))
    g["ones_row_f32"] = np.ones((1, 128), np.float32)
    mask = np.zeros((128, 4, 512), np.float32)
    for m in range(4):
        for i in range(128):
            mask[i, m, m * 128 + i:] = 1.0
    g["mask_a"] = _bf(mask.reshape(128, 4 * 512))
    idx = np.arange(128)
    g["mask_sl"] = _bf((idx[None, :] < idx[:, None]).astype(np.float32))
    g["mask_iu"] = _bf((idx[:, None] <= idx[None, :]).astype(np.float32))
    return g


def kernel(**inputs):
    g = _prep_inputs(inputs)
    x = np.asarray(inputs["L0_attn_a_x"], dtype=np.float32)
    nc = build_program()
    in_maps = []
    for b in range(8):
        m = dict(g)
        m["xT_in"] = np.ascontiguousarray(x[b].T)
        in_maps.append(m)
    res = run_bass_kernel_spmd(nc, in_maps, core_ids=list(range(8)))
    out = np.stack([np.ascontiguousarray(r["outT"].T) for r in res.results])
    return out


# version 05
# speedup vs baseline: 1.3415x; 1.3415x over previous
import numpy as np
import ml_dtypes
import concourse.bass as bass
import concourse.mybir as mybir
from concourse.bass_utils import run_bass_kernel_spmd
from contextlib import ExitStack

F32 = mybir.dt.float32
BF16 = mybir.dt.bfloat16
AF = mybir.ActivationFunctionType
ALU = mybir.AluOpType
AX = mybir.AxisListType

CONST_VALS = (1e-5, 1e-6, 1.0, 0.5, 2.0, -1.0, 0.0, 1e-30)

T = 8192
D = 1024
DI_A = 2048
NCHUNK = 16
CHUNK = 512
NE = 32
H = 16
DK = DV = 64
CB = 128
NCB = T // CB



class Ctr:
    def __init__(self, sem, name=""):
        self.sem = sem
        self.n = 0
        self.name = name

    def inc(self, k=1):
        self.n += k
        return self.n


class Emit:
    ENGINES = ("sync", "gpsimd", "tensor", "vector", "scalar")

    def __init__(self):
        self.q = {e: [] for e in self.ENGINES}

    def _add(self, eng, fn):
        self.q[eng].append(fn)

    def sy(self, fn): self._add("sync", fn)
    def gp(self, fn): self._add("gpsimd", fn)
    def te(self, fn): self._add("tensor", fn)
    def ve(self, fn): self._add("vector", fn)
    def se(self, fn): self._add("scalar", fn)


class Ctx:
    def __init__(self, nc, top):
        self.nc = nc
        self.top = top
        self.em = Emit()
        E = top.enter_context
        self.ld = Ctr(E(nc.semaphore("s_ld")), "ld")
        self.st = Ctr(E(nc.semaphore("s_st")), "st")
        self.cs = Ctr(E(nc.semaphore("s_cs")), "cs")
        self.te = Ctr(E(nc.semaphore("s_te")), "te")
        self.ve = Ctr(E(nc.semaphore("s_ve")), "ve")
        self.se = Ctr(E(nc.semaphore("s_se")), "se")
        self.ge = Ctr(E(nc.semaphore("s_ge")), "ge")
        self.bar = Ctr(E(nc.semaphore("s_bar")), "bar")
        self.stage_marks = []
        self.ps = [E(nc.psum_tensor(f"ps{i}", [128, 512], F32)) for i in range(6)]
        self.pt = [E(nc.psum_tensor(f"pt{i}", [128, 1024], BF16)) for i in range(2)]

    def mark(self, name):
        self.stage_marks.append((name, {e: len(self.em.q[e]) for e in Emit.ENGINES}))

    def barrier(self):
        em, c = self.em, self
        ld_n, st_n, cs_n = c.ld.n, c.st.n, c.cs.n
        em.sy(lambda e: e.wait_ge(c.ld.sem, ld_n))
        em.sy(lambda e: e.wait_ge(c.st.sem, st_n))
        em.gp(lambda e: e.wait_ge(c.cs.sem, cs_n))
        target = self.bar.n + 5
        for eng in Emit.ENGINES:
            self.em._add(eng, lambda e: e.sem_inc(c.bar.sem, 1))
            self.em._add(eng, lambda e: e.wait_ge(c.bar.sem, target))
        self.bar.n = target


def _r(ap_, pat, **kw):
    return ap_.rearrange(pat, **kw)



def emit_prologue(c, dram, sb):
    em, nc = c.em, c.nc

    def cast_dma(dst, src):
        em.gp(lambda e, a=dst, b=src: e.dma_start(out=a, in_=b).then_inc(c.cs.sem, 16))
        c.cs.inc(16)

    for e_i in range(NE):
        cast_dma(dram["guw_bf"][e_i], dram["guwT"][e_i])
        cast_dma(dram["dw_bf"][e_i], dram["dwT"][e_i])
    for name in ("ff_w1", "ff_w3", "ff_w2"):
        w = dram[name]
        nrow = w.shape[0]
        for i in range(0, nrow, 1024):
            cast_dma(dram[name + "_bf"][i:i + 1024], w[i:i + 1024])

    def ld(dst, src):
        em.sy(lambda e, a=dst, b=src: e.dma_start(out=a, in_=b).then_inc(c.ld.sem, 16))
        c.ld.inc(16)

    def ldcast(dst, src):
        cast_dma(dst, src)

    ld(sb["ident"][:], dram["ident_bf"])
    ld(sb["ident_f32"][:], dram["ident_f32"])
    ld(sb["ones_col_bf"][:], dram["ones_col_bf"])
    ld(sb["ones_row_bf"][:], dram["ones_row_bf"])
    ld(sb["ones_col_f32"][:], dram["ones_col_f32"])
    ld(sb["ones_row_f32"][:], dram["ones_row_f32"])
    ld(sb["mask_a"][:], dram["mask_a"])
    ld(sb["consts"][:], dram["const_row"])
    for i in range(6):
        em.ve(lambda e, i=i: e.memset(c.ps[i][:, :], 0.0))
    em.ve(lambda e: e.engine_nop().then_inc(c.ve.sem, 1))
    c.ve.inc()
    c.barrier()



def emit_attn_a_p1(c, dram, sb, st):
    em, nc = c.em, c.nc
    E = st.enter_context
    xt = E(nc.sbuf_tensor("a1_xt", [128, 8 * 512], F32))
    xn = E(nc.sbuf_tensor("a1_xn", [128, 8 * 512], BF16))
    st_m = E(nc.sbuf_tensor("a1_st_m", [1, 512], F32))
    st_ms = E(nc.sbuf_tensor("a1_st_ms", [1, 512], F32))
    st_r = E(nc.sbuf_tensor("a1_st_r", [1, 512], F32))
    st_mr = E(nc.sbuf_tensor("a1_st_mr", [1, 512], F32))
    rdB = E(nc.sbuf_tensor("a1_rdB", [128, 512], F32))
    mrB = E(nc.sbuf_tensor("a1_mrB", [128, 512], F32))
    tmp1 = E(nc.sbuf_tensor("a1_tmp1", [128, 512], F32))
    tmp2 = E(nc.sbuf_tensor("a1_tmp2", [128, 512], F32))
    qT = E(nc.sbuf_tensor("a1_qT", [128, 16 * 512], BF16))
    kT = E(nc.sbuf_tensor("a1_kT", [128, 16 * 512], BF16))
    vt = E(nc.sbuf_tensor("a1_vt", [128, 4 * 2048], BF16))
    wq = E(nc.sbuf_tensor("a1_wq", [128, 8 * 2048], BF16))
    wk = E(nc.sbuf_tensor("a1_wk", [128, 8 * 2048], BF16))
    wv = E(nc.sbuf_tensor("a1_wv", [128, 8 * 2048], BF16))
    bq = E(nc.sbuf_tensor("a1_bq", [128, 16], F32))
    bk = E(nc.sbuf_tensor("a1_bk", [128, 16], F32))
    nbq = E(nc.sbuf_tensor("a1_nbq", [128, 16], F32))
    nbk = E(nc.sbuf_tensor("a1_nbk", [128, 16], F32))
    bv_row = E(nc.sbuf_tensor("a1_bv_row", [1, 2048], BF16))

    for w_sb, name in ((wq, "aa_wq"), (wk, "aa_wk"), (wv, "aa_wv")):
        em.gp(lambda e, a=w_sb, b=dram[name]: e.dma_start(
            out=_r(a[:], "p (k n) -> p k n", n=2048),
            in_=_r(b, "(k p) n -> p k n", p=128)).then_inc(c.cs.sem, 16))
        c.cs.inc(16)
    em.sy(lambda e: e.dma_start(out=bq[:], in_=dram["aa_bq_col"]).then_inc(c.ld.sem, 16)); c.ld.inc(16)
    em.sy(lambda e: e.dma_start(out=bk[:], in_=dram["aa_bk_col"]).then_inc(c.ld.sem, 16)); c.ld.inc(16)
    em.sy(lambda e: e.dma_start(out=nbq[:], in_=dram["aa_nbq_col"]).then_inc(c.ld.sem, 16)); c.ld.inc(16)
    em.sy(lambda e: e.dma_start(out=nbk[:], in_=dram["aa_nbk_col"]).then_inc(c.ld.sem, 16)); c.ld.inc(16)
    em.gp(lambda e: e.dma_start(out=bv_row[:], in_=dram["aa_bv_row"]).then_inc(c.cs.sem, 16)); c.cs.inc(16)
    ld_w_done = c.ld.n
    cs_w_done = c.cs.n

    x0r = _r(dram["xT_in"], "(k p) t -> p k t", p=128)
    qr = _r(dram["aa_q"], "(m p) t -> p m t", p=128)
    kr = _r(dram["aa_k"], "(m p) t -> p m t", p=128)

    ps = c.ps

    def chunk(ci):
        t0 = ci * CHUNK
        em.ve(lambda e, v=c.st.n: e.wait_ge(c.st.sem, v))
        em.sy(lambda e: e.dma_start(out=_r(xt[:], "p (k t) -> p k t", t=512),
                                    in_=x0r[:, :, t0:t0 + 512]).then_inc(c.ld.sem, 16))
        ldx = c.ld.inc(16)
        em.te(lambda e: e.wait_ge(c.ld.sem, max(ldx, ld_w_done)))
        em.te(lambda e: e.wait_ge(c.cs.sem, cs_w_done))
        for i in range(8):
            em.te(lambda e, i=i: e.matmul(ps[0][0:1, :], sb["ones_col_f32"][:],
                                          xt[:, i * 512:(i + 1) * 512],
                                          start=(i == 0), stop=(i == 7)))
        em.se(lambda e, v=ldx: e.wait_ge(c.ld.sem, v))
        for i in range(8):
            buf = tmp1 if i % 2 == 0 else tmp2
            if i >= 2:
                em.se(lambda e, v=c.te.n + (i - 1): e.wait_ge(c.te.sem, v))
            em.se(lambda e, i=i, buf=buf: e.activation(
                buf[:], xt[:, i * 512:(i + 1) * 512], AF.Square).then_inc(c.se.sem, 1))
            se_sq = c.se.inc()
            em.te(lambda e, v=se_sq: e.wait_ge(c.se.sem, v))
            mm = lambda e, i=i, buf=buf: e.matmul(ps[0][32:33, :], sb["ones_col_f32"][:],
                                                  buf[:], start=(i == 0), stop=(i == 7))
            if i == 7:
                em.te(lambda e, mm=mm: mm(e).then_inc(c.te.sem, 1))
            else:
                em.te(lambda e, mm=mm: mm(e).then_inc(c.te.sem, 1))
        te_stat = c.te.n + 8
        c.te.n = te_stat
        em.ve(lambda e: e.wait_ge(c.te.sem, te_stat))
        em.ve(lambda e: e.tensor_scalar(st_m[:], ps[0][0:1, :], 1.0 / D, None, ALU.mult))
        em.ve(lambda e: e.tensor_scalar(st_ms[:], ps[0][32:33, :], 1.0 / D, None, ALU.mult))
        em.ve(lambda e: e.tensor_tensor(st_r[:], st_m[:], st_m[:], ALU.mult))
        em.ve(lambda e: e.tensor_tensor(st_r[:], st_ms[:], st_r[:], ALU.subtract))
        em.ve(lambda e: e.engine_nop().then_inc(c.ve.sem, 1))
        ve_var = c.ve.inc()
        em.se(lambda e: e.wait_ge(c.ve.sem, ve_var))
        em.se(lambda e: e.activation(st_r[:], st_r[:], AF.Ln, bias=1e-5))
        em.se(lambda e: e.activation(st_r[:], st_r[:], AF.Exp, scale=-0.5))
        em.se(lambda e: e.copy(st_mr[:], st_m[:]).then_inc(c.se.sem, 1))
        se_rstd = c.se.inc()
        em.ve(lambda e: e.wait_ge(c.se.sem, se_rstd))
        em.ve(lambda e: e.tensor_tensor(st_mr[:], st_mr[:], st_r[:], ALU.mult))
        em.ve(lambda e: e.engine_nop().then_inc(c.ve.sem, 1))
        ve_stats = c.ve.inc()
        em.te(lambda e: e.wait_ge(c.ve.sem, ve_stats))
        em.te(lambda e: e.matmul(ps[1][:, :], sb["ones_row_f32"][:], st_r[:],
                                 start=True, stop=True))
        em.te(lambda e: e.matmul(ps[2][:, :], sb["ones_row_f32"][:], st_mr[:],
                                 start=True, stop=True).then_inc(c.te.sem, 1))
        te_bc = c.te.inc()
        em.ve(lambda e: e.wait_ge(c.te.sem, te_bc))
        em.ve(lambda e: e.tensor_copy(rdB[:], ps[1][:, :]))
        em.ve(lambda e: e.tensor_copy(mrB[:], ps[2][:, :]))
        for i in range(8):
            em.ve(lambda e, i=i: e.tensor_tensor(
                tmp1[:], xt[:, i * 512:(i + 1) * 512], rdB[:], ALU.mult))
            em.ve(lambda e, i=i: e.tensor_tensor(
                xn[:, i * 512:(i + 1) * 512], tmp1[:], mrB[:], ALU.subtract))
        em.ve(lambda e: e.engine_nop().then_inc(c.ve.sem, 1))
        ve_xn = c.ve.inc()
        em.te(lambda e: e.wait_ge(c.ve.sem, ve_xn))

        for (w_sb, b_col, nb_col, out_sb) in ((wq, bq, nbq, qT), (wk, bk, nbk, kT)):
            def proj_qk(w_sb=w_sb, b_col=b_col, nb_col=nb_col, out_sb=out_sb):
                for m in range(16):
                    bank = 3 + (m % 2)
                    ve_need = c.ve.n
                    if m >= 2:
                        em.te(lambda e, v=ve_need: e.wait_ge(c.ve.sem, v))
                    for k in range(8):
                        mmf = lambda e, m=m, k=k, bank=bank, w_sb=w_sb: e.matmul(
                            ps[bank][:, :], w_sb[:, k * 2048 + m * 128: k * 2048 + m * 128 + 128],
                            xn[:, k * 512:(k + 1) * 512], start=(k == 0), stop=(k == 7))
                        if k == 7:
                            em.te(lambda e, mmf=mmf: mmf(e).then_inc(c.te.sem, 1))
                        else:
                            em.te(mmf)
                    te_mm = c.te.inc()
                    em.se(lambda e, v=te_mm: e.wait_ge(c.te.sem, v))
                    em.se(lambda e, v=c.ve.n: e.wait_ge(c.ve.sem, v))
                    em.se(lambda e, m=m, bank=bank, b_col=b_col: e.activation(
                        tmp1[:], ps[bank][:, :], AF.Relu, bias=b_col[:, m:m + 1]))
                    em.se(lambda e, m=m, bank=bank, nb_col=nb_col: e.activation(
                        tmp2[:], ps[bank][:, :], AF.Relu, bias=nb_col[:, m:m + 1], scale=-1.0))
                    em.se(lambda e: e.activation(tmp2[:], tmp2[:], AF.Exp, scale=-1.0).then_inc(c.se.sem, 1))
                    se_phi = c.se.inc()
                    em.ve(lambda e, v=se_phi: e.wait_ge(c.se.sem, v))
                    em.ve(lambda e, m=m, out_sb=out_sb: e.tensor_tensor(
                        out_sb[:, m * 512:(m + 1) * 512], tmp1[:], tmp2[:], ALU.add))
                    em.ve(lambda e: e.engine_nop().then_inc(c.ve.sem, 1))
                    c.ve.inc()
            proj_qk()
        cs_bv = cs_w_done
        em.te(lambda e: e.wait_ge(c.cs.sem, cs_bv))
        em.te(lambda e, v=c.ve.n: e.wait_ge(c.ve.sem, v))
        for tt in range(4):
            for n in range(4):
                g = tt * 4 + n
                bank = 3 + (g % 2)
                if g >= 2:
                    em.te(lambda e, v=c.ve.n: e.wait_ge(c.ve.sem, v))
                for k in range(8):
                    em.te(lambda e, tt=tt, n=n, k=k, bank=bank: e.matmul(
                        ps[bank][:, :], xn[:, k * 512 + tt * 128: k * 512 + tt * 128 + 128],
                        wv[:, k * 2048 + n * 512: k * 2048 + n * 512 + 512],
                        start=(k == 0), stop=False))
                em.te(lambda e, n=n, bank=bank: e.matmul(
                    ps[bank][:, :], sb["ones_row_bf"][0:1, 0:128],
                    bv_row[0:1, n * 512:(n + 1) * 512],
                    start=False, stop=True).then_inc(c.te.sem, 1))
                te_v = c.te.inc()
                em.ve(lambda e, v=te_v: e.wait_ge(c.te.sem, v))
                em.ve(lambda e, tt=tt, n=n, bank=bank: e.tensor_copy(
                    vt[:, tt * 2048 + n * 512: tt * 2048 + n * 512 + 512], ps[bank][:, :]))
                em.ve(lambda e: e.engine_nop().then_inc(c.ve.sem, 1))
                c.ve.inc()
        ve_all = c.ve.n
        em.sy(lambda e, v=ve_all: e.wait_ge(c.ve.sem, v))
        em.sy(lambda e, t0=t0: e.dma_start(
            out=qr[:, :, t0:t0 + 512],
            in_=_r(qT[:], "p (m t) -> p m t", t=512)).then_inc(c.st.sem, 16))
        c.st.inc(16)
        em.sy(lambda e, t0=t0: e.dma_start(
            out=kr[:, :, t0:t0 + 512],
            in_=_r(kT[:], "p (m t) -> p m t", t=512)).then_inc(c.st.sem, 16))
        c.st.inc(16)
        em.sy(lambda e, t0=t0: e.dma_start(
            out=_r(dram["aa_v"][t0:t0 + 512, :], "(c p) n -> p c n", p=128),
            in_=_r(vt[:], "p (c n) -> p c n", n=2048)).then_inc(c.st.sem, 16))
        c.st.inc(16)
        em.sy(lambda e, v=c.te.n: e.wait_ge(c.te.sem, v))

    for ci in range(NCHUNK):
        chunk(ci)
    c.barrier()



def emit_attn_a_p2(c, dram, sb, st):
    em, nc = c.em, c.nc
    E = st.enter_context
    S = E(nc.sbuf_tensor("a2_S", [128, 16 * 2048], BF16))
    z = E(nc.sbuf_tensor("a2_z", [128, 16], F32))
    zbf = E(nc.sbuf_tensor("a2_zbf", [128, 16], BF16))
    qT = E(nc.sbuf_tensor("a2_qT", [128, 16 * 512], BF16))
    kT = E(nc.sbuf_tensor("a2_kT", [128, 16 * 512], BF16))
    vt = E(nc.sbuf_tensor("a2_vt", [128, 4 * 2048], BF16))
    ktok = E(nc.sbuf_tensor("a2_ktok", [128, 4 * 2048], BF16))
    msc = E(nc.sbuf_tensor("a2_msc", [128, 4 * 512], BF16))
    yT = E(nc.sbuf_tensor("a2_yT", [128, 16 * 512], BF16))
    den = E(nc.sbuf_tensor("a2_den", [1, 512], F32))
    rdB = E(nc.sbuf_tensor("a2_rdB", [128, 512], F32))

    em.gp(lambda e: e.memset(S[:], 0.0))
    em.gp(lambda e: e.memset(z[:], 0.0))
    em.gp(lambda e: e.memset(zbf[:], 0.0))
    em.gp(lambda e: e.engine_nop().then_inc(c.ge.sem, 1))
    ge_init = c.ge.inc()
    cs_w = c.cs.n
    em.te(lambda e: e.wait_ge(c.ge.sem, ge_init))
    em.te(lambda e: e.wait_ge(c.cs.sem, cs_w))
    em.ve(lambda e: e.wait_ge(c.ge.sem, ge_init))

    qr = _r(dram["aa_q"], "(m p) t -> p m t", p=128)
    kr = _r(dram["aa_k"], "(m p) t -> p m t", p=128)
    yr = _r(dram["aa_y"], "(m p) t -> p m t", p=128)
    ps, pt = c.ps, c.pt

    def chunk(ci):
        t0 = ci * CHUNK
        em.ve(lambda e, v=c.st.n: e.wait_ge(c.st.sem, v))
        em.sy(lambda e: e.dma_start(out=_r(qT[:], "p (m t) -> p m t", t=512),
                                    in_=qr[:, :, t0:t0 + 512]).then_inc(c.ld.sem, 16))
        c.ld.inc(16)
        em.sy(lambda e: e.dma_start(out=_r(kT[:], "p (m t) -> p m t", t=512),
                                    in_=kr[:, :, t0:t0 + 512]).then_inc(c.ld.sem, 16))
        c.ld.inc(16)
        em.sy(lambda e: e.dma_start(
            out=_r(vt[:], "p (c n) -> p c n", n=2048),
            in_=_r(dram["aa_v"][t0:t0 + 512, :], "(c p) n -> p c n", p=128)).then_inc(c.ld.sem, 16))
        c.ld.inc(16)
        ldx = c.ld.n
        em.te(lambda e, v=ldx: e.wait_ge(c.ld.sem, v))

        for m in range(4):
            bank = m % 2
            if ci > 0 or m >= 2:
                em.te(lambda e, v=c.ve.n: e.wait_ge(c.ve.sem, v))
            for i in range(16):
                mmf = lambda e, m=m, i=i, bank=bank: e.matmul(
                    ps[bank][:, :], kT[:, i * 512 + m * 128: i * 512 + m * 128 + 128],
                    qT[:, i * 512:(i + 1) * 512], start=(i == 0), stop=(i == 15))
                if i == 15:
                    em.te(lambda e, mmf=mmf: mmf(e).then_inc(c.te.sem, 1))
                else:
                    em.te(mmf)
            te_s = c.te.inc()
            em.ve(lambda e, v=te_s: e.wait_ge(c.te.sem, v))
            em.ve(lambda e, m=m, bank=bank: e.tensor_tensor(
                msc[:, m * 512:(m + 1) * 512], ps[bank][:, :],
                sb["mask_a"][:, m * 512:(m + 1) * 512], ALU.mult))
            em.ve(lambda e: e.engine_nop().then_inc(c.ve.sem, 1))
            c.ve.inc()
        ve_msc = c.ve.n

        for tt in range(4):
            ptb = pt[tt % 2]
            if tt >= 2:
                em.te(lambda e, v=c.ve.n: e.wait_ge(c.ve.sem, v))
            for half in range(2):
                if half == 1:
                    em.te(lambda e, v=c.ve.n: e.wait_ge(c.ve.sem, v))
                for i8 in range(8):
                    i = half * 8 + i8
                    trf = lambda e, tt=tt, i=i, i8=i8, ptb=ptb: e.transpose(
                        ptb[:, i8 * 128:(i8 + 1) * 128],
                        kT[:, i * 512 + tt * 128: i * 512 + tt * 128 + 128], sb["ident"][:])
                    if i8 == 7:
                        em.te(lambda e, trf=trf: trf(e).then_inc(c.te.sem, 1))
                    else:
                        em.te(trf)
                te_tr = c.te.inc()
                em.ve(lambda e, v=te_tr: e.wait_ge(c.te.sem, v))
                em.ve(lambda e, tt=tt, half=half, ptb=ptb: e.tensor_copy(
                    ktok[:, tt * 2048 + half * 1024: tt * 2048 + half * 1024 + 1024], ptb[:, :]))
                em.ve(lambda e: e.engine_nop().then_inc(c.ve.sem, 1))
                c.ve.inc()

        em.te(lambda e, v=ve_msc: e.wait_ge(c.ve.sem, v))
        for m in range(4):
            em.te(lambda e, m=m: e.matmul(
                ps[5][0:1, :], sb["ones_col_bf"][:], msc[:, m * 512:(m + 1) * 512],
                start=(m == 0), stop=False))
        for i in range(16):
            mmf = lambda e, i=i: e.matmul(
                ps[5][0:1, :], zbf[:, i:i + 1], qT[:, i * 512:(i + 1) * 512],
                start=False, stop=(i == 15))
            if i == 15:
                em.te(lambda e, mmf=mmf: mmf(e).then_inc(c.te.sem, 1))
            else:
                em.te(mmf)
        te_den = c.te.inc()
        em.ve(lambda e, v=te_den: e.wait_ge(c.te.sem, v))
        em.ve(lambda e: e.tensor_scalar(den[:], ps[5][0:1, :], 1e-6, None, ALU.add))
        em.ve(lambda e: e.tensor_scalar(sb["junk"][:], sb["junk"][:], 1.0, None, ALU.mult))
        em.ve(lambda e: e.reciprocal(den[:], den[:]))
        em.ve(lambda e: e.engine_nop().then_inc(c.ve.sem, 1))
        ve_den = c.ve.inc()
        em.te(lambda e, v=ve_den: e.wait_ge(c.ve.sem, v))
        em.te(lambda e: e.matmul(ps[4][:, :], sb["ones_row_f32"][:], den[:],
                                 start=True, stop=True).then_inc(c.te.sem, 1))
        te_bc = c.te.inc()
        em.ve(lambda e, v=te_bc: e.wait_ge(c.te.sem, v))
        em.ve(lambda e: e.tensor_copy(rdB[:], ps[4][:, :]))
        em.ve(lambda e: e.engine_nop().then_inc(c.ve.sem, 1))
        ve_rdb = c.ve.inc()

        em.te(lambda e, v=ve_rdb: e.wait_ge(c.ve.sem, v))
        for j in range(16):
            bank = 2 + (j % 2)
            if j >= 2:
                em.te(lambda e, v=c.ve.n: e.wait_ge(c.ve.sem, v))
            first = True
            if ci > 0:
                for i in range(16):
                    em.te(lambda e, i=i, j=j, bank=bank, first=first and i == 0: e.matmul(
                        ps[bank][:, :], S[:, i * 2048 + j * 128: i * 2048 + j * 128 + 128],
                        qT[:, i * 512:(i + 1) * 512], start=(i == 0), stop=False))
                first = False
            for m in range(4):
                last = (m == 3)
                mmf = lambda e, m=m, j=j, bank=bank, first=(first and m == 0), last=last: e.matmul(
                    ps[bank][:, :], vt[:, m * 2048 + j * 128: m * 2048 + j * 128 + 128],
                    msc[:, m * 512:(m + 1) * 512], start=first, stop=last)
                if last:
                    em.te(lambda e, mmf=mmf: mmf(e).then_inc(c.te.sem, 1))
                else:
                    em.te(mmf)
            te_num = c.te.inc()
            em.ve(lambda e, v=te_num: e.wait_ge(c.te.sem, v))
            em.ve(lambda e, j=j, bank=bank: e.tensor_tensor(
                yT[:, j * 512:(j + 1) * 512], ps[bank][:, :], rdB[:], ALU.mult))
            em.ve(lambda e: e.engine_nop().then_inc(c.ve.sem, 1))
            c.ve.inc()

        for i in range(16):
            for n in range(4):
                g = i * 4 + n
                bank = (g % 2)
                if g >= 2:
                    em.te(lambda e, v=c.ve.n: e.wait_ge(c.ve.sem, v))
                for tt in range(4):
                    last = (tt == 3)
                    mmf = lambda e, i=i, n=n, tt=tt, bank=bank, last=last: e.matmul(
                        ps[bank][:, :], ktok[:, tt * 2048 + i * 128: tt * 2048 + i * 128 + 128],
                        vt[:, tt * 2048 + n * 512: tt * 2048 + n * 512 + 512],
                        start=(tt == 0), stop=last)
                    if last:
                        em.te(lambda e, mmf=mmf: mmf(e).then_inc(c.te.sem, 1))
                    else:
                        em.te(mmf)
                te_sd = c.te.inc()
                em.ve(lambda e, v=te_sd: e.wait_ge(c.te.sem, v))
                em.ve(lambda e, i=i, n=n, bank=bank: e.tensor_tensor(
                    S[:, i * 2048 + n * 512: i * 2048 + n * 512 + 512],
                    ps[bank][:, :],
                    S[:, i * 2048 + n * 512: i * 2048 + n * 512 + 512], ALU.add))
                em.ve(lambda e: e.engine_nop().then_inc(c.ve.sem, 1))
                c.ve.inc()
        em.te(lambda e, v=c.ve.n: e.wait_ge(c.ve.sem, v))
        for i in range(16):
            for tt in range(4):
                last = (tt == 3)
                mmf = lambda e, i=i, tt=tt, last=last: e.matmul(
                    ps[5][:, i:i + 1], ktok[:, tt * 2048 + i * 128: tt * 2048 + i * 128 + 128],
                    sb["ones_col_bf"][:], start=(tt == 0), stop=last)
                if last and i == 15:
                    em.te(lambda e, mmf=mmf: mmf(e).then_inc(c.te.sem, 1))
                else:
                    em.te(mmf)
        te_z = c.te.inc()
        em.ve(lambda e, v=te_z: e.wait_ge(c.te.sem, v))
        em.ve(lambda e: e.tensor_tensor(zbf[:], ps[5][:, 0:16], z[:], ALU.add))
        em.ve(lambda e: e.tensor_tensor(z[:], ps[5][:, 0:16], z[:], ALU.add))
        em.ve(lambda e: e.engine_nop().then_inc(c.ve.sem, 1))
        c.ve.inc()

        ve_yT = c.ve.n
        em.sy(lambda e, v=ve_yT: e.wait_ge(c.ve.sem, v))
        em.sy(lambda e, t0=t0: e.dma_start(
            out=yr[:, :, t0:t0 + 512],
            in_=_r(yT[:], "p (m t) -> p m t", t=512)).then_inc(c.st.sem, 16))
        c.st.inc(16)
        em.sy(lambda e, v=c.te.n: e.wait_ge(c.te.sem, v))
        em.sy(lambda e, v=c.ve.n: e.wait_ge(c.ve.sem, v))

    for ci in range(NCHUNK):
        chunk(ci)
    if "dbg_z" in dram:
        em.sy(lambda e, v=c.ve.n: e.wait_ge(c.ve.sem, v))
        em.sy(lambda e: e.dma_start(out=dram["dbg_z"], in_=z[:]).then_inc(c.st.sem, 16))
        c.st.inc(16)
        em.sy(lambda e: e.dma_start(out=dram["dbg_den"], in_=den[:]).then_inc(c.st.sem, 16))
        c.st.inc(16)
        em.sy(lambda e: e.dma_start(out=dram["dbg_rdB"], in_=rdB[:]).then_inc(c.st.sem, 16))
        c.st.inc(16)
    c.barrier()



def emit_attn_a_p3(c, dram, sb, st):
    em, nc = c.em, c.nc
    E = st.enter_context
    yT = E(nc.sbuf_tensor("a3_yT", [128, 16 * 512], BF16))
    xt = E(nc.sbuf_tensor("a3_xt", [128, 8 * 512], F32))
    wo = E(nc.sbuf_tensor("a3_wo", [128, 16 * 1024], BF16))
    bo_row = E(nc.sbuf_tensor("a3_bo_row", [1, 1024], BF16))

    em.gp(lambda e: e.dma_start(
        out=_r(wo[:], "p (k n) -> p k n", n=1024),
        in_=_r(dram["aa_wo"], "(k p) n -> p k n", p=128)).then_inc(c.cs.sem, 16))
    c.cs.inc(16)
    em.gp(lambda e: e.dma_start(out=bo_row[:], in_=dram["aa_bo_row"]).then_inc(c.cs.sem, 16))
    c.cs.inc(16)
    cs_w = c.cs.n
    em.te(lambda e: e.wait_ge(c.cs.sem, cs_w))

    yr = _r(dram["aa_y"], "(m p) t -> p m t", p=128)
    x0r = _r(dram["xT_in"], "(k p) t -> p k t", p=128)
    x1r = _r(dram["x1T"], "(k p) t -> p k t", p=128)
    ps = c.ps

    def chunk(ci):
        t0 = ci * CHUNK
        em.ve(lambda e, v=c.st.n: e.wait_ge(c.st.sem, v))
        em.sy(lambda e, t0=t0: e.dma_start(out=_r(yT[:], "p (m t) -> p m t", t=512),
                                           in_=yr[:, :, t0:t0 + 512]).then_inc(c.ld.sem, 16))
        c.ld.inc(16)
        em.sy(lambda e, t0=t0: e.dma_start(out=_r(xt[:], "p (k t) -> p k t", t=512),
                                           in_=x0r[:, :, t0:t0 + 512]).then_inc(c.ld.sem, 16))
        ldx = c.ld.inc(16)
        em.te(lambda e, v=ldx: e.wait_ge(c.ld.sem, v))
        for m in range(8):
            bank = m % 2
            if ci > 0 or m >= 2:
                em.te(lambda e, v=c.ve.n: e.wait_ge(c.ve.sem, v))
            for i in range(16):
                em.te(lambda e, m=m, i=i, bank=bank: e.matmul(
                    ps[bank][:, :], wo[:, i * 1024 + m * 128: i * 1024 + m * 128 + 128],
                    yT[:, i * 512:(i + 1) * 512], start=(i == 0), stop=False))
            em.te(lambda e, m=m, bank=bank: e.matmul(
                ps[bank][:, :], bo_row[0:1, m * 128:(m + 1) * 128],
                sb["ones_row_bf"][:], start=False, stop=True).then_inc(c.te.sem, 1))
            te_o = c.te.inc()
            em.ve(lambda e, v=te_o: e.wait_ge(c.te.sem, v))
            em.ve(lambda e, m=m, bank=bank: e.tensor_tensor(
                xt[:, m * 512:(m + 1) * 512], ps[bank][:, :],
                xt[:, m * 512:(m + 1) * 512], ALU.add))
            em.ve(lambda e: e.engine_nop().then_inc(c.ve.sem, 1))
            c.ve.inc()
        ve_x1 = c.ve.n
        em.sy(lambda e, v=ve_x1: e.wait_ge(c.ve.sem, v))
        em.sy(lambda e, t0=t0: e.dma_start(
            out=x1r[:, :, t0:t0 + 512],
            in_=_r(xt[:], "p (k t) -> p k t", t=512)).then_inc(c.st.sem, 16))
        c.st.inc(16)
        em.sy(lambda e, v=c.te.n: e.wait_ge(c.te.sem, v))

    for ci in range(NCHUNK):
        chunk(ci)
    c.barrier()




def emit_ln_to(c, sb, xt, xn, aux, ldx, ld_extra=0):
    em, ps = c.em, c.ps
    em.te(lambda e, v=max(ldx, ld_extra): e.wait_ge(c.ld.sem, v))
    for i in range(8):
        em.te(lambda e, i=i: e.matmul(ps[0][0:1, :], sb["ones_col_f32"][:],
                                      xt[:, i * 512:(i + 1) * 512],
                                      start=(i == 0), stop=(i == 7)))
    em.se(lambda e, v=ldx: e.wait_ge(c.ld.sem, v))
    te0 = c.te.n
    for i in range(8):
        buf = aux["tmp1"] if i % 2 == 0 else aux["tmp2"]
        if i >= 2:
            em.se(lambda e, v=te0 + (i - 1): e.wait_ge(c.te.sem, v))
        em.se(lambda e, i=i, buf=buf: e.activation(
            buf[:], xt[:, i * 512:(i + 1) * 512], AF.Square).then_inc(c.se.sem, 1))
        se_sq = c.se.inc()
        em.te(lambda e, v=se_sq: e.wait_ge(c.se.sem, v))
        em.te(lambda e, i=i, buf=buf: e.matmul(
            ps[0][32:33, :], sb["ones_col_f32"][:], buf[:],
            start=(i == 0), stop=(i == 7)).then_inc(c.te.sem, 1))
        c.te.inc()
    te_stat = c.te.n
    em.ve(lambda e, v=te_stat: e.wait_ge(c.te.sem, v))
    em.ve(lambda e: e.tensor_scalar(aux["st_m"][:], ps[0][0:1, :], 1.0 / D, None, ALU.mult))
    em.ve(lambda e: e.tensor_scalar(aux["st_ms"][:], ps[0][32:33, :], 1.0 / D, None, ALU.mult))
    em.ve(lambda e: e.tensor_tensor(aux["st_r"][:], aux["st_m"][:], aux["st_m"][:], ALU.mult))
    em.ve(lambda e: e.tensor_tensor(aux["st_r"][:], aux["st_ms"][:], aux["st_r"][:], ALU.subtract))
    em.ve(lambda e: e.engine_nop().then_inc(c.ve.sem, 1))
    ve_var = c.ve.inc()
    em.se(lambda e, v=ve_var: e.wait_ge(c.ve.sem, v))
    em.se(lambda e: e.activation(aux["st_r"][:], aux["st_r"][:], AF.Ln, bias=1e-5))
    em.se(lambda e: e.activation(aux["st_r"][:], aux["st_r"][:], AF.Exp, scale=-0.5))
    em.se(lambda e: e.copy(aux["st_mr"][:], aux["st_m"][:]).then_inc(c.se.sem, 1))
    se_rstd = c.se.inc()
    em.ve(lambda e, v=se_rstd: e.wait_ge(c.se.sem, v))
    em.ve(lambda e: e.tensor_tensor(aux["st_mr"][:], aux["st_mr"][:], aux["st_r"][:], ALU.mult))
    em.ve(lambda e: e.engine_nop().then_inc(c.ve.sem, 1))
    ve_stats = c.ve.inc()
    em.te(lambda e, v=ve_stats: e.wait_ge(c.ve.sem, v))
    em.te(lambda e: e.matmul(ps[1][:, :], sb["ones_row_f32"][:], aux["st_r"][:],
                             start=True, stop=True))
    em.te(lambda e: e.matmul(ps[2][:, :], sb["ones_row_f32"][:], aux["st_mr"][:],
                             start=True, stop=True).then_inc(c.te.sem, 1))
    te_bc = c.te.inc()
    em.ve(lambda e, v=te_bc: e.wait_ge(c.te.sem, v))
    em.ve(lambda e: e.tensor_copy(aux["rdB"][:], ps[1][:, :]))
    em.ve(lambda e: e.tensor_copy(aux["mrB"][:], ps[2][:, :]))
    for i in range(8):
        em.ve(lambda e, i=i: e.tensor_tensor(
            aux["tmp1"][:], xt[:, i * 512:(i + 1) * 512], aux["rdB"][:], ALU.mult))
        em.ve(lambda e, i=i: e.tensor_tensor(
            xn[:, i * 512:(i + 1) * 512], aux["tmp1"][:], aux["mrB"][:], ALU.subtract))
    em.ve(lambda e: e.engine_nop().then_inc(c.ve.sem, 1))
    return c.ve.inc()


def _mk_ln_aux(c, st, pref):
    E = st.enter_context
    nc = c.nc
    return {
        "st_m": E(nc.sbuf_tensor(pref + "_m", [1, 512], F32)),
        "st_ms": E(nc.sbuf_tensor(pref + "_ms", [1, 512], F32)),
        "st_r": E(nc.sbuf_tensor(pref + "_r", [1, 512], F32)),
        "st_mr": E(nc.sbuf_tensor(pref + "_mr", [1, 512], F32)),
        "rdB": E(nc.sbuf_tensor(pref + "_rdB", [128, 512], F32)),
        "mrB": E(nc.sbuf_tensor(pref + "_mrB", [128, 512], F32)),
        "tmp1": E(nc.sbuf_tensor(pref + "_tmp1", [128, 512], F32)),
        "tmp2": E(nc.sbuf_tensor(pref + "_tmp2", [128, 512], F32)),
    }


def emit_moe_gate(c, dram, sb, st, x_in, tag=""):
    em, nc = c.em, c.nc
    E = st.enter_context
    xt = E(nc.sbuf_tensor(tag + "mg_xt", [128, 8 * 512], F32))
    xn = E(nc.sbuf_tensor(tag + "mg_xn", [128, 8 * 512], BF16))
    aux = _mk_ln_aux(c, st, tag + "mg")
    gw = E(nc.sbuf_tensor(tag + "mg_gw", [128, 8 * 32], BF16))
    pexp = [E(nc.sbuf_tensor(f"{tag}mg_p{tt}", [128, 32], F32)) for tt in range(4)]
    work = [E(nc.sbuf_tensor(f"{tag}mg_w{tt}", [128, 32], F32)) for tt in range(4)]
    tmp = [E(nc.sbuf_tensor(f"{tag}mg_t{tt}", [128, 32], F32)) for tt in range(4)]
    mcol = [E(nc.sbuf_tensor(f"{tag}mg_mc{tt}", [128, 1], F32)) for tt in range(4)]
    nmx = [E(nc.sbuf_tensor(f"{tag}mg_nm{tt}", [128, 1], F32)) for tt in range(4)]
    comb = E(nc.sbuf_tensor(tag + "mg_comb", [128, 4 * 32], F32))

    em.gp(lambda e: e.dma_start(
        out=_r(gw[:], "p (k n) -> p k n", n=32),
        in_=_r(dram["gate_wT"], "(k p) n -> p k n", p=128)).then_inc(c.cs.sem, 16))
    cs_w = c.cs.inc(16)
    em.te(lambda e: e.wait_ge(c.cs.sem, cs_w))

    xr = _r(dram[x_in], "(k p) t -> p k t", p=128)
    xnr = _r(dram["moe_xn"], "(k p) t -> p k t", p=128)
    ps = c.ps

    def chunk(ci):
        t0 = ci * CHUNK
        em.ve(lambda e, v=c.st.n: e.wait_ge(c.st.sem, v))
        em.sy(lambda e: e.dma_start(out=_r(xt[:], "p (k t) -> p k t", t=512),
                                    in_=xr[:, :, t0:t0 + 512]).then_inc(c.ld.sem, 16))
        ldx = c.ld.inc(16)
        ve_xn = emit_ln_to(c, sb, xt, xn, aux, ldx)
        em.sy(lambda e, v=ve_xn: e.wait_ge(c.ve.sem, v))
        em.sy(lambda e, t0=t0: e.dma_start(
            out=xnr[:, :, t0:t0 + 512],
            in_=_r(xn[:], "p (k t) -> p k t", t=512)).then_inc(c.st.sem, 16))
        c.st.inc(16)
        em.te(lambda e, v=ve_xn: e.wait_ge(c.ve.sem, v))
        for tt in range(4):
            bank = 2 + tt
            for k in range(8):
                mm = lambda e, tt=tt, k=k, bank=bank: e.matmul(
                    ps[bank][:, 0:32], xn[:, k * 512 + tt * 128: k * 512 + tt * 128 + 128],
                    gw[:, k * 32:(k + 1) * 32], start=(k == 0), stop=(k == 7))
                if k == 7:
                    em.te(lambda e, mm=mm: mm(e).then_inc(c.te.sem, 1))
                else:
                    em.te(mm)
            c.te.inc()
        te_lg = c.te.n
        for tt in range(4):
            em.ve(lambda e, v=te_lg - 3 + tt: e.wait_ge(c.te.sem, v))
            em.ve(lambda e, tt=tt: e.reduce_max(mcol[tt][:], ps[2 + tt][:, 0:32], AX.X))
        for tt in range(4):
            em.ve(lambda e, tt=tt: e.tensor_scalar(nmx[tt][:], mcol[tt][:], -1.0, None, ALU.mult))
        em.ve(lambda e: e.engine_nop().then_inc(c.ve.sem, 1))
        ve_mx = c.ve.inc()
        em.se(lambda e, v=ve_mx: e.wait_ge(c.ve.sem, v))
        for tt in range(4):
            em.se(lambda e, tt=tt: e.activation(pexp[tt][:], ps[2 + tt][:, 0:32],
                                                AF.Exp, bias=nmx[tt][:]))
        em.se(lambda e: e.copy(work[0][:], pexp[0][:]))
        em.se(lambda e: e.copy(work[1][:], pexp[1][:]))
        em.se(lambda e: e.copy(work[2][:], pexp[2][:]))
        em.se(lambda e: e.copy(work[3][:], pexp[3][:]).then_inc(c.se.sem, 1))
        se_p = c.se.inc()
        em.ve(lambda e, v=se_p: e.wait_ge(c.se.sem, v))
        for it in range(8):
            for tt in range(4):
                em.ve(lambda e, tt=tt: e.reduce_max(mcol[tt][:], work[tt][:], AX.X))
            if it < 7:
                for tt in range(4):
                    em.ve(lambda e, tt=tt: e.tensor_scalar(
                        tmp[tt][:], work[tt][:], mcol[tt][:], -1e30, ALU.is_ge, ALU.mult))
                for tt in range(4):
                    em.ve(lambda e, tt=tt: e.tensor_tensor(
                        work[tt][:], work[tt][:], tmp[tt][:], ALU.add))
        for tt in range(4):
            em.ve(lambda e, tt=tt: e.tensor_scalar(
                tmp[tt][:], pexp[tt][:], mcol[tt][:], None, ALU.is_ge))
        for tt in range(4):
            em.ve(lambda e, tt=tt: e.tensor_tensor(
                comb[:, tt * 32:(tt + 1) * 32], pexp[tt][:], tmp[tt][:], ALU.mult))
        for tt in range(4):
            em.ve(lambda e, tt=tt: e.reduce_sum(mcol[tt][:], comb[:, tt * 32:(tt + 1) * 32], AX.X))
        for tt in range(4):
            em.ve(lambda e, tt=tt: e.reciprocal(nmx[tt][:], mcol[tt][:]))
        for tt in range(4):
            em.ve(lambda e, tt=tt: e.tensor_scalar(
                comb[:, tt * 32:(tt + 1) * 32], comb[:, tt * 32:(tt + 1) * 32],
                nmx[tt][:], None, ALU.mult))
        em.ve(lambda e: e.engine_nop().then_inc(c.ve.sem, 1))
        ve_cb = c.ve.inc()
        em.sy(lambda e, v=ve_cb: e.wait_ge(c.ve.sem, v))
        em.sy(lambda e, t0=t0: e.dma_start(
            out=_r(dram["moe_comb"][t0:t0 + 512, :], "(tt p) e -> p tt e", p=128),
            in_=_r(comb[:], "p (tt e) -> p tt e", e=32)).then_inc(c.st.sem, 16))
        c.st.inc(16)
        em.sy(lambda e, v=c.te.n: e.wait_ge(c.te.sem, v))
        em.sy(lambda e, v=c.ve.n: e.wait_ge(c.ve.sem, v))

    for ci in range(NCHUNK):
        chunk(ci)
    c.barrier()


def emit_moe_exp(c, dram, sb, st, x_in, x_out, tag=""):
    em, nc = c.em, c.nc
    E = st.enter_context
    BLK = 2048
    acc = E(nc.sbuf_tensor(tag + "me_acc", [128, 16 * 1024], F32))
    xnb = E(nc.sbuf_tensor(tag + "me_xn", [128, 8 * 2048], BF16))
    wg = [E(nc.sbuf_tensor(f"{tag}me_wg{b}", [128, 8 * 1024], BF16)) for b in range(2)]
    wd = [E(nc.sbuf_tensor(f"{tag}me_wd{b}", [128, 4 * 1024], BF16)) for b in range(2)]
    combb = E(nc.sbuf_tensor(tag + "me_comb", [128, 16 * 32], F32))
    hT = E(nc.sbuf_tensor(tag + "me_hT", [128, 2 * 4 * 512], BF16))
    sg = E(nc.sbuf_tensor(tag + "me_sg", [128, 2 * 512], BF16))
    x1c = E(nc.sbuf_tensor(tag + "me_x1c", [128, 8 * 512], F32))
    x2c = E(nc.sbuf_tensor(tag + "me_x2c", [128, 8 * 512], F32))

    xr = _r(dram[x_in], "(k p) t -> p k t", p=128)
    xo = _r(dram[x_out], "(k p) t -> p k t", p=128)
    xnr = _r(dram["moe_xn"], "(k p) t -> p k t", p=128)
    ps = c.ps

    def block(bi):
        tb = bi * BLK
        em.sy(lambda e, v=c.ve.n: e.wait_ge(c.ve.sem, v))
        em.sy(lambda e, tb=tb: e.dma_start(
            out=_r(xnb[:], "p (k t) -> p k t", t=BLK),
            in_=xnr[:, :, tb:tb + BLK]).then_inc(c.ld.sem, 16))
        c.ld.inc(16)
        em.sy(lambda e, tb=tb: e.dma_start(
            out=_r(combb[:], "p (tt e) -> p tt e", e=32),
            in_=_r(dram["moe_comb"][tb:tb + BLK, :], "(tt p) e -> p tt e", p=128)).then_inc(c.ld.sem, 16))
        ld_blk = c.ld.inc(16)
        em.gp(lambda e, v=c.te.n: e.wait_ge(c.te.sem, v))
        em.gp(lambda e: e.memset(acc[:], 0.0).then_inc(c.ge.sem, 1))
        ge_acc = c.ge.inc()
        em.te(lambda e, v=ld_blk: e.wait_ge(c.ld.sem, v))
        em.ve(lambda e, v=ld_blk: e.wait_ge(c.ld.sem, v))
        em.ve(lambda e, v=ge_acc: e.wait_ge(c.ge.sem, v))

        te_expert_done = {}
        evac_h = {}
        evac_pr = {}
        evac_dn = {}
        te_dn_last = {}
        te_wd_last = {}
        dn_seq = []
        for ei in range(NE):
            b = ei % 2
            if ei >= 2:
                wv_ = te_wd_last.get((bi, ei - 2), te_expert_done[ei - 2])
                em.sy(lambda e, v=max(wv_, te_expert_done[ei - 2]): e.wait_ge(c.te.sem, v))
            em.sy(lambda e, ei=ei, b=b: e.dma_start(
                out=_r(wg[b][:], "p (k n) -> p k n", n=1024),
                in_=_r(dram["guw_bf"][ei], "(k p) n -> p k n", p=128)).then_inc(c.ld.sem, 16))
            c.ld.inc(16)
            em.sy(lambda e, ei=ei, b=b: e.dma_start(
                out=_r(wd[b][:], "p (k n) -> p k n", n=1024),
                in_=_r(dram["dw_bf"][ei], "(k p) n -> p k n", p=128)).then_inc(c.ld.sem, 16))
            ld_w = c.ld.inc(16)
            em.te(lambda e, v=ld_w: e.wait_ge(c.ld.sem, v))

            def emit_down(gg):
                eig, g4g = divmod(gg, 4)
                eig = eig % NE
                bw = eig % 2
                hbuf = gg % 2
                em.te(lambda e, v=evac_h[gg]: e.wait_ge(c.ve.sem, v))
                for grp8 in range(8):
                    tt4, nh = divmod(grp8, 2)
                    bank = 4 + grp8 % 2
                    key = (gg, grp8)
                    prev = (gg, grp8 - 2) if grp8 >= 2 else (gg - 1, 6 + grp8)
                    if prev in evac_dn:
                        em.te(lambda e, v=evac_dn[prev]: e.wait_ge(c.ve.sem, v))
                    for k in range(4):
                        mm = lambda e, tt4=tt4, nh=nh, k=k, bank=bank, bw=bw, hbuf=hbuf: e.matmul(
                            ps[bank][:, :],
                            hT[:, hbuf * 2048 + k * 512 + tt4 * 128: hbuf * 2048 + k * 512 + tt4 * 128 + 128],
                            wd[bw][:, k * 1024 + nh * 512: k * 1024 + nh * 512 + 512],
                            start=(k == 0), stop=(k == 3))
                        if k == 3:
                            em.te(lambda e, mm=mm: mm(e).then_inc(c.te.sem, 1))
                        else:
                            em.te(mm)
                    te_dn = c.te.inc()
                    te_dn_last[gg] = te_dn
                    if g4g == 3:
                        te_wd_last[(gg // (4 * NE), eig)] = te_dn
                    tok = g4g * 4 + tt4
                    em.ve(lambda e, v=te_dn: e.wait_ge(c.te.sem, v))
                    em.ve(lambda e, tok=tok, nh=nh, bank=bank, eig=eig: e.scalar_tensor_tensor(
                        acc[:, tok * 1024 + nh * 512: tok * 1024 + nh * 512 + 512],
                        ps[bank][:, :], combb[:, tok * 32 + eig: tok * 32 + eig + 1],
                        acc[:, tok * 1024 + nh * 512: tok * 1024 + nh * 512 + 512],
                        ALU.mult, ALU.add).then_inc(c.ve.sem, 1))
                    evac_dn[key] = c.ve.inc()

            for g4 in range(4):
                gg = (bi * NE + ei) * 4 + g4
                tg = g4 * 512
                hbuf = gg % 2
                prev2 = gg - 2
                for pr in range(4):
                    bg = pr % 2
                    bu = 2 + pr % 2
                    key = (gg, pr)
                    prevk = (gg, pr - 2) if pr >= 2 else (gg - 1, 2 + pr)
                    if prevk in evac_pr:
                        em.te(lambda e, v=evac_pr[prevk]: e.wait_ge(c.ve.sem, v))

                    for k in range(8):
                        em.te(lambda e, pr=pr, k=k, bg=bg, b=b, tg=tg: e.matmul(
                            ps[bg][:, :], wg[b][:, k * 1024 + pr * 128: k * 1024 + pr * 128 + 128],
                            xnb[:, k * 2048 + tg: k * 2048 + tg + 512],
                            start=(k == 0), stop=(k == 7)))
                    for k in range(8):
                        mm = lambda e, pr=pr, k=k, bu=bu, b=b, tg=tg: e.matmul(
                            ps[bu][:, :], wg[b][:, k * 1024 + 512 + pr * 128: k * 1024 + 512 + pr * 128 + 128],
                            xnb[:, k * 2048 + tg: k * 2048 + tg + 512],
                            start=(k == 0), stop=(k == 7))
                        if k == 7:
                            em.te(lambda e, mm=mm: mm(e).then_inc(c.te.sem, 1))
                        else:
                            em.te(mm)
                    te_gu = c.te.inc()
                    sgb = pr % 2
                    em.se(lambda e, v=te_gu: e.wait_ge(c.te.sem, v))
                    if prevk in evac_pr:
                        em.se(lambda e, v=evac_pr[prevk]: e.wait_ge(c.ve.sem, v))
                    em.se(lambda e, bg=bg, sgb=sgb: e.activation(
                        sg[:, sgb * 512:(sgb + 1) * 512], ps[bg][:, :], AF.Silu).then_inc(c.se.sem, 1))
                    se_sg = c.se.inc()
                    em.ve(lambda e, v=se_sg: e.wait_ge(c.se.sem, v))
                    if pr == 0 and prev2 in te_dn_last:
                        em.ve(lambda e, v=te_dn_last[prev2]: e.wait_ge(c.te.sem, v))
                    em.ve(lambda e, pr=pr, bu=bu, sgb=sgb, hbuf=hbuf: e.tensor_tensor(
                        hT[:, hbuf * 2048 + pr * 512: hbuf * 2048 + (pr + 1) * 512],
                        ps[bu][:, :], sg[:, sgb * 512:(sgb + 1) * 512], ALU.mult).then_inc(c.ve.sem, 1))
                    evac_pr[key] = c.ve.inc()
                evac_h[gg] = evac_pr[(gg, 3)]
                dn_seq.append(gg)
                if len(dn_seq) >= 2:
                    emit_down(dn_seq.pop(0))
            te_expert_done[ei] = c.te.n
        while dn_seq:
            emit_down(dn_seq.pop(0))
        if True:
            pass
        for g4 in range(4):
            t0 = tb + g4 * 512
            em.sy(lambda e, t0=t0, g4=g4: e.dma_start(
                out=_r(x1c[:], "p (k t) -> p k t", t=512),
                in_=xr[:, :, t0:t0 + 512]).then_inc(c.ld.sem, 16))
            ld_x1 = c.ld.inc(16)
            em.te(lambda e, v=ld_x1: e.wait_ge(c.ld.sem, v))
            em.te(lambda e, v=c.ve.n: e.wait_ge(c.ve.sem, v))
            em.ve(lambda e, v=ld_x1: e.wait_ge(c.ld.sem, v))
            for j in range(8):
                bank = j % 2
                if g4 > 0 or j >= 2:
                    em.te(lambda e, v=c.ve.n: e.wait_ge(c.ve.sem, v))
                for tt4 in range(4):
                    tok = g4 * 4 + tt4
                    mm = lambda e, j=j, tok=tok, tt4=tt4, bank=bank: e.transpose(
                        ps[bank][:, tt4 * 128:(tt4 + 1) * 128],
                        acc[:, tok * 1024 + j * 128: tok * 1024 + j * 128 + 128],
                        sb["ident_f32"][:])
                    if tt4 == 3:
                        em.te(lambda e, mm=mm: mm(e).then_inc(c.te.sem, 1))
                    else:
                        em.te(mm)
                te_tr = c.te.inc()
                em.ve(lambda e, v=te_tr: e.wait_ge(c.te.sem, v))
                em.ve(lambda e, j=j, bank=bank: e.tensor_tensor(
                    x2c[:, j * 512:(j + 1) * 512], ps[bank][:, :],
                    x1c[:, j * 512:(j + 1) * 512], ALU.add))
                em.ve(lambda e: e.engine_nop().then_inc(c.ve.sem, 1))
                c.ve.inc()
            ve_x2 = c.ve.n
            em.sy(lambda e, v=ve_x2: e.wait_ge(c.ve.sem, v))
            em.sy(lambda e, t0=t0: e.dma_start(
                out=xo[:, :, t0:t0 + 512],
                in_=_r(x2c[:], "p (k t) -> p k t", t=512)).then_inc(c.st.sem, 16))
            c.st.inc(16)
            em.ve(lambda e, v=c.st.n: e.wait_ge(c.st.sem, v))
        em.sy(lambda e, v=c.te.n: e.wait_ge(c.te.sem, v))
        em.sy(lambda e, v=c.ve.n: e.wait_ge(c.ve.sem, v))

    for bi in range(T // BLK):
        block(bi)
    c.barrier()




def emit_attn_b_p1(c, dram, sb, st, x_in):
    em, nc = c.em, c.nc
    E = st.enter_context
    xt = E(nc.sbuf_tensor("b1_xt", [128, 8 * 512], F32))
    xln = E(nc.sbuf_tensor("b1_xln", [128, 8 * 512], F32))
    xln_bf = E(nc.sbuf_tensor("b1_xlnbf", [128, 8 * 512], BF16))
    aux = _mk_ln_aux(c, st, "b1")
    hist = E(nc.sbuf_tensor("b1_hist", [128, 8 * 3], F32))
    scr = E(nc.sbuf_tensor("b1_scr", [128, 8 * 515], F32))
    h0 = xt
    cpre = xt
    ext = scr
    qtok = scr
    lasb = scr
    h_bf = E(nc.sbuf_tensor("b1_hbf", [128, 8 * 512], BF16))
    qsq = E(nc.sbuf_tensor("b1_qsq", [128, 1024], F32))
    ssq = E(nc.sbuf_tensor("b1_ssq", [128, 16], F32))
    rn = E(nc.sbuf_tensor("b1_rn", [128, 32], F32))
    qv = E(nc.sbuf_tensor("b1_qv", [128, 4 * 1024], BF16))
    qhat = qv
    vsb = qv
    cg = E(nc.sbuf_tensor("b1_cg", [128, 8 * 512], BF16))
    qch = cg
    gsb = cg
    absb = E(nc.sbuf_tensor("b1_a", [128, 512], F32))
    btsb = E(nc.sbuf_tensor("b1_bt", [128, 4 * 16], F32))
    s1 = E(nc.sbuf_tensor("b1_s1", [64, 512], BF16))
    wq = E(nc.sbuf_tensor("b1_wq", [128, 8 * 1024], BF16))
    wk = E(nc.sbuf_tensor("b1_wk", [128, 8 * 1024], BF16))
    wv = E(nc.sbuf_tensor("b1_wv", [128, 8 * 1024], BF16))
    wa = E(nc.sbuf_tensor("b1_wa", [128, 8 * 1024], BF16))
    wb = E(nc.sbuf_tensor("b1_wb", [128, 8 * 16], BF16))
    wu1 = E(nc.sbuf_tensor("b1_wu1", [128, 8 * 64], BF16))
    wu2 = E(nc.sbuf_tensor("b1_wu2", [64, 1024], BF16))
    nw_col = E(nc.sbuf_tensor("b1_nw", [128, 8], F32))
    cw_col = E(nc.sbuf_tensor("b1_cw", [128, 32], F32))
    cb_col = E(nc.sbuf_tensor("b1_cb", [128, 8], F32))
    ab_col = E(nc.sbuf_tensor("b1_ab", [128, 8], F32))
    bb_row = E(nc.sbuf_tensor("b1_bb", [1, 16], BF16))
    u1b_col = E(nc.sbuf_tensor("b1_u1b", [64, 1], F32))
    u2b_col = E(nc.sbuf_tensor("b1_u2b", [128, 8], F32))

    for w_sb, name, n in ((wq, "ab_q_w", 1024), (wk, "ab_k_w", 1024),
                          (wv, "ab_v_w", 1024), (wa, "ab_alpha_w", 1024),
                          (wb, "ab_beta_w", 16), (wu1, "ab_u1_w", 64)):
        em.gp(lambda e, a=w_sb, b=dram[name], n=n: e.dma_start(
            out=_r(a[:], "p (k n) -> p k n", n=n),
            in_=_r(b, "(k p) n -> p k n", p=128)).then_inc(c.cs.sem, 16))
        c.cs.inc(16)
    em.gp(lambda e: e.dma_start(out=wu2[:], in_=dram["ab_u2_w"]).then_inc(c.cs.sem, 16))
    c.cs.inc(16)
    em.gp(lambda e: e.dma_start(out=bb_row[:], in_=_r(dram["ab_beta_b"], "a b -> a b")).then_inc(c.cs.sem, 16))
    c.cs.inc(16)
    for d_sb, name in ((nw_col, "ab_norm_in_w"), (cw_col, "ab_conv_w"),
                       (cb_col, "ab_conv_b"), (ab_col, "ab_alpha_b"),
                       (u1b_col, "ab_u1_b"), (u2b_col, "ab_u2_b")):
        em.sy(lambda e, a=d_sb, b=dram[name]: e.dma_start(out=a[:], in_=b).then_inc(c.ld.sem, 16))
        c.ld.inc(16)
    em.gp(lambda e: e.memset(hist[:], 0.0).then_inc(c.ge.sem, 1))
    ge_h = c.ge.inc()
    cs_w = c.cs.n
    ld_w = c.ld.n
    em.te(lambda e: e.wait_ge(c.cs.sem, cs_w))
    em.se(lambda e: e.wait_ge(c.ld.sem, ld_w))
    em.ve(lambda e: e.wait_ge(c.ge.sem, ge_h))
    em.ve(lambda e: e.wait_ge(c.ld.sem, ld_w))

    xr = _r(dram[x_in], "(k p) t -> p k t", p=128)
    xlnr = _r(dram["ab_xln"], "(k p) t -> p k t", p=128)
    qhr = _r(dram["ab_q"], "(k p) t -> p k t", p=128)
    khr = _r(dram["ab_k"], "(k p) t -> p k t", p=128)
    lar = _r(dram["ab_la"], "(k p) t -> p k t", p=128)
    gr = _r(dram["ab_gate"], "(k p) t -> p k t", p=128)
    ps, pt = c.ps, c.pt

    def chunk(ci):
        t0 = ci * CHUNK
        em.ve(lambda e, v=c.st.n: e.wait_ge(c.st.sem, v))
        em.sy(lambda e: e.dma_start(out=_r(xt[:], "p (k t) -> p k t", t=512),
                                    in_=xr[:, :, t0:t0 + 512]).then_inc(c.ld.sem, 16))
        ldx = c.ld.inc(16)
        ve_xln = emit_ln_to(c, sb, xt, xln, aux, ldx)
        for i in range(8):
            em.ve(lambda e, i=i: e.tensor_copy(
                xln_bf[:, i * 512:(i + 1) * 512], xln[:, i * 512:(i + 1) * 512]))
        em.ve(lambda e: e.engine_nop().then_inc(c.ve.sem, 1))
        ve_xbf = c.ve.inc()
        em.sy(lambda e, v=ve_xbf: e.wait_ge(c.ve.sem, v))
        em.sy(lambda e, t0=t0: e.dma_start(
            out=xlnr[:, :, t0:t0 + 512],
            in_=_r(xln[:], "p (k t) -> p k t", t=512)).then_inc(c.st.sem, 16))
        c.st.inc(16)
        em.te(lambda e, v=ve_xln: e.wait_ge(c.ve.sem, v))
        te0 = c.te.n
        for i in range(8):
            buf = aux["tmp1"] if i % 2 == 0 else aux["tmp2"]
            em.se(lambda e, v=ve_xln: e.wait_ge(c.ve.sem, v)) if i == 0 else None
            if i >= 2:
                em.se(lambda e, v=te0 + (i - 1): e.wait_ge(c.te.sem, v))
            em.se(lambda e, i=i, buf=buf: e.activation(
                buf[:], xln[:, i * 512:(i + 1) * 512], AF.Square).then_inc(c.se.sem, 1))
            se_sq = c.se.inc()
            em.te(lambda e, v=se_sq: e.wait_ge(c.se.sem, v))
            em.te(lambda e, i=i, buf=buf: e.matmul(
                ps[0][0:1, :], sb["ones_col_f32"][:], buf[:],
                start=(i == 0), stop=(i == 7)).then_inc(c.te.sem, 1))
            c.te.inc()
        te_rms = c.te.n
        em.ve(lambda e, v=te_rms: e.wait_ge(c.te.sem, v))
        em.ve(lambda e: e.tensor_scalar(aux["st_ms"][:], ps[0][0:1, :], 1.0 / D, None, ALU.mult))
        em.ve(lambda e: e.engine_nop().then_inc(c.ve.sem, 1))
        ve_ms = c.ve.inc()
        em.se(lambda e, v=ve_ms: e.wait_ge(c.ve.sem, v))
        em.se(lambda e: e.activation(aux["st_r"][:], aux["st_ms"][:], AF.Ln, bias=1e-6))
        em.se(lambda e: e.activation(aux["st_r"][:], aux["st_r"][:], AF.Exp, scale=-0.5).then_inc(c.se.sem, 1))
        se_r = c.se.inc()
        em.te(lambda e, v=se_r: e.wait_ge(c.se.sem, v))
        em.te(lambda e: e.matmul(ps[1][:, :], sb["ones_row_f32"][:], aux["st_r"][:],
                                 start=True, stop=True).then_inc(c.te.sem, 1))
        te_bc = c.te.inc()
        em.ve(lambda e, v=te_bc: e.wait_ge(c.te.sem, v))
        em.ve(lambda e: e.tensor_copy(aux["rdB"][:], ps[1][:, :]))
        em.ve(lambda e: e.tensor_scalar(sb["junk"][:], sb["junk"][:], 1.0, None, ALU.mult))
        for i in range(8):
            em.ve(lambda e, i=i: e.tensor_tensor(
                aux["tmp1"][:], xln[:, i * 512:(i + 1) * 512], aux["rdB"][:], ALU.mult))
            em.ve(lambda e, i=i: e.tensor_scalar(
                h0[:, i * 512:(i + 1) * 512], aux["tmp1"][:], nw_col[:, i:i + 1], None, ALU.mult))
        for i in range(8):
            em.ve(lambda e, i=i: e.tensor_copy(ext[:, i * 515: i * 515 + 3],
                                               hist[:, i * 3:(i + 1) * 3]))
            em.ve(lambda e, i=i: e.tensor_copy(ext[:, i * 515 + 3: i * 515 + 515],
                                               h0[:, i * 512:(i + 1) * 512]))
        for i in range(8):
            em.ve(lambda e, i=i: e.tensor_scalar(
                cpre[:, i * 512:(i + 1) * 512], ext[:, i * 515: i * 515 + 512],
                cw_col[:, i * 4 + 0: i * 4 + 1], None, ALU.mult))
        for j in range(1, 4):
            for i in range(8):
                em.ve(lambda e, i=i, j=j: e.scalar_tensor_tensor(
                    cpre[:, i * 512:(i + 1) * 512], ext[:, i * 515 + j: i * 515 + j + 512],
                    cw_col[:, i * 4 + j: i * 4 + j + 1],
                    cpre[:, i * 512:(i + 1) * 512], ALU.mult, ALU.add))
        for i in range(8):
            em.ve(lambda e, i=i: e.tensor_copy(hist[:, i * 3:(i + 1) * 3],
                                               ext[:, i * 515 + 512: i * 515 + 515]))
        em.ve(lambda e: e.engine_nop().then_inc(c.ve.sem, 1))
        ve_cpre = c.ve.inc()
        em.se(lambda e, v=ve_cpre: e.wait_ge(c.ve.sem, v))
        for i in range(8):
            em.se(lambda e, i=i: e.activation(
                h_bf[:, i * 512:(i + 1) * 512], cpre[:, i * 512:(i + 1) * 512],
                AF.Silu, bias=cb_col[:, i:i + 1]).then_inc(c.se.sem, 1))
            c.se.inc()
        se_h = c.se.n
        em.te(lambda e, v=se_h: e.wait_ge(c.se.sem, v))

        for (w_sb, out_r) in ((wq, qhr), (wk, khr)):
            def qk(w_sb=w_sb, out_r=out_r):
                for tt in range(4):
                    for nh in range(2):
                        bank = (tt * 2 + nh) % 2
                        em.te(lambda e, v=c.ve.n: e.wait_ge(c.ve.sem, v))
                        for k in range(8):
                            mm = lambda e, tt=tt, nh=nh, k=k, bank=bank, w_sb=w_sb: e.matmul(
                                ps[bank][:, :], h_bf[:, k * 512 + tt * 128: k * 512 + tt * 128 + 128],
                                w_sb[:, k * 1024 + nh * 512: k * 1024 + nh * 512 + 512],
                                start=(k == 0), stop=(k == 7))
                            if k == 7:
                                em.te(lambda e, mm=mm: mm(e).then_inc(c.te.sem, 1))
                            else:
                                em.te(mm)
                        te_q = c.te.inc()
                        em.ve(lambda e, v=te_q: e.wait_ge(c.te.sem, v))
                        em.ve(lambda e, tt=tt, nh=nh, bank=bank: e.tensor_copy(
                            qtok[:, tt * 1024 + nh * 512: tt * 1024 + nh * 512 + 512],
                            ps[bank][:, :]))
                        em.ve(lambda e: e.engine_nop().then_inc(c.ve.sem, 1))
                        c.ve.inc()
                for tt in range(4):
                    em.se(lambda e, v=c.ve.n: e.wait_ge(c.ve.sem, v))
                    em.se(lambda e, tt=tt: e.activation(
                        qsq[:, 0:512], qtok[:, tt * 1024: tt * 1024 + 512], AF.Square))
                    em.se(lambda e, tt=tt: e.activation(
                        qsq[:, 512:1024], qtok[:, tt * 1024 + 512: tt * 1024 + 1024],
                        AF.Square).then_inc(c.se.sem, 1))
                    se_sq2 = c.se.inc()
                    em.ve(lambda e, v=se_sq2: e.wait_ge(c.se.sem, v))
                    for hh in range(16):
                        em.ve(lambda e, hh=hh: e.reduce_sum(
                            ssq[:, hh:hh + 1], qsq[:, hh * 64:(hh + 1) * 64], AX.X))
                    em.ve(lambda e: e.engine_nop().then_inc(c.ve.sem, 1))
                    ve_ss = c.ve.inc()
                    em.se(lambda e, v=ve_ss: e.wait_ge(c.ve.sem, v))
                    em.se(lambda e: e.activation(rn[:, 0:16], ssq[:, 0:16], AF.Ln, bias=1e-30))
                    em.se(lambda e: e.activation(rn[:, 0:16], rn[:, 0:16], AF.Exp, scale=0.5).then_inc(c.se.sem, 1))
                    se_rn = c.se.inc()
                    em.ve(lambda e, v=se_rn: e.wait_ge(c.se.sem, v))
                    em.ve(lambda e: e.tensor_scalar(rn[:, 16:32], rn[:, 0:16], 1e-6, None, ALU.add))
                    em.ve(lambda e: e.tensor_scalar(sb["junk"][:], sb["junk"][:], 1.0, None, ALU.mult))
                    em.ve(lambda e: e.reciprocal(rn[:, 0:16], rn[:, 16:32]))
                    em.ve(lambda e: e.tensor_scalar(sb["junk"][:], sb["junk"][:], 1.0, None, ALU.mult))
                    for hh in range(16):
                        em.ve(lambda e, tt=tt, hh=hh: e.tensor_scalar(
                            qhat[:, tt * 1024 + hh * 64: tt * 1024 + (hh + 1) * 64],
                            qtok[:, tt * 1024 + hh * 64: tt * 1024 + (hh + 1) * 64],
                            rn[:, hh:hh + 1], None, ALU.mult))
                    em.ve(lambda e: e.engine_nop().then_inc(c.ve.sem, 1))
                    c.ve.inc()
                ve_qh = c.ve.n
                em.te(lambda e, v=ve_qh: e.wait_ge(c.ve.sem, v))
                em.ve(lambda e, v=c.st.n: e.wait_ge(c.st.sem, v))
                for tt in range(4):
                    ptb = pt[tt % 2]
                    if tt >= 2:
                        em.te(lambda e, v=c.ve.n: e.wait_ge(c.ve.sem, v))
                    for j in range(8):
                        mm = lambda e, tt=tt, j=j, ptb=ptb: e.transpose(
                            ptb[:, j * 128:(j + 1) * 128],
                            qhat[:, tt * 1024 + j * 128: tt * 1024 + j * 128 + 128],
                            sb["ident"][:])
                        if j == 7:
                            em.te(lambda e, mm=mm: mm(e).then_inc(c.te.sem, 1))
                        else:
                            em.te(mm)
                    te_tr = c.te.inc()
                    em.ve(lambda e, v=te_tr: e.wait_ge(c.te.sem, v))
                    for j in range(8):
                        em.ve(lambda e, tt=tt, j=j, ptb=ptb: e.tensor_copy(
                            qch[:, j * 512 + tt * 128: j * 512 + tt * 128 + 128],
                            ptb[:, j * 128:(j + 1) * 128]))
                    em.ve(lambda e: e.engine_nop().then_inc(c.ve.sem, 1))
                    c.ve.inc()
                ve_qch = c.ve.n
                em.sy(lambda e, v=ve_qch: e.wait_ge(c.ve.sem, v))
                em.sy(lambda e, t0=t0, out_r=out_r: e.dma_start(
                    out=out_r[:, :, t0:t0 + 512],
                    in_=_r(qch[:], "p (k t) -> p k t", t=512)).then_inc(c.st.sem, 16))
                c.st.inc(16)
            qk()
        for tt in range(4):
            for nh in range(2):
                g = tt * 2 + nh
                bank = g % 2
                em.te(lambda e, v=c.ve.n: e.wait_ge(c.ve.sem, v))
                for k in range(8):
                    mm = lambda e, tt=tt, nh=nh, k=k, bank=bank: e.matmul(
                        ps[bank][:, :], h_bf[:, k * 512 + tt * 128: k * 512 + tt * 128 + 128],
                        wv[:, k * 1024 + nh * 512: k * 1024 + nh * 512 + 512],
                        start=(k == 0), stop=(k == 7))
                    if k == 7:
                        em.te(lambda e, mm=mm: mm(e).then_inc(c.te.sem, 1))
                    else:
                        em.te(mm)
                te_v = c.te.inc()
                em.ve(lambda e, v=te_v: e.wait_ge(c.te.sem, v))
                em.ve(lambda e, tt=tt, nh=nh, bank=bank: e.tensor_copy(
                    vsb[:, tt * 1024 + nh * 512: tt * 1024 + nh * 512 + 512], ps[bank][:, :]))
                em.ve(lambda e: e.engine_nop().then_inc(c.ve.sem, 1))
                c.ve.inc()
        em.sy(lambda e, v=c.ve.n: e.wait_ge(c.ve.sem, v))
        em.sy(lambda e, t0=t0: e.dma_start(
            out=_r(dram["ab_v"][t0:t0 + 512, :], "(q p) n -> p q n", p=128),
            in_=_r(vsb[:], "p (q n) -> p q n", n=1024)).then_inc(c.st.sem, 16))
        c.st.inc(16)
        for m in range(8):
            bank = m % 2
            em.te(lambda e, v=c.ve.n: e.wait_ge(c.ve.sem, v))
            em.te(lambda e, v=c.se.n: e.wait_ge(c.se.sem, v))
            for k in range(8):
                mm = lambda e, m=m, k=k, bank=bank: e.matmul(
                    ps[bank][:, :], wa[:, k * 1024 + m * 128: k * 1024 + m * 128 + 128],
                    h_bf[:, k * 512:(k + 1) * 512], start=(k == 0), stop=(k == 7))
                if k == 7:
                    em.te(lambda e, mm=mm: mm(e).then_inc(c.te.sem, 1))
                else:
                    em.te(mm)
            te_a = c.te.inc()
            em.se(lambda e, v=te_a: e.wait_ge(c.te.sem, v))
            em.se(lambda e, m=m, bank=bank: e.activation(
                absb[:], ps[bank][:, :], AF.Sigmoid, bias=ab_col[:, m:m + 1]))
            em.se(lambda e, m=m: e.activation(
                lasb[:, m * 512:(m + 1) * 512], absb[:], AF.Ln).then_inc(c.se.sem, 1))
            c.se.inc()
        se_la = c.se.n
        em.sy(lambda e, v=se_la: e.wait_ge(c.se.sem, v))
        em.sy(lambda e, t0=t0: e.dma_start(
            out=lar[:, :, t0:t0 + 512],
            in_=_r(lasb[:, 0:4096], "p (k t) -> p k t", t=512)).then_inc(c.st.sem, 16))
        c.st.inc(16)
        for tt in range(4):
            em.te(lambda e, v=c.se.n: e.wait_ge(c.se.sem, v))
            for k in range(8):
                em.te(lambda e, tt=tt, k=k: e.matmul(
                    ps[4][:, 0:16], h_bf[:, k * 512 + tt * 128: k * 512 + tt * 128 + 128],
                    wb[:, k * 16:(k + 1) * 16], start=(k == 0), stop=False))
            em.te(lambda e: e.matmul(
                ps[4][:, 0:16], sb["ones_row_bf"][0:1, 0:128], bb_row[:],
                start=False, stop=True).then_inc(c.te.sem, 1))
            te_b = c.te.inc()
            em.se(lambda e, v=te_b: e.wait_ge(c.te.sem, v))
            em.se(lambda e, tt=tt: e.activation(
                btsb[:, tt * 16:(tt + 1) * 16], ps[4][:, 0:16], AF.Sigmoid).then_inc(c.se.sem, 1))
            c.se.inc()
        se_bt = c.se.n
        em.sy(lambda e, v=se_bt: e.wait_ge(c.se.sem, v))
        em.sy(lambda e, t0=t0: e.dma_start(
            out=_r(dram["ab_beta"][t0:t0 + 512, :], "(q p) n -> p q n", p=128),
            in_=_r(btsb[:], "p (q n) -> p q n", n=16)).then_inc(c.st.sem, 16))
        c.st.inc(16)
        em.se(lambda e, v=c.st.n: e.wait_ge(c.st.sem, v))
        em.te(lambda e, v=ve_xbf: e.wait_ge(c.ve.sem, v))
        em.te(lambda e, v=c.se.n: e.wait_ge(c.se.sem, v))
        for k in range(8):
            mm = lambda e, k=k: e.matmul(
                ps[5][0:64, :], wu1[:, k * 64:(k + 1) * 64],
                xln_bf[:, k * 512:(k + 1) * 512], start=(k == 0), stop=(k == 7))
            if k == 7:
                em.te(lambda e, mm=mm: mm(e).then_inc(c.te.sem, 1))
            else:
                em.te(mm)
        te_u1 = c.te.inc()
        em.se(lambda e, v=te_u1: e.wait_ge(c.te.sem, v))
        em.se(lambda e: e.activation(s1[:], ps[5][0:64, :], AF.Silu,
                                     bias=u1b_col[:]).then_inc(c.se.sem, 1))
        se_s1 = c.se.inc()
        em.te(lambda e, v=se_s1: e.wait_ge(c.se.sem, v))
        for m in range(8):
            bank = 2 + m % 2
            if m >= 2:
                em.te(lambda e, v=c.se.n: e.wait_ge(c.se.sem, v))
            em.te(lambda e, m=m, bank=bank: e.matmul(
                ps[bank][:, :], wu2[:, m * 128:(m + 1) * 128], s1[:],
                start=True, stop=True).then_inc(c.te.sem, 1))
            te_u2 = c.te.inc()
            em.se(lambda e, v=te_u2: e.wait_ge(c.te.sem, v))
            em.se(lambda e, m=m, bank=bank: e.activation(
                gsb[:, m * 512:(m + 1) * 512], ps[bank][:, :], AF.Sigmoid,
                bias=u2b_col[:, m:m + 1]).then_inc(c.se.sem, 1))
            c.se.inc()
        se_g = c.se.n
        em.sy(lambda e, v=se_g: e.wait_ge(c.se.sem, v))
        em.sy(lambda e, t0=t0: e.dma_start(
            out=gr[:, :, t0:t0 + 512],
            in_=_r(gsb[:], "p (k t) -> p k t", t=512)).then_inc(c.st.sem, 16))
        c.st.inc(16)
        em.sy(lambda e, v=c.te.n: e.wait_ge(c.te.sem, v))
        em.sy(lambda e, v=c.ve.n: e.wait_ge(c.ve.sem, v))
        em.sy(lambda e, v=c.se.n: e.wait_ge(c.se.sem, v))

    for ci in range(NCHUNK):
        chunk(ci)
    c.barrier()



def emit_attn_b_p2(c, dram, sb, st):
    em, nc = c.em, c.nc
    E = st.enter_context
    qc = E(nc.sbuf_tensor("b2_q", [64, 16 * 128], BF16))
    kc = E(nc.sbuf_tensor("b2_k", [64, 16 * 128], BF16))
    lac = E(nc.sbuf_tensor("b2_la", [64, 16 * 128], F32))
    lam = E(nc.sbuf_tensor("b2_lam", [64, 16 * 128], F32))
    zer = E(nc.sbuf_tensor("b2_zer", [64, 128], F32))
    anc = E(nc.sbuf_tensor("b2_anc", [64, 16 * 2], F32))
    anc2 = E(nc.sbuf_tensor("b2_anc2", [64, 16 * 2], F32))
    e1 = E(nc.sbuf_tensor("b2_e1", [64, 16 * 128], F32))
    eg = E(nc.sbuf_tensor("b2_eg", [64, 16 * 128], F32))
    egx = E(nc.sbuf_tensor("b2_egx", [64, 16 * 64], F32))
    fk = E(nc.sbuf_tensor("b2_fk", [64, 16 * 128], BF16))
    fq = E(nc.sbuf_tensor("b2_fq", [64, 16 * 128], BF16))
    gk = E(nc.sbuf_tensor("b2_gk", [64, 16 * 128], BF16))
    gkx = E(nc.sbuf_tensor("b2_gkx", [64, 16 * 64], BF16))
    ktl = E(nc.sbuf_tensor("b2_ktl", [64, 16 * 128], BF16))
    qtl = E(nc.sbuf_tensor("b2_qtl", [64, 16 * 128], BF16))
    eLC = E(nc.sbuf_tensor("b2_eLC", [64, 16], F32))
    scU = E(nc.sbuf_tensor("b2_scU", [64, 16], F32))
    Usb = E(nc.sbuf_tensor("b2_U", [64, 16 * 128], BF16))
    Utk = E(nc.sbuf_tensor("b2_Utk", [128, 16 * 64], BF16))
    vsb = E(nc.sbuf_tensor("b2_v", [128, 16 * 64], BF16))
    vb = E(nc.sbuf_tensor("b2_vb", [128, 16 * 64], F32))
    btsb = E(nc.sbuf_tensor("b2_bt", [128, 16], F32))
    nbt = E(nc.sbuf_tensor("b2_nbt", [128, 16], F32))
    Nsb = E(nc.sbuf_tensor("b2_N", [128, 16 * 128], BF16))
    NTsb = E(nc.sbuf_tensor("b2_NT", [128, 16 * 128], BF16))
    N2 = E(nc.sbuf_tensor("b2_N2", [128, 16 * 128], BF16))
    NT2 = E(nc.sbuf_tensor("b2_NT2", [128, 16 * 128], BF16))
    AqT = E(nc.sbuf_tensor("b2_AqT", [128, 16 * 128], BF16))
    Xsb = E(nc.sbuf_tensor("b2_X", [128, 16 * 64], BF16))
    Ssb = E(nc.sbuf_tensor("b2_S", [64, 16 * 64], BF16))
    ysb = E(nc.sbuf_tensor("b2_y", [128, 16 * 64], F32))
    ysq = E(nc.sbuf_tensor("b2_ysq", [128, 16 * 64], F32))
    yss = E(nc.sbuf_tensor("b2_yss", [128, 16], F32))
    yhat = E(nc.sbuf_tensor("b2_yhat", [128, 16 * 64], BF16))
    hnB = E(nc.sbuf_tensor("b2_hnB", [128, 1024], F32))
    ml = E(nc.sbuf_tensor("b2_ml", [128, 128], BF16))
    mu = E(nc.sbuf_tensor("b2_mu", [128, 128], BF16))
    hn_row = E(nc.sbuf_tensor("b2_hn", [1, 1024], F32))

    em.sy(lambda e: e.dma_start(out=ml[:], in_=dram["mask_sl"]).then_inc(c.ld.sem, 16))
    c.ld.inc(16)
    em.sy(lambda e: e.dma_start(out=mu[:], in_=dram["mask_iu"]).then_inc(c.ld.sem, 16))
    c.ld.inc(16)
    em.sy(lambda e: e.dma_start(out=hn_row[:], in_=dram["ab_head_norm_row"]).then_inc(c.ld.sem, 16))
    ld_cst = c.ld.inc(16)
    em.gp(lambda e: e.memset(Ssb[:], 0.0))
    em.gp(lambda e: e.memset(zer[:], 0.0).then_inc(c.ge.sem, 1))
    ge_i = c.ge.inc()
    em.te(lambda e, v=ld_cst: e.wait_ge(c.ld.sem, v))
    em.te(lambda e: e.matmul(c.ps[0][:, 0:512], sb["ones_row_f32"][:], hn_row[0:1, 0:512],
                             start=True, stop=True))
    em.te(lambda e: e.matmul(c.ps[1][:, 0:512], sb["ones_row_f32"][:], hn_row[0:1, 512:1024],
                             start=True, stop=True).then_inc(c.te.sem, 1))
    te_hn = c.te.inc()
    em.ve(lambda e, v=te_hn: e.wait_ge(c.te.sem, v))
    em.ve(lambda e: e.tensor_copy(hnB[:, 0:512], c.ps[0][:, 0:512]))
    em.ve(lambda e: e.tensor_copy(hnB[:, 512:1024], c.ps[1][:, 0:512]))
    em.ve(lambda e, v=ge_i: e.wait_ge(c.ge.sem, v))
    em.ve(lambda e: e.engine_nop().then_inc(c.ve.sem, 1))
    c.ve.inc()

    qhr = _r(dram["ab_q"], "(h p) t -> p h t", p=64)
    khr = _r(dram["ab_k"], "(h p) t -> p h t", p=64)
    lar = _r(dram["ab_la"], "(h p) t -> p h t", p=64)
    ps, pt = c.ps, c.pt

    def chunk(ci):
        t0 = ci * CB
        em.sy(lambda e, v=c.te.n: e.wait_ge(c.te.sem, v))
        em.sy(lambda e, v=c.ve.n: e.wait_ge(c.ve.sem, v))
        em.sy(lambda e, t0=t0: e.dma_start(
            out=_r(qc[:], "p (h t) -> p h t", t=128),
            in_=qhr[:, :, t0:t0 + CB]).then_inc(c.ld.sem, 16))
        c.ld.inc(16)
        em.sy(lambda e, t0=t0: e.dma_start(
            out=_r(kc[:], "p (h t) -> p h t", t=128),
            in_=khr[:, :, t0:t0 + CB]).then_inc(c.ld.sem, 16))
        c.ld.inc(16)
        em.sy(lambda e, t0=t0: e.dma_start(
            out=_r(lac[:], "p (h t) -> p h t", t=128),
            in_=lar[:, :, t0:t0 + CB]).then_inc(c.ld.sem, 16))
        c.ld.inc(16)
        em.sy(lambda e, t0=t0: e.dma_start(
            out=vsb[:], in_=dram["ab_v"][t0:t0 + CB, :]).then_inc(c.ld.sem, 16))
        c.ld.inc(16)
        em.sy(lambda e, t0=t0: e.dma_start(
            out=btsb[:],
            in_=dram["ab_beta"][t0:t0 + CB, :]).then_inc(c.ld.sem, 16))
        ldq = c.ld.inc(16)
        em.ve(lambda e, v=ldq: e.wait_ge(c.ld.sem, v))
        for h in range(16):
            em.ve(lambda e, h=h: e.tensor_tensor_scan(
                lam[:, h * 128:(h + 1) * 128], lac[:, h * 128:(h + 1) * 128],
                zer[:], 0.0, ALU.add, ALU.add))
        for h in range(16):
            for i in range(2):
                em.ve(lambda e, h=h, i=i: e.tensor_tensor(
                    anc[:, h * 2 + i: h * 2 + i + 1],
                    lam[:, h * 128 + i * 64: h * 128 + i * 64 + 1],
                    lam[:, h * 128 + i * 64 + 63: h * 128 + i * 64 + 64], ALU.add))
        em.ve(lambda e: e.tensor_copy(
            _r(eLC[:], "p (h o) -> p h o", o=1),
            _r(lam[:], "p (h t) -> p h t", t=128)[:, :, 127:128]))
        em.ve(lambda e: e.tensor_scalar(anc2[:], anc[:], 0.5, None, ALU.mult))
        em.ve(lambda e: e.engine_nop())
        em.ve(lambda e: e.engine_nop())
        for h in range(16):
            for i in range(2):
                em.ve(lambda e, h=h, i=i: e.tensor_scalar(
                    e1[:, h * 128 + i * 64: h * 128 + (i + 1) * 64],
                    lam[:, h * 128 + i * 64: h * 128 + (i + 1) * 64],
                    anc2[:, h * 2 + i: h * 2 + i + 1], None, ALU.subtract))
                em.ve(lambda e, h=h, i=i: e.tensor_scalar(
                    eg[:, h * 128 + i * 64: h * 128 + (i + 1) * 64],
                    lam[:, h * 128 + i * 64: h * 128 + (i + 1) * 64],
                    anc2[:, h * 2 + i: h * 2 + i + 1], -1.0, ALU.subtract, ALU.mult))
            em.ve(lambda e, h=h: e.tensor_scalar(
                egx[:, h * 64:(h + 1) * 64], lam[:, h * 128: h * 128 + 64],
                anc2[:, h * 2 + 1: h * 2 + 2], -1.0, ALU.subtract, ALU.mult))
        for h in range(16):
            em.ve(lambda e, h=h: e.tensor_tensor(
                scU[:, h:h + 1], eLC[:, h:h + 1], anc2[:, h * 2 + 1: h * 2 + 2], ALU.subtract))
        em.ve(lambda e: e.engine_nop().then_inc(c.ve.sem, 1))
        ve_exp_in = c.ve.inc()
        em.se(lambda e, v=ve_exp_in: e.wait_ge(c.ve.sem, v))
        em.se(lambda e: e.activation(e1[:], e1[:], AF.Exp))
        em.se(lambda e: e.activation(eg[:], eg[:], AF.Exp))
        em.se(lambda e: e.activation(egx[:], egx[:], AF.Exp))
        em.se(lambda e: e.activation(lam[:], lam[:], AF.Exp))
        em.se(lambda e: e.activation(eLC[:], eLC[:], AF.Exp))
        em.se(lambda e: e.activation(scU[:], scU[:], AF.Exp).then_inc(c.se.sem, 1))
        se_exp = c.se.inc()
        em.ve(lambda e, v=se_exp: e.wait_ge(c.se.sem, v))
        em.ve(lambda e: e.tensor_tensor(fk[:], kc[:], e1[:], ALU.mult))
        em.ve(lambda e: e.tensor_tensor(fq[:], qc[:], e1[:], ALU.mult))
        em.ve(lambda e: e.tensor_tensor(gk[:], kc[:], eg[:], ALU.mult))
        for h in range(16):
            em.ve(lambda e, h=h: e.tensor_tensor(
                gkx[:, h * 64:(h + 1) * 64], kc[:, h * 128: h * 128 + 64],
                egx[:, h * 64:(h + 1) * 64], ALU.mult))
        em.ve(lambda e: e.tensor_tensor(ktl[:], kc[:], lam[:], ALU.mult))
        em.ve(lambda e: e.tensor_tensor(qtl[:], qc[:], lam[:], ALU.mult))
        for h in range(16):
            em.ve(lambda e, h=h: e.tensor_scalar(
                Usb[:, h * 128: h * 128 + 64], gkx[:, h * 64:(h + 1) * 64],
                scU[:, h:h + 1], None, ALU.mult))
            em.ve(lambda e, h=h: e.tensor_scalar(
                Usb[:, h * 128 + 64:(h + 1) * 128], gk[:, h * 128 + 64: h * 128 + 128],
                scU[:, h:h + 1], None, ALU.mult))
        em.ve(lambda e: e.tensor_scalar(nbt[:], btsb[:], -1.0, None, ALU.mult))
        for h in range(16):
            em.ve(lambda e, h=h: e.tensor_scalar(
                vb[:, h * 64:(h + 1) * 64], vsb[:, h * 64:(h + 1) * 64],
                btsb[:, h:h + 1], None, ALU.mult))
        em.ve(lambda e: e.engine_nop().then_inc(c.ve.sem, 1))
        ve_fact = c.ve.inc()
        em.te(lambda e, v=ve_fact: e.wait_ge(c.ve.sem, v))
        if ci == 0 and "d2_lam" in dram:
            for nm, buf in (("d2_lam", lam), ("d2_e1", e1), ("d2_fk", fk), ("d2_gk", gk)):
                em.sy(lambda e, v=ve_fact: e.wait_ge(c.ve.sem, v))
                em.sy(lambda e, nm=nm, buf=buf: e.dma_start(out=dram[nm], in_=buf[:]).then_inc(c.st.sem, 16))
                c.st.inc(16)

        for h in range(16):
            bkN = h % 2
            bkA = 2 + h % 2
            if h >= 2:
                em.te(lambda e, v=c.ve.n: e.wait_ge(c.ve.sem, v))
            hs = slice(h * 128, h * 128 + 64)
            em.te(lambda e, h=h, bkN=bkN: e.matmul(
                ps[bkN][0:64, 0:64], fk[:, h * 128: h * 128 + 64],
                gk[:, h * 128: h * 128 + 64], start=True, stop=True))
            em.te(lambda e, h=h, bkN=bkN: e.matmul(
                ps[bkN][64:128, 0:64], fk[:, h * 128 + 64: h * 128 + 128],
                gkx[:, h * 64:(h + 1) * 64], start=True, stop=True,
                tile_position=(0, 64)))
            em.te(lambda e, h=h, bkN=bkN: e.matmul(
                ps[bkN][64:128, 64:128], fk[:, h * 128 + 64: h * 128 + 128],
                gk[:, h * 128 + 64: h * 128 + 128], start=True, stop=True,
                tile_position=(0, 64)).then_inc(c.te.sem, 1))
            te_N = c.te.inc()
            em.te(lambda e, h=h, bkA=bkA: e.matmul(
                ps[bkA][0:64, 0:64], gk[:, h * 128: h * 128 + 64],
                fq[:, h * 128: h * 128 + 64], start=True, stop=True))
            em.te(lambda e, h=h, bkA=bkA: e.matmul(
                ps[bkA][0:64, 64:128], gkx[:, h * 64:(h + 1) * 64],
                fq[:, h * 128 + 64: h * 128 + 128], start=True, stop=True))
            em.te(lambda e, h=h, bkA=bkA: e.matmul(
                ps[bkA][64:128, 64:128], gk[:, h * 128 + 64: h * 128 + 128],
                fq[:, h * 128 + 64: h * 128 + 128], start=True, stop=True,
                tile_position=(0, 64)).then_inc(c.te.sem, 1))
            te_A = c.te.inc()
            em.ve(lambda e, v=te_A: e.wait_ge(c.te.sem, v))
            em.ve(lambda e, h=h, bkN=bkN: e.tensor_tensor(
                Nsb[:, h * 128:(h + 1) * 128], ps[bkN][:, 0:128], ml[:], ALU.mult))
            em.ve(lambda e, h=h, bkA=bkA: e.tensor_tensor(
                AqT[:, h * 128:(h + 1) * 128], ps[bkA][:, 0:128], mu[:], ALU.mult))
            em.ve(lambda e, h=h: e.tensor_scalar(
                Nsb[:, h * 128:(h + 1) * 128], Nsb[:, h * 128:(h + 1) * 128],
                nbt[:, h:h + 1], None, ALU.mult))
            em.ve(lambda e: e.engine_nop().then_inc(c.ve.sem, 1))
            c.ve.inc()
        ve_N = c.ve.n
        em.te(lambda e, v=ve_N: e.wait_ge(c.ve.sem, v))
        for half in range(2):
            ptb = pt[half]
            for hh in range(8):
                h = half * 8 + hh
                mm = lambda e, h=h, hh=hh, ptb=ptb: e.transpose(
                    ptb[:, hh * 128:(hh + 1) * 128], Nsb[:, h * 128:(h + 1) * 128],
                    sb["ident"][:])
                if hh == 7:
                    em.te(lambda e, mm=mm: mm(e).then_inc(c.te.sem, 1))
                else:
                    em.te(mm)
            te_tr = c.te.inc()
            em.ve(lambda e, v=te_tr: e.wait_ge(c.te.sem, v))
            em.ve(lambda e, half=half, ptb=ptb: e.tensor_copy(
                NTsb[:, half * 1024:(half + 1) * 1024], ptb[:, :]))
            em.ve(lambda e: e.engine_nop().then_inc(c.ve.sem, 1))
            c.ve.inc()
        if ci == 0 and "d2_N" in dram:
            for nm, buf in (("d2_N", Nsb), ("d2_AqT", AqT), ("d2_NT", NTsb)):
                em.sy(lambda e, v=c.ve.n: e.wait_ge(c.ve.sem, v))
                em.sy(lambda e, nm=nm, buf=buf: e.dma_start(out=dram[nm], in_=buf[:]).then_inc(c.st.sem, 16))
                c.st.inc(16)
        em.te(lambda e, v=c.ve.n: e.wait_ge(c.ve.sem, v))
        for h in range(16):
            mm = lambda e, h=h: e.transpose(
                pt[0][:, h * 64:(h + 1) * 64], Usb[:, h * 128:(h + 1) * 128],
                sb["ident"][0:64, 0:64])
            if h == 15:
                em.te(lambda e, mm=mm: mm(e).then_inc(c.te.sem, 1))
            else:
                em.te(mm)
        te_utr = c.te.inc()
        em.ve(lambda e, v=te_utr: e.wait_ge(c.te.sem, v))
        em.ve(lambda e: e.tensor_copy(Utk[:], pt[0][:, 0:1024]))
        em.ve(lambda e: e.engine_nop().then_inc(c.ve.sem, 1))
        c.ve.inc()

        em.te(lambda e, v=c.ve.n: e.wait_ge(c.ve.sem, v))
        for h in range(16):
            bank = 4 + h % 2
            if h >= 2:
                em.te(lambda e, v=c.ve.n: e.wait_ge(c.ve.sem, v))
            mm = lambda e, h=h, bank=bank: e.matmul(
                ps[bank][:, 0:64], ktl[:, h * 128:(h + 1) * 128],
                Ssb[:, h * 64:(h + 1) * 64], start=True, stop=True)
            em.te(lambda e, mm=mm: mm(e).then_inc(c.te.sem, 1))
            te_t1 = c.te.inc()
            em.ve(lambda e, v=te_t1: e.wait_ge(c.te.sem, v))
            em.ve(lambda e, h=h, bank=bank: e.scalar_tensor_tensor(
                Xsb[:, h * 64:(h + 1) * 64], ps[bank][:, 0:64],
                nbt[:, h:h + 1], vb[:, h * 64:(h + 1) * 64], ALU.mult, ALU.add))
            em.ve(lambda e: e.engine_nop().then_inc(c.ve.sem, 1))
            c.ve.inc()
        if ci == 0 and "d2_X0" in dram:
            em.sy(lambda e, v=c.ve.n: e.wait_ge(c.ve.sem, v))
            em.sy(lambda e: e.dma_start(out=dram["d2_X0"], in_=Xsb[:]).then_inc(c.st.sem, 16))
            c.st.inc(16)
            em.sy(lambda e: e.dma_start(out=dram["d2_U"], in_=Utk[:]).then_inc(c.st.sem, 16))
            c.st.inc(16)
        curN, curNT = Nsb, NTsb
        nxtN, nxtNT = N2, NT2
        for j in range(7):
            for half in range(2):
                bank = half
                em.te(lambda e, v=c.ve.n: e.wait_ge(c.ve.sem, v))
                for hh in range(8):
                    h = half * 8 + hh
                    em.te(lambda e, h=h, hh=hh, bank=bank: e.matmul(
                        ps[bank][:, hh * 64:(hh + 1) * 64], sb["ident"][:],
                        Xsb[:, h * 64:(h + 1) * 64], start=True, stop=False))
                    mm = lambda e, h=h, hh=hh, bank=bank, curNT=curNT: e.matmul(
                        ps[bank][:, hh * 64:(hh + 1) * 64], curNT[:, h * 128:(h + 1) * 128],
                        Xsb[:, h * 64:(h + 1) * 64], start=False, stop=True)
                    if hh == 7:
                        em.te(lambda e, mm=mm: mm(e).then_inc(c.te.sem, 1))
                    else:
                        em.te(mm)
                te_ap = c.te.inc()
                em.ve(lambda e, v=te_ap: e.wait_ge(c.te.sem, v))
                em.ve(lambda e, half=half, bank=bank: e.tensor_copy(
                    Xsb[:, half * 512:(half + 1) * 512], ps[bank][:, :]))
                em.ve(lambda e: e.engine_nop().then_inc(c.ve.sem, 1))
                c.ve.inc()
            if j < 6:
                for h in range(16):
                    bank = 2 + h % 2
                    bankT = 4 + h % 2
                    if h >= 2:
                        em.te(lambda e, v=c.ve.n: e.wait_ge(c.ve.sem, v))
                    em.te(lambda e, h=h, bank=bank, curN=curN, curNT=curNT: e.matmul(
                        ps[bank][:, 0:128], curNT[:, h * 128:(h + 1) * 128],
                        curN[:, h * 128:(h + 1) * 128], start=True, stop=True).then_inc(c.te.sem, 1))
                    te_s1_ = c.te.inc()
                    em.te(lambda e, h=h, bankT=bankT, curN=curN, curNT=curNT: e.matmul(
                        ps[bankT][:, 0:128], curN[:, h * 128:(h + 1) * 128],
                        curNT[:, h * 128:(h + 1) * 128], start=True, stop=True).then_inc(c.te.sem, 1))
                    te_s2_ = c.te.inc()
                    em.ve(lambda e, v=te_s1_: e.wait_ge(c.te.sem, v))
                    em.ve(lambda e, h=h, bank=bank, nxtN=nxtN: e.tensor_copy(
                        nxtN[:, h * 128:(h + 1) * 128], ps[bank][:, 0:128]))
                    em.ve(lambda e, v=te_s2_: e.wait_ge(c.te.sem, v))
                    em.ve(lambda e, h=h, bankT=bankT, nxtNT=nxtNT: e.tensor_copy(
                        nxtNT[:, h * 128:(h + 1) * 128], ps[bankT][:, 0:128]))
                    em.ve(lambda e: e.engine_nop().then_inc(c.ve.sem, 1))
                    c.ve.inc()
                curN, curNT, nxtN, nxtNT = nxtN, nxtNT, curN, curNT
        if ci == 0 and "d2_X" in dram:
            em.sy(lambda e, v=c.ve.n: e.wait_ge(c.ve.sem, v))
            em.sy(lambda e: e.dma_start(out=dram["d2_X"], in_=Xsb[:]).then_inc(c.st.sem, 16))
            c.st.inc(16)
        for half in range(2):
            bank = half
            em.te(lambda e, v=c.ve.n: e.wait_ge(c.ve.sem, v))
            for hh in range(8):
                h = half * 8 + hh
                em.te(lambda e, h=h, hh=hh, bank=bank: e.matmul(
                    ps[bank][:, hh * 64:(hh + 1) * 64], qtl[:, h * 128:(h + 1) * 128],
                    Ssb[:, h * 64:(h + 1) * 64], start=True, stop=False))
                mm = lambda e, h=h, hh=hh, bank=bank: e.matmul(
                    ps[bank][:, hh * 64:(hh + 1) * 64], AqT[:, h * 128:(h + 1) * 128],
                    Xsb[:, h * 64:(h + 1) * 64], start=False, stop=True)
                if hh == 7:
                    em.te(lambda e, mm=mm: mm(e).then_inc(c.te.sem, 1))
                else:
                    em.te(mm)
            te_out = c.te.inc()
            em.ve(lambda e, v=te_out: e.wait_ge(c.te.sem, v))
            em.ve(lambda e, half=half, bank=bank: e.tensor_copy(
                ysb[:, half * 512:(half + 1) * 512], ps[bank][:, :]))
            em.ve(lambda e: e.engine_nop().then_inc(c.ve.sem, 1))
            c.ve.inc()
        ve_y = c.ve.n
        for h in range(16):
            bank = 2 + h % 2
            if h >= 2:
                em.te(lambda e, v=c.ve.n: e.wait_ge(c.ve.sem, v))
            mm = lambda e, h=h, bank=bank: e.matmul(
                ps[bank][0:64, 0:64], Utk[:, h * 64:(h + 1) * 64],
                Xsb[:, h * 64:(h + 1) * 64], start=True, stop=True)
            em.te(lambda e, mm=mm: mm(e).then_inc(c.te.sem, 1))
            te_su = c.te.inc()
            em.ve(lambda e, v=te_su: e.wait_ge(c.te.sem, v))
            em.ve(lambda e, h=h, bank=bank: e.scalar_tensor_tensor(
                Ssb[:, h * 64:(h + 1) * 64], Ssb[:, h * 64:(h + 1) * 64],
                eLC[:, h:h + 1], ps[bank][0:64, 0:64], ALU.mult, ALU.add))
            em.ve(lambda e: e.engine_nop().then_inc(c.ve.sem, 1))
            c.ve.inc()
        if ci == 0 and "d2_y" in dram:
            em.sy(lambda e, v=c.ve.n: e.wait_ge(c.ve.sem, v))
            em.sy(lambda e: e.dma_start(out=dram["d2_y"], in_=ysb[:]).then_inc(c.st.sem, 16))
            c.st.inc(16)
            em.sy(lambda e: e.dma_start(out=dram["d2_S"], in_=Ssb[:]).then_inc(c.st.sem, 16))
            c.st.inc(16)
        em.se(lambda e, v=ve_y: e.wait_ge(c.ve.sem, v))
        em.se(lambda e: e.activation(ysq[:], ysb[:], AF.Square).then_inc(c.se.sem, 1))
        se_ysq = c.se.inc()
        em.ve(lambda e, v=se_ysq: e.wait_ge(c.se.sem, v))
        for h in range(16):
            em.ve(lambda e, h=h: e.reduce_sum(yss[:, h:h + 1], ysq[:, h * 64:(h + 1) * 64], AX.X))
        em.ve(lambda e: e.engine_nop())
        em.ve(lambda e: e.engine_nop())
        em.ve(lambda e: e.tensor_scalar(yss[:], yss[:], 1.0 / 64.0, None, ALU.mult))
        em.ve(lambda e: e.engine_nop().then_inc(c.ve.sem, 1))
        ve_yss = c.ve.inc()
        em.se(lambda e, v=ve_yss: e.wait_ge(c.ve.sem, v))
        em.se(lambda e: e.activation(yss[:], yss[:], AF.Ln, bias=1e-6))
        em.se(lambda e: e.activation(yss[:], yss[:], AF.Exp, scale=-0.5).then_inc(c.se.sem, 1))
        se_yr = c.se.inc()
        em.ve(lambda e, v=se_yr: e.wait_ge(c.se.sem, v))
        for h in range(16):
            em.ve(lambda e, h=h: e.tensor_scalar(
                ysb[:, h * 64:(h + 1) * 64], ysb[:, h * 64:(h + 1) * 64],
                yss[:, h:h + 1], None, ALU.mult))
        em.ve(lambda e: e.tensor_tensor(yhat[:], ysb[:], hnB[:], ALU.mult))
        em.ve(lambda e: e.engine_nop().then_inc(c.ve.sem, 1))
        ve_yhat = c.ve.inc()
        em.sy(lambda e, v=ve_yhat: e.wait_ge(c.ve.sem, v))
        em.sy(lambda e, t0=t0: e.dma_start(
            out=dram["ab_y"][t0:t0 + CB, :],
            in_=yhat[:]).then_inc(c.st.sem, 16))
        c.st.inc(16)

    for ci in range(NCB):
        chunk(ci)
    c.barrier()



def emit_attn_b_p3(c, dram, sb, st, x_in, x_out):
    em, nc = c.em, c.nc
    E = st.enter_context
    yh = E(nc.sbuf_tensor("b3_yh", [128, 4 * 1024], BF16))
    yhT = E(nc.sbuf_tensor("b3_yhT", [128, 8 * 512], BF16))
    gsb = E(nc.sbuf_tensor("b3_g", [128, 8 * 512], BF16))
    xlns = E(nc.sbuf_tensor("b3_xln", [128, 8 * 512], F32))
    x2s = E(nc.sbuf_tensor("b3_x2", [128, 8 * 512], F32))
    yc = E(nc.sbuf_tensor("b3_yc", [128, 8 * 512], F32))
    y2 = E(nc.sbuf_tensor("b3_y2", [128, 8 * 512], BF16))
    gall = E(nc.sbuf_tensor("b3_gall", [128, 32 * 512], BF16))
    sgt = E(nc.sbuf_tensor("b3_sgt", [128, 512], BF16))
    wo = E(nc.sbuf_tensor("b3_wo", [128, 8 * 1024], BF16))
    w13 = [E(nc.sbuf_tensor(f"b3_w13_{b}", [128, 2 * 8 * 512], BF16)) for b in range(2)]
    w2b = [E(nc.sbuf_tensor(f"b3_w2_{b}", [128, 4 * 1024], BF16)) for b in range(3)]
    stm = E(nc.sbuf_tensor("b3_stm", [1, 512], F32))
    str_ = E(nc.sbuf_tensor("b3_str", [1, 512], F32))
    rB = E(nc.sbuf_tensor("b3_rB", [128, 512], F32))
    tmp = E(nc.sbuf_tensor("b3_tmp", [128, 512], F32))
    tmp2 = E(nc.sbuf_tensor("b3_tmp2", [128, 512], F32))
    fn_col = E(nc.sbuf_tensor("b3_fn", [128, 8], F32))

    em.gp(lambda e: e.dma_start(
        out=_r(wo[:], "p (k n) -> p k n", n=1024),
        in_=_r(dram["ab_out_w"], "(k p) n -> p k n", p=128)).then_inc(c.cs.sem, 16))
    c.cs.inc(16)
    em.sy(lambda e: e.dma_start(out=fn_col[:], in_=dram["ab_ff_norm_w"]).then_inc(c.ld.sem, 16))
    c.ld.inc(16)
    cs_w = c.cs.n
    ld_w = c.ld.n
    em.te(lambda e: e.wait_ge(c.cs.sem, cs_w))
    em.ve(lambda e: e.wait_ge(c.ld.sem, ld_w))

    xlnr = _r(dram["ab_xln"], "(k p) t -> p k t", p=128)
    gr = _r(dram["ab_gate"], "(k p) t -> p k t", p=128)
    xr = _r(dram[x_in], "(k p) t -> p k t", p=128)
    xo = _r(dram[x_out], "(k p) t -> p k t", p=128)
    ps, pt = c.ps, c.pt

    def chunk(ci):
        t0 = ci * CHUNK
        em.ve(lambda e, v=c.st.n: e.wait_ge(c.st.sem, v))
        em.sy(lambda e, v=c.te.n: e.wait_ge(c.te.sem, v))
        em.sy(lambda e, t0=t0: e.dma_start(
            out=_r(yh[:], "p (q n) -> p q n", n=1024),
            in_=_r(dram["ab_y"][t0:t0 + 512, :], "(q p) n -> p q n", p=128)).then_inc(c.ld.sem, 16))
        c.ld.inc(16)
        em.sy(lambda e, t0=t0: e.dma_start(
            out=_r(gsb[:], "p (k t) -> p k t", t=512),
            in_=gr[:, :, t0:t0 + 512]).then_inc(c.ld.sem, 16))
        c.ld.inc(16)
        em.sy(lambda e, t0=t0: e.dma_start(
            out=_r(xlns[:], "p (k t) -> p k t", t=512),
            in_=xlnr[:, :, t0:t0 + 512]).then_inc(c.ld.sem, 16))
        c.ld.inc(16)
        em.sy(lambda e, t0=t0: e.dma_start(
            out=_r(x2s[:], "p (k t) -> p k t", t=512),
            in_=xr[:, :, t0:t0 + 512]).then_inc(c.ld.sem, 16))
        ldx = c.ld.inc(16)
        em.te(lambda e, v=ldx: e.wait_ge(c.ld.sem, v))
        for tt in range(4):
            ptb = pt[tt % 2]
            if tt >= 2:
                em.te(lambda e, v=c.ve.n: e.wait_ge(c.ve.sem, v))
            for j in range(8):
                mm = lambda e, tt=tt, j=j, ptb=ptb: e.transpose(
                    ptb[:, j * 128:(j + 1) * 128],
                    yh[:, tt * 1024 + j * 128: tt * 1024 + j * 128 + 128], sb["ident"][:])
                if j == 7:
                    em.te(lambda e, mm=mm: mm(e).then_inc(c.te.sem, 1))
                else:
                    em.te(mm)
            te_tr = c.te.inc()
            em.ve(lambda e, v=te_tr: e.wait_ge(c.te.sem, v))
            for j in range(8):
                em.ve(lambda e, tt=tt, j=j, ptb=ptb: e.tensor_copy(
                    yhT[:, j * 512 + tt * 128: j * 512 + tt * 128 + 128],
                    ptb[:, j * 128:(j + 1) * 128]))
            em.ve(lambda e: e.engine_nop().then_inc(c.ve.sem, 1))
            c.ve.inc()
        ve_yhT = c.ve.n
        em.te(lambda e, v=ve_yhT: e.wait_ge(c.ve.sem, v))
        for m in range(8):
            bank = m % 2
            if m >= 2:
                em.te(lambda e, v=c.ve.n: e.wait_ge(c.ve.sem, v))
            for k in range(8):
                mm = lambda e, m=m, k=k, bank=bank: e.matmul(
                    ps[bank][:, :], wo[:, k * 1024 + m * 128: k * 1024 + m * 128 + 128],
                    yhT[:, k * 512:(k + 1) * 512], start=(k == 0), stop=(k == 7))
                if k == 7:
                    em.te(lambda e, mm=mm: mm(e).then_inc(c.te.sem, 1))
                else:
                    em.te(mm)
            te_o = c.te.inc()
            em.ve(lambda e, v=te_o: e.wait_ge(c.te.sem, v))
            em.ve(lambda e, m=m, bank=bank: e.tensor_tensor(
                tmp[:], ps[bank][:, :], gsb[:, m * 512:(m + 1) * 512], ALU.mult))
            em.ve(lambda e: e.tensor_scalar(sb["junk"][:], sb["junk"][:], 1.0, None, ALU.mult))
            em.ve(lambda e, m=m: e.tensor_tensor(
                yc[:, m * 512:(m + 1) * 512], tmp[:], xlns[:, m * 512:(m + 1) * 512], ALU.add))
            em.ve(lambda e: e.engine_nop().then_inc(c.ve.sem, 1))
            c.ve.inc()
        ve_yc = c.ve.n
        te0 = c.te.n
        em.te(lambda e, v=ve_yc: e.wait_ge(c.ve.sem, v))
        for i in range(8):
            buf = tmp if i % 2 == 0 else tmp2
            if i >= 2:
                em.se(lambda e, v=te0 + (i - 1): e.wait_ge(c.te.sem, v))
            else:
                em.se(lambda e, v=ve_yc: e.wait_ge(c.ve.sem, v))
            em.se(lambda e, i=i, buf=buf: e.activation(
                buf[:], yc[:, i * 512:(i + 1) * 512], AF.Square).then_inc(c.se.sem, 1))
            se_sq = c.se.inc()
            em.te(lambda e, v=se_sq: e.wait_ge(c.se.sem, v))
            em.te(lambda e, i=i, buf=buf: e.matmul(
                ps[4][0:1, :], sb["ones_col_f32"][:], buf[:],
                start=(i == 0), stop=(i == 7)).then_inc(c.te.sem, 1))
            c.te.inc()
        te_ss = c.te.n
        em.ve(lambda e, v=te_ss: e.wait_ge(c.te.sem, v))
        em.ve(lambda e: e.tensor_scalar(stm[:], ps[4][0:1, :], 1.0 / D, None, ALU.mult))
        em.ve(lambda e: e.engine_nop().then_inc(c.ve.sem, 1))
        ve_ss = c.ve.inc()
        em.se(lambda e, v=ve_ss: e.wait_ge(c.ve.sem, v))
        em.se(lambda e: e.activation(str_[:], stm[:], AF.Ln, bias=1e-6))
        em.se(lambda e: e.activation(str_[:], str_[:], AF.Exp, scale=-0.5).then_inc(c.se.sem, 1))
        se_rs = c.se.inc()
        em.te(lambda e, v=se_rs: e.wait_ge(c.se.sem, v))
        em.te(lambda e: e.matmul(ps[5][:, :], sb["ones_row_f32"][:], str_[:],
                                 start=True, stop=True).then_inc(c.te.sem, 1))
        te_rb = c.te.inc()
        em.ve(lambda e, v=te_rb: e.wait_ge(c.te.sem, v))
        em.ve(lambda e: e.tensor_copy(rB[:], ps[5][:, :]))
        em.ve(lambda e: e.tensor_scalar(sb["junk"][:], sb["junk"][:], 1.0, None, ALU.mult))
        for i in range(8):
            em.ve(lambda e, i=i: e.tensor_tensor(
                tmp[:], yc[:, i * 512:(i + 1) * 512], rB[:], ALU.mult))
            em.ve(lambda e, i=i: e.tensor_scalar(
                y2[:, i * 512:(i + 1) * 512], tmp[:], fn_col[:, i:i + 1], None, ALU.mult))
        em.ve(lambda e: e.engine_nop().then_inc(c.ve.sem, 1))
        ve_y2 = c.ve.inc()
        em.te(lambda e, v=ve_y2: e.wait_ge(c.ve.sem, v))
        te_n13 = {}
        for n in range(8):
            b = n % 2
            if n >= 2:
                em.sy(lambda e, v=te_n13[n - 2]: e.wait_ge(c.te.sem, v))
            em.sy(lambda e, n=n, b=b: e.dma_start(
                out=_r(w13[b][:, 0:8 * 512], "p (k t) -> p k t", t=512),
                in_=_r(dram["ff_w1_bf"][:, n * 512:(n + 1) * 512], "(k p) t -> p k t", p=128)).then_inc(c.ld.sem, 16))
            c.ld.inc(16)
            em.sy(lambda e, n=n, b=b: e.dma_start(
                out=_r(w13[b][:, 8 * 512:16 * 512], "p (k t) -> p k t", t=512),
                in_=_r(dram["ff_w3_bf"][:, n * 512:(n + 1) * 512], "(k p) t -> p k t", p=128)).then_inc(c.ld.sem, 16))
            ld_w13 = c.ld.inc(16)
            em.te(lambda e, v=ld_w13: e.wait_ge(c.ld.sem, v))
            for mm4 in range(4):
                bank = mm4 % 2
                em.te(lambda e, v=c.ve.n: e.wait_ge(c.ve.sem, v))
                for k in range(8):
                    em.te(lambda e, n=n, b=b, mm4=mm4, k=k, bank=bank: e.matmul(
                        ps[bank][:, :], w13[b][:, k * 512 + mm4 * 128: k * 512 + mm4 * 128 + 128],
                        y2[:, k * 512:(k + 1) * 512], start=(k == 0), stop=(k == 7)))
                for k in range(8):
                    mm = lambda e, n=n, b=b, mm4=mm4, k=k, bank=bank: e.matmul(
                        ps[2 + bank][:, :], w13[b][:, 8 * 512 + k * 512 + mm4 * 128: 8 * 512 + k * 512 + mm4 * 128 + 128],
                        y2[:, k * 512:(k + 1) * 512], start=(k == 0), stop=(k == 7))
                    if k == 7:
                        em.te(lambda e, mm=mm: mm(e).then_inc(c.te.sem, 1))
                    else:
                        em.te(mm)
                te_g = c.te.inc()
                em.se(lambda e, v=te_g: e.wait_ge(c.te.sem, v))
                em.se(lambda e, v=c.ve.n: e.wait_ge(c.ve.sem, v))
                em.se(lambda e, bank=bank: e.activation(
                    sgt[:], ps[bank][:, :], AF.Silu).then_inc(c.se.sem, 1))
                se_sg = c.se.inc()
                em.ve(lambda e, v=se_sg: e.wait_ge(c.se.sem, v))
                em.ve(lambda e, n=n, mm4=mm4, bank=bank: e.tensor_tensor(
                    gall[:, n * 4 * 512 + mm4 * 512: n * 4 * 512 + (mm4 + 1) * 512],
                    ps[2 + bank][:, :], sgt[:], ALU.mult))
                em.ve(lambda e: e.engine_nop().then_inc(c.ve.sem, 1))
                c.ve.inc()
            te_n13[n] = c.te.n
        ve_gall = c.ve.n
        if ci == 0 and "d3_yc" in dram:
            em.sy(lambda e, v=ve_gall: e.wait_ge(c.ve.sem, v))
            em.sy(lambda e: e.dma_start(out=dram["d3_yc"], in_=yc[:]).then_inc(c.st.sem, 16))
            c.st.inc(16)
            em.sy(lambda e: e.dma_start(out=dram["d3_y2"], in_=y2[:]).then_inc(c.st.sem, 16))
            c.st.inc(16)
            em.sy(lambda e: e.dma_start(out=dram["d3_gall"], in_=gall[:]).then_inc(c.st.sem, 16))
            c.st.inc(16)
        em.te(lambda e, v=ve_gall: e.wait_ge(c.ve.sem, v))
        for half in range(2):
            ms = [half * 4 + i for i in range(4)]
            em.te(lambda e, v=c.ve.n: e.wait_ge(c.ve.sem, v))
            te_n_done = {}
            for n in range(8):
                b2_ = n % 3
                if n >= 1:
                    em.sy(lambda e, v=te_n_done[n - 1]: e.wait_ge(c.te.sem, v))
                em.sy(lambda e, n=n, b2_=b2_: e.dma_start(
                    out=_r(w2b[b2_][:, 0:4 * 1024], "p (k t) -> p k t", t=1024),
                    in_=_r(dram["ff_w2_bf"][n * 512:(n + 1) * 512, :],
                           "(k p) t -> p k t", p=128)).then_inc(c.ld.sem, 16))
                ld_w2 = c.ld.inc(16)
                em.te(lambda e, v=ld_w2: e.wait_ge(c.ld.sem, v))
                for mi, m in enumerate(ms):
                    bank = 2 + mi
                    for k4 in range(4):
                        mm = lambda e, n=n, m=m, k4=k4, b2_=b2_, bank=bank: e.matmul(
                            ps[bank][:, :],
                            w2b[b2_][:, k4 * 1024 + m * 128: k4 * 1024 + m * 128 + 128],
                            gall[:, (n * 4 + k4) * 512: (n * 4 + k4 + 1) * 512],
                            start=(n == 0 and k4 == 0), stop=(n == 7 and k4 == 3))
                        if mi == 3 and k4 == 3:
                            em.te(lambda e, mm=mm: mm(e).then_inc(c.te.sem, 1))
                        else:
                            em.te(mm)
                te_n_done[n] = c.te.inc()
            for mi, m in enumerate(ms):
                bank = 2 + mi
                em.ve(lambda e, v=te_n_done[7]: e.wait_ge(c.te.sem, v))
                em.ve(lambda e, m=m, bank=bank: e.tensor_tensor(
                    tmp[:], ps[bank][:, :], yc[:, m * 512:(m + 1) * 512], ALU.add))
                em.ve(lambda e: e.tensor_scalar(sb["junk"][:], sb["junk"][:], 1.0, None, ALU.mult))
                em.ve(lambda e, m=m: e.tensor_tensor(
                    x2s[:, m * 512:(m + 1) * 512], tmp[:],
                    x2s[:, m * 512:(m + 1) * 512], ALU.add))
                em.ve(lambda e: e.engine_nop().then_inc(c.ve.sem, 1))
                c.ve.inc()
        ve_x3 = c.ve.n
        em.sy(lambda e, v=ve_x3: e.wait_ge(c.ve.sem, v))
        em.sy(lambda e, t0=t0: e.dma_start(
            out=xo[:, :, t0:t0 + 512],
            in_=_r(x2s[:], "p (k t) -> p k t", t=512)).then_inc(c.st.sem, 16))
        c.st.inc(16)
        em.sy(lambda e, v=c.te.n: e.wait_ge(c.te.sem, v))
        em.sy(lambda e, v=c.ve.n: e.wait_ge(c.ve.sem, v))

    for ci in range(NCHUNK):
        chunk(ci)
    c.barrier()



def build_program(n_stages=99, dbg=(), only=None):
    nc = bass.Bass("TRN2", target_bir_lowering=False)
    dram = {}

    def din(name, shape, dtype=F32):
        dram[name] = nc.dram_tensor(name, shape, dtype, kind="ExternalInput").ap()

    def dint(name, shape, dtype=F32):
        kind = "ExternalOutput" if name in dbg else "Internal"
        dram[name] = nc.dram_tensor(name, shape, dtype, kind=kind).ap()

    din("xT_in", [D, T])
    din("aa_wq", [D, DI_A]); din("aa_wk", [D, DI_A]); din("aa_wv", [D, DI_A])
    din("aa_wo", [DI_A, D])
    din("aa_bq_col", [128, 16]); din("aa_bk_col", [128, 16])
    din("aa_nbq_col", [128, 16]); din("aa_nbk_col", [128, 16])
    din("aa_bv_row", [1, DI_A], BF16); din("aa_bo_row", [1, D], BF16)
    din("gate_wT", [D, NE])
    din("guwT", [NE, D, D]); din("dwT", [NE, 512, D])
    din("ab_norm_in_w", [128, 8]); din("ab_conv_w", [128, 32]); din("ab_conv_b", [128, 8])
    din("ab_q_w", [D, D]); din("ab_k_w", [D, D]); din("ab_v_w", [D, D])
    din("ab_alpha_w", [D, D]); din("ab_alpha_b", [128, 8])
    din("ab_beta_w", [D, H]); din("ab_beta_b", [1, H])
    din("ab_head_norm_row", [1, D])
    din("ab_u1_w", [D, 64]); din("ab_u1_b", [64, 1])
    din("ab_u2_w", [64, D]); din("ab_u2_b", [128, 8])
    din("ab_out_w", [D, D]); din("ab_ff_norm_w", [128, 8])
    din("ff_w1", [D, 4096]); din("ff_w3", [D, 4096]); din("ff_w2", [4096, D])
    din("ident_bf", [128, 128], BF16); din("ident_f32", [128, 128])
    din("ones_col_bf", [128, 1], BF16); din("ones_row_bf", [1, 512], BF16)
    din("ones_col_f32", [128, 1]); din("ones_row_f32", [1, 128])
    din("mask_a", [128, 4 * 512], BF16)
    din("mask_sl", [128, 128], BF16); din("mask_iu", [128, 128], BF16)
    din("const_row", [128, 8])
    dram["outT"] = nc.dram_tensor("outT", [D, T], F32, kind="ExternalOutput").ap()
    dint("x1T", [D, T]); dint("x2T", [D, T]); dint("x3T", [D, T])
    dint("aa_q", [DI_A, T], BF16); dint("aa_k", [DI_A, T], BF16)
    dint("aa_v", [T, DI_A], BF16); dint("aa_y", [DI_A, T], BF16)
    if "dbg_b2" in dbg:
        dram["d3_yc"] = nc.dram_tensor("d3_yc", [128, 4096], F32, kind="ExternalOutput").ap()
        dram["d3_y2"] = nc.dram_tensor("d3_y2", [128, 4096], BF16, kind="ExternalOutput").ap()
        dram["d3_gall"] = nc.dram_tensor("d3_gall", [128, 32 * 512], BF16, kind="ExternalOutput").ap()
        for nm, shp, dt in (("d2_lam", [64, 2048], F32), ("d2_e1", [64, 2048], F32),
                            ("d2_fk", [64, 2048], BF16), ("d2_gk", [64, 2048], BF16),
                            ("d2_N", [128, 2048], BF16), ("d2_AqT", [128, 2048], BF16),
                            ("d2_NT", [128, 2048], BF16), ("d2_X0", [128, 1024], BF16),
                            ("d2_X", [128, 1024], BF16), ("d2_y", [128, 1024], F32),
                            ("d2_U", [128, 1024], BF16), ("d2_S", [64, 1024], BF16)):
            dram[nm] = nc.dram_tensor(nm, shp, dt, kind="ExternalOutput").ap()
    if "dbg_z" in dbg:
        dram["dbg_z"] = nc.dram_tensor("dbg_z", [128, 16], F32, kind="ExternalOutput").ap()
        dram["dbg_den"] = nc.dram_tensor("dbg_den", [1, 512], F32, kind="ExternalOutput").ap()
        dram["dbg_rdB"] = nc.dram_tensor("dbg_rdB", [128, 512], F32, kind="ExternalOutput").ap()
    dint("guw_bf", [NE, D, D], BF16); dint("dw_bf", [NE, 512, D], BF16)
    dint("moe_xn", [D, T], BF16); dint("moe_comb", [T, NE])
    dint("ab_xln", [D, T]); dint("ab_q", [D, T], BF16); dint("ab_k", [D, T], BF16)
    dint("ab_v", [T, D], BF16); dint("ab_la", [D, T]); dint("ab_beta", [T, H])
    dint("ab_gate", [D, T], BF16); dint("ab_y", [T, D], BF16)
    dint("ff_w1_bf", [D, 4096], BF16); dint("ff_w3_bf", [D, 4096], BF16)
    dint("ff_w2_bf", [4096, D], BF16)

    with ExitStack() as top:
        c = Ctx(nc, top)
        E = top.enter_context
        sb = {
            "ident": E(nc.sbuf_tensor("g_ident", [128, 128], BF16)),
            "ident_f32": E(nc.sbuf_tensor("g_ident_f32", [128, 128], F32)),
            "ones_col_bf": E(nc.sbuf_tensor("g_ones_col_bf", [128, 1], BF16)),
            "ones_row_bf": E(nc.sbuf_tensor("g_ones_row_bf", [1, 512], BF16)),
            "ones_col_f32": E(nc.sbuf_tensor("g_ones_col_f32", [128, 1], F32)),
            "ones_row_f32": E(nc.sbuf_tensor("g_ones_row_f32", [1, 128], F32)),
            "mask_a": E(nc.sbuf_tensor("g_mask_a", [128, 4 * 512], BF16)),
            "consts": E(nc.sbuf_tensor("g_consts", [128, 8], F32)),
            "junk": E(nc.sbuf_tensor("g_junk", [128, 16], F32)),
        }
        for j, v in enumerate(CONST_VALS):
            nc.const_aps.register(F32, v, sb["consts"][:, j:j + 1])
        emit_prologue(c, dram, sb)
        if only == "attnb":
            stages = [
                lambda c, d, s2, st: emit_attn_b_p1(c, d, s2, st, "xT_in"),
                lambda c, d, s2, st: emit_attn_b_p2(c, d, s2, st),
                lambda c, d, s2, st: emit_attn_b_p3(c, d, s2, st, "xT_in", "x3T"),
            ]
        else:
            stages = [
            emit_attn_a_p1, emit_attn_a_p2, emit_attn_a_p3,
            lambda c, d, s2, st: emit_moe_gate(c, d, s2, st, "x1T", "m1"),
            lambda c, d, s2, st: emit_moe_exp(c, d, s2, st, "x1T", "x2T", "m1"),
            lambda c, d, s2, st: emit_attn_b_p1(c, d, s2, st, "x2T"),
            lambda c, d, s2, st: emit_attn_b_p2(c, d, s2, st),
            lambda c, d, s2, st: emit_attn_b_p3(c, d, s2, st, "x2T", "x3T"),
            lambda c, d, s2, st: emit_moe_gate(c, d, s2, st, "x3T", "m2"),
            lambda c, d, s2, st: emit_moe_exp(c, d, s2, st, "x3T", "outT", "m2"),
            ]
        stage_names = ["a1", "a2", "a3", "mg1", "me1", "b1", "b2", "b3", "mg2", "me2"]
        if only == "attnb":
            stage_names = ["b1", "b2", "b3"]
        for i, fn in enumerate(stages):
            if i >= n_stages:
                break
            c.mark(stage_names[i] if i < len(stage_names) else f"s{i}")
            with ExitStack() as st:
                fn(c, dram, sb, st)
        block = E(nc.Block())

        def run_queue(e, eng, fns):
            bounds = [(idx[eng], nm) for nm, idx in c.stage_marks]
            bi = 0
            cur = None
            for i, f in enumerate(fns):
                while bi < len(bounds) and bounds[bi][0] == i:
                    if cur is not None:
                        nc.leave_named_scope(*cur, False)
                    sid, _ = nc.enter_named_scope(bounds[bi][1], False)
                    cur = (bounds[bi][1], sid)
                    bi += 1
                f(e)
            if cur is not None:
                nc.leave_named_scope(*cur, False)

        for eng in Emit.ENGINES:
            fns = c.em.q[eng]
            if fns:
                getattr(block, eng)(lambda e, eng=eng, fns=fns: run_queue(e, eng, fns))
    return nc



def _bf(x):
    return np.ascontiguousarray(x).astype(ml_dtypes.bfloat16)


K_CONST = CONST_VALS


def _prep_inputs(inputs):
    g = {}
    f32 = lambda x: np.ascontiguousarray(np.asarray(x), dtype=np.float32)
    tr = lambda x: np.ascontiguousarray(np.asarray(x).T)
    g["aa_wq"] = f32(inputs["L0_attn_a_Wq"]); g["aa_wk"] = f32(inputs["L0_attn_a_Wk"])
    g["aa_wv"] = f32(inputs["L0_attn_a_Wv"]); g["aa_wo"] = f32(inputs["L0_attn_a_Wo"])
    bq = f32(inputs["L0_attn_a_bq"]); bk = f32(inputs["L0_attn_a_bk"])
    col = lambda b: np.ascontiguousarray(b.reshape(-1, 128).T)
    g["aa_bq_col"] = col(bq); g["aa_bk_col"] = col(bk)
    g["aa_nbq_col"] = col(-bq); g["aa_nbk_col"] = col(-bk)
    g["aa_bv_row"] = _bf(f32(inputs["L0_attn_a_bv"]).reshape(1, -1))
    g["aa_bo_row"] = _bf(f32(inputs["L0_attn_a_bo"]).reshape(1, -1))
    g["gate_wT"] = tr(f32(inputs["L0_ffn_gate_w"]))
    g["guwT"] = np.ascontiguousarray(np.asarray(inputs["L0_ffn_gate_up_w"], dtype=np.float32).transpose(0, 2, 1))
    g["dwT"] = np.ascontiguousarray(np.asarray(inputs["L0_ffn_down_w"], dtype=np.float32).transpose(0, 2, 1))
    g["ab_norm_in_w"] = col(f32(inputs["L0_attn_b_norm_in_w"]))
    cw = f32(inputs["L0_attn_b_conv_w"]).reshape(1024, 4).reshape(8, 128, 4)
    g["ab_conv_w"] = np.ascontiguousarray(cw.transpose(1, 0, 2).reshape(128, 32))
    g["ab_conv_b"] = col(f32(inputs["L0_attn_b_conv_b"]))
    g["ab_q_w"] = f32(inputs["L0_attn_b_q_w"]); g["ab_k_w"] = f32(inputs["L0_attn_b_k_w"])
    g["ab_v_w"] = f32(inputs["L0_attn_b_v_w"]); g["ab_alpha_w"] = f32(inputs["L0_attn_b_alpha_w"])
    g["ab_alpha_b"] = col(f32(inputs["L0_attn_b_alpha_b"]))
    g["ab_beta_w"] = f32(inputs["L0_attn_b_beta_w"])
    g["ab_beta_b"] = f32(inputs["L0_attn_b_beta_b"]).reshape(1, -1)
    g["ab_head_norm_row"] = f32(inputs["L0_attn_b_head_norm_w"]).reshape(1, -1)
    g["ab_u1_w"] = f32(inputs["L0_attn_b_u1_w"])
    g["ab_u1_b"] = f32(inputs["L0_attn_b_u1_b"]).reshape(-1, 1)
    g["ab_u2_w"] = f32(inputs["L0_attn_b_u2_w"])
    g["ab_u2_b"] = col(f32(inputs["L0_attn_b_u2_b"]))
    g["ab_out_w"] = f32(inputs["L0_attn_b_out_w"])
    g["ab_ff_norm_w"] = col(f32(inputs["L0_attn_b_ff_norm_w"]))
    g["ff_w1"] = f32(inputs["L0_attn_b_ff_w1"]); g["ff_w3"] = f32(inputs["L0_attn_b_ff_w3"])
    g["ff_w2"] = f32(inputs["L0_attn_b_ff_w2"])
    g["ident_bf"] = _bf(np.eye(128, dtype=np.float32))
    g["ident_f32"] = np.eye(128, dtype=np.float32)
    g["ones_col_bf"] = _bf(np.ones((128, 1), np.float32))
    g["ones_row_bf"] = _bf(np.ones((1, 512), np.float32))
    g["ones_col_f32"] = np.ones((128, 1), np.float32)
    g["const_row"] = np.tile(np.array(K_CONST, np.float32), (128, 1))
    g["ones_row_f32"] = np.ones((1, 128), np.float32)
    mask = np.zeros((128, 4, 512), np.float32)
    for m in range(4):
        for i in range(128):
            mask[i, m, m * 128 + i:] = 1.0
    g["mask_a"] = _bf(mask.reshape(128, 4 * 512))
    idx = np.arange(128)
    g["mask_sl"] = _bf((idx[None, :] < idx[:, None]).astype(np.float32))
    g["mask_iu"] = _bf((idx[:, None] <= idx[None, :]).astype(np.float32))
    return g


def kernel(**inputs):
    g = _prep_inputs(inputs)
    x = np.asarray(inputs["L0_attn_a_x"], dtype=np.float32)
    nc = build_program()
    in_maps = []
    for b in range(8):
        m = dict(g)
        m["xT_in"] = np.ascontiguousarray(x[b].T)
        in_maps.append(m)
    res = run_bass_kernel_spmd(nc, in_maps, core_ids=list(range(8)))
    out = np.stack([np.ascontiguousarray(r["outT"].T) for r in res.results])
    return out


# version 08 (final)
# speedup vs baseline: 1.4311x; 1.0668x over previous
import numpy as np
import ml_dtypes
import concourse.bass as bass
import concourse.mybir as mybir
from concourse.bass_utils import run_bass_kernel_spmd
from contextlib import ExitStack

F32 = mybir.dt.float32
BF16 = mybir.dt.bfloat16
AF = mybir.ActivationFunctionType
ALU = mybir.AluOpType
AX = mybir.AxisListType

CONST_VALS = (1e-5, 1e-6, 1.0, 0.5, 2.0, -1.0, 0.0, 1e-30)
B2_LVL = 9
B2_R1 = 3

T = 8192
D = 1024
DI_A = 2048
NCHUNK = 16
CHUNK = 512
NE = 32
H = 16
DK = DV = 64
CB = 128
NCB = T // CB



class Ctr:
    def __init__(self, sem, name=""):
        self.sem = sem
        self.n = 0
        self.name = name

    def inc(self, k=1):
        self.n += k
        return self.n


class Emit:
    ENGINES = ("sync", "gpsimd", "tensor", "vector", "scalar")

    def __init__(self):
        self.q = {e: [] for e in self.ENGINES}

    def _add(self, eng, fn):
        self.q[eng].append(fn)

    def sy(self, fn): self._add("sync", fn)
    def gp(self, fn): self._add("gpsimd", fn)
    def te(self, fn): self._add("tensor", fn)
    def ve(self, fn): self._add("vector", fn)
    def se(self, fn): self._add("scalar", fn)


class Ctx:
    def __init__(self, nc, top):
        self.nc = nc
        self.top = top
        self.em = Emit()
        E = top.enter_context
        self.ld = Ctr(E(nc.semaphore("s_ld")), "ld")
        self.st = Ctr(E(nc.semaphore("s_st")), "st")
        self.cs = Ctr(E(nc.semaphore("s_cs")), "cs")
        self.te = Ctr(E(nc.semaphore("s_te")), "te")
        self.ve = Ctr(E(nc.semaphore("s_ve")), "ve")
        self.se = Ctr(E(nc.semaphore("s_se")), "se")
        self.ge = Ctr(E(nc.semaphore("s_ge")), "ge")
        self.bar = Ctr(E(nc.semaphore("s_bar")), "bar")
        self.stage_marks = []
        self.ps = [E(nc.psum_tensor(f"ps{i}", [128, 512], F32)) for i in range(6)]
        self.pt = [E(nc.psum_tensor(f"pt{i}", [128, 1024], BF16)) for i in range(2)]

    def mark(self, name):
        self.stage_marks.append((name, {e: len(self.em.q[e]) for e in Emit.ENGINES}))

    def barrier(self):
        em, c = self.em, self
        ld_n, st_n, cs_n = c.ld.n, c.st.n, c.cs.n
        em.sy(lambda e: e.wait_ge(c.ld.sem, ld_n))
        em.sy(lambda e: e.wait_ge(c.st.sem, st_n))
        em.gp(lambda e: e.wait_ge(c.cs.sem, cs_n))
        target = self.bar.n + 5
        for eng in Emit.ENGINES:
            self.em._add(eng, lambda e: e.sem_inc(c.bar.sem, 1))
            self.em._add(eng, lambda e: e.wait_ge(c.bar.sem, target))
        self.bar.n = target


def _r(ap_, pat, **kw):
    return ap_.rearrange(pat, **kw)



def emit_prologue(c, dram, sb):
    em, nc = c.em, c.nc

    def cast_dma(dst, src):
        em.gp(lambda e, a=dst, b=src: e.dma_start(out=a, in_=b).then_inc(c.cs.sem, 16))
        c.cs.inc(16)

    for e_i in range(NE):
        cast_dma(dram["guw_bf"][e_i], dram["guwT"][e_i])
        cast_dma(dram["dw_bf"][e_i], dram["dwT"][e_i])
    for name in ("ff_w1", "ff_w3", "ff_w2"):
        w = dram[name]
        nrow = w.shape[0]
        for i in range(0, nrow, 1024):
            cast_dma(dram[name + "_bf"][i:i + 1024], w[i:i + 1024])

    def ld(dst, src):
        em.sy(lambda e, a=dst, b=src: e.dma_start(out=a, in_=b).then_inc(c.ld.sem, 16))
        c.ld.inc(16)

    def ldcast(dst, src):
        cast_dma(dst, src)

    ld(sb["ident"][:], dram["ident_bf"])
    ld(sb["ident_f32"][:], dram["ident_f32"])
    ld(sb["ones_col_bf"][:], dram["ones_col_bf"])
    ld(sb["ones_row_bf"][:], dram["ones_row_bf"])
    ld(sb["ones_col_f32"][:], dram["ones_col_f32"])
    ld(sb["ones_row_f32"][:], dram["ones_row_f32"])
    ld(sb["mask_a"][:], dram["mask_a"])
    ld(sb["consts"][:], dram["const_row"])
    for i in range(6):
        em.ve(lambda e, i=i: e.memset(c.ps[i][:, :], 0.0))
    em.ve(lambda e: e.engine_nop().then_inc(c.ve.sem, 1))
    c.ve.inc()
    c.barrier()



def emit_attn_a_p1(c, dram, sb, st):
    em, nc = c.em, c.nc
    E = st.enter_context
    xt = E(nc.sbuf_tensor("a1_xt", [128, 8 * 512], F32))
    xn = E(nc.sbuf_tensor("a1_xn", [128, 8 * 512], BF16))
    st_m = E(nc.sbuf_tensor("a1_st_m", [1, 512], F32))
    st_ms = E(nc.sbuf_tensor("a1_st_ms", [1, 512], F32))
    st_r = E(nc.sbuf_tensor("a1_st_r", [1, 512], F32))
    st_mr = E(nc.sbuf_tensor("a1_st_mr", [1, 512], F32))
    rdB = E(nc.sbuf_tensor("a1_rdB", [128, 512], F32))
    mrB = E(nc.sbuf_tensor("a1_mrB", [128, 512], F32))
    tmp1 = E(nc.sbuf_tensor("a1_tmp1", [128, 512], F32))
    tmp2 = E(nc.sbuf_tensor("a1_tmp2", [128, 512], F32))
    qT = E(nc.sbuf_tensor("a1_qT", [128, 16 * 512], BF16))
    kT = E(nc.sbuf_tensor("a1_kT", [128, 16 * 512], BF16))
    vt = E(nc.sbuf_tensor("a1_vt", [128, 4 * 2048], BF16))
    wq = E(nc.sbuf_tensor("a1_wq", [128, 8 * 2048], BF16))
    wk = E(nc.sbuf_tensor("a1_wk", [128, 8 * 2048], BF16))
    wv = E(nc.sbuf_tensor("a1_wv", [128, 8 * 2048], BF16))
    bq = E(nc.sbuf_tensor("a1_bq", [128, 16], F32))
    bk = E(nc.sbuf_tensor("a1_bk", [128, 16], F32))
    nbq = E(nc.sbuf_tensor("a1_nbq", [128, 16], F32))
    nbk = E(nc.sbuf_tensor("a1_nbk", [128, 16], F32))
    bv_row = E(nc.sbuf_tensor("a1_bv_row", [1, 2048], BF16))

    for w_sb, name in ((wq, "aa_wq"), (wk, "aa_wk"), (wv, "aa_wv")):
        em.gp(lambda e, a=w_sb, b=dram[name]: e.dma_start(
            out=_r(a[:], "p (k n) -> p k n", n=2048),
            in_=_r(b, "(k p) n -> p k n", p=128)).then_inc(c.cs.sem, 16))
        c.cs.inc(16)
    em.sy(lambda e: e.dma_start(out=bq[:], in_=dram["aa_bq_col"]).then_inc(c.ld.sem, 16)); c.ld.inc(16)
    em.sy(lambda e: e.dma_start(out=bk[:], in_=dram["aa_bk_col"]).then_inc(c.ld.sem, 16)); c.ld.inc(16)
    em.sy(lambda e: e.dma_start(out=nbq[:], in_=dram["aa_nbq_col"]).then_inc(c.ld.sem, 16)); c.ld.inc(16)
    em.sy(lambda e: e.dma_start(out=nbk[:], in_=dram["aa_nbk_col"]).then_inc(c.ld.sem, 16)); c.ld.inc(16)
    em.gp(lambda e: e.dma_start(out=bv_row[:], in_=dram["aa_bv_row"]).then_inc(c.cs.sem, 16)); c.cs.inc(16)
    ld_w_done = c.ld.n
    cs_w_done = c.cs.n

    x0r = _r(dram["xT_in"], "(k p) t -> p k t", p=128)
    qr = _r(dram["aa_q"], "(m p) t -> p m t", p=128)
    kr = _r(dram["aa_k"], "(m p) t -> p m t", p=128)

    ps = c.ps

    def chunk(ci):
        t0 = ci * CHUNK
        em.ve(lambda e, v=c.st.n: e.wait_ge(c.st.sem, v))
        em.sy(lambda e: e.dma_start(out=_r(xt[:], "p (k t) -> p k t", t=512),
                                    in_=x0r[:, :, t0:t0 + 512]).then_inc(c.ld.sem, 16))
        ldx = c.ld.inc(16)
        em.te(lambda e: e.wait_ge(c.ld.sem, max(ldx, ld_w_done)))
        em.te(lambda e: e.wait_ge(c.cs.sem, cs_w_done))
        for i in range(8):
            em.te(lambda e, i=i: e.matmul(ps[0][0:1, :], sb["ones_col_f32"][:],
                                          xt[:, i * 512:(i + 1) * 512],
                                          start=(i == 0), stop=(i == 7)))
        em.se(lambda e, v=ldx: e.wait_ge(c.ld.sem, v))
        for i in range(8):
            buf = tmp1 if i % 2 == 0 else tmp2
            if i >= 2:
                em.se(lambda e, v=c.te.n + (i - 1): e.wait_ge(c.te.sem, v))
            em.se(lambda e, i=i, buf=buf: e.activation(
                buf[:], xt[:, i * 512:(i + 1) * 512], AF.Square).then_inc(c.se.sem, 1))
            se_sq = c.se.inc()
            em.te(lambda e, v=se_sq: e.wait_ge(c.se.sem, v))
            mm = lambda e, i=i, buf=buf: e.matmul(ps[0][32:33, :], sb["ones_col_f32"][:],
                                                  buf[:], start=(i == 0), stop=(i == 7))
            if i == 7:
                em.te(lambda e, mm=mm: mm(e).then_inc(c.te.sem, 1))
            else:
                em.te(lambda e, mm=mm: mm(e).then_inc(c.te.sem, 1))
        te_stat = c.te.n + 8
        c.te.n = te_stat
        em.ve(lambda e: e.wait_ge(c.te.sem, te_stat))
        em.ve(lambda e: e.tensor_scalar(st_m[:], ps[0][0:1, :], 1.0 / D, None, ALU.mult))
        em.ve(lambda e: e.tensor_scalar(st_ms[:], ps[0][32:33, :], 1.0 / D, None, ALU.mult))
        em.ve(lambda e: e.tensor_tensor(st_r[:], st_m[:], st_m[:], ALU.mult))
        em.ve(lambda e: e.tensor_tensor(st_r[:], st_ms[:], st_r[:], ALU.subtract))
        em.ve(lambda e: e.engine_nop().then_inc(c.ve.sem, 1))
        ve_var = c.ve.inc()
        em.se(lambda e: e.wait_ge(c.ve.sem, ve_var))
        em.se(lambda e: e.activation(st_r[:], st_r[:], AF.Ln, bias=1e-5))
        em.se(lambda e: e.activation(st_r[:], st_r[:], AF.Exp, scale=-0.5))
        em.se(lambda e: e.copy(st_mr[:], st_m[:]).then_inc(c.se.sem, 1))
        se_rstd = c.se.inc()
        em.ve(lambda e: e.wait_ge(c.se.sem, se_rstd))
        em.ve(lambda e: e.tensor_tensor(st_mr[:], st_mr[:], st_r[:], ALU.mult))
        em.ve(lambda e: e.engine_nop().then_inc(c.ve.sem, 1))
        ve_stats = c.ve.inc()
        em.te(lambda e: e.wait_ge(c.ve.sem, ve_stats))
        em.te(lambda e: e.matmul(ps[1][:, :], sb["ones_row_f32"][:], st_r[:],
                                 start=True, stop=True))
        em.te(lambda e: e.matmul(ps[2][:, :], sb["ones_row_f32"][:], st_mr[:],
                                 start=True, stop=True).then_inc(c.te.sem, 1))
        te_bc = c.te.inc()
        em.ve(lambda e: e.wait_ge(c.te.sem, te_bc))
        em.ve(lambda e: e.tensor_copy(rdB[:], ps[1][:, :]))
        em.ve(lambda e: e.tensor_copy(mrB[:], ps[2][:, :]))
        for i in range(8):
            em.ve(lambda e, i=i: e.tensor_tensor(
                tmp1[:], xt[:, i * 512:(i + 1) * 512], rdB[:], ALU.mult))
            em.ve(lambda e, i=i: e.tensor_tensor(
                xn[:, i * 512:(i + 1) * 512], tmp1[:], mrB[:], ALU.subtract))
        em.ve(lambda e: e.engine_nop().then_inc(c.ve.sem, 1))
        ve_xn = c.ve.inc()
        em.te(lambda e: e.wait_ge(c.ve.sem, ve_xn))

        for (w_sb, b_col, nb_col, out_sb) in ((wq, bq, nbq, qT), (wk, bk, nbk, kT)):
            def proj_qk(w_sb=w_sb, b_col=b_col, nb_col=nb_col, out_sb=out_sb):
                for m in range(16):
                    bank = 3 + (m % 2)
                    ve_need = c.ve.n
                    if m >= 2:
                        em.te(lambda e, v=ve_need: e.wait_ge(c.ve.sem, v))
                    for k in range(8):
                        mmf = lambda e, m=m, k=k, bank=bank, w_sb=w_sb: e.matmul(
                            ps[bank][:, :], w_sb[:, k * 2048 + m * 128: k * 2048 + m * 128 + 128],
                            xn[:, k * 512:(k + 1) * 512], start=(k == 0), stop=(k == 7))
                        if k == 7:
                            em.te(lambda e, mmf=mmf: mmf(e).then_inc(c.te.sem, 1))
                        else:
                            em.te(mmf)
                    te_mm = c.te.inc()
                    em.se(lambda e, v=te_mm: e.wait_ge(c.te.sem, v))
                    em.se(lambda e, v=c.ve.n: e.wait_ge(c.ve.sem, v))
                    em.se(lambda e, m=m, bank=bank, b_col=b_col: e.activation(
                        tmp1[:], ps[bank][:, :], AF.Relu, bias=b_col[:, m:m + 1]))
                    em.se(lambda e, m=m, bank=bank, nb_col=nb_col: e.activation(
                        tmp2[:], ps[bank][:, :], AF.Relu, bias=nb_col[:, m:m + 1], scale=-1.0))
                    em.se(lambda e: e.activation(tmp2[:], tmp2[:], AF.Exp, scale=-1.0).then_inc(c.se.sem, 1))
                    se_phi = c.se.inc()
                    em.ve(lambda e, v=se_phi: e.wait_ge(c.se.sem, v))
                    em.ve(lambda e, m=m, out_sb=out_sb: e.tensor_tensor(
                        out_sb[:, m * 512:(m + 1) * 512], tmp1[:], tmp2[:], ALU.add))
                    em.ve(lambda e: e.engine_nop().then_inc(c.ve.sem, 1))
                    c.ve.inc()
            proj_qk()
        cs_bv = cs_w_done
        em.te(lambda e: e.wait_ge(c.cs.sem, cs_bv))
        em.te(lambda e, v=c.ve.n: e.wait_ge(c.ve.sem, v))
        for tt in range(4):
            for n in range(4):
                g = tt * 4 + n
                bank = 3 + (g % 2)
                if g >= 2:
                    em.te(lambda e, v=c.ve.n: e.wait_ge(c.ve.sem, v))
                for k in range(8):
                    em.te(lambda e, tt=tt, n=n, k=k, bank=bank: e.matmul(
                        ps[bank][:, :], xn[:, k * 512 + tt * 128: k * 512 + tt * 128 + 128],
                        wv[:, k * 2048 + n * 512: k * 2048 + n * 512 + 512],
                        start=(k == 0), stop=False))
                em.te(lambda e, n=n, bank=bank: e.matmul(
                    ps[bank][:, :], sb["ones_row_bf"][0:1, 0:128],
                    bv_row[0:1, n * 512:(n + 1) * 512],
                    start=False, stop=True).then_inc(c.te.sem, 1))
                te_v = c.te.inc()
                em.ve(lambda e, v=te_v: e.wait_ge(c.te.sem, v))
                em.ve(lambda e, tt=tt, n=n, bank=bank: e.tensor_copy(
                    vt[:, tt * 2048 + n * 512: tt * 2048 + n * 512 + 512], ps[bank][:, :]))
                em.ve(lambda e: e.engine_nop().then_inc(c.ve.sem, 1))
                c.ve.inc()
        ve_all = c.ve.n
        em.sy(lambda e, v=ve_all: e.wait_ge(c.ve.sem, v))
        em.sy(lambda e, t0=t0: e.dma_start(
            out=qr[:, :, t0:t0 + 512],
            in_=_r(qT[:], "p (m t) -> p m t", t=512)).then_inc(c.st.sem, 16))
        c.st.inc(16)
        em.sy(lambda e, t0=t0: e.dma_start(
            out=kr[:, :, t0:t0 + 512],
            in_=_r(kT[:], "p (m t) -> p m t", t=512)).then_inc(c.st.sem, 16))
        c.st.inc(16)
        em.sy(lambda e, t0=t0: e.dma_start(
            out=_r(dram["aa_v"][t0:t0 + 512, :], "(c p) n -> p c n", p=128),
            in_=_r(vt[:], "p (c n) -> p c n", n=2048)).then_inc(c.st.sem, 16))
        c.st.inc(16)
        em.sy(lambda e, v=c.te.n: e.wait_ge(c.te.sem, v))

    for ci in range(NCHUNK):
        chunk(ci)
    c.barrier()



def emit_attn_a_p2(c, dram, sb, st):
    em, nc = c.em, c.nc
    E = st.enter_context
    S = E(nc.sbuf_tensor("a2_S", [128, 16 * 2048], BF16))
    z = E(nc.sbuf_tensor("a2_z", [128, 16], F32))
    zbf = E(nc.sbuf_tensor("a2_zbf", [128, 16], BF16))
    qT = E(nc.sbuf_tensor("a2_qT", [128, 16 * 512], BF16))
    kT = E(nc.sbuf_tensor("a2_kT", [128, 16 * 512], BF16))
    vt = E(nc.sbuf_tensor("a2_vt", [128, 4 * 2048], BF16))
    ktok = E(nc.sbuf_tensor("a2_ktok", [128, 4 * 2048], BF16))
    msc = E(nc.sbuf_tensor("a2_msc", [128, 4 * 512], BF16))
    yT = E(nc.sbuf_tensor("a2_yT", [128, 16 * 512], BF16))
    den = E(nc.sbuf_tensor("a2_den", [1, 512], F32))
    rdB = E(nc.sbuf_tensor("a2_rdB", [128, 512], F32))

    em.gp(lambda e: e.memset(S[:], 0.0))
    em.gp(lambda e: e.memset(z[:], 0.0))
    em.gp(lambda e: e.memset(zbf[:], 0.0))
    em.gp(lambda e: e.engine_nop().then_inc(c.ge.sem, 1))
    ge_init = c.ge.inc()
    cs_w = c.cs.n
    em.te(lambda e: e.wait_ge(c.ge.sem, ge_init))
    em.te(lambda e: e.wait_ge(c.cs.sem, cs_w))
    em.ve(lambda e: e.wait_ge(c.ge.sem, ge_init))

    qr = _r(dram["aa_q"], "(m p) t -> p m t", p=128)
    kr = _r(dram["aa_k"], "(m p) t -> p m t", p=128)
    yr = _r(dram["aa_y"], "(m p) t -> p m t", p=128)
    ps, pt = c.ps, c.pt

    def chunk(ci):
        t0 = ci * CHUNK
        em.ve(lambda e, v=c.st.n: e.wait_ge(c.st.sem, v))
        em.sy(lambda e: e.dma_start(out=_r(qT[:], "p (m t) -> p m t", t=512),
                                    in_=qr[:, :, t0:t0 + 512]).then_inc(c.ld.sem, 16))
        c.ld.inc(16)
        em.sy(lambda e: e.dma_start(out=_r(kT[:], "p (m t) -> p m t", t=512),
                                    in_=kr[:, :, t0:t0 + 512]).then_inc(c.ld.sem, 16))
        c.ld.inc(16)
        em.sy(lambda e: e.dma_start(
            out=_r(vt[:], "p (c n) -> p c n", n=2048),
            in_=_r(dram["aa_v"][t0:t0 + 512, :], "(c p) n -> p c n", p=128)).then_inc(c.ld.sem, 16))
        c.ld.inc(16)
        ldx = c.ld.n
        em.te(lambda e, v=ldx: e.wait_ge(c.ld.sem, v))

        for m in range(4):
            bank = m % 2
            if ci > 0 or m >= 2:
                em.te(lambda e, v=c.ve.n: e.wait_ge(c.ve.sem, v))
            for i in range(16):
                mmf = lambda e, m=m, i=i, bank=bank: e.matmul(
                    ps[bank][:, :], kT[:, i * 512 + m * 128: i * 512 + m * 128 + 128],
                    qT[:, i * 512:(i + 1) * 512], start=(i == 0), stop=(i == 15))
                if i == 15:
                    em.te(lambda e, mmf=mmf: mmf(e).then_inc(c.te.sem, 1))
                else:
                    em.te(mmf)
            te_s = c.te.inc()
            em.ve(lambda e, v=te_s: e.wait_ge(c.te.sem, v))
            em.ve(lambda e, m=m, bank=bank: e.tensor_tensor(
                msc[:, m * 512:(m + 1) * 512], ps[bank][:, :],
                sb["mask_a"][:, m * 512:(m + 1) * 512], ALU.mult))
            em.ve(lambda e: e.engine_nop().then_inc(c.ve.sem, 1))
            c.ve.inc()
        ve_msc = c.ve.n

        for tt in range(4):
            ptb = pt[tt % 2]
            if tt >= 2:
                em.te(lambda e, v=c.ve.n: e.wait_ge(c.ve.sem, v))
            for half in range(2):
                if half == 1:
                    em.te(lambda e, v=c.ve.n: e.wait_ge(c.ve.sem, v))
                for i8 in range(8):
                    i = half * 8 + i8
                    trf = lambda e, tt=tt, i=i, i8=i8, ptb=ptb: e.transpose(
                        ptb[:, i8 * 128:(i8 + 1) * 128],
                        kT[:, i * 512 + tt * 128: i * 512 + tt * 128 + 128], sb["ident"][:])
                    if i8 == 7:
                        em.te(lambda e, trf=trf: trf(e).then_inc(c.te.sem, 1))
                    else:
                        em.te(trf)
                te_tr = c.te.inc()
                em.ve(lambda e, v=te_tr: e.wait_ge(c.te.sem, v))
                em.ve(lambda e, tt=tt, half=half, ptb=ptb: e.tensor_copy(
                    ktok[:, tt * 2048 + half * 1024: tt * 2048 + half * 1024 + 1024], ptb[:, :]))
                em.ve(lambda e: e.engine_nop().then_inc(c.ve.sem, 1))
                c.ve.inc()

        em.te(lambda e, v=ve_msc: e.wait_ge(c.ve.sem, v))
        for m in range(4):
            em.te(lambda e, m=m: e.matmul(
                ps[5][0:1, :], sb["ones_col_bf"][:], msc[:, m * 512:(m + 1) * 512],
                start=(m == 0), stop=False))
        for i in range(16):
            mmf = lambda e, i=i: e.matmul(
                ps[5][0:1, :], zbf[:, i:i + 1], qT[:, i * 512:(i + 1) * 512],
                start=False, stop=(i == 15))
            if i == 15:
                em.te(lambda e, mmf=mmf: mmf(e).then_inc(c.te.sem, 1))
            else:
                em.te(mmf)
        te_den = c.te.inc()
        em.ve(lambda e, v=te_den: e.wait_ge(c.te.sem, v))
        em.ve(lambda e: e.tensor_scalar(den[:], ps[5][0:1, :], 1e-6, None, ALU.add))
        em.ve(lambda e: e.tensor_scalar(sb["junk"][:], sb["junk"][:], 1.0, None, ALU.mult))
        em.ve(lambda e: e.reciprocal(den[:], den[:]))
        em.ve(lambda e: e.engine_nop().then_inc(c.ve.sem, 1))
        ve_den = c.ve.inc()
        em.te(lambda e, v=ve_den: e.wait_ge(c.ve.sem, v))
        em.te(lambda e: e.matmul(ps[4][:, :], sb["ones_row_f32"][:], den[:],
                                 start=True, stop=True).then_inc(c.te.sem, 1))
        te_bc = c.te.inc()
        em.ve(lambda e, v=te_bc: e.wait_ge(c.te.sem, v))
        em.ve(lambda e: e.tensor_copy(rdB[:], ps[4][:, :]))
        em.ve(lambda e: e.engine_nop().then_inc(c.ve.sem, 1))
        ve_rdb = c.ve.inc()

        em.te(lambda e, v=ve_rdb: e.wait_ge(c.ve.sem, v))
        for j in range(16):
            bank = 2 + (j % 2)
            if j >= 2:
                em.te(lambda e, v=c.ve.n: e.wait_ge(c.ve.sem, v))
            first = True
            if ci > 0:
                for i in range(16):
                    em.te(lambda e, i=i, j=j, bank=bank, first=first and i == 0: e.matmul(
                        ps[bank][:, :], S[:, i * 2048 + j * 128: i * 2048 + j * 128 + 128],
                        qT[:, i * 512:(i + 1) * 512], start=(i == 0), stop=False))
                first = False
            for m in range(4):
                last = (m == 3)
                mmf = lambda e, m=m, j=j, bank=bank, first=(first and m == 0), last=last: e.matmul(
                    ps[bank][:, :], vt[:, m * 2048 + j * 128: m * 2048 + j * 128 + 128],
                    msc[:, m * 512:(m + 1) * 512], start=first, stop=last)
                if last:
                    em.te(lambda e, mmf=mmf: mmf(e).then_inc(c.te.sem, 1))
                else:
                    em.te(mmf)
            te_num = c.te.inc()
            em.ve(lambda e, v=te_num: e.wait_ge(c.te.sem, v))
            em.ve(lambda e, j=j, bank=bank: e.tensor_tensor(
                yT[:, j * 512:(j + 1) * 512], ps[bank][:, :], rdB[:], ALU.mult))
            em.ve(lambda e: e.engine_nop().then_inc(c.ve.sem, 1))
            c.ve.inc()

        for i in range(16):
            for n in range(4):
                g = i * 4 + n
                bank = (g % 2)
                if g >= 2:
                    em.te(lambda e, v=c.ve.n: e.wait_ge(c.ve.sem, v))
                for tt in range(4):
                    last = (tt == 3)
                    mmf = lambda e, i=i, n=n, tt=tt, bank=bank, last=last: e.matmul(
                        ps[bank][:, :], ktok[:, tt * 2048 + i * 128: tt * 2048 + i * 128 + 128],
                        vt[:, tt * 2048 + n * 512: tt * 2048 + n * 512 + 512],
                        start=(tt == 0), stop=last)
                    if last:
                        em.te(lambda e, mmf=mmf: mmf(e).then_inc(c.te.sem, 1))
                    else:
                        em.te(mmf)
                te_sd = c.te.inc()
                em.ve(lambda e, v=te_sd: e.wait_ge(c.te.sem, v))
                em.ve(lambda e, i=i, n=n, bank=bank: e.tensor_tensor(
                    S[:, i * 2048 + n * 512: i * 2048 + n * 512 + 512],
                    ps[bank][:, :],
                    S[:, i * 2048 + n * 512: i * 2048 + n * 512 + 512], ALU.add))
                em.ve(lambda e: e.engine_nop().then_inc(c.ve.sem, 1))
                c.ve.inc()
        em.te(lambda e, v=c.ve.n: e.wait_ge(c.ve.sem, v))
        for i in range(16):
            for tt in range(4):
                last = (tt == 3)
                mmf = lambda e, i=i, tt=tt, last=last: e.matmul(
                    ps[5][:, i:i + 1], ktok[:, tt * 2048 + i * 128: tt * 2048 + i * 128 + 128],
                    sb["ones_col_bf"][:], start=(tt == 0), stop=last)
                if last and i == 15:
                    em.te(lambda e, mmf=mmf: mmf(e).then_inc(c.te.sem, 1))
                else:
                    em.te(mmf)
        te_z = c.te.inc()
        em.ve(lambda e, v=te_z: e.wait_ge(c.te.sem, v))
        em.ve(lambda e: e.tensor_tensor(zbf[:], ps[5][:, 0:16], z[:], ALU.add))
        em.ve(lambda e: e.tensor_tensor(z[:], ps[5][:, 0:16], z[:], ALU.add))
        em.ve(lambda e: e.engine_nop().then_inc(c.ve.sem, 1))
        c.ve.inc()

        ve_yT = c.ve.n
        em.sy(lambda e, v=ve_yT: e.wait_ge(c.ve.sem, v))
        em.sy(lambda e, t0=t0: e.dma_start(
            out=yr[:, :, t0:t0 + 512],
            in_=_r(yT[:], "p (m t) -> p m t", t=512)).then_inc(c.st.sem, 16))
        c.st.inc(16)
        em.sy(lambda e, v=c.te.n: e.wait_ge(c.te.sem, v))
        em.sy(lambda e, v=c.ve.n: e.wait_ge(c.ve.sem, v))

    for ci in range(NCHUNK):
        chunk(ci)
    if "dbg_z" in dram:
        em.sy(lambda e, v=c.ve.n: e.wait_ge(c.ve.sem, v))
        em.sy(lambda e: e.dma_start(out=dram["dbg_z"], in_=z[:]).then_inc(c.st.sem, 16))
        c.st.inc(16)
        em.sy(lambda e: e.dma_start(out=dram["dbg_den"], in_=den[:]).then_inc(c.st.sem, 16))
        c.st.inc(16)
        em.sy(lambda e: e.dma_start(out=dram["dbg_rdB"], in_=rdB[:]).then_inc(c.st.sem, 16))
        c.st.inc(16)
    c.barrier()



def emit_attn_a_p3(c, dram, sb, st):
    em, nc = c.em, c.nc
    E = st.enter_context
    yT = E(nc.sbuf_tensor("a3_yT", [128, 16 * 512], BF16))
    xt = E(nc.sbuf_tensor("a3_xt", [128, 8 * 512], F32))
    wo = E(nc.sbuf_tensor("a3_wo", [128, 16 * 1024], BF16))
    bo_row = E(nc.sbuf_tensor("a3_bo_row", [1, 1024], BF16))

    em.gp(lambda e: e.dma_start(
        out=_r(wo[:], "p (k n) -> p k n", n=1024),
        in_=_r(dram["aa_wo"], "(k p) n -> p k n", p=128)).then_inc(c.cs.sem, 16))
    c.cs.inc(16)
    em.gp(lambda e: e.dma_start(out=bo_row[:], in_=dram["aa_bo_row"]).then_inc(c.cs.sem, 16))
    c.cs.inc(16)
    cs_w = c.cs.n
    em.te(lambda e: e.wait_ge(c.cs.sem, cs_w))

    yr = _r(dram["aa_y"], "(m p) t -> p m t", p=128)
    x0r = _r(dram["xT_in"], "(k p) t -> p k t", p=128)
    x1r = _r(dram["x1T"], "(k p) t -> p k t", p=128)
    ps = c.ps

    def chunk(ci):
        t0 = ci * CHUNK
        em.ve(lambda e, v=c.st.n: e.wait_ge(c.st.sem, v))
        em.sy(lambda e, t0=t0: e.dma_start(out=_r(yT[:], "p (m t) -> p m t", t=512),
                                           in_=yr[:, :, t0:t0 + 512]).then_inc(c.ld.sem, 16))
        c.ld.inc(16)
        em.sy(lambda e, t0=t0: e.dma_start(out=_r(xt[:], "p (k t) -> p k t", t=512),
                                           in_=x0r[:, :, t0:t0 + 512]).then_inc(c.ld.sem, 16))
        ldx = c.ld.inc(16)
        em.te(lambda e, v=ldx: e.wait_ge(c.ld.sem, v))
        for m in range(8):
            bank = m % 2
            if ci > 0 or m >= 2:
                em.te(lambda e, v=c.ve.n: e.wait_ge(c.ve.sem, v))
            for i in range(16):
                em.te(lambda e, m=m, i=i, bank=bank: e.matmul(
                    ps[bank][:, :], wo[:, i * 1024 + m * 128: i * 1024 + m * 128 + 128],
                    yT[:, i * 512:(i + 1) * 512], start=(i == 0), stop=False))
            em.te(lambda e, m=m, bank=bank: e.matmul(
                ps[bank][:, :], bo_row[0:1, m * 128:(m + 1) * 128],
                sb["ones_row_bf"][:], start=False, stop=True).then_inc(c.te.sem, 1))
            te_o = c.te.inc()
            em.ve(lambda e, v=te_o: e.wait_ge(c.te.sem, v))
            em.ve(lambda e, m=m, bank=bank: e.tensor_tensor(
                xt[:, m * 512:(m + 1) * 512], ps[bank][:, :],
                xt[:, m * 512:(m + 1) * 512], ALU.add))
            em.ve(lambda e: e.engine_nop().then_inc(c.ve.sem, 1))
            c.ve.inc()
        ve_x1 = c.ve.n
        em.sy(lambda e, v=ve_x1: e.wait_ge(c.ve.sem, v))
        em.sy(lambda e, t0=t0: e.dma_start(
            out=x1r[:, :, t0:t0 + 512],
            in_=_r(xt[:], "p (k t) -> p k t", t=512)).then_inc(c.st.sem, 16))
        c.st.inc(16)
        em.sy(lambda e, v=c.te.n: e.wait_ge(c.te.sem, v))

    for ci in range(NCHUNK):
        chunk(ci)
    c.barrier()




def emit_ln_to(c, sb, xt, xn, aux, ldx, ld_extra=0):
    em, ps = c.em, c.ps
    em.te(lambda e, v=max(ldx, ld_extra): e.wait_ge(c.ld.sem, v))
    for i in range(8):
        em.te(lambda e, i=i: e.matmul(ps[0][0:1, :], sb["ones_col_f32"][:],
                                      xt[:, i * 512:(i + 1) * 512],
                                      start=(i == 0), stop=(i == 7)))
    em.se(lambda e, v=ldx: e.wait_ge(c.ld.sem, v))
    te0 = c.te.n
    for i in range(8):
        buf = aux["tmp1"] if i % 2 == 0 else aux["tmp2"]
        if i >= 2:
            em.se(lambda e, v=te0 + (i - 1): e.wait_ge(c.te.sem, v))
        em.se(lambda e, i=i, buf=buf: e.activation(
            buf[:], xt[:, i * 512:(i + 1) * 512], AF.Square).then_inc(c.se.sem, 1))
        se_sq = c.se.inc()
        em.te(lambda e, v=se_sq: e.wait_ge(c.se.sem, v))
        em.te(lambda e, i=i, buf=buf: e.matmul(
            ps[0][32:33, :], sb["ones_col_f32"][:], buf[:],
            start=(i == 0), stop=(i == 7)).then_inc(c.te.sem, 1))
        c.te.inc()
    te_stat = c.te.n
    em.ve(lambda e, v=te_stat: e.wait_ge(c.te.sem, v))
    em.ve(lambda e: e.tensor_scalar(aux["st_m"][:], ps[0][0:1, :], 1.0 / D, None, ALU.mult))
    em.ve(lambda e: e.tensor_scalar(aux["st_ms"][:], ps[0][32:33, :], 1.0 / D, None, ALU.mult))
    em.ve(lambda e: e.tensor_tensor(aux["st_r"][:], aux["st_m"][:], aux["st_m"][:], ALU.mult))
    em.ve(lambda e: e.tensor_tensor(aux["st_r"][:], aux["st_ms"][:], aux["st_r"][:], ALU.subtract))
    em.ve(lambda e: e.engine_nop().then_inc(c.ve.sem, 1))
    ve_var = c.ve.inc()
    em.se(lambda e, v=ve_var: e.wait_ge(c.ve.sem, v))
    em.se(lambda e: e.activation(aux["st_r"][:], aux["st_r"][:], AF.Ln, bias=1e-5))
    em.se(lambda e: e.activation(aux["st_r"][:], aux["st_r"][:], AF.Exp, scale=-0.5))
    em.se(lambda e: e.copy(aux["st_mr"][:], aux["st_m"][:]).then_inc(c.se.sem, 1))
    se_rstd = c.se.inc()
    em.ve(lambda e, v=se_rstd: e.wait_ge(c.se.sem, v))
    em.ve(lambda e: e.tensor_tensor(aux["st_mr"][:], aux["st_mr"][:], aux["st_r"][:], ALU.mult))
    em.ve(lambda e: e.engine_nop().then_inc(c.ve.sem, 1))
    ve_stats = c.ve.inc()
    em.te(lambda e, v=ve_stats: e.wait_ge(c.ve.sem, v))
    em.te(lambda e: e.matmul(ps[1][:, :], sb["ones_row_f32"][:], aux["st_r"][:],
                             start=True, stop=True))
    em.te(lambda e: e.matmul(ps[2][:, :], sb["ones_row_f32"][:], aux["st_mr"][:],
                             start=True, stop=True).then_inc(c.te.sem, 1))
    te_bc = c.te.inc()
    em.ve(lambda e, v=te_bc: e.wait_ge(c.te.sem, v))
    em.ve(lambda e: e.tensor_copy(aux["rdB"][:], ps[1][:, :]))
    em.ve(lambda e: e.tensor_copy(aux["mrB"][:], ps[2][:, :]))
    for i in range(8):
        em.ve(lambda e, i=i: e.tensor_tensor(
            aux["tmp1"][:], xt[:, i * 512:(i + 1) * 512], aux["rdB"][:], ALU.mult))
        em.ve(lambda e, i=i: e.tensor_tensor(
            xn[:, i * 512:(i + 1) * 512], aux["tmp1"][:], aux["mrB"][:], ALU.subtract))
    em.ve(lambda e: e.engine_nop().then_inc(c.ve.sem, 1))
    return c.ve.inc()


def _mk_ln_aux(c, st, pref):
    E = st.enter_context
    nc = c.nc
    return {
        "st_m": E(nc.sbuf_tensor(pref + "_m", [1, 512], F32)),
        "st_ms": E(nc.sbuf_tensor(pref + "_ms", [1, 512], F32)),
        "st_r": E(nc.sbuf_tensor(pref + "_r", [1, 512], F32)),
        "st_mr": E(nc.sbuf_tensor(pref + "_mr", [1, 512], F32)),
        "rdB": E(nc.sbuf_tensor(pref + "_rdB", [128, 512], F32)),
        "mrB": E(nc.sbuf_tensor(pref + "_mrB", [128, 512], F32)),
        "tmp1": E(nc.sbuf_tensor(pref + "_tmp1", [128, 512], F32)),
        "tmp2": E(nc.sbuf_tensor(pref + "_tmp2", [128, 512], F32)),
    }


def emit_moe_gate(c, dram, sb, st, x_in, tag=""):
    em, nc = c.em, c.nc
    E = st.enter_context
    xt = E(nc.sbuf_tensor(tag + "mg_xt", [128, 8 * 512], F32))
    xn = E(nc.sbuf_tensor(tag + "mg_xn", [128, 8 * 512], BF16))
    aux = _mk_ln_aux(c, st, tag + "mg")
    gw = E(nc.sbuf_tensor(tag + "mg_gw", [128, 8 * 32], BF16))
    pexp = [E(nc.sbuf_tensor(f"{tag}mg_p{tt}", [128, 32], F32)) for tt in range(4)]
    work = [E(nc.sbuf_tensor(f"{tag}mg_w{tt}", [128, 32], F32)) for tt in range(4)]
    tmp = [E(nc.sbuf_tensor(f"{tag}mg_t{tt}", [128, 32], F32)) for tt in range(4)]
    mcol = [E(nc.sbuf_tensor(f"{tag}mg_mc{tt}", [128, 1], F32)) for tt in range(4)]
    nmx = [E(nc.sbuf_tensor(f"{tag}mg_nm{tt}", [128, 1], F32)) for tt in range(4)]
    comb = E(nc.sbuf_tensor(tag + "mg_comb", [128, 4 * 32], F32))

    em.gp(lambda e: e.dma_start(
        out=_r(gw[:], "p (k n) -> p k n", n=32),
        in_=_r(dram["gate_wT"], "(k p) n -> p k n", p=128)).then_inc(c.cs.sem, 16))
    cs_w = c.cs.inc(16)
    em.te(lambda e: e.wait_ge(c.cs.sem, cs_w))

    xr = _r(dram[x_in], "(k p) t -> p k t", p=128)
    xnr = _r(dram["moe_xn"], "(k p) t -> p k t", p=128)
    ps = c.ps

    def chunk(ci):
        t0 = ci * CHUNK
        em.ve(lambda e, v=c.st.n: e.wait_ge(c.st.sem, v))
        em.sy(lambda e: e.dma_start(out=_r(xt[:], "p (k t) -> p k t", t=512),
                                    in_=xr[:, :, t0:t0 + 512]).then_inc(c.ld.sem, 16))
        ldx = c.ld.inc(16)
        ve_xn = emit_ln_to(c, sb, xt, xn, aux, ldx)
        em.sy(lambda e, v=ve_xn: e.wait_ge(c.ve.sem, v))
        em.sy(lambda e, t0=t0: e.dma_start(
            out=xnr[:, :, t0:t0 + 512],
            in_=_r(xn[:], "p (k t) -> p k t", t=512)).then_inc(c.st.sem, 16))
        c.st.inc(16)
        em.te(lambda e, v=ve_xn: e.wait_ge(c.ve.sem, v))
        for tt in range(4):
            bank = 2 + tt
            for k in range(8):
                mm = lambda e, tt=tt, k=k, bank=bank: e.matmul(
                    ps[bank][:, 0:32], xn[:, k * 512 + tt * 128: k * 512 + tt * 128 + 128],
                    gw[:, k * 32:(k + 1) * 32], start=(k == 0), stop=(k == 7))
                if k == 7:
                    em.te(lambda e, mm=mm: mm(e).then_inc(c.te.sem, 1))
                else:
                    em.te(mm)
            c.te.inc()
        te_lg = c.te.n
        for tt in range(4):
            em.ve(lambda e, v=te_lg - 3 + tt: e.wait_ge(c.te.sem, v))
            em.ve(lambda e, tt=tt: e.reduce_max(mcol[tt][:], ps[2 + tt][:, 0:32], AX.X))
        for tt in range(4):
            em.ve(lambda e, tt=tt: e.tensor_scalar(nmx[tt][:], mcol[tt][:], -1.0, None, ALU.mult))
        em.ve(lambda e: e.engine_nop().then_inc(c.ve.sem, 1))
        ve_mx = c.ve.inc()
        em.se(lambda e, v=ve_mx: e.wait_ge(c.ve.sem, v))
        for tt in range(4):
            em.se(lambda e, tt=tt: e.activation(pexp[tt][:], ps[2 + tt][:, 0:32],
                                                AF.Exp, bias=nmx[tt][:]))
        em.se(lambda e: e.copy(work[0][:], pexp[0][:]))
        em.se(lambda e: e.copy(work[1][:], pexp[1][:]))
        em.se(lambda e: e.copy(work[2][:], pexp[2][:]))
        em.se(lambda e: e.copy(work[3][:], pexp[3][:]).then_inc(c.se.sem, 1))
        se_p = c.se.inc()
        em.ve(lambda e, v=se_p: e.wait_ge(c.se.sem, v))
        for it in range(8):
            for tt in range(4):
                em.ve(lambda e, tt=tt: e.reduce_max(mcol[tt][:], work[tt][:], AX.X))
            if it < 7:
                for tt in range(4):
                    em.ve(lambda e, tt=tt: e.tensor_scalar(
                        tmp[tt][:], work[tt][:], mcol[tt][:], -1e30, ALU.is_ge, ALU.mult))
                for tt in range(4):
                    em.ve(lambda e, tt=tt: e.tensor_tensor(
                        work[tt][:], work[tt][:], tmp[tt][:], ALU.add))
        for tt in range(4):
            em.ve(lambda e, tt=tt: e.tensor_scalar(
                tmp[tt][:], pexp[tt][:], mcol[tt][:], None, ALU.is_ge))
        for tt in range(4):
            em.ve(lambda e, tt=tt: e.tensor_tensor(
                comb[:, tt * 32:(tt + 1) * 32], pexp[tt][:], tmp[tt][:], ALU.mult))
        for tt in range(4):
            em.ve(lambda e, tt=tt: e.reduce_sum(mcol[tt][:], comb[:, tt * 32:(tt + 1) * 32], AX.X))
        for tt in range(4):
            em.ve(lambda e, tt=tt: e.reciprocal(nmx[tt][:], mcol[tt][:]))
        for tt in range(4):
            em.ve(lambda e, tt=tt: e.tensor_scalar(
                comb[:, tt * 32:(tt + 1) * 32], comb[:, tt * 32:(tt + 1) * 32],
                nmx[tt][:], None, ALU.mult))
        em.ve(lambda e: e.engine_nop().then_inc(c.ve.sem, 1))
        ve_cb = c.ve.inc()
        em.sy(lambda e, v=ve_cb: e.wait_ge(c.ve.sem, v))
        em.sy(lambda e, t0=t0: e.dma_start(
            out=_r(dram["moe_comb"][t0:t0 + 512, :], "(tt p) e -> p tt e", p=128),
            in_=_r(comb[:], "p (tt e) -> p tt e", e=32)).then_inc(c.st.sem, 16))
        c.st.inc(16)
        em.sy(lambda e, v=c.te.n: e.wait_ge(c.te.sem, v))
        em.sy(lambda e, v=c.ve.n: e.wait_ge(c.ve.sem, v))

    for ci in range(NCHUNK):
        chunk(ci)
    c.barrier()


def emit_moe_exp(c, dram, sb, st, x_in, x_out, tag=""):
    em, nc = c.em, c.nc
    E = st.enter_context
    BLK = 2048
    acc = E(nc.sbuf_tensor(tag + "me_acc", [128, 16 * 1024], F32))
    xnb = E(nc.sbuf_tensor(tag + "me_xn", [128, 8 * 2048], BF16))
    wg = [E(nc.sbuf_tensor(f"{tag}me_wg{b}", [128, 8 * 1024], BF16)) for b in range(2)]
    wd = [E(nc.sbuf_tensor(f"{tag}me_wd{b}", [128, 4 * 1024], BF16)) for b in range(2)]
    combb = E(nc.sbuf_tensor(tag + "me_comb", [128, 16 * 32], F32))
    hT = E(nc.sbuf_tensor(tag + "me_hT", [128, 2 * 4 * 512], BF16))
    sg = E(nc.sbuf_tensor(tag + "me_sg", [128, 2 * 512], BF16))
    x1c = E(nc.sbuf_tensor(tag + "me_x1c", [128, 8 * 512], F32))
    x2c = E(nc.sbuf_tensor(tag + "me_x2c", [128, 8 * 512], F32))

    xr = _r(dram[x_in], "(k p) t -> p k t", p=128)
    xo = _r(dram[x_out], "(k p) t -> p k t", p=128)
    xnr = _r(dram["moe_xn"], "(k p) t -> p k t", p=128)
    ps = c.ps

    def block(bi):
        tb = bi * BLK
        em.sy(lambda e, v=c.ve.n: e.wait_ge(c.ve.sem, v))
        em.sy(lambda e, tb=tb: e.dma_start(
            out=_r(xnb[:], "p (k t) -> p k t", t=BLK),
            in_=xnr[:, :, tb:tb + BLK]).then_inc(c.ld.sem, 16))
        c.ld.inc(16)
        em.sy(lambda e, tb=tb: e.dma_start(
            out=_r(combb[:], "p (tt e) -> p tt e", e=32),
            in_=_r(dram["moe_comb"][tb:tb + BLK, :], "(tt p) e -> p tt e", p=128)).then_inc(c.ld.sem, 16))
        ld_blk = c.ld.inc(16)
        em.gp(lambda e, v=c.te.n: e.wait_ge(c.te.sem, v))
        em.gp(lambda e: e.memset(acc[:], 0.0).then_inc(c.ge.sem, 1))
        ge_acc = c.ge.inc()
        em.te(lambda e, v=ld_blk: e.wait_ge(c.ld.sem, v))
        em.ve(lambda e, v=ld_blk: e.wait_ge(c.ld.sem, v))
        em.ve(lambda e, v=ge_acc: e.wait_ge(c.ge.sem, v))

        te_expert_done = {}
        evac_h = {}
        evac_pr = {}
        evac_dn = {}
        te_dn_last = {}
        te_wd_last = {}
        dn_seq = []
        for ei in range(NE):
            b = ei % 2
            if ei >= 2:
                wv_ = te_wd_last.get((bi, ei - 2), te_expert_done[ei - 2])
                em.sy(lambda e, v=max(wv_, te_expert_done[ei - 2]): e.wait_ge(c.te.sem, v))
            em.sy(lambda e, ei=ei, b=b: e.dma_start(
                out=_r(wg[b][:], "p (k n) -> p k n", n=1024),
                in_=_r(dram["guw_bf"][ei], "(k p) n -> p k n", p=128)).then_inc(c.ld.sem, 16))
            c.ld.inc(16)
            em.sy(lambda e, ei=ei, b=b: e.dma_start(
                out=_r(wd[b][:], "p (k n) -> p k n", n=1024),
                in_=_r(dram["dw_bf"][ei], "(k p) n -> p k n", p=128)).then_inc(c.ld.sem, 16))
            ld_w = c.ld.inc(16)
            em.te(lambda e, v=ld_w: e.wait_ge(c.ld.sem, v))

            def emit_down(gg):
                eig, g4g = divmod(gg, 4)
                eig = eig % NE
                bw = eig % 2
                hbuf = gg % 2
                em.te(lambda e, v=evac_h[gg]: e.wait_ge(c.ve.sem, v))
                for grp8 in range(8):
                    tt4, nh = divmod(grp8, 2)
                    bank = 4 + grp8 % 2
                    key = (gg, grp8)
                    prev = (gg, grp8 - 2) if grp8 >= 2 else (gg - 1, 6 + grp8)
                    if prev in evac_dn:
                        em.te(lambda e, v=evac_dn[prev]: e.wait_ge(c.ve.sem, v))
                    for k in range(4):
                        mm = lambda e, tt4=tt4, nh=nh, k=k, bank=bank, bw=bw, hbuf=hbuf: e.matmul(
                            ps[bank][:, :],
                            hT[:, hbuf * 2048 + k * 512 + tt4 * 128: hbuf * 2048 + k * 512 + tt4 * 128 + 128],
                            wd[bw][:, k * 1024 + nh * 512: k * 1024 + nh * 512 + 512],
                            start=(k == 0), stop=(k == 3))
                        if k == 3:
                            em.te(lambda e, mm=mm: mm(e).then_inc(c.te.sem, 1))
                        else:
                            em.te(mm)
                    te_dn = c.te.inc()
                    te_dn_last[gg] = te_dn
                    if g4g == 3:
                        te_wd_last[(gg // (4 * NE), eig)] = te_dn
                    tok = g4g * 4 + tt4
                    em.ve(lambda e, v=te_dn: e.wait_ge(c.te.sem, v))
                    em.ve(lambda e, tok=tok, nh=nh, bank=bank, eig=eig: e.scalar_tensor_tensor(
                        acc[:, tok * 1024 + nh * 512: tok * 1024 + nh * 512 + 512],
                        ps[bank][:, :], combb[:, tok * 32 + eig: tok * 32 + eig + 1],
                        acc[:, tok * 1024 + nh * 512: tok * 1024 + nh * 512 + 512],
                        ALU.mult, ALU.add).then_inc(c.ve.sem, 1))
                    evac_dn[key] = c.ve.inc()

            for g4 in range(4):
                gg = (bi * NE + ei) * 4 + g4
                tg = g4 * 512
                hbuf = gg % 2
                prev2 = gg - 2
                for pr in range(4):
                    bg = pr % 2
                    bu = 2 + pr % 2
                    key = (gg, pr)
                    prevk = (gg, pr - 2) if pr >= 2 else (gg - 1, 2 + pr)
                    if prevk in evac_pr:
                        em.te(lambda e, v=evac_pr[prevk]: e.wait_ge(c.ve.sem, v))

                    for k in range(8):
                        em.te(lambda e, pr=pr, k=k, bg=bg, b=b, tg=tg: e.matmul(
                            ps[bg][:, :], wg[b][:, k * 1024 + pr * 128: k * 1024 + pr * 128 + 128],
                            xnb[:, k * 2048 + tg: k * 2048 + tg + 512],
                            start=(k == 0), stop=(k == 7)))
                    for k in range(8):
                        mm = lambda e, pr=pr, k=k, bu=bu, b=b, tg=tg: e.matmul(
                            ps[bu][:, :], wg[b][:, k * 1024 + 512 + pr * 128: k * 1024 + 512 + pr * 128 + 128],
                            xnb[:, k * 2048 + tg: k * 2048 + tg + 512],
                            start=(k == 0), stop=(k == 7))
                        if k == 7:
                            em.te(lambda e, mm=mm: mm(e).then_inc(c.te.sem, 1))
                        else:
                            em.te(mm)
                    te_gu = c.te.inc()
                    sgb = pr % 2
                    em.se(lambda e, v=te_gu: e.wait_ge(c.te.sem, v))
                    if prevk in evac_pr:
                        em.se(lambda e, v=evac_pr[prevk]: e.wait_ge(c.ve.sem, v))
                    em.se(lambda e, bg=bg, sgb=sgb: e.activation(
                        sg[:, sgb * 512:(sgb + 1) * 512], ps[bg][:, :], AF.Silu).then_inc(c.se.sem, 1))
                    se_sg = c.se.inc()
                    em.ve(lambda e, v=se_sg: e.wait_ge(c.se.sem, v))
                    if pr == 0 and prev2 in te_dn_last:
                        em.ve(lambda e, v=te_dn_last[prev2]: e.wait_ge(c.te.sem, v))
                    em.ve(lambda e, pr=pr, bu=bu, sgb=sgb, hbuf=hbuf: e.tensor_tensor(
                        hT[:, hbuf * 2048 + pr * 512: hbuf * 2048 + (pr + 1) * 512],
                        ps[bu][:, :], sg[:, sgb * 512:(sgb + 1) * 512], ALU.mult).then_inc(c.ve.sem, 1))
                    evac_pr[key] = c.ve.inc()
                evac_h[gg] = evac_pr[(gg, 3)]
                dn_seq.append(gg)
                if len(dn_seq) >= 2:
                    emit_down(dn_seq.pop(0))
            te_expert_done[ei] = c.te.n
        while dn_seq:
            emit_down(dn_seq.pop(0))
        if True:
            pass
        for g4 in range(4):
            t0 = tb + g4 * 512
            em.sy(lambda e, t0=t0, g4=g4: e.dma_start(
                out=_r(x1c[:], "p (k t) -> p k t", t=512),
                in_=xr[:, :, t0:t0 + 512]).then_inc(c.ld.sem, 16))
            ld_x1 = c.ld.inc(16)
            em.te(lambda e, v=ld_x1: e.wait_ge(c.ld.sem, v))
            em.te(lambda e, v=c.ve.n: e.wait_ge(c.ve.sem, v))
            em.ve(lambda e, v=ld_x1: e.wait_ge(c.ld.sem, v))
            for j in range(8):
                bank = j % 2
                if g4 > 0 or j >= 2:
                    em.te(lambda e, v=c.ve.n: e.wait_ge(c.ve.sem, v))
                for tt4 in range(4):
                    tok = g4 * 4 + tt4
                    mm = lambda e, j=j, tok=tok, tt4=tt4, bank=bank: e.transpose(
                        ps[bank][:, tt4 * 128:(tt4 + 1) * 128],
                        acc[:, tok * 1024 + j * 128: tok * 1024 + j * 128 + 128],
                        sb["ident_f32"][:])
                    if tt4 == 3:
                        em.te(lambda e, mm=mm: mm(e).then_inc(c.te.sem, 1))
                    else:
                        em.te(mm)
                te_tr = c.te.inc()
                em.ve(lambda e, v=te_tr: e.wait_ge(c.te.sem, v))
                em.ve(lambda e, j=j, bank=bank: e.tensor_tensor(
                    x2c[:, j * 512:(j + 1) * 512], ps[bank][:, :],
                    x1c[:, j * 512:(j + 1) * 512], ALU.add))
                em.ve(lambda e: e.engine_nop().then_inc(c.ve.sem, 1))
                c.ve.inc()
            ve_x2 = c.ve.n
            em.sy(lambda e, v=ve_x2: e.wait_ge(c.ve.sem, v))
            em.sy(lambda e, t0=t0: e.dma_start(
                out=xo[:, :, t0:t0 + 512],
                in_=_r(x2c[:], "p (k t) -> p k t", t=512)).then_inc(c.st.sem, 16))
            c.st.inc(16)
            em.ve(lambda e, v=c.st.n: e.wait_ge(c.st.sem, v))
        em.sy(lambda e, v=c.te.n: e.wait_ge(c.te.sem, v))
        em.sy(lambda e, v=c.ve.n: e.wait_ge(c.ve.sem, v))

    for bi in range(T // BLK):
        block(bi)
    c.barrier()




def emit_attn_b_p1(c, dram, sb, st, x_in):
    em, nc = c.em, c.nc
    E = st.enter_context
    xt = E(nc.sbuf_tensor("b1_xt", [128, 8 * 512], F32))
    xln = E(nc.sbuf_tensor("b1_xln", [128, 8 * 512], F32))
    xln_bf = E(nc.sbuf_tensor("b1_xlnbf", [128, 8 * 512], BF16))
    aux = _mk_ln_aux(c, st, "b1")
    hist = E(nc.sbuf_tensor("b1_hist", [128, 8 * 3], F32))
    scr = E(nc.sbuf_tensor("b1_scr", [128, 8 * 515], F32))
    h0 = xt
    cpre = xt
    ext = scr
    qtok = scr
    lasb = scr
    h_bf = E(nc.sbuf_tensor("b1_hbf", [128, 8 * 512], BF16))
    qsq = E(nc.sbuf_tensor("b1_qsq", [128, 1024], F32))
    ssq = E(nc.sbuf_tensor("b1_ssq", [128, 16], F32))
    rn = E(nc.sbuf_tensor("b1_rn", [128, 32], F32))
    qv = E(nc.sbuf_tensor("b1_qv", [128, 4 * 1024], BF16))
    qhat = qv
    vsb = qv
    cg = E(nc.sbuf_tensor("b1_cg", [128, 8 * 512], BF16))
    qch = cg
    gsb = cg
    absb = E(nc.sbuf_tensor("b1_a", [128, 512], F32))
    btsb = E(nc.sbuf_tensor("b1_bt", [128, 4 * 16], F32))
    s1 = E(nc.sbuf_tensor("b1_s1", [64, 512], BF16))
    wq = E(nc.sbuf_tensor("b1_wq", [128, 8 * 1024], BF16))
    wk = E(nc.sbuf_tensor("b1_wk", [128, 8 * 1024], BF16))
    wv = E(nc.sbuf_tensor("b1_wv", [128, 8 * 1024], BF16))
    wa = E(nc.sbuf_tensor("b1_wa", [128, 8 * 1024], BF16))
    wb = E(nc.sbuf_tensor("b1_wb", [128, 8 * 16], BF16))
    wu1 = E(nc.sbuf_tensor("b1_wu1", [128, 8 * 64], BF16))
    wu2 = E(nc.sbuf_tensor("b1_wu2", [64, 1024], BF16))
    nw_col = E(nc.sbuf_tensor("b1_nw", [128, 8], F32))
    cw_col = E(nc.sbuf_tensor("b1_cw", [128, 32], F32))
    cb_col = E(nc.sbuf_tensor("b1_cb", [128, 8], F32))
    ab_col = E(nc.sbuf_tensor("b1_ab", [128, 8], F32))
    bb_row = E(nc.sbuf_tensor("b1_bb", [1, 16], BF16))
    u1b_col = E(nc.sbuf_tensor("b1_u1b", [64, 1], F32))
    u2b_col = E(nc.sbuf_tensor("b1_u2b", [128, 8], F32))

    for w_sb, name, n in ((wq, "ab_q_w", 1024), (wk, "ab_k_w", 1024),
                          (wv, "ab_v_w", 1024), (wa, "ab_alpha_w", 1024),
                          (wb, "ab_beta_w", 16), (wu1, "ab_u1_w", 64)):
        em.gp(lambda e, a=w_sb, b=dram[name], n=n: e.dma_start(
            out=_r(a[:], "p (k n) -> p k n", n=n),
            in_=_r(b, "(k p) n -> p k n", p=128)).then_inc(c.cs.sem, 16))
        c.cs.inc(16)
    em.gp(lambda e: e.dma_start(out=wu2[:], in_=dram["ab_u2_w"]).then_inc(c.cs.sem, 16))
    c.cs.inc(16)
    em.gp(lambda e: e.dma_start(out=bb_row[:], in_=_r(dram["ab_beta_b"], "a b -> a b")).then_inc(c.cs.sem, 16))
    c.cs.inc(16)
    for d_sb, name in ((nw_col, "ab_norm_in_w"), (cw_col, "ab_conv_w"),
                       (cb_col, "ab_conv_b"), (ab_col, "ab_alpha_b"),
                       (u1b_col, "ab_u1_b"), (u2b_col, "ab_u2_b")):
        em.sy(lambda e, a=d_sb, b=dram[name]: e.dma_start(out=a[:], in_=b).then_inc(c.ld.sem, 16))
        c.ld.inc(16)
    em.gp(lambda e: e.memset(hist[:], 0.0).then_inc(c.ge.sem, 1))
    ge_h = c.ge.inc()
    cs_w = c.cs.n
    ld_w = c.ld.n
    em.te(lambda e: e.wait_ge(c.cs.sem, cs_w))
    em.se(lambda e: e.wait_ge(c.ld.sem, ld_w))
    em.ve(lambda e: e.wait_ge(c.ge.sem, ge_h))
    em.ve(lambda e: e.wait_ge(c.ld.sem, ld_w))

    xr = _r(dram[x_in], "(k p) t -> p k t", p=128)
    xlnr = _r(dram["ab_xln"], "(k p) t -> p k t", p=128)
    qhr = _r(dram["ab_q"], "(k p) t -> p k t", p=128)
    khr = _r(dram["ab_k"], "(k p) t -> p k t", p=128)
    lar = _r(dram["ab_la"], "(k p) t -> p k t", p=128)
    gr = _r(dram["ab_gate"], "(k p) t -> p k t", p=128)
    ps, pt = c.ps, c.pt

    def chunk(ci):
        t0 = ci * CHUNK
        em.ve(lambda e, v=c.st.n: e.wait_ge(c.st.sem, v))
        em.sy(lambda e: e.dma_start(out=_r(xt[:], "p (k t) -> p k t", t=512),
                                    in_=xr[:, :, t0:t0 + 512]).then_inc(c.ld.sem, 16))
        ldx = c.ld.inc(16)
        ve_xln = emit_ln_to(c, sb, xt, xln, aux, ldx)
        for i in range(8):
            em.ve(lambda e, i=i: e.tensor_copy(
                xln_bf[:, i * 512:(i + 1) * 512], xln[:, i * 512:(i + 1) * 512]))
        em.ve(lambda e: e.engine_nop().then_inc(c.ve.sem, 1))
        ve_xbf = c.ve.inc()
        em.sy(lambda e, v=ve_xbf: e.wait_ge(c.ve.sem, v))
        em.sy(lambda e, t0=t0: e.dma_start(
            out=xlnr[:, :, t0:t0 + 512],
            in_=_r(xln[:], "p (k t) -> p k t", t=512)).then_inc(c.st.sem, 16))
        c.st.inc(16)
        em.te(lambda e, v=ve_xln: e.wait_ge(c.ve.sem, v))
        te0 = c.te.n
        for i in range(8):
            buf = aux["tmp1"] if i % 2 == 0 else aux["tmp2"]
            em.se(lambda e, v=ve_xln: e.wait_ge(c.ve.sem, v)) if i == 0 else None
            if i >= 2:
                em.se(lambda e, v=te0 + (i - 1): e.wait_ge(c.te.sem, v))
            em.se(lambda e, i=i, buf=buf: e.activation(
                buf[:], xln[:, i * 512:(i + 1) * 512], AF.Square).then_inc(c.se.sem, 1))
            se_sq = c.se.inc()
            em.te(lambda e, v=se_sq: e.wait_ge(c.se.sem, v))
            em.te(lambda e, i=i, buf=buf: e.matmul(
                ps[0][0:1, :], sb["ones_col_f32"][:], buf[:],
                start=(i == 0), stop=(i == 7)).then_inc(c.te.sem, 1))
            c.te.inc()
        te_rms = c.te.n
        em.ve(lambda e, v=te_rms: e.wait_ge(c.te.sem, v))
        em.ve(lambda e: e.tensor_scalar(aux["st_ms"][:], ps[0][0:1, :], 1.0 / D, None, ALU.mult))
        em.ve(lambda e: e.engine_nop().then_inc(c.ve.sem, 1))
        ve_ms = c.ve.inc()
        em.se(lambda e, v=ve_ms: e.wait_ge(c.ve.sem, v))
        em.se(lambda e: e.activation(aux["st_r"][:], aux["st_ms"][:], AF.Ln, bias=1e-6))
        em.se(lambda e: e.activation(aux["st_r"][:], aux["st_r"][:], AF.Exp, scale=-0.5).then_inc(c.se.sem, 1))
        se_r = c.se.inc()
        em.te(lambda e, v=se_r: e.wait_ge(c.se.sem, v))
        em.te(lambda e: e.matmul(ps[1][:, :], sb["ones_row_f32"][:], aux["st_r"][:],
                                 start=True, stop=True).then_inc(c.te.sem, 1))
        te_bc = c.te.inc()
        em.ve(lambda e, v=te_bc: e.wait_ge(c.te.sem, v))
        em.ve(lambda e: e.tensor_copy(aux["rdB"][:], ps[1][:, :]))
        em.ve(lambda e: e.tensor_scalar(sb["junk"][:], sb["junk"][:], 1.0, None, ALU.mult))
        for i in range(8):
            em.ve(lambda e, i=i: e.tensor_tensor(
                aux["tmp1"][:], xln[:, i * 512:(i + 1) * 512], aux["rdB"][:], ALU.mult))
            em.ve(lambda e, i=i: e.tensor_scalar(
                h0[:, i * 512:(i + 1) * 512], aux["tmp1"][:], nw_col[:, i:i + 1], None, ALU.mult))
        for i in range(8):
            em.ve(lambda e, i=i: e.tensor_copy(ext[:, i * 515: i * 515 + 3],
                                               hist[:, i * 3:(i + 1) * 3]))
            em.ve(lambda e, i=i: e.tensor_copy(ext[:, i * 515 + 3: i * 515 + 515],
                                               h0[:, i * 512:(i + 1) * 512]))
        for i in range(8):
            em.ve(lambda e, i=i: e.tensor_scalar(
                cpre[:, i * 512:(i + 1) * 512], ext[:, i * 515: i * 515 + 512],
                cw_col[:, i * 4 + 0: i * 4 + 1], None, ALU.mult))
        for j in range(1, 4):
            for i in range(8):
                em.ve(lambda e, i=i, j=j: e.scalar_tensor_tensor(
                    cpre[:, i * 512:(i + 1) * 512], ext[:, i * 515 + j: i * 515 + j + 512],
                    cw_col[:, i * 4 + j: i * 4 + j + 1],
                    cpre[:, i * 512:(i + 1) * 512], ALU.mult, ALU.add))
        for i in range(8):
            em.ve(lambda e, i=i: e.tensor_copy(hist[:, i * 3:(i + 1) * 3],
                                               ext[:, i * 515 + 512: i * 515 + 515]))
        em.ve(lambda e: e.engine_nop().then_inc(c.ve.sem, 1))
        ve_cpre = c.ve.inc()
        em.se(lambda e, v=ve_cpre: e.wait_ge(c.ve.sem, v))
        for i in range(8):
            em.se(lambda e, i=i: e.activation(
                h_bf[:, i * 512:(i + 1) * 512], cpre[:, i * 512:(i + 1) * 512],
                AF.Silu, bias=cb_col[:, i:i + 1]).then_inc(c.se.sem, 1))
            c.se.inc()
        se_h = c.se.n
        em.te(lambda e, v=se_h: e.wait_ge(c.se.sem, v))

        for (w_sb, out_r) in ((wq, qhr), (wk, khr)):
            def qk(w_sb=w_sb, out_r=out_r):
                for tt in range(4):
                    for nh in range(2):
                        bank = (tt * 2 + nh) % 2
                        em.te(lambda e, v=c.ve.n: e.wait_ge(c.ve.sem, v))
                        for k in range(8):
                            mm = lambda e, tt=tt, nh=nh, k=k, bank=bank, w_sb=w_sb: e.matmul(
                                ps[bank][:, :], h_bf[:, k * 512 + tt * 128: k * 512 + tt * 128 + 128],
                                w_sb[:, k * 1024 + nh * 512: k * 1024 + nh * 512 + 512],
                                start=(k == 0), stop=(k == 7))
                            if k == 7:
                                em.te(lambda e, mm=mm: mm(e).then_inc(c.te.sem, 1))
                            else:
                                em.te(mm)
                        te_q = c.te.inc()
                        em.ve(lambda e, v=te_q: e.wait_ge(c.te.sem, v))
                        em.ve(lambda e, tt=tt, nh=nh, bank=bank: e.tensor_copy(
                            qtok[:, tt * 1024 + nh * 512: tt * 1024 + nh * 512 + 512],
                            ps[bank][:, :]))
                        em.ve(lambda e: e.engine_nop().then_inc(c.ve.sem, 1))
                        c.ve.inc()
                for tt in range(4):
                    em.se(lambda e, v=c.ve.n: e.wait_ge(c.ve.sem, v))
                    em.se(lambda e, tt=tt: e.activation(
                        qsq[:, 0:512], qtok[:, tt * 1024: tt * 1024 + 512], AF.Square))
                    em.se(lambda e, tt=tt: e.activation(
                        qsq[:, 512:1024], qtok[:, tt * 1024 + 512: tt * 1024 + 1024],
                        AF.Square).then_inc(c.se.sem, 1))
                    se_sq2 = c.se.inc()
                    em.ve(lambda e, v=se_sq2: e.wait_ge(c.se.sem, v))
                    for hh in range(16):
                        em.ve(lambda e, hh=hh: e.reduce_sum(
                            ssq[:, hh:hh + 1], qsq[:, hh * 64:(hh + 1) * 64], AX.X))
                    em.ve(lambda e: e.engine_nop().then_inc(c.ve.sem, 1))
                    ve_ss = c.ve.inc()
                    em.se(lambda e, v=ve_ss: e.wait_ge(c.ve.sem, v))
                    em.se(lambda e: e.activation(rn[:, 0:16], ssq[:, 0:16], AF.Ln, bias=1e-30))
                    em.se(lambda e: e.activation(rn[:, 0:16], rn[:, 0:16], AF.Exp, scale=0.5).then_inc(c.se.sem, 1))
                    se_rn = c.se.inc()
                    em.ve(lambda e, v=se_rn: e.wait_ge(c.se.sem, v))
                    em.ve(lambda e: e.tensor_scalar(rn[:, 16:32], rn[:, 0:16], 1e-6, None, ALU.add))
                    em.ve(lambda e: e.tensor_scalar(sb["junk"][:], sb["junk"][:], 1.0, None, ALU.mult))
                    em.ve(lambda e: e.reciprocal(rn[:, 0:16], rn[:, 16:32]))
                    em.ve(lambda e: e.tensor_scalar(sb["junk"][:], sb["junk"][:], 1.0, None, ALU.mult))
                    for hh in range(16):
                        em.ve(lambda e, tt=tt, hh=hh: e.tensor_scalar(
                            qhat[:, tt * 1024 + hh * 64: tt * 1024 + (hh + 1) * 64],
                            qtok[:, tt * 1024 + hh * 64: tt * 1024 + (hh + 1) * 64],
                            rn[:, hh:hh + 1], None, ALU.mult))
                    em.ve(lambda e: e.engine_nop().then_inc(c.ve.sem, 1))
                    c.ve.inc()
                ve_qh = c.ve.n
                em.te(lambda e, v=ve_qh: e.wait_ge(c.ve.sem, v))
                em.ve(lambda e, v=c.st.n: e.wait_ge(c.st.sem, v))
                for tt in range(4):
                    ptb = pt[tt % 2]
                    if tt >= 2:
                        em.te(lambda e, v=c.ve.n: e.wait_ge(c.ve.sem, v))
                    for j in range(8):
                        mm = lambda e, tt=tt, j=j, ptb=ptb: e.transpose(
                            ptb[:, j * 128:(j + 1) * 128],
                            qhat[:, tt * 1024 + j * 128: tt * 1024 + j * 128 + 128],
                            sb["ident"][:])
                        if j == 7:
                            em.te(lambda e, mm=mm: mm(e).then_inc(c.te.sem, 1))
                        else:
                            em.te(mm)
                    te_tr = c.te.inc()
                    em.ve(lambda e, v=te_tr: e.wait_ge(c.te.sem, v))
                    for j in range(8):
                        em.ve(lambda e, tt=tt, j=j, ptb=ptb: e.tensor_copy(
                            qch[:, j * 512 + tt * 128: j * 512 + tt * 128 + 128],
                            ptb[:, j * 128:(j + 1) * 128]))
                    em.ve(lambda e: e.engine_nop().then_inc(c.ve.sem, 1))
                    c.ve.inc()
                ve_qch = c.ve.n
                em.sy(lambda e, v=ve_qch: e.wait_ge(c.ve.sem, v))
                em.sy(lambda e, t0=t0, out_r=out_r: e.dma_start(
                    out=out_r[:, :, t0:t0 + 512],
                    in_=_r(qch[:], "p (k t) -> p k t", t=512)).then_inc(c.st.sem, 16))
                c.st.inc(16)
            qk()
        for tt in range(4):
            for nh in range(2):
                g = tt * 2 + nh
                bank = g % 2
                em.te(lambda e, v=c.ve.n: e.wait_ge(c.ve.sem, v))
                for k in range(8):
                    mm = lambda e, tt=tt, nh=nh, k=k, bank=bank: e.matmul(
                        ps[bank][:, :], h_bf[:, k * 512 + tt * 128: k * 512 + tt * 128 + 128],
                        wv[:, k * 1024 + nh * 512: k * 1024 + nh * 512 + 512],
                        start=(k == 0), stop=(k == 7))
                    if k == 7:
                        em.te(lambda e, mm=mm: mm(e).then_inc(c.te.sem, 1))
                    else:
                        em.te(mm)
                te_v = c.te.inc()
                em.ve(lambda e, v=te_v: e.wait_ge(c.te.sem, v))
                em.ve(lambda e, tt=tt, nh=nh, bank=bank: e.tensor_copy(
                    vsb[:, tt * 1024 + nh * 512: tt * 1024 + nh * 512 + 512], ps[bank][:, :]))
                em.ve(lambda e: e.engine_nop().then_inc(c.ve.sem, 1))
                c.ve.inc()
        em.sy(lambda e, v=c.ve.n: e.wait_ge(c.ve.sem, v))
        em.sy(lambda e, t0=t0: e.dma_start(
            out=_r(dram["ab_v"][t0:t0 + 512, :], "(q p) n -> p q n", p=128),
            in_=_r(vsb[:], "p (q n) -> p q n", n=1024)).then_inc(c.st.sem, 16))
        c.st.inc(16)
        for m in range(8):
            bank = m % 2
            em.te(lambda e, v=c.ve.n: e.wait_ge(c.ve.sem, v))
            em.te(lambda e, v=c.se.n: e.wait_ge(c.se.sem, v))
            for k in range(8):
                mm = lambda e, m=m, k=k, bank=bank: e.matmul(
                    ps[bank][:, :], wa[:, k * 1024 + m * 128: k * 1024 + m * 128 + 128],
                    h_bf[:, k * 512:(k + 1) * 512], start=(k == 0), stop=(k == 7))
                if k == 7:
                    em.te(lambda e, mm=mm: mm(e).then_inc(c.te.sem, 1))
                else:
                    em.te(mm)
            te_a = c.te.inc()
            em.se(lambda e, v=te_a: e.wait_ge(c.te.sem, v))
            em.se(lambda e, m=m, bank=bank: e.activation(
                absb[:], ps[bank][:, :], AF.Sigmoid, bias=ab_col[:, m:m + 1]))
            em.se(lambda e, m=m: e.activation(
                lasb[:, m * 512:(m + 1) * 512], absb[:], AF.Ln).then_inc(c.se.sem, 1))
            c.se.inc()
        se_la = c.se.n
        em.sy(lambda e, v=se_la: e.wait_ge(c.se.sem, v))
        em.sy(lambda e, t0=t0: e.dma_start(
            out=lar[:, :, t0:t0 + 512],
            in_=_r(lasb[:, 0:4096], "p (k t) -> p k t", t=512)).then_inc(c.st.sem, 16))
        c.st.inc(16)
        for tt in range(4):
            em.te(lambda e, v=c.se.n: e.wait_ge(c.se.sem, v))
            for k in range(8):
                em.te(lambda e, tt=tt, k=k: e.matmul(
                    ps[4][:, 0:16], h_bf[:, k * 512 + tt * 128: k * 512 + tt * 128 + 128],
                    wb[:, k * 16:(k + 1) * 16], start=(k == 0), stop=False))
            em.te(lambda e: e.matmul(
                ps[4][:, 0:16], sb["ones_row_bf"][0:1, 0:128], bb_row[:],
                start=False, stop=True).then_inc(c.te.sem, 1))
            te_b = c.te.inc()
            em.se(lambda e, v=te_b: e.wait_ge(c.te.sem, v))
            em.se(lambda e, tt=tt: e.activation(
                btsb[:, tt * 16:(tt + 1) * 16], ps[4][:, 0:16], AF.Sigmoid).then_inc(c.se.sem, 1))
            c.se.inc()
        se_bt = c.se.n
        em.sy(lambda e, v=se_bt: e.wait_ge(c.se.sem, v))
        em.sy(lambda e, t0=t0: e.dma_start(
            out=_r(dram["ab_beta"][t0:t0 + 512, :], "(q p) n -> p q n", p=128),
            in_=_r(btsb[:], "p (q n) -> p q n", n=16)).then_inc(c.st.sem, 16))
        c.st.inc(16)
        em.se(lambda e, v=c.st.n: e.wait_ge(c.st.sem, v))
        em.te(lambda e, v=ve_xbf: e.wait_ge(c.ve.sem, v))
        em.te(lambda e, v=c.se.n: e.wait_ge(c.se.sem, v))
        for k in range(8):
            mm = lambda e, k=k: e.matmul(
                ps[5][0:64, :], wu1[:, k * 64:(k + 1) * 64],
                xln_bf[:, k * 512:(k + 1) * 512], start=(k == 0), stop=(k == 7))
            if k == 7:
                em.te(lambda e, mm=mm: mm(e).then_inc(c.te.sem, 1))
            else:
                em.te(mm)
        te_u1 = c.te.inc()
        em.se(lambda e, v=te_u1: e.wait_ge(c.te.sem, v))
        em.se(lambda e: e.activation(s1[:], ps[5][0:64, :], AF.Silu,
                                     bias=u1b_col[:]).then_inc(c.se.sem, 1))
        se_s1 = c.se.inc()
        em.te(lambda e, v=se_s1: e.wait_ge(c.se.sem, v))
        for m in range(8):
            bank = 2 + m % 2
            if m >= 2:
                em.te(lambda e, v=c.se.n: e.wait_ge(c.se.sem, v))
            em.te(lambda e, m=m, bank=bank: e.matmul(
                ps[bank][:, :], wu2[:, m * 128:(m + 1) * 128], s1[:],
                start=True, stop=True).then_inc(c.te.sem, 1))
            te_u2 = c.te.inc()
            em.se(lambda e, v=te_u2: e.wait_ge(c.te.sem, v))
            em.se(lambda e, m=m, bank=bank: e.activation(
                gsb[:, m * 512:(m + 1) * 512], ps[bank][:, :], AF.Sigmoid,
                bias=u2b_col[:, m:m + 1]).then_inc(c.se.sem, 1))
            c.se.inc()
        se_g = c.se.n
        em.sy(lambda e, v=se_g: e.wait_ge(c.se.sem, v))
        em.sy(lambda e, t0=t0: e.dma_start(
            out=gr[:, :, t0:t0 + 512],
            in_=_r(gsb[:], "p (k t) -> p k t", t=512)).then_inc(c.st.sem, 16))
        c.st.inc(16)
        em.sy(lambda e, v=c.te.n: e.wait_ge(c.te.sem, v))
        em.sy(lambda e, v=c.ve.n: e.wait_ge(c.ve.sem, v))
        em.sy(lambda e, v=c.se.n: e.wait_ge(c.se.sem, v))

    for ci in range(NCHUNK):
        chunk(ci)
    c.barrier()



def emit_attn_b_p2(c, dram, sb, st):
    em, nc = c.em, c.nc
    E = st.enter_context
    qc = E(nc.sbuf_tensor("b2_q", [64, 16 * 128], BF16))
    kc = E(nc.sbuf_tensor("b2_k", [64, 16 * 128], BF16))
    lac = E(nc.sbuf_tensor("b2_la", [64, 16 * 128], F32))
    vsb = E(nc.sbuf_tensor("b2_v", [128, 16 * 64], BF16))
    btsb = E(nc.sbuf_tensor("b2_bt", [128, 16], F32))
    zer = E(nc.sbuf_tensor("b2_zer", [64, 128], F32))
    lam = E(nc.sbuf_tensor("b2_lam", [64, 16 * 128], F32))
    anc = E(nc.sbuf_tensor("b2_anc", [64, 16 * 2], F32))
    anc2 = E(nc.sbuf_tensor("b2_anc2", [64, 16 * 2], F32))
    eLCp = E(nc.sbuf_tensor("b2_eLCp", [64, 16], F32))
    e1p = E(nc.sbuf_tensor("b2_e1p", [64, 16 * 128], F32))
    e1x = E(nc.sbuf_tensor("b2_e1x", [64, 16 * 128], F32))
    eg = E(nc.sbuf_tensor("b2_eg", [64, 16 * 128], F32))
    egx = E(nc.sbuf_tensor("b2_egx", [64, 16 * 64], F32))
    scU = E(nc.sbuf_tensor("b2_scU", [64, 16], F32))
    fk = E(nc.sbuf_tensor("b2_fk", [64, 16 * 128], BF16))
    fq = E(nc.sbuf_tensor("b2_fq", [64, 16 * 128], BF16))
    gk = E(nc.sbuf_tensor("b2_gk", [64, 16 * 128], BF16))
    gkx = E(nc.sbuf_tensor("b2_gkx", [64, 16 * 64], BF16))
    Usb = E(nc.sbuf_tensor("b2_U", [64, 16 * 128], BF16))
    ktl = [E(nc.sbuf_tensor(f"b2_ktl{p}", [64, 16 * 128], BF16)) for p in range(2)]
    qtl = [E(nc.sbuf_tensor(f"b2_qtl{p}", [64, 16 * 128], BF16)) for p in range(2)]
    nbt = [E(nc.sbuf_tensor(f"b2_nbt{p}", [128, 16], F32)) for p in range(2)]
    vb = [E(nc.sbuf_tensor(f"b2_vb{p}", [128, 16 * 64], F32)) for p in range(2)]
    eLC = [E(nc.sbuf_tensor(f"b2_eLC{p}", [64, 16], F32)) for p in range(2)]
    Nsb = [E(nc.sbuf_tensor(f"b2_N{p}", [128, 16 * 128], BF16)) for p in range(2)]
    NTsb = [E(nc.sbuf_tensor(f"b2_NT{p}", [128, 16 * 128], BF16)) for p in range(2)]
    AqT = [E(nc.sbuf_tensor(f"b2_AqT{p}", [128, 16 * 128], BF16)) for p in range(2)]
    Utk = [E(nc.sbuf_tensor(f"b2_Utk{p}", [128, 16 * 64], BF16)) for p in range(2)]
    ysb = [E(nc.sbuf_tensor(f"b2_y{p}", [128, 16 * 64], F32)) for p in range(2)]
    yhat = [E(nc.sbuf_tensor(f"b2_yh{p}", [128, 16 * 64], BF16)) for p in range(2)]
    N2 = E(nc.sbuf_tensor("b2_N2", [128, 16 * 128], BF16))
    NT2 = E(nc.sbuf_tensor("b2_NT2", [128, 16 * 128], BF16))
    Xa = E(nc.sbuf_tensor("b2_Xa", [128, 16 * 64], BF16))
    Xb = E(nc.sbuf_tensor("b2_Xb", [128, 16 * 64], BF16))
    Ssb = E(nc.sbuf_tensor("b2_S", [64, 16 * 64], BF16))
    Xt = E(nc.sbuf_tensor("b2_Xt", [128, 16 * 64], F32))
    St = E(nc.sbuf_tensor("b2_Stt", [64, 16 * 64], F32))
    ysq = [E(nc.sbuf_tensor(f"b2_ysq{p}", [128, 16 * 64], F32)) for p in range(2)]
    yss = E(nc.sbuf_tensor("b2_yss", [128, 16], F32))
    hnB = E(nc.sbuf_tensor("b2_hnB", [128, 1024], BF16))
    ml = E(nc.sbuf_tensor("b2_ml", [128, 128], BF16))
    mu = E(nc.sbuf_tensor("b2_mu", [128, 128], BF16))

    em.sy(lambda e: e.dma_start(out=ml[:], in_=dram["mask_sl"]).then_inc(c.ld.sem, 16))
    c.ld.inc(16)
    em.sy(lambda e: e.dma_start(out=mu[:], in_=dram["mask_iu"]).then_inc(c.ld.sem, 16))
    c.ld.inc(16)
    em.sy(lambda e: e.dma_start(out=hnB[:], in_=dram["ab_head_norm_bc"]).then_inc(c.ld.sem, 16))
    ld_cst = c.ld.inc(16)
    em.gp(lambda e: e.memset(Ssb[:], 0.0))
    em.gp(lambda e: e.memset(zer[:], 0.0).then_inc(c.ge.sem, 1))
    ge_init = c.ge.inc()
    te_hn = c.te.n
    ve_init = c.ve.n

    qhr = _r(dram["ab_q"], "(h p) t -> p h t", p=64)
    khr = _r(dram["ab_k"], "(h p) t -> p h t", p=64)
    lar = _r(dram["ab_la"], "(h p) t -> p h t", p=64)
    ps, pt = c.ps, c.pt

    M = {}
    for ci in range(-2, 66):
        M[ci] = {"ld": 0, "ld_lac": 0, "ve_dec": 0, "se_exp": 0, "pve_done": 0,
                 "Nev": ve_init, "ntse": 0, "use": 0, "pte": te_hn, "sq_ev_se": 0,
                 "r1te": 0, "r1ve": ve_init, "r1se": 0, "oute": 0, "outse": 0,
                 "supse": 0,
                 "supte": 0, "supve": ve_init, "sqse": 0, "ynve": 0,
                 "st": 0}

    def emit_ynorm(ci):
        p = ci % 2
        t0 = ci * CB
        em.ve(lambda e, v=M[ci]["sqse"]: e.wait_ge(c.se.sem, v))
        for h in range(H):
            em.ve(lambda e, h=h, p=p: e.reduce_sum(
                yss[:, h:h + 1], ysq[p][:, h * 64:(h + 1) * 64], AX.X))
        em.ve(lambda e: e.engine_nop())
        em.ve(lambda e: e.engine_nop())
        em.ve(lambda e: e.tensor_scalar(yss[:], yss[:], 1.0 / 64.0, None, ALU.mult))
        em.ve(lambda e: e.engine_nop().then_inc(c.ve.sem, 1))
        ve_yss = c.ve.inc()
        em.se(lambda e, v=ve_yss: e.wait_ge(c.ve.sem, v))
        em.se(lambda e: e.activation(yss[:], yss[:], AF.Ln, bias=1e-6))
        em.se(lambda e: e.activation(yss[:], yss[:], AF.Exp, scale=-0.5).then_inc(c.se.sem, 1))
        se_yr = c.se.inc()
        em.ve(lambda e, v=se_yr: e.wait_ge(c.se.sem, v))
        em.ve(lambda e, v=M[ci - 2]["st"]: e.wait_ge(c.st.sem, v))
        for h in range(H):
            em.ve(lambda e, h=h, p=p: e.tensor_scalar(
                ysb[p][:, h * 64:(h + 1) * 64], ysb[p][:, h * 64:(h + 1) * 64],
                yss[:, h:h + 1], None, ALU.mult))
        em.ve(lambda e, p=p: e.tensor_tensor(yhat[p][:], ysb[p][:], hnB[:], ALU.mult))
        em.ve(lambda e: e.engine_nop().then_inc(c.ve.sem, 1))
        M[ci]["ynve"] = c.ve.inc()
        em.sy(lambda e, v=M[ci]["ynve"]: e.wait_ge(c.ve.sem, v))
        em.sy(lambda e, v=M[ci - 1]["st"]: e.wait_ge(c.st.sem, v))
        em.sy(lambda e, t0=t0, p=p: e.dma_start(
            out=dram["ab_y"][t0:t0 + CB, :], in_=yhat[p][:]).then_inc(c.st.sem, 16))
        M[ci]["st"] = c.st.inc(16)

    def emit_pref(ci):
        p = ci % 2
        t0 = ci * CB
        if ci >= 2 and B2_LVL >= 6:
            emit_ynorm(ci - 2)
        if ci == 0:
            em.sy(lambda e: e.wait_ge(c.ld.sem, ld_cst))
        em.sy(lambda e, v=M[ci - 1]["ve_dec"]: e.wait_ge(c.ve.sem, v))
        em.sy(lambda e, t0=t0: e.dma_start(
            out=_r(lac[:], "p (h t) -> p h t", t=128),
            in_=lar[:, :, t0:t0 + CB]).then_inc(c.ld.sem, 16))
        M[ci]["ld_lac"] = c.ld.inc(16)
        em.sy(lambda e, v=M[ci - 1]["pve_done"]: e.wait_ge(c.ve.sem, v))
        em.sy(lambda e, t0=t0: e.dma_start(
            out=_r(qc[:], "p (h t) -> p h t", t=128),
            in_=qhr[:, :, t0:t0 + CB]).then_inc(c.ld.sem, 16))
        c.ld.inc(16)
        em.sy(lambda e, t0=t0: e.dma_start(
            out=_r(kc[:], "p (h t) -> p h t", t=128),
            in_=khr[:, :, t0:t0 + CB]).then_inc(c.ld.sem, 16))
        c.ld.inc(16)
        em.sy(lambda e, t0=t0: e.dma_start(
            out=vsb[:], in_=dram["ab_v"][t0:t0 + CB, :]).then_inc(c.ld.sem, 16))
        c.ld.inc(16)
        em.sy(lambda e, t0=t0: e.dma_start(
            out=btsb[:], in_=dram["ab_beta"][t0:t0 + CB, :]).then_inc(c.ld.sem, 16))
        M[ci]["ld"] = c.ld.inc(16)

        if ci == 0:
            em.ve(lambda e: e.wait_ge(c.ge.sem, ge_init))
        em.ve(lambda e, v=M[ci]["ld"]: e.wait_ge(c.ld.sem, v))
        em.ve(lambda e, v=max(M[ci - 1]["pte"], M[ci - 2]["supte"]): e.wait_ge(c.te.sem, v))
        em.ve(lambda e, v=M[ci - 1]["se_exp"]: e.wait_ge(c.se.sem, v))
        for h in range(H):
            em.ve(lambda e, h=h: e.tensor_tensor_scan(
                lam[:, h * 128:(h + 1) * 128], lac[:, h * 128:(h + 1) * 128],
                zer[:], 0.0, ALU.add, ALU.add))
        for h in range(H):
            for i in range(2):
                em.ve(lambda e, h=h, i=i: e.tensor_tensor(
                    anc[:, h * 2 + i: h * 2 + i + 1],
                    lam[:, h * 128 + i * 64: h * 128 + i * 64 + 1],
                    lam[:, h * 128 + i * 64 + 63: h * 128 + i * 64 + 64], ALU.add))
        em.ve(lambda e: e.tensor_copy(
            _r(eLCp[:], "p (h o) -> p h o", o=1),
            _r(lam[:], "p (h t) -> p h t", t=128)[:, :, 127:128]))
        em.ve(lambda e: e.tensor_scalar(anc2[:], anc[:], 0.5, None, ALU.mult))
        em.ve(lambda e: e.engine_nop())
        em.ve(lambda e: e.engine_nop())
        for h in range(H):
            for i in range(2):
                em.ve(lambda e, h=h, i=i: e.tensor_scalar(
                    e1p[:, h * 128 + i * 64: h * 128 + (i + 1) * 64],
                    lam[:, h * 128 + i * 64: h * 128 + (i + 1) * 64],
                    anc2[:, h * 2 + i: h * 2 + i + 1], None, ALU.subtract))
            em.ve(lambda e, h=h: e.tensor_scalar(
                egx[:, h * 64:(h + 1) * 64], lam[:, h * 128: h * 128 + 64],
                anc2[:, h * 2 + 1: h * 2 + 2], -1.0, ALU.subtract, ALU.mult))
        for h in range(H):
            em.ve(lambda e, h=h: e.tensor_tensor(
                scU[:, h:h + 1], eLCp[:, h:h + 1], anc2[:, h * 2 + 1: h * 2 + 2], ALU.subtract))
        em.ve(lambda e: e.engine_nop().then_inc(c.ve.sem, 1))
        M[ci]["ve_dec"] = c.ve.inc()

        em.se(lambda e, v=M[ci]["ve_dec"]: e.wait_ge(c.ve.sem, v))
        em.se(lambda e: e.activation(eg[:], e1p[:], AF.Exp, scale=-1.0))
        em.se(lambda e: e.activation(e1x[:], e1p[:], AF.Exp))
        em.se(lambda e: e.activation(egx[:], egx[:], AF.Exp))
        em.se(lambda e: e.activation(lam[:], lam[:], AF.Exp))
        em.se(lambda e, p=p: e.activation(eLC[p][:], eLCp[:], AF.Exp))
        em.se(lambda e: e.activation(scU[:], scU[:], AF.Exp).then_inc(c.se.sem, 1))
        M[ci]["se_exp"] = c.se.inc()

        em.ve(lambda e, v=M[ci]["se_exp"]: e.wait_ge(c.se.sem, v))
        em.ve(lambda e, v=M[ci]["ld"]: e.wait_ge(c.ld.sem, v))
        em.ve(lambda e: e.tensor_tensor(fk[:], kc[:], e1x[:], ALU.mult))
        em.ve(lambda e: e.tensor_tensor(fq[:], qc[:], e1x[:], ALU.mult))
        em.ve(lambda e: e.tensor_tensor(gk[:], kc[:], eg[:], ALU.mult))
        for h in range(H):
            em.ve(lambda e, h=h: e.tensor_tensor(
                gkx[:, h * 64:(h + 1) * 64], kc[:, h * 128: h * 128 + 64],
                egx[:, h * 64:(h + 1) * 64], ALU.mult))
        em.ve(lambda e, p=p: e.tensor_tensor(ktl[p][:], kc[:], lam[:], ALU.mult))
        em.ve(lambda e, p=p: e.tensor_tensor(qtl[p][:], qc[:], lam[:], ALU.mult))
        for h in range(H):
            em.ve(lambda e, h=h: e.tensor_scalar(
                Usb[:, h * 128: h * 128 + 64], gkx[:, h * 64:(h + 1) * 64],
                scU[:, h:h + 1], None, ALU.mult))
            em.ve(lambda e, h=h: e.tensor_scalar(
                Usb[:, h * 128 + 64:(h + 1) * 128], gk[:, h * 128 + 64: h * 128 + 128],
                scU[:, h:h + 1], None, ALU.mult))
        em.ve(lambda e, p=p: e.tensor_scalar(nbt[p][:], btsb[:], -1.0, None, ALU.mult))
        for h in range(H):
            em.ve(lambda e, h=h, p=p: e.tensor_scalar(
                vb[p][:, h * 64:(h + 1) * 64], vsb[:, h * 64:(h + 1) * 64],
                btsb[:, h:h + 1], None, ALU.mult))
        em.ve(lambda e: e.engine_nop().then_inc(c.ve.sem, 1))
        M[ci]["pve_done"] = c.ve.inc()

        em.te(lambda e, v=M[ci]["pve_done"]: e.wait_ge(c.ve.sem, v))
        evacN = {}
        for h in range(H):
            bkN = h % 2
            bkA = 2 + h % 2
            if h >= 2:
                em.te(lambda e, v=evacN[h - 2]: e.wait_ge(c.ve.sem, v))
            em.te(lambda e, h=h, bkN=bkN: e.matmul(
                ps[bkN][0:64, 0:64], fk[:, h * 128: h * 128 + 64],
                gk[:, h * 128: h * 128 + 64], start=True, stop=True))
            em.te(lambda e, h=h, bkN=bkN: e.matmul(
                ps[bkN][64:128, 0:64], fk[:, h * 128 + 64: h * 128 + 128],
                gkx[:, h * 64:(h + 1) * 64], start=True, stop=True,
                tile_position=(0, 64)))
            em.te(lambda e, h=h, bkN=bkN: e.matmul(
                ps[bkN][64:128, 64:128], fk[:, h * 128 + 64: h * 128 + 128],
                gk[:, h * 128 + 64: h * 128 + 128], start=True, stop=True,
                tile_position=(0, 64)).then_inc(c.te.sem, 1))
            te_N = c.te.inc()
            em.te(lambda e, h=h, bkA=bkA: e.matmul(
                ps[bkA][0:64, 0:64], gk[:, h * 128: h * 128 + 64],
                fq[:, h * 128: h * 128 + 64], start=True, stop=True))
            em.te(lambda e, h=h, bkA=bkA: e.matmul(
                ps[bkA][0:64, 64:128], gkx[:, h * 64:(h + 1) * 64],
                fq[:, h * 128 + 64: h * 128 + 128], start=True, stop=True))
            em.te(lambda e, h=h, bkA=bkA: e.matmul(
                ps[bkA][64:128, 64:128], gk[:, h * 128 + 64: h * 128 + 128],
                fq[:, h * 128 + 64: h * 128 + 128], start=True, stop=True,
                tile_position=(0, 64)).then_inc(c.te.sem, 1))
            te_A = c.te.inc()
            em.ve(lambda e, v=te_A: e.wait_ge(c.te.sem, v))
            if h == 0:
                em.ve(lambda e, v=M[ci - 2]["sq_ev_se"]: e.wait_ge(c.se.sem, v))
                em.ve(lambda e, v=max(M[ci - 2]["oute"], M[ci - 2]["supte"]):
                      e.wait_ge(c.te.sem, v))
            em.ve(lambda e, h=h, bkN=bkN, p=p: e.scalar_tensor_tensor(
                Nsb[p][:, h * 128:(h + 1) * 128], ps[bkN][:, 0:128],
                nbt[p][:, h:h + 1], ml[:], ALU.mult, ALU.mult))
            em.ve(lambda e, h=h, bkA=bkA, p=p: e.tensor_tensor(
                AqT[p][:, h * 128:(h + 1) * 128], ps[bkA][:, 0:128], mu[:], ALU.mult))
            em.ve(lambda e: e.engine_nop().then_inc(c.ve.sem, 1))
            evacN[h] = c.ve.inc()
        M[ci]["Nev"] = c.ve.n

        em.te(lambda e, v=M[ci]["Nev"]: e.wait_ge(c.ve.sem, v))
        em.te(lambda e, v=M[ci - 1]["use"]: e.wait_ge(c.ve.sem, v))
        ntse = {}
        for half in range(2):
            ptb = pt[half]
            for hh in range(8):
                h = half * 8 + hh
                mm = lambda e, h=h, hh=hh, ptb=ptb: e.transpose(
                    ptb[:, hh * 128:(hh + 1) * 128], Nsb[p][:, h * 128:(h + 1) * 128],
                    sb["ident"][:])
                if hh == 7:
                    em.te(lambda e, mm=mm: mm(e).then_inc(c.te.sem, 1))
                else:
                    em.te(mm)
            te_tr = c.te.inc()
            em.ve(lambda e, v=te_tr: e.wait_ge(c.te.sem, v))
            em.ve(lambda e, half=half, ptb=ptb, p=p: e.tensor_copy(
                NTsb[p][:, half * 1024:(half + 1) * 1024], ptb[:, :]).then_inc(c.ve.sem, 1))
            ntse[half] = c.ve.inc()
        M[ci]["ntse"] = c.ve.n
        em.te(lambda e, v=ntse[0]: e.wait_ge(c.ve.sem, v))
        for h in range(H):
            mm = lambda e, h=h: e.transpose(
                pt[0][:, h * 64:(h + 1) * 64], Usb[:, h * 128:(h + 1) * 128],
                sb["ident"][0:64, 0:64])
            if h == 15:
                em.te(lambda e, mm=mm: mm(e).then_inc(c.te.sem, 1))
            else:
                em.te(mm)
        te_utr = c.te.inc()
        M[ci]["pte"] = c.te.n
        em.ve(lambda e, v=te_utr: e.wait_ge(c.te.sem, v))
        em.ve(lambda e, p=p: e.tensor_copy(Utk[p][:], pt[0][:, 0:1024]).then_inc(c.ve.sem, 1))
        M[ci]["use"] = c.ve.inc()

    def emit_suf(ci):
        p = ci % 2
        if B2_LVL < 1:
            return
        em.te(lambda e, v=M[ci - 1]["supve"]: e.wait_ge(c.ve.sem, v))
        em.te(lambda e, v=ge_init: e.wait_ge(c.ge.sem, v))
        em.te(lambda e, v=M[ci - 1]["outse"]: e.wait_ge(c.se.sem, v))
        r1te = {}
        for g in range(2):
            for hh in range(8):
                h = g * 8 + hh
                mm = lambda e, h=h, hh=hh, g=g, p=p: e.matmul(
                    ps[4 + g][:, hh * 64:(hh + 1) * 64], ktl[p][:, h * 128:(h + 1) * 128],
                    Ssb[:, h * 64:(h + 1) * 64], start=True, stop=True)
                if hh == 7:
                    em.te(lambda e, mm=mm: mm(e).then_inc(c.te.sem, 1))
                else:
                    em.te(mm)
            r1te[g] = c.te.inc()
        if B2_R1 < 2:
            return
        em.se(lambda e, v=M[ci - 1]["r1ve"]: e.wait_ge(c.ve.sem, v))
        for g in range(2):
            em.se(lambda e, v=r1te[g]: e.wait_ge(c.te.sem, v))
            em.se(lambda e, g=g: e.copy(
                Xt[:, g * 512:(g + 1) * 512], ps[4 + g][:, :]).then_inc(c.se.sem, 1))
        M[ci]["r1se"] = c.se.inc(2)
        em.ve(lambda e, v=M[ci]["r1se"]: e.wait_ge(c.se.sem, v))
        em.ve(lambda e, v=M[ci - 1]["supte"]: e.wait_ge(c.te.sem, v))
        for h in range(H):
            em.ve(lambda e, h=h, p=p: e.scalar_tensor_tensor(
                Xa[:, h * 64:(h + 1) * 64], Xt[:, h * 64:(h + 1) * 64],
                nbt[p][:, h:h + 1], vb[p][:, h * 64:(h + 1) * 64],
                ALU.mult, ALU.add))
        em.ve(lambda e: e.engine_nop().then_inc(c.ve.sem, 1))
        M[ci]["r1ve"] = c.ve.inc()

        if B2_LVL < 2:
            return
        curN, curNT, nxtN, nxtNT = Nsb[p], NTsb[p], N2, NT2
        Xcur, Xnxt = Xa, Xb
        se_prev = {0: 0, 1: 0}
        nev_guard = M[ci + 1]["Nev"] if ci < NCB - 1 else M[ci]["Nev"]
        ptw = None
        for j in range(7):
            for half in range(2):
                bank = 4 + half
                if j == 0:
                    em.te(lambda e, v=M[ci]["r1ve"]: e.wait_ge(c.ve.sem, v))
                else:
                    em.te(lambda e, v=max(se_prev[half], ptw or 0):
                          e.wait_ge(c.se.sem, v))
                for hh in range(8):
                    h = half * 8 + hh
                    em.te(lambda e, h=h, hh=hh, bank=bank, Xcur=Xcur: e.matmul(
                        ps[bank][:, hh * 64:(hh + 1) * 64], sb["ident"][:],
                        Xcur[:, h * 64:(h + 1) * 64], start=True, stop=False))
                    mm = lambda e, h=h, hh=hh, bank=bank, curNT=curNT, Xcur=Xcur: e.matmul(
                        ps[bank][:, hh * 64:(hh + 1) * 64], curNT[:, h * 128:(h + 1) * 128],
                        Xcur[:, h * 64:(h + 1) * 64], start=False, stop=True)
                    if hh == 7:
                        em.te(lambda e, mm=mm: mm(e).then_inc(c.te.sem, 1))
                    else:
                        em.te(mm)
                te_ap = c.te.inc()
                em.se(lambda e, v=te_ap: e.wait_ge(c.te.sem, v))
                em.se(lambda e, half=half, bank=bank, Xnxt=Xnxt: e.copy(
                    Xnxt[:, half * 512:(half + 1) * 512],
                    ps[bank][:, :]).then_inc(c.se.sem, 1))
                se_prev[half] = c.se.inc()
            if j < 6 and B2_LVL >= 3:
                for wave, (srcL, srcR, dst) in enumerate(
                        ((curNT, curN, nxtN), (curN, curNT, nxtNT))):
                    if ptw is None:
                        em.te(lambda e, v=nev_guard: e.wait_ge(c.ve.sem, v))
                    else:
                        em.te(lambda e, v=ptw: e.wait_ge(c.se.sem, v))
                    sq_te = {}
                    for b in range(4):
                        for hh in range(4):
                            h = b * 4 + hh
                            mm = lambda e, h=h, hh=hh, b=b, srcL=srcL, srcR=srcR: e.matmul(
                                ps[b][:, hh * 128:(hh + 1) * 128],
                                srcL[:, h * 128:(h + 1) * 128],
                                srcR[:, h * 128:(h + 1) * 128], start=True, stop=True)
                            if hh == 3:
                                em.te(lambda e, mm=mm: mm(e).then_inc(c.te.sem, 1))
                            else:
                                em.te(mm)
                        sq_te[b] = c.te.inc()
                    for b in range(4):
                        em.se(lambda e, v=sq_te[b]: e.wait_ge(c.te.sem, v))
                        em.se(lambda e, b=b, dst=dst: e.copy(
                            dst[:, b * 512:(b + 1) * 512], ps[b][:, :]).then_inc(c.se.sem, 1))
                        c.se.inc()
                    ptw = c.se.n
                curN, curNT, nxtN, nxtNT = nxtN, nxtNT, curN, curNT
            Xcur, Xnxt = Xnxt, Xcur
        M[ci]["sq_ev_se"] = ptw if ptw is not None else 0

        if B2_LVL < 4:
            return
        oute = {}
        for g in range(2):
            bank = 4 + g
            em.te(lambda e, v=se_prev[g]: e.wait_ge(c.se.sem, v))
            for hh in range(8):
                h = g * 8 + hh
                em.te(lambda e, h=h, hh=hh, bank=bank, p=p: e.matmul(
                    ps[bank][:, hh * 64:(hh + 1) * 64], qtl[p][:, h * 128:(h + 1) * 128],
                    Ssb[:, h * 64:(h + 1) * 64], start=True, stop=False))
                mm = lambda e, h=h, hh=hh, bank=bank, p=p, Xcur=Xcur: e.matmul(
                    ps[bank][:, hh * 64:(hh + 1) * 64], AqT[p][:, h * 128:(h + 1) * 128],
                    Xcur[:, h * 64:(h + 1) * 64], start=False, stop=True)
                if hh == 7:
                    em.te(lambda e, mm=mm: mm(e).then_inc(c.te.sem, 1))
                else:
                    em.te(mm)
            oute[g] = c.te.inc()
        M[ci]["oute"] = c.te.n
        em.se(lambda e, v=M[ci - 2]["ynve"]: e.wait_ge(c.ve.sem, v))
        for g in range(2):
            em.se(lambda e, v=oute[g]: e.wait_ge(c.te.sem, v))
            em.se(lambda e, g=g, p=p: e.copy(
                ysb[p][:, g * 512:(g + 1) * 512], ps[4 + g][:, :]).then_inc(c.se.sem, 1))
        M[ci]["outse"] = c.se.inc(2) - 2 + 2

        if B2_LVL < 5:
            return
        em.te(lambda e, v=nev_guard: e.wait_ge(c.ve.sem, v))
        em.te(lambda e, v=M[ci]["sq_ev_se"]: e.wait_ge(c.se.sem, v))
        supte = {}
        for g in range(2):
            bank = 2 + g
            for hh in range(8):
                h = g * 8 + hh
                mm = lambda e, h=h, hh=hh, bank=bank, p=p, Xcur=Xcur: e.matmul(
                    ps[bank][0:64, hh * 64:(hh + 1) * 64], Utk[p][:, h * 64:(h + 1) * 64],
                    Xcur[:, h * 64:(h + 1) * 64], start=True, stop=True)
                if hh == 7:
                    em.te(lambda e, mm=mm: mm(e).then_inc(c.te.sem, 1))
                else:
                    em.te(mm)
            supte[g] = c.te.inc()
        M[ci]["supte"] = c.te.n
        em.se(lambda e, v=M[ci - 1]["supve"]: e.wait_ge(c.ve.sem, v))
        supse = {}
        for g in range(2):
            em.se(lambda e, v=supte[g]: e.wait_ge(c.te.sem, v))
            em.se(lambda e, g=g: e.copy(
                St[:, g * 512:(g + 1) * 512], ps[2 + g][0:64, :]).then_inc(c.se.sem, 1))
        M[ci]["supse"] = c.se.inc(2)
        em.ve(lambda e, v=M[ci]["supse"]: e.wait_ge(c.se.sem, v))
        for h in range(H):
            em.ve(lambda e, h=h, p=p: e.scalar_tensor_tensor(
                Ssb[:, h * 64:(h + 1) * 64], Ssb[:, h * 64:(h + 1) * 64],
                eLC[p][:, h:h + 1], St[:, h * 64:(h + 1) * 64],
                ALU.mult, ALU.add))
        em.ve(lambda e: e.engine_nop().then_inc(c.ve.sem, 1))
        M[ci]["supve"] = c.ve.inc()

        if B2_LVL < 6:
            return
        em.se(lambda e, v=M[ci - 2]["ynve"]: e.wait_ge(c.ve.sem, v))
        em.se(lambda e, p=p: e.activation(ysq[p][:], ysb[p][:], AF.Square).then_inc(c.se.sem, 1))
        M[ci]["sqse"] = c.se.inc()

    emit_pref(0)
    for ci in range(1, NCB):
        emit_pref(ci)
        emit_suf(ci - 1)
    emit_suf(NCB - 1)
    if B2_LVL >= 6:
        emit_ynorm(NCB - 2)
        emit_ynorm(NCB - 1)
    c.barrier()



def emit_attn_b_p3(c, dram, sb, st, x_in, x_out):
    em, nc = c.em, c.nc
    E = st.enter_context
    yh = E(nc.sbuf_tensor("b3_yh", [128, 4 * 1024], BF16))
    yhT = E(nc.sbuf_tensor("b3_yhT", [128, 8 * 512], BF16))
    gsb = E(nc.sbuf_tensor("b3_g", [128, 8 * 512], BF16))
    xlns = E(nc.sbuf_tensor("b3_xln", [128, 8 * 512], F32))
    x2s = E(nc.sbuf_tensor("b3_x2", [128, 8 * 512], F32))
    yc = E(nc.sbuf_tensor("b3_yc", [128, 8 * 512], F32))
    y2 = E(nc.sbuf_tensor("b3_y2", [128, 8 * 512], BF16))
    gall = E(nc.sbuf_tensor("b3_gall", [128, 32 * 512], BF16))
    sgt = E(nc.sbuf_tensor("b3_sgt", [128, 512], BF16))
    wo = E(nc.sbuf_tensor("b3_wo", [128, 8 * 1024], BF16))
    w13 = [E(nc.sbuf_tensor(f"b3_w13_{b}", [128, 2 * 8 * 512], BF16)) for b in range(2)]
    w2b = [E(nc.sbuf_tensor(f"b3_w2_{b}", [128, 4 * 1024], BF16)) for b in range(3)]
    stm = E(nc.sbuf_tensor("b3_stm", [1, 512], F32))
    str_ = E(nc.sbuf_tensor("b3_str", [1, 512], F32))
    rB = E(nc.sbuf_tensor("b3_rB", [128, 512], F32))
    tmp = E(nc.sbuf_tensor("b3_tmp", [128, 512], F32))
    tmp2 = E(nc.sbuf_tensor("b3_tmp2", [128, 512], F32))
    fn_col = E(nc.sbuf_tensor("b3_fn", [128, 8], F32))

    em.gp(lambda e: e.dma_start(
        out=_r(wo[:], "p (k n) -> p k n", n=1024),
        in_=_r(dram["ab_out_w"], "(k p) n -> p k n", p=128)).then_inc(c.cs.sem, 16))
    c.cs.inc(16)
    em.sy(lambda e: e.dma_start(out=fn_col[:], in_=dram["ab_ff_norm_w"]).then_inc(c.ld.sem, 16))
    c.ld.inc(16)
    cs_w = c.cs.n
    ld_w = c.ld.n
    em.te(lambda e: e.wait_ge(c.cs.sem, cs_w))
    em.ve(lambda e: e.wait_ge(c.ld.sem, ld_w))

    xlnr = _r(dram["ab_xln"], "(k p) t -> p k t", p=128)
    gr = _r(dram["ab_gate"], "(k p) t -> p k t", p=128)
    xr = _r(dram[x_in], "(k p) t -> p k t", p=128)
    xo = _r(dram[x_out], "(k p) t -> p k t", p=128)
    ps, pt = c.ps, c.pt

    def chunk(ci):
        t0 = ci * CHUNK
        em.ve(lambda e, v=c.st.n: e.wait_ge(c.st.sem, v))
        em.sy(lambda e, v=c.te.n: e.wait_ge(c.te.sem, v))
        em.sy(lambda e, t0=t0: e.dma_start(
            out=_r(yh[:], "p (q n) -> p q n", n=1024),
            in_=_r(dram["ab_y"][t0:t0 + 512, :], "(q p) n -> p q n", p=128)).then_inc(c.ld.sem, 16))
        c.ld.inc(16)
        em.sy(lambda e, t0=t0: e.dma_start(
            out=_r(gsb[:], "p (k t) -> p k t", t=512),
            in_=gr[:, :, t0:t0 + 512]).then_inc(c.ld.sem, 16))
        c.ld.inc(16)
        em.sy(lambda e, t0=t0: e.dma_start(
            out=_r(xlns[:], "p (k t) -> p k t", t=512),
            in_=xlnr[:, :, t0:t0 + 512]).then_inc(c.ld.sem, 16))
        c.ld.inc(16)
        em.sy(lambda e, t0=t0: e.dma_start(
            out=_r(x2s[:], "p (k t) -> p k t", t=512),
            in_=xr[:, :, t0:t0 + 512]).then_inc(c.ld.sem, 16))
        ldx = c.ld.inc(16)
        em.te(lambda e, v=ldx: e.wait_ge(c.ld.sem, v))
        for tt in range(4):
            ptb = pt[tt % 2]
            if tt >= 2:
                em.te(lambda e, v=c.ve.n: e.wait_ge(c.ve.sem, v))
            for j in range(8):
                mm = lambda e, tt=tt, j=j, ptb=ptb: e.transpose(
                    ptb[:, j * 128:(j + 1) * 128],
                    yh[:, tt * 1024 + j * 128: tt * 1024 + j * 128 + 128], sb["ident"][:])
                if j == 7:
                    em.te(lambda e, mm=mm: mm(e).then_inc(c.te.sem, 1))
                else:
                    em.te(mm)
            te_tr = c.te.inc()
            em.ve(lambda e, v=te_tr: e.wait_ge(c.te.sem, v))
            for j in range(8):
                em.ve(lambda e, tt=tt, j=j, ptb=ptb: e.tensor_copy(
                    yhT[:, j * 512 + tt * 128: j * 512 + tt * 128 + 128],
                    ptb[:, j * 128:(j + 1) * 128]))
            em.ve(lambda e: e.engine_nop().then_inc(c.ve.sem, 1))
            c.ve.inc()
        ve_yhT = c.ve.n
        em.te(lambda e, v=ve_yhT: e.wait_ge(c.ve.sem, v))
        for m in range(8):
            bank = m % 2
            if m >= 2:
                em.te(lambda e, v=c.ve.n: e.wait_ge(c.ve.sem, v))
            for k in range(8):
                mm = lambda e, m=m, k=k, bank=bank: e.matmul(
                    ps[bank][:, :], wo[:, k * 1024 + m * 128: k * 1024 + m * 128 + 128],
                    yhT[:, k * 512:(k + 1) * 512], start=(k == 0), stop=(k == 7))
                if k == 7:
                    em.te(lambda e, mm=mm: mm(e).then_inc(c.te.sem, 1))
                else:
                    em.te(mm)
            te_o = c.te.inc()
            em.ve(lambda e, v=te_o: e.wait_ge(c.te.sem, v))
            em.ve(lambda e, m=m, bank=bank: e.tensor_tensor(
                tmp[:], ps[bank][:, :], gsb[:, m * 512:(m + 1) * 512], ALU.mult))
            em.ve(lambda e: e.tensor_scalar(sb["junk"][:], sb["junk"][:], 1.0, None, ALU.mult))
            em.ve(lambda e, m=m: e.tensor_tensor(
                yc[:, m * 512:(m + 1) * 512], tmp[:], xlns[:, m * 512:(m + 1) * 512], ALU.add))
            em.ve(lambda e: e.engine_nop().then_inc(c.ve.sem, 1))
            c.ve.inc()
        ve_yc = c.ve.n
        te0 = c.te.n
        em.te(lambda e, v=ve_yc: e.wait_ge(c.ve.sem, v))
        for i in range(8):
            buf = tmp if i % 2 == 0 else tmp2
            if i >= 2:
                em.se(lambda e, v=te0 + (i - 1): e.wait_ge(c.te.sem, v))
            else:
                em.se(lambda e, v=ve_yc: e.wait_ge(c.ve.sem, v))
            em.se(lambda e, i=i, buf=buf: e.activation(
                buf[:], yc[:, i * 512:(i + 1) * 512], AF.Square).then_inc(c.se.sem, 1))
            se_sq = c.se.inc()
            em.te(lambda e, v=se_sq: e.wait_ge(c.se.sem, v))
            em.te(lambda e, i=i, buf=buf: e.matmul(
                ps[4][0:1, :], sb["ones_col_f32"][:], buf[:],
                start=(i == 0), stop=(i == 7)).then_inc(c.te.sem, 1))
            c.te.inc()
        te_ss = c.te.n
        em.ve(lambda e, v=te_ss: e.wait_ge(c.te.sem, v))
        em.ve(lambda e: e.tensor_scalar(stm[:], ps[4][0:1, :], 1.0 / D, None, ALU.mult))
        em.ve(lambda e: e.engine_nop().then_inc(c.ve.sem, 1))
        ve_ss = c.ve.inc()
        em.se(lambda e, v=ve_ss: e.wait_ge(c.ve.sem, v))
        em.se(lambda e: e.activation(str_[:], stm[:], AF.Ln, bias=1e-6))
        em.se(lambda e: e.activation(str_[:], str_[:], AF.Exp, scale=-0.5).then_inc(c.se.sem, 1))
        se_rs = c.se.inc()
        em.te(lambda e, v=se_rs: e.wait_ge(c.se.sem, v))
        em.te(lambda e: e.matmul(ps[5][:, :], sb["ones_row_f32"][:], str_[:],
                                 start=True, stop=True).then_inc(c.te.sem, 1))
        te_rb = c.te.inc()
        em.ve(lambda e, v=te_rb: e.wait_ge(c.te.sem, v))
        em.ve(lambda e: e.tensor_copy(rB[:], ps[5][:, :]))
        em.ve(lambda e: e.tensor_scalar(sb["junk"][:], sb["junk"][:], 1.0, None, ALU.mult))
        for i in range(8):
            em.ve(lambda e, i=i: e.tensor_tensor(
                tmp[:], yc[:, i * 512:(i + 1) * 512], rB[:], ALU.mult))
            em.ve(lambda e, i=i: e.tensor_scalar(
                y2[:, i * 512:(i + 1) * 512], tmp[:], fn_col[:, i:i + 1], None, ALU.mult))
        em.ve(lambda e: e.engine_nop().then_inc(c.ve.sem, 1))
        ve_y2 = c.ve.inc()
        em.te(lambda e, v=ve_y2: e.wait_ge(c.ve.sem, v))
        te_n13 = {}
        for n in range(8):
            b = n % 2
            if n >= 2:
                em.sy(lambda e, v=te_n13[n - 2]: e.wait_ge(c.te.sem, v))
            em.sy(lambda e, n=n, b=b: e.dma_start(
                out=_r(w13[b][:, 0:8 * 512], "p (k t) -> p k t", t=512),
                in_=_r(dram["ff_w1_bf"][:, n * 512:(n + 1) * 512], "(k p) t -> p k t", p=128)).then_inc(c.ld.sem, 16))
            c.ld.inc(16)
            em.sy(lambda e, n=n, b=b: e.dma_start(
                out=_r(w13[b][:, 8 * 512:16 * 512], "p (k t) -> p k t", t=512),
                in_=_r(dram["ff_w3_bf"][:, n * 512:(n + 1) * 512], "(k p) t -> p k t", p=128)).then_inc(c.ld.sem, 16))
            ld_w13 = c.ld.inc(16)
            em.te(lambda e, v=ld_w13: e.wait_ge(c.ld.sem, v))
            for mm4 in range(4):
                bank = mm4 % 2
                em.te(lambda e, v=c.ve.n: e.wait_ge(c.ve.sem, v))
                for k in range(8):
                    em.te(lambda e, n=n, b=b, mm4=mm4, k=k, bank=bank: e.matmul(
                        ps[bank][:, :], w13[b][:, k * 512 + mm4 * 128: k * 512 + mm4 * 128 + 128],
                        y2[:, k * 512:(k + 1) * 512], start=(k == 0), stop=(k == 7)))
                for k in range(8):
                    mm = lambda e, n=n, b=b, mm4=mm4, k=k, bank=bank: e.matmul(
                        ps[2 + bank][:, :], w13[b][:, 8 * 512 + k * 512 + mm4 * 128: 8 * 512 + k * 512 + mm4 * 128 + 128],
                        y2[:, k * 512:(k + 1) * 512], start=(k == 0), stop=(k == 7))
                    if k == 7:
                        em.te(lambda e, mm=mm: mm(e).then_inc(c.te.sem, 1))
                    else:
                        em.te(mm)
                te_g = c.te.inc()
                em.se(lambda e, v=te_g: e.wait_ge(c.te.sem, v))
                em.se(lambda e, v=c.ve.n: e.wait_ge(c.ve.sem, v))
                em.se(lambda e, bank=bank: e.activation(
                    sgt[:], ps[bank][:, :], AF.Silu).then_inc(c.se.sem, 1))
                se_sg = c.se.inc()
                em.ve(lambda e, v=se_sg: e.wait_ge(c.se.sem, v))
                em.ve(lambda e, n=n, mm4=mm4, bank=bank: e.tensor_tensor(
                    gall[:, n * 4 * 512 + mm4 * 512: n * 4 * 512 + (mm4 + 1) * 512],
                    ps[2 + bank][:, :], sgt[:], ALU.mult))
                em.ve(lambda e: e.engine_nop().then_inc(c.ve.sem, 1))
                c.ve.inc()
            te_n13[n] = c.te.n
        ve_gall = c.ve.n
        if ci == 0 and "d3_yc" in dram:
            em.sy(lambda e, v=ve_gall: e.wait_ge(c.ve.sem, v))
            em.sy(lambda e: e.dma_start(out=dram["d3_yc"], in_=yc[:]).then_inc(c.st.sem, 16))
            c.st.inc(16)
            em.sy(lambda e: e.dma_start(out=dram["d3_y2"], in_=y2[:]).then_inc(c.st.sem, 16))
            c.st.inc(16)
            em.sy(lambda e: e.dma_start(out=dram["d3_gall"], in_=gall[:]).then_inc(c.st.sem, 16))
            c.st.inc(16)
        em.te(lambda e, v=ve_gall: e.wait_ge(c.ve.sem, v))
        for half in range(2):
            ms = [half * 4 + i for i in range(4)]
            em.te(lambda e, v=c.ve.n: e.wait_ge(c.ve.sem, v))
            te_n_done = {}
            for n in range(8):
                b2_ = n % 3
                if n >= 1:
                    em.sy(lambda e, v=te_n_done[n - 1]: e.wait_ge(c.te.sem, v))
                em.sy(lambda e, n=n, b2_=b2_: e.dma_start(
                    out=_r(w2b[b2_][:, 0:4 * 1024], "p (k t) -> p k t", t=1024),
                    in_=_r(dram["ff_w2_bf"][n * 512:(n + 1) * 512, :],
                           "(k p) t -> p k t", p=128)).then_inc(c.ld.sem, 16))
                ld_w2 = c.ld.inc(16)
                em.te(lambda e, v=ld_w2: e.wait_ge(c.ld.sem, v))
                for mi, m in enumerate(ms):
                    bank = 2 + mi
                    for k4 in range(4):
                        mm = lambda e, n=n, m=m, k4=k4, b2_=b2_, bank=bank: e.matmul(
                            ps[bank][:, :],
                            w2b[b2_][:, k4 * 1024 + m * 128: k4 * 1024 + m * 128 + 128],
                            gall[:, (n * 4 + k4) * 512: (n * 4 + k4 + 1) * 512],
                            start=(n == 0 and k4 == 0), stop=(n == 7 and k4 == 3))
                        if mi == 3 and k4 == 3:
                            em.te(lambda e, mm=mm: mm(e).then_inc(c.te.sem, 1))
                        else:
                            em.te(mm)
                te_n_done[n] = c.te.inc()
            for mi, m in enumerate(ms):
                bank = 2 + mi
                em.ve(lambda e, v=te_n_done[7]: e.wait_ge(c.te.sem, v))
                em.ve(lambda e, m=m, bank=bank: e.tensor_tensor(
                    tmp[:], ps[bank][:, :], yc[:, m * 512:(m + 1) * 512], ALU.add))
                em.ve(lambda e: e.tensor_scalar(sb["junk"][:], sb["junk"][:], 1.0, None, ALU.mult))
                em.ve(lambda e, m=m: e.tensor_tensor(
                    x2s[:, m * 512:(m + 1) * 512], tmp[:],
                    x2s[:, m * 512:(m + 1) * 512], ALU.add))
                em.ve(lambda e: e.engine_nop().then_inc(c.ve.sem, 1))
                c.ve.inc()
        ve_x3 = c.ve.n
        em.sy(lambda e, v=ve_x3: e.wait_ge(c.ve.sem, v))
        em.sy(lambda e, t0=t0: e.dma_start(
            out=xo[:, :, t0:t0 + 512],
            in_=_r(x2s[:], "p (k t) -> p k t", t=512)).then_inc(c.st.sem, 16))
        c.st.inc(16)
        em.sy(lambda e, v=c.te.n: e.wait_ge(c.te.sem, v))
        em.sy(lambda e, v=c.ve.n: e.wait_ge(c.ve.sem, v))

    for ci in range(NCHUNK):
        chunk(ci)
    c.barrier()



def build_program(n_stages=99, dbg=(), only=None):
    nc = bass.Bass("TRN2", target_bir_lowering=False)
    dram = {}

    def din(name, shape, dtype=F32):
        dram[name] = nc.dram_tensor(name, shape, dtype, kind="ExternalInput").ap()

    def dint(name, shape, dtype=F32):
        kind = "ExternalOutput" if name in dbg else "Internal"
        dram[name] = nc.dram_tensor(name, shape, dtype, kind=kind).ap()

    din("xT_in", [D, T])
    din("aa_wq", [D, DI_A]); din("aa_wk", [D, DI_A]); din("aa_wv", [D, DI_A])
    din("aa_wo", [DI_A, D])
    din("aa_bq_col", [128, 16]); din("aa_bk_col", [128, 16])
    din("aa_nbq_col", [128, 16]); din("aa_nbk_col", [128, 16])
    din("aa_bv_row", [1, DI_A], BF16); din("aa_bo_row", [1, D], BF16)
    din("gate_wT", [D, NE])
    din("guwT", [NE, D, D]); din("dwT", [NE, 512, D])
    din("ab_norm_in_w", [128, 8]); din("ab_conv_w", [128, 32]); din("ab_conv_b", [128, 8])
    din("ab_q_w", [D, D]); din("ab_k_w", [D, D]); din("ab_v_w", [D, D])
    din("ab_alpha_w", [D, D]); din("ab_alpha_b", [128, 8])
    din("ab_beta_w", [D, H]); din("ab_beta_b", [1, H])
    din("ab_head_norm_row", [1, D]); din("ab_head_norm_bc", [128, D], BF16)
    din("ab_u1_w", [D, 64]); din("ab_u1_b", [64, 1])
    din("ab_u2_w", [64, D]); din("ab_u2_b", [128, 8])
    din("ab_out_w", [D, D]); din("ab_ff_norm_w", [128, 8])
    din("ff_w1", [D, 4096]); din("ff_w3", [D, 4096]); din("ff_w2", [4096, D])
    din("ident_bf", [128, 128], BF16); din("ident_f32", [128, 128])
    din("ones_col_bf", [128, 1], BF16); din("ones_row_bf", [1, 512], BF16)
    din("ones_col_f32", [128, 1]); din("ones_row_f32", [1, 128])
    din("mask_a", [128, 4 * 512], BF16)
    din("mask_sl", [128, 128], BF16); din("mask_iu", [128, 128], BF16)
    din("const_row", [128, 8])
    dram["outT"] = nc.dram_tensor("outT", [D, T], F32, kind="ExternalOutput").ap()
    dint("x1T", [D, T]); dint("x2T", [D, T]); dint("x3T", [D, T])
    dint("aa_q", [DI_A, T], BF16); dint("aa_k", [DI_A, T], BF16)
    dint("aa_v", [T, DI_A], BF16); dint("aa_y", [DI_A, T], BF16)
    if "dbg_b2" in dbg:
        dram["d3_yc"] = nc.dram_tensor("d3_yc", [128, 4096], F32, kind="ExternalOutput").ap()
        dram["d3_y2"] = nc.dram_tensor("d3_y2", [128, 4096], BF16, kind="ExternalOutput").ap()
        dram["d3_gall"] = nc.dram_tensor("d3_gall", [128, 32 * 512], BF16, kind="ExternalOutput").ap()
        for nm, shp, dt in (("d2_lam", [64, 2048], F32), ("d2_e1", [64, 2048], F32),
                            ("d2_fk", [64, 2048], BF16), ("d2_gk", [64, 2048], BF16),
                            ("d2_N", [128, 2048], BF16), ("d2_AqT", [128, 2048], BF16),
                            ("d2_NT", [128, 2048], BF16), ("d2_X0", [128, 1024], BF16),
                            ("d2_X", [128, 1024], BF16), ("d2_y", [128, 1024], F32),
                            ("d2_U", [128, 1024], BF16), ("d2_S", [64, 1024], BF16)):
            dram[nm] = nc.dram_tensor(nm, shp, dt, kind="ExternalOutput").ap()
    if "dbg_z" in dbg:
        dram["dbg_z"] = nc.dram_tensor("dbg_z", [128, 16], F32, kind="ExternalOutput").ap()
        dram["dbg_den"] = nc.dram_tensor("dbg_den", [1, 512], F32, kind="ExternalOutput").ap()
        dram["dbg_rdB"] = nc.dram_tensor("dbg_rdB", [128, 512], F32, kind="ExternalOutput").ap()
    dint("guw_bf", [NE, D, D], BF16); dint("dw_bf", [NE, 512, D], BF16)
    dint("moe_xn", [D, T], BF16); dint("moe_comb", [T, NE])
    dint("ab_xln", [D, T]); dint("ab_q", [D, T], BF16); dint("ab_k", [D, T], BF16)
    dint("ab_v", [T, D], BF16); dint("ab_la", [D, T]); dint("ab_beta", [T, H])
    dint("ab_gate", [D, T], BF16); dint("ab_y", [T, D], BF16)
    dint("ff_w1_bf", [D, 4096], BF16); dint("ff_w3_bf", [D, 4096], BF16)
    dint("ff_w2_bf", [4096, D], BF16)

    with ExitStack() as top:
        c = Ctx(nc, top)
        E = top.enter_context
        sb = {
            "ident": E(nc.sbuf_tensor("g_ident", [128, 128], BF16)),
            "ident_f32": E(nc.sbuf_tensor("g_ident_f32", [128, 128], F32)),
            "ones_col_bf": E(nc.sbuf_tensor("g_ones_col_bf", [128, 1], BF16)),
            "ones_row_bf": E(nc.sbuf_tensor("g_ones_row_bf", [1, 512], BF16)),
            "ones_col_f32": E(nc.sbuf_tensor("g_ones_col_f32", [128, 1], F32)),
            "ones_row_f32": E(nc.sbuf_tensor("g_ones_row_f32", [1, 128], F32)),
            "mask_a": E(nc.sbuf_tensor("g_mask_a", [128, 4 * 512], BF16)),
            "consts": E(nc.sbuf_tensor("g_consts", [128, 8], F32)),
            "junk": E(nc.sbuf_tensor("g_junk", [128, 16], F32)),
        }
        for j, v in enumerate(CONST_VALS):
            nc.const_aps.register(F32, v, sb["consts"][:, j:j + 1])
        emit_prologue(c, dram, sb)
        if only == "attnb":
            stages = [
                lambda c, d, s2, st: emit_attn_b_p1(c, d, s2, st, "xT_in"),
                lambda c, d, s2, st: emit_attn_b_p2(c, d, s2, st),
                lambda c, d, s2, st: emit_attn_b_p3(c, d, s2, st, "xT_in", "x3T"),
            ]
        elif only == "b2":
            stages = [lambda c, d, s2, st: emit_attn_b_p2(c, d, s2, st)]
        else:
            stages = [
            emit_attn_a_p1, emit_attn_a_p2, emit_attn_a_p3,
            lambda c, d, s2, st: emit_moe_gate(c, d, s2, st, "x1T", "m1"),
            lambda c, d, s2, st: emit_moe_exp(c, d, s2, st, "x1T", "x2T", "m1"),
            lambda c, d, s2, st: emit_attn_b_p1(c, d, s2, st, "x2T"),
            lambda c, d, s2, st: emit_attn_b_p2(c, d, s2, st),
            lambda c, d, s2, st: emit_attn_b_p3(c, d, s2, st, "x2T", "x3T"),
            lambda c, d, s2, st: emit_moe_gate(c, d, s2, st, "x3T", "m2"),
            lambda c, d, s2, st: emit_moe_exp(c, d, s2, st, "x3T", "outT", "m2"),
            ]
        stage_names = ["a1", "a2", "a3", "mg1", "me1", "b1", "b2", "b3", "mg2", "me2"]
        if only == "attnb":
            stage_names = ["b1", "b2", "b3"]
        elif only == "b2":
            stage_names = ["b2"]
        for i, fn in enumerate(stages):
            if i >= n_stages:
                break
            c.mark(stage_names[i] if i < len(stage_names) else f"s{i}")
            with ExitStack() as st:
                fn(c, dram, sb, st)
        block = E(nc.Block())

        def run_queue(e, eng, fns):
            bounds = [(idx[eng], nm) for nm, idx in c.stage_marks]
            bi = 0
            cur = None
            for i, f in enumerate(fns):
                while bi < len(bounds) and bounds[bi][0] == i:
                    if cur is not None:
                        nc.leave_named_scope(*cur, False)
                    sid, _ = nc.enter_named_scope(bounds[bi][1], False)
                    cur = (bounds[bi][1], sid)
                    bi += 1
                f(e)
            if cur is not None:
                nc.leave_named_scope(*cur, False)

        for eng in Emit.ENGINES:
            fns = c.em.q[eng]
            if fns:
                getattr(block, eng)(lambda e, eng=eng, fns=fns: run_queue(e, eng, fns))
    return nc



def _bf(x):
    return np.ascontiguousarray(x).astype(ml_dtypes.bfloat16)


K_CONST = CONST_VALS


def _prep_inputs(inputs):
    g = {}
    f32 = lambda x: np.ascontiguousarray(np.asarray(x), dtype=np.float32)
    tr = lambda x: np.ascontiguousarray(np.asarray(x).T)
    g["aa_wq"] = f32(inputs["L0_attn_a_Wq"]); g["aa_wk"] = f32(inputs["L0_attn_a_Wk"])
    g["aa_wv"] = f32(inputs["L0_attn_a_Wv"]); g["aa_wo"] = f32(inputs["L0_attn_a_Wo"])
    bq = f32(inputs["L0_attn_a_bq"]); bk = f32(inputs["L0_attn_a_bk"])
    col = lambda b: np.ascontiguousarray(b.reshape(-1, 128).T)
    g["aa_bq_col"] = col(bq); g["aa_bk_col"] = col(bk)
    g["aa_nbq_col"] = col(-bq); g["aa_nbk_col"] = col(-bk)
    g["aa_bv_row"] = _bf(f32(inputs["L0_attn_a_bv"]).reshape(1, -1))
    g["aa_bo_row"] = _bf(f32(inputs["L0_attn_a_bo"]).reshape(1, -1))
    g["gate_wT"] = tr(f32(inputs["L0_ffn_gate_w"]))
    g["guwT"] = np.ascontiguousarray(np.asarray(inputs["L0_ffn_gate_up_w"], dtype=np.float32).transpose(0, 2, 1))
    g["dwT"] = np.ascontiguousarray(np.asarray(inputs["L0_ffn_down_w"], dtype=np.float32).transpose(0, 2, 1))
    g["ab_norm_in_w"] = col(f32(inputs["L0_attn_b_norm_in_w"]))
    cw = f32(inputs["L0_attn_b_conv_w"]).reshape(1024, 4).reshape(8, 128, 4)
    g["ab_conv_w"] = np.ascontiguousarray(cw.transpose(1, 0, 2).reshape(128, 32))
    g["ab_conv_b"] = col(f32(inputs["L0_attn_b_conv_b"]))
    g["ab_q_w"] = f32(inputs["L0_attn_b_q_w"]); g["ab_k_w"] = f32(inputs["L0_attn_b_k_w"])
    g["ab_v_w"] = f32(inputs["L0_attn_b_v_w"]); g["ab_alpha_w"] = f32(inputs["L0_attn_b_alpha_w"])
    g["ab_alpha_b"] = col(f32(inputs["L0_attn_b_alpha_b"]))
    g["ab_beta_w"] = f32(inputs["L0_attn_b_beta_w"])
    g["ab_beta_b"] = f32(inputs["L0_attn_b_beta_b"]).reshape(1, -1)
    g["ab_head_norm_row"] = f32(inputs["L0_attn_b_head_norm_w"]).reshape(1, -1)
    g["ab_head_norm_bc"] = _bf(np.tile(g["ab_head_norm_row"], (128, 1)))
    g["ab_u1_w"] = f32(inputs["L0_attn_b_u1_w"])
    g["ab_u1_b"] = f32(inputs["L0_attn_b_u1_b"]).reshape(-1, 1)
    g["ab_u2_w"] = f32(inputs["L0_attn_b_u2_w"])
    g["ab_u2_b"] = col(f32(inputs["L0_attn_b_u2_b"]))
    g["ab_out_w"] = f32(inputs["L0_attn_b_out_w"])
    g["ab_ff_norm_w"] = col(f32(inputs["L0_attn_b_ff_norm_w"]))
    g["ff_w1"] = f32(inputs["L0_attn_b_ff_w1"]); g["ff_w3"] = f32(inputs["L0_attn_b_ff_w3"])
    g["ff_w2"] = f32(inputs["L0_attn_b_ff_w2"])
    g["ident_bf"] = _bf(np.eye(128, dtype=np.float32))
    g["ident_f32"] = np.eye(128, dtype=np.float32)
    g["ones_col_bf"] = _bf(np.ones((128, 1), np.float32))
    g["ones_row_bf"] = _bf(np.ones((1, 512), np.float32))
    g["ones_col_f32"] = np.ones((128, 1), np.float32)
    g["const_row"] = np.tile(np.array(K_CONST, np.float32), (128, 1))
    g["ones_row_f32"] = np.ones((1, 128), np.float32)
    mask = np.zeros((128, 4, 512), np.float32)
    for m in range(4):
        for i in range(128):
            mask[i, m, m * 128 + i:] = 1.0
    g["mask_a"] = _bf(mask.reshape(128, 4 * 512))
    idx = np.arange(128)
    g["mask_sl"] = _bf((idx[None, :] < idx[:, None]).astype(np.float32))
    g["mask_iu"] = _bf((idx[:, None] <= idx[None, :]).astype(np.float32))
    return g


def kernel(**inputs):
    g = _prep_inputs(inputs)
    x = np.asarray(inputs["L0_attn_a_x"], dtype=np.float32)
    nc = build_program()
    in_maps = []
    for b in range(8):
        m = dict(g)
        m["xT_in"] = np.ascontiguousarray(x[b].T)
        in_maps.append(m)
    res = run_bass_kernel_spmd(nc, in_maps, core_ids=list(range(8)))
    out = np.stack([np.ascontiguousarray(r["outT"].T) for r in res.results])
    return out
